# version 02
# baseline (speedup 1.0000x reference)
import contextlib
import numpy as np
import concourse.bass as bass
import concourse.mybir as mybir
from concourse.bass_utils import run_bass_kernel_spmd
from concourse.bass_types import DynSlice
from concourse.bass_core_rust import IndirectOffsetOnAxis

F32 = mybir.dt.float32
I32 = mybir.dt.int32
AL = mybir.AluOpType
AF = mybir.ActivationFunctionType
AX = mybir.AxisListType

T = 8192
D = 1024
E = 16
HALF = 4096
CC = 64
CCH = 32
CAP = 640
NSL = CAP // 128
WID = D + E + 1
SENT = 1 << 20
LN_EPS = 1e-5
WKV_EPS = 1e-8
NB = T // 512
NBO = HALF // 512
BW = 512
FEXP = 2048
FSH = 2048
FCM = 4096
TH_ITERS = 38
TH_PAD = 3e-4


class Prog:

    def __init__(self, debug_taps=()):
        self.nc = bass.Bass("TRN2", target_bir_lowering=False, debug=False,
                            detect_race_conditions=False)
        self.dram = {}
        self.taps = {}
        self.debug_taps = set(debug_taps)
        self.stack = contextlib.ExitStack()
        self.semv = {}

    def din(self, name, shape, dtype=F32):
        ap = self.nc.dram_tensor(name, shape, dtype, kind="ExternalInput").ap()
        self.dram[name] = ap
        return ap

    def dout(self, name, shape, dtype=F32):
        ap = self.nc.dram_tensor(name, shape, dtype, kind="ExternalOutput").ap()
        self.dram[name] = ap
        return ap

    def dscr(self, name, shape, dtype=F32):
        ap = self.nc.dram_tensor(name, shape, dtype, kind="Internal").ap()
        self.dram[name] = ap
        return ap

    def sb(self, name, shape, dtype=F32, stack=None):
        return (stack or self.stack).enter_context(
            self.nc.sbuf_tensor(name, shape, dtype))

    def ps(self, name, shape, dtype=F32, stack=None):
        return (stack or self.stack).enter_context(
            self.nc.psum_tensor(name, shape, dtype))

    def sem(self, name):
        h = self.stack.enter_context(self.nc.semaphore(name))
        self.semv[name] = 0
        return h

    def inc(self, name, instr, n=1):
        instr.then_inc(getattr(self, name), n)
        self.semv[name] += n
        return self.semv[name]

    def dinc(self, name, instr):
        return self.inc(name, instr, 16)

    def wait(self, eng, name, v=None):
        eng.wait_ge(getattr(self, name), self.semv[name] if v is None else v)

    def vf(self, instr):
        self.inc("s_ve", instr)
        self.nc.vector.wait_ge(self.s_ve, self.semv["s_ve"])

    def dma_open(self, eng):
        j = self._rr
        self._rr = (self._rr + 1) % 8
        if self._dtarget[j] > 0:
            eng.wait_ge(self.dsem[j], self._dtarget[j])
        return [j, None]

    def dma_put(self, tok, instr):
        j = tok[0]
        instr.then_inc(self.dsem[j], 16)
        self.semv[f"s_d{j}"] += 16

    def dma_close(self, tok):
        j = tok[0]
        tok[1] = self.semv[f"s_d{j}"]
        self._dtarget[j] = tok[1]
        return tok

    def dma_batch(self, eng, emits):
        tok = self.dma_open(eng)
        for fn in emits:
            self.dma_put(tok, fn())
        return self.dma_close(tok)

    def dma_wait(self, eng, tok):
        eng.wait_ge(self.dsem[tok[0]], tok[1])

    def dma_wait_all(self, eng):
        for j in range(8):
            if self._dtarget[j] > 0:
                eng.wait_ge(self.dsem[j], self._dtarget[j])

    def tap(self, name, src_tile, shape=None):
        if name not in self.debug_taps:
            return
        sc = self.nc.scalar
        self.wait(sc, "s_ve")
        self.wait(sc, "s_ac")
        self.wait(sc, "s_pe")
        shape = shape or list(src_tile.shape)
        ap = self.dout("tap_" + name, shape)
        self.taps[name] = ap
        tok = self.dma_open(sc)
        self.dma_put(tok, sc.dma_start(out=ap, in_=src_tile))
        self.dma_close(tok)
        nc = self.nc
        for eng in (nc.vector, nc.scalar, nc.tensor, nc.gpsimd, nc.sync):
            self.dma_wait(eng, tok)

    def barrier(self, skip_gp=False):
        nc = self.nc
        self.dma_wait_all(nc.sync)
        self.dma_wait_all(nc.scalar)
        engines = (nc.sync, nc.tensor, nc.vector, nc.scalar)
        if not skip_gp:
            self.wait(nc.gpsimd, "s_gp")
            engines = engines + (nc.gpsimd,)
        for eng in engines:
            self.inc("s_bar", eng.nop())
        tgt = self.semv["s_bar"]
        for eng in engines:
            eng.wait_ge(self.s_bar, tgt)


def build(debug_taps=()):
    p = Prog(debug_taps)
    nc = p.nc

    p.din("x_fm", [D, T])
    p.din("x_own_tm", [HALF, D])
    p.din("x_own_fm", [D, HALF])
    for nm in ("wr_t", "ww_t", "wk_t", "wv_t", "wo_t", "wrec_t"):
        p.din(nm, [D, D])
    p.din("wkey_t", [D, FCM])
    p.din("wval_t", [FCM, D])
    p.din("wsg_t", [D, FSH])
    p.din("wsu_t", [D, FSH])
    p.din("wsd_t", [FSH, D])
    p.din("wg_t", [E, D, FEXP])
    p.din("wu_t", [E, D, FEXP])
    p.din("wd_t", [E, FEXP, D])
    p.din("gate_t", [D, E])
    for nm in ("ln1g", "ln1b", "ln2g", "ln2b", "td"):
        p.din(nm, [128, 8])
    p.din("ident", [128, 128])
    p.din("tri", [128, 128])
    p.din("maskc", [128, CC])
    p.din("tokidG", [128, CC])
    p.din("selinit", [128, 2 * E])
    p.din("hofft1", [128, 1])
    p.din("hofft2", [128, 1])
    p.din("onesc", [128, 1])
    p.din("onesr", [1, 128])
    p.din("hseg", [1, 1], I32)

    p.dscr("e_sp", [8, 128, T])
    p.dscr("ev_sp", [8, 128, T])
    p.dscr("r_sp", [8, 128, T])
    p.dscr("wkv_sp", [8, 128, T])
    p.dscr("xn_sp", [8, 128, HALF])
    p.dscr("xn3_sp", [8, 128, HALF])
    p.dscr("y1_sp", [8, 128, HALF])
    p.dscr("sh_sp", [8, 128, HALF])
    p.dscr("vvh_sp", [8, 128, HALF])
    for e in range(E):
        p.dscr(f"xtok{e}", [CAP, WID])
    p.dscr("wid_sp", [128, CC * (E + 1)])
    p.dscr("idxe_sp", [128, CC * E], I32)
    p.dscr("eoT_sp", [E * NSL * 128, D])
    p.dscr("idT_sp", [E * NSL, 128], I32)
    p.dscr("macc", [HALF + 1, D])
    p.dout("out", [HALF, D])

    for s in ("s_gp", "s_pe", "s_ve", "s_ac", "s_bar"):
        setattr(p, s, p.sem(s))
    p.dsem = [p.sem(f"s_d{i}") for i in range(8)]
    p._rr = 0
    p._dtarget = [0] * 8

    with nc.Block() as _block:
        emit(p)

    p.stack.close()
    return p


def emit(p):
    nc = p.nc
    sy, te, ve, ac, gp = nc.sync, nc.tensor, nc.vector, nc.scalar, nc.gpsimd

    ident = p.sb("c_ident", [128, 128])
    onesc = p.sb("c_onesc", [128, 1])
    onesr = p.sb("c_onesr", [1, 128])
    lnp = p.sb("c_lnp", [128, 4 * 8])
    decay = p.sb("c_decay", [128, 8])
    hsegt = p.sb("c_hseg", [1, 1], I32)

    dram = p.dram

    tok0 = p.dma_open(sy)
    for nm, dst in (("ident", ident), ("onesc", onesc),
                    ("onesr", onesr), ("hseg", hsegt)):
        p.dma_put(tok0, sy.dma_start(out=dst[:], in_=dram[nm]))
    for i, nm in enumerate(("ln1g", "ln1b", "ln2g", "ln2b")):
        p.dma_put(tok0, sy.dma_start(out=lnp[:, i * 8:(i + 1) * 8], in_=dram[nm]))
    p.dma_put(tok0, sy.dma_start(out=decay[:], in_=dram["td"]))
    p.dma_close(tok0)
    p.dma_wait(ac, tok0)
    p.inc("s_ac", ac.activation(decay[:], decay[:], AF.Exp))

    p.dma_wait(sy, tok0)
    hv = sy.value_load(hsegt[0:1, 0:1])
    p.dma_wait(gp, tok0)
    hvg = gp.value_load(hsegt[0:1, 0:1])

    with contextlib.ExitStack() as sel_st:
        tri = p.sb("m_tri", [128, 128], stack=sel_st)
        gatet = p.sb("m_gate", [128, 8 * E], stack=sel_st)
        maskc = p.sb("m_maskc", [128, CC], stack=sel_st)
        tokidG = p.sb("m_tokidG", [128, CC], stack=sel_st)
        scores3 = p.sb("m_scores3", [128, E * CC], stack=sel_st)
        theta = p.sb("m_theta", [128, E], stack=sel_st)
        gsumP = p.sb("m_gsumP", [128, E], stack=sel_st)
        idxL = p.sb("m_idxL", [128, CC * E], I32, stack=sel_st)
        zeros = p.sb("m_zeros", [128, WID], stack=sel_st)

        tokc = p.dma_open(sy)
        for nm, dst in (("tri", tri), ("maskc", maskc), ("tokidG", tokidG)):
            p.dma_put(tokc, sy.dma_start(out=dst[:], in_=dram[nm]))
        p.dma_put(tokc, sy.dma_start(
            out=gatet[:].rearrange("p (k e) -> p k e", k=8),
            in_=dram["gate_t"].rearrange("(k p) e -> p k e", p=128)))
        p.dma_close(tokc)
        p.inc("s_ve", ve.memset(zeros[:], 0.0))

        p.wait(gp, "s_ve")
        mr = dram["macc"][0:HALF].rearrange("(n p) d -> n p d", p=128)
        for i in range(HALF // 128):
            p.dinc("s_gp", gp.dma_start(out=mr[i], in_=zeros[:, :D]))
        p.dinc("s_gp", gp.dma_start(out=dram["macc"][HALF:HALF + 1, :],
                                    in_=zeros[0:1, 0:D]))
        for e in range(E):
            xr = dram[f"xtok{e}"].rearrange("(n p) w -> n p w", p=128)
            for i in range(NSL):
                p.dinc("s_gp", gp.dma_start(out=xr[i], in_=zeros[:, :WID]))

        p.barrier(skip_gp=True)

        with contextlib.ExitStack() as st:
            ph1(p, st, ident=ident, onesc=onesc, onesr=onesr, gatet=gatet,
                scores3=scores3, gsumP=gsumP)
            p.barrier(skip_gp=True)

        with contextlib.ExitStack() as st:
            selection(p, st, ident=ident, tri=tri, onesc=onesc, onesr=onesr,
                      maskc=maskc, scores3=scores3, theta=theta, gsumP=gsumP,
                      tokidG=tokidG, idxL=idxL)
            p.barrier()

    with contextlib.ExitStack() as st:
        phMprep(p, st, hvg=hvg)
        ph2(p, st, onesr=onesr, onesc=onesc, lnp=lnp)
        p.barrier()

    with contextlib.ExitStack() as st:
        ph3(p, st, decay=decay)
        p.barrier()

    with contextlib.ExitStack() as st:
        phM(p, st, ident=ident)
        p.barrier()

    with contextlib.ExitStack() as st:
        eoFori(p, st)
        ph4a(p, st, onesr=onesr, onesc=onesc, lnp=lnp, hv=hv)
        p.barrier()
    with contextlib.ExitStack() as st:
        ph4sh(p, st)
        p.barrier()
    with contextlib.ExitStack() as st:
        ph4b(p, st)
        p.barrier()
    with contextlib.ExitStack() as st:
        ph4c(p, st, ident=ident)
        p.barrier()


def ln_rows(p, stats_ps, a_slice, b_slice, scr):
    nc = p.nc
    ve, ac = nc.vector, nc.scalar
    W = a_slice.shape[1]
    ve.tensor_scalar_mul(a_slice, stats_ps[:, 0:W], 1.0 / D)
    ve.tensor_scalar_mul(b_slice, stats_ps[:, W:2 * W], 1.0 / D)
    ve.tensor_mul(scr, a_slice, a_slice)
    ve.tensor_sub(b_slice, b_slice, scr[:])
    p.inc("s_ve", ve.tensor_scalar_add(b_slice, b_slice, LN_EPS))
    p.wait(ac, "s_ve")
    p.inc("s_ac", ac.activation(b_slice, b_slice, AF.Sqrt))
    p.wait(ve, "s_ac")
    p.inc("s_ve", ve.reciprocal(b_slice, b_slice))
    p.wait(ve, "s_ve")
    ve.tensor_mul(scr, a_slice, b_slice)
    ve.tensor_copy(a_slice, b_slice)
    p.inc("s_ve", ve.tensor_scalar_mul(b_slice, scr, -1.0))


PH1_PARTS = {"blocks", "gsum", "staging", "transpose", "logits"}


def ph1(p, st, *, ident, onesc, onesr, gatet, scores3, gsumP):
    nc = p.nc
    sy, te, ve, ac = nc.sync, nc.tensor, nc.vector, nc.scalar
    dram = p.dram

    xb = [p.sb(f"p1_xb{i}", [128, 8 * BW], stack=st) for i in range(2)]
    xsq = p.sb("p1_xsq", [128, 8 * BW], stack=st)
    arow = p.sb("p1_arow", [1, T], stack=st)
    brow = p.sb("p1_brow", [1, T], stack=st)
    st18 = p.sb("p1_st18", [18, T], stack=st)
    srow = p.sb("p1_srow", [1, 2 * BW], stack=st)
    scr = p.sb("p1_scr", [1, BW], stack=st)
    a3 = p.sb("p1_a3", [128, CC], stack=st)
    b3 = p.sb("p1_b3", [128, CC], stack=st)
    gx3 = p.sb("p1_gx3", [128, E * CC], stack=st)
    grow = p.sb("p1_grow", [1, E], stack=st)
    l3 = p.sb("p1_l3", [128, E * CC], stack=st)
    t3 = p.sb("p1_t3", [128, E * CC], stack=st)
    sume = p.sb("p1_sume", [128, 8 * CC], stack=st)

    gxp = p.ps("p1_gxp", [16, BW], stack=st)
    stp = p.ps("p1_stp", [1, 2 * BW], stack=st)
    trp = [p.ps(f"p1_trp{i}", [128, 32], stack=st) for i in range(2)]
    bcp = p.ps("p1_bcp", [128, E], stack=st)
    gsp = p.ps("p1_gsp", [1, E], stack=st)

    xr = dram["x_fm"].rearrange("(k p) t -> k p t", p=128)

    ld_mark = []
    ac_mark = []
    pe_mark = []
    ve_mark = []

    for blk in range(NB if "blocks" in PH1_PARTS else 0):
        if blk >= 2:
            sy.wait_ge(p.s_pe, pe_mark[blk - 2])
        tok = p.dma_open(sy)
        for k in range(8):
            p.dma_put(tok, sy.dma_start(
                out=xb[blk % 2][:, k * BW:(k + 1) * BW],
                in_=xr[k, :, blk * BW:(blk + 1) * BW]))
        ld_mark.append(p.dma_close(tok))

        p.dma_wait(ac, ld_mark[blk])
        if blk >= 1:
            ac.wait_ge(p.s_pe, pe_mark[blk - 1])
        p.inc("s_ac", ac.activation(xsq[:], xb[blk % 2][:], AF.Square))
        ac_mark.append(p.semv["s_ac"])

        te.wait_ge(p.s_ac, ac_mark[blk])
        if blk >= 1:
            te.wait_ge(p.s_ve, ve_mark[blk - 1])
        for k in range(8):
            kw = slice(k * BW, (k + 1) * BW)
            te.matmul(gxp[:], gatet[:, k * E:(k + 1) * E], xb[blk % 2][:, kw],
                      start=(k == 0), stop=(k == 7))
            te.matmul(stp[:, 0:BW], onesc[:], xb[blk % 2][:, kw],
                      start=(k == 0), stop=(k == 7))
            i = te.matmul(stp[:, BW:2 * BW], onesc[:], xsq[:, kw],
                          start=(k == 0), stop=(k == 7))
        p.inc("s_pe", i)
        pe_mark.append(p.semv["s_pe"])

        ve.wait_ge(p.s_pe, pe_mark[blk])
        ve.tensor_copy(st18[0:16, blk * BW:(blk + 1) * BW], gxp[:])
        ve.tensor_copy(srow[:], stp[:])
        ln_rows(p, srow,
                arow[:, blk * BW:(blk + 1) * BW],
                brow[:, blk * BW:(blk + 1) * BW], scr[:])
        ve_mark.append(p.semv["s_ve"])

    p.tap("arow", arow[:])
    p.tap("brow", brow[:])

    if "gsum" not in PH1_PARTS:
        return
    p.dma_wait_all(te)
    for k in range(8):
        i = te.matmul(gsp[:], onesc[:], gatet[:, k * E:(k + 1) * E],
                      start=(k == 0), stop=(k == 7))
    p.inc("s_pe", i)
    p.wait(ve, "s_pe")
    p.inc("s_ve", ve.tensor_copy(grow[:], gsp[:]))
    p.wait(te, "s_ve")
    i = te.matmul(bcp[:], onesr[:], grow[:], start=True, stop=True)
    p.inc("s_pe", i)
    p.wait(ve, "s_pe")
    p.inc("s_ve", ve.tensor_copy(gsumP[:], bcp[:]))

    if "staging" not in PH1_PARTS:
        return
    p.wait(sy, "s_ve")
    stok = p.dma_open(sy)
    p.dma_put(stok, sy.dma_start(out=st18[16:17, :], in_=arow[:]))
    p.dma_put(stok, sy.dma_start(out=st18[17:18, :], in_=brow[:]))
    stg_ld = p.dma_close(stok)

    if "transpose" not in PH1_PARTS:
        return
    p.dma_wait(te, stg_ld)
    ve_evac_base = p.semv["s_ve"]
    for tb in range(CC):
        if tb >= 2:
            te.wait_ge(p.s_ve, ve_evac_base + tb - 1)
        i = te.transpose(trp[tb % 2][:, 0:18],
                         st18[:, tb * 128:(tb + 1) * 128], ident[0:18, 0:18])
        p.inc("s_pe", i)
        ve.wait_ge(p.s_pe, p.semv["s_pe"])
        gx3v = gx3[:].rearrange("p (e c) -> p e c", c=CC)
        ve.tensor_copy(gx3v[:, :, tb:tb + 1],
                       trp[tb % 2][:, 0:16].rearrange("p (e c) -> p e c", c=1))
        ve.tensor_copy(a3[:, tb:tb + 1], trp[tb % 2][:, 16:17])
        p.inc("s_ve", ve.tensor_copy(b3[:, tb:tb + 1], trp[tb % 2][:, 17:18]))

    if "logits" not in PH1_PARTS:
        return
    a3bc = a3[:].unsqueeze(1).broadcast_to([128, E, CC])
    b3bc = b3[:].unsqueeze(1).broadcast_to([128, E, CC])
    gsbc = gsumP[:].unsqueeze(2).broadcast_to([128, E, CC])
    l3v = l3[:].rearrange("p (e c) -> p e c", c=CC)
    t3v = t3[:].rearrange("p (e c) -> p e c", c=CC)
    p.wait(ve, "s_ve")
    ve.tensor_mul(l3v, gx3[:].rearrange("p (e c) -> p e c", c=CC), a3bc)
    ve.tensor_mul(t3v, b3bc, gsbc)
    p.inc("s_ve", ve.tensor_add(l3[:], l3[:], t3[:]))
    p.tap("logits3", l3[:])
    p.tap("gx3", gx3[:])
    p.tap("a3", a3[:])
    p.tap("b3", b3[:])
    p.tap("gsumP", gsumP[:])
    p.wait(ac, "s_ve")
    p.inc("s_ac", ac.activation(l3[:], l3[:], AF.Exp))
    p.wait(ve, "s_ac")
    l3e = l3[:].rearrange("p (e c) -> p e c", c=CC)
    s8 = sume[:].rearrange("p (e c) -> p e c", c=CC)
    ve.tensor_add(s8[:, 0:8, :], l3e[:, 0:8, :], l3e[:, 8:16, :])
    ve.tensor_add(s8[:, 0:4, :], s8[:, 0:4, :], s8[:, 4:8, :])
    ve.tensor_add(s8[:, 0:2, :], s8[:, 0:2, :], s8[:, 2:4, :])
    p.vf(ve.tensor_add(s8[:, 0:1, :], s8[:, 0:1, :], s8[:, 1:2, :]))
    p.inc("s_ve", ve.reciprocal(sume[:, 0:CC], sume[:, 0:CC]))
    p.wait(ve, "s_ve")
    p.tap("sumr", sume[:, 0:CC])
    rbc = sume[:, 0:CC].unsqueeze(1).broadcast_to([128, E, CC])
    p.inc("s_ve", ve.tensor_mul(
        scores3[:].rearrange("p (e c) -> p e c", c=CC), l3e, rbc))
    p.tap("scores3", scores3[:])


def selection(p, st, *, ident, tri, onesc, onesr, maskc, scores3, theta,
              gsumP, tokidG, idxL):
    dram = p.dram
    """Binary-search theta per expert; build w (e-minor) + scatter slots."""
    nc = p.nc
    sy, te, ve, ac = nc.sync, nc.tensor, nc.vector, nc.scalar

    lohit = p.sb("sl_lohi", [128, 2 * E], stack=st)
    lo = lohit[:, 0:E]
    hi = lohit[:, E:2 * E]
    mid = p.sb("sl_mid", [128, E], stack=st)
    ge = p.sb("sl_ge", [128, E * CC], stack=st)
    red = p.sb("sl_red", [128, E], stack=st)
    crow = p.sb("sl_crow", [1, E], stack=st)
    cntP = p.sb("sl_cntP", [128, E], stack=st)
    cntN = p.sb("sl_cntN", [128, E], stack=st)
    tmp = p.sb("sl_tmp", [128, E], stack=st)
    tmp2 = p.sb("sl_tmp2", [128, E], stack=st)
    w3 = p.sb("sl_w3", [128, E * CC], stack=st)
    cand = p.sb("sl_cand", [128, E * CC], stack=st)
    jj = p.sb("sl_jj", [128, E * CC], stack=st)
    rtot = p.sb("sl_rtot", [128, E], stack=st)
    offp = p.sb("sl_offp", [128, E], stack=st)
    idxf = p.sb("sl_idxf", [128, E * CC], stack=st)
    idx3 = p.sb("sl_idx3", [128, E * CC], I32, stack=st)
    wm2 = p.sb("sl_wm2", [128, CC * (E + 1)], stack=st)

    cp = p.ps("sl_cp", [1, E], stack=st)
    bp = p.ps("sl_bp", [128, E], stack=st)
    op = p.ps("sl_op", [128, E], stack=st)

    s3v = scores3[:].rearrange("p (e c) -> p e c", c=CC)
    gev = ge[:].rearrange("p (e c) -> p e c", c=CC)

    itok = p.dma_open(sy)
    p.dma_put(itok, sy.dma_start(out=lohit[:], in_=dram["selinit"]))
    p.dma_close(itok)
    p.dma_wait(ve, itok)

    for it in range(TH_ITERS):
        p.vf(ve.tensor_add(mid[:], lo, hi))
        p.vf(ve.tensor_scalar_mul(mid[:], mid[:], 0.5))
        midbc = mid[:].unsqueeze(2).broadcast_to([128, E, CC])
        ve.tensor_tensor(gev, s3v, midbc, AL.is_ge)
        p.inc("s_ve", ve.tensor_reduce(
            red[:].unsqueeze(2), gev, AX.X, AL.add))
        if it == 0:
            p.tap("red0", red[:])
            p.tap("mid0", mid[:])
        p.wait(te, "s_ve")
        i = te.matmul(cp[:], onesc[:], red[:], start=True, stop=True)
        p.inc("s_pe", i)
        p.wait(ve, "s_pe")
        p.inc("s_ve", ve.tensor_copy(crow[:], cp[:]))
        p.wait(te, "s_ve")
        i = te.matmul(bp[:], onesr[:], crow[:], start=True, stop=True)
        p.inc("s_pe", i)
        p.wait(ve, "s_pe")
        if it == 0 and "cnt0" in p.debug_taps:
            p.vf(ve.tensor_copy(tmp[:], bp[:]))
            p.tap("cnt0", tmp[:])
        p.vf(ve.tensor_single_scalar(cntP[:], bp[:], 1024.0, AL.is_ge))
        p.vf(ve.tensor_sub(tmp[:], mid[:], lo))
        p.vf(ve.tensor_mul(tmp[:], tmp[:], cntP[:]))
        p.vf(ve.tensor_sub(tmp2[:], mid[:], hi))
        p.vf(ve.tensor_add(lo, lo, tmp[:]))
        p.vf(ve.tensor_scalar(cntN[:], cntP[:], -1.0, 1.0, AL.mult, AL.add))
        p.vf(ve.tensor_mul(tmp2[:], tmp2[:], cntN[:]))
        p.vf(ve.tensor_add(hi, hi, tmp2[:]))

    p.vf(ve.tensor_copy(theta[:], lo))
    p.tap("theta", theta[:])

    thbc = theta[:].unsqueeze(2).broadcast_to([128, E, CC])
    w3v = w3[:].rearrange("p (e c) -> p e c", c=CC)
    ve.tensor_tensor(w3v, s3v, thbc, AL.is_ge)
    ve.tensor_mul(w3[:], w3[:], scores3[:])
    p.vf(ve.tensor_scalar_mul(tmp[:], theta[:], 1.0 - TH_PAD))
    tpbc = tmp[:].unsqueeze(2).broadcast_to([128, E, CC])
    cndv = cand[:].rearrange("p (e c) -> p e c", c=CC)
    ve.tensor_tensor(cndv, s3v, tpbc, AL.is_ge)
    mbc = maskc[:].unsqueeze(1).broadcast_to([128, E, CC])
    p.inc("s_ve", ve.tensor_mul(cndv, cndv, mbc))
    p.tap("cand", cand[:])

    onesbc = onesc[:].broadcast_to([128, CC])
    for e in range(E):
        i = ve.tensor_tensor_scan(jj[:, e * CC:(e + 1) * CC],
                                  onesbc, cand[:, e * CC:(e + 1) * CC],
                                  0.0, AL.mult, AL.add)
        if e == E - 1:
            p.vf(i)
    p.inc("s_ve", ve.tensor_copy(rtot[:], jj[:].rearrange(
        "p (e c) -> p e c", c=CC)[:, :, CC - 1:CC]))
    p.wait(te, "s_ve")
    i = te.matmul(op[:], tri[:], rtot[:], start=True, stop=True)
    p.inc("s_pe", i)
    p.wait(ve, "s_pe")
    p.vf(ve.tensor_copy(offp[:], op[:]))
    offbc = offp[:].unsqueeze(2).broadcast_to([128, E, CC])
    jv = jj[:].rearrange("p (e c) -> p e c", c=CC)
    ve.tensor_add(jv, jv, offbc)
    ve.tensor_single_scalar(ge[:], jj[:], float(CAP), AL.is_le)
    ve.tensor_mul(ge[:], ge[:], cand[:])
    ve.tensor_scalar_add(idxf[:], jj[:], float(-1 - SENT))
    ve.tensor_mul(idxf[:], idxf[:], ge[:])
    ve.tensor_scalar_add(idxf[:], idxf[:], float(SENT))
    p.inc("s_ve", ve.tensor_copy(idx3[:], idxf[:]))
    p.tap("jj", jj[:])

    widv = wm2[:].rearrange("p (c w) -> p c w", w=E + 1)
    w3p = w3[:].rearrange("p (e c) -> p c e", c=CC, e=E)
    ve.tensor_copy(widv[:, :, 0:E], w3p)
    ve.tensor_copy(widv[:, :, E:E + 1], tokidG[:].unsqueeze(2))
    idxev = idxL[:].rearrange("p (c e) -> p c e", e=E)
    idx3p = idx3[:].rearrange("p (e c) -> p c e", c=CC, e=E)
    p.inc("s_ve", ve.tensor_copy(idxev, idx3p))
    p.wait(sy, "s_ve")
    sptok = p.dma_open(sy)
    p.dma_put(sptok, sy.dma_start(out=dram["wid_sp"], in_=wm2[:, 0:CC * (E + 1)]))
    p.dma_put(sptok, sy.dma_start(out=dram["idxe_sp"], in_=idxL[:]))
    p.dma_close(sptok)
    p.tap("widsp", wm2[:, 0:CC * (E + 1)])
    p.tap("idxLf", idxf[:])


def phMprep(p, st, *, hvg):
    nc = p.nc
    gp = nc.gpsimd
    dram = p.dram

    wide = p.sb("mp_wide", [128, WID], stack=st)
    ixt = p.sb("mp_ixt", [128, E], I32, stack=st)

    xtm = dram["x_own_tm"]
    widr = dram["wid_sp"]
    idxr = dram["idxe_sp"]

    gbase = p.semv["s_gp"]
    PER = 3 * 16 + E * 16
    with gp.Fori(0, CCH) as i:
        gp.wait_ge(p.s_gp, i * PER + gbase)
        gp.dma_start(out=wide[:, 0:D],
                     in_=xtm[DynSlice(i * 128, 128), :]).then_inc(p.s_gp, 16)
        gp.dma_start(out=wide[:, D:WID],
                     in_=widr[:, DynSlice((hvg + i) * (E + 1), E + 1)]
                     ).then_inc(p.s_gp, 16)
        gp.dma_start(out=ixt[:],
                     in_=idxr[:, DynSlice((hvg + i) * E, E)]).then_inc(p.s_gp, 16)
        gp.wait_ge(p.s_gp, i * PER + gbase + 48)
        for e in range(E):
            gp.indirect_dma_start(
                dram[f"xtok{e}"],
                IndirectOffsetOnAxis(ixt[:, e:e + 1], 0),
                wide[:], None,
                bounds_check=CAP - 1, oob_is_err=False).then_inc(p.s_gp, 16)
    p.semv["s_gp"] += CCH * PER
    gp.wait_ge(p.s_gp, p.semv["s_gp"])


def ph2(p, st, *, onesr, onesc, lnp):
    nc = p.nc
    sy, te, ve, ac = nc.sync, nc.tensor, nc.vector, nc.scalar
    sc = nc.scalar
    dram = p.dram

    wts = {}
    wtok = p.dma_open(sy)
    for nm in ("wr_t", "ww_t", "wk_t", "wv_t"):
        wts[nm] = p.sb("p2_" + nm, [128, 8 * D], stack=st)
        p.dma_put(wtok, sy.dma_start(
            out=wts[nm][:].rearrange("p (k e) -> p k e", k=8),
            in_=dram[nm].rearrange("(k p) e -> p k e", p=128)))
    w_ld = p.dma_close(wtok)

    xb = [p.sb("p2_xb0", [128, 8 * BW], stack=st)] * 2
    xsq = p.sb("p2_xsq", [128, 8 * BW], stack=st)
    xn = p.sb("p2_xn", [128, 8 * BW], stack=st)
    xn2 = xb[0]
    ab = p.sb("p2_ab", [128, 4 * BW], stack=st)
    srow = p.sb("p2_srow", [1, 2 * BW], stack=st)
    a1r = p.sb("p2_a1r", [1, BW], stack=st)
    b1r = p.sb("p2_b1r", [1, BW], stack=st)
    scr = p.sb("p2_scr", [1, BW], stack=st)
    ew = p.sb("p2_ew", [128, BW], stack=st)
    eb = [p.sb("p2_eb0", [128, BW], stack=st)] * 2
    evb = [p.sb("p2_evb0", [128, BW], stack=st)] * 2
    rb = [p.sb("p2_rb0", [128, BW], stack=st)] * 2

    bc_ps = p.ps("p2_bc", [128, BW], stack=st)
    st_ps = p.ps("p2_st", [1, 2 * BW], stack=st)
    mm_ps = [p.ps(f"p2_mm{i}", [128, BW], stack=st) for i in range(2)]

    xr = dram["x_fm"].rearrange("(k p) t -> k p t", p=128)

    def bcast(row_slice, dst_slice):
        p.wait(te, "s_ve")
        i = te.matmul(bc_ps[:], onesr[:], row_slice, start=True, stop=True)
        p.inc("s_pe", i)
        p.wait(ve, "s_pe")
        p.inc("s_ve", ve.tensor_copy(dst_slice, bc_ps[:]))

    pe_mark = [0]
    for blk in range(NB):
        tw = slice(blk * BW, (blk + 1) * BW)
        if blk >= 1:
            sy.wait_ge(p.s_pe, pe_mark[-1])
            sy.wait_ge(p.s_ve, p.semv["s_ve"])
        if blk >= 1:
            p.dma_wait(ve, _SPILLS[(id(p), blk - 1, 7)])
            p.dma_wait(ac, _SPILLS[(id(p), blk - 1, 7)])
        tok = p.dma_open(sy)
        for k in range(8):
            p.dma_put(tok, sy.dma_start(
                out=xb[blk % 2][:, k * BW:(k + 1) * BW], in_=xr[k, :, tw]))
        ldm = p.dma_close(tok)

        p.dma_wait(ac, ldm)
        p.wait(ac, "s_ve")
        p.inc("s_ac", ac.activation(xsq[:], xb[blk % 2][:], AF.Square))
        p.wait(te, "s_ac")
        p.dma_wait(te, ldm)
        if blk == 0:
            p.dma_wait(te, w_ld)
        p.wait(te, "s_ve")
        for k in range(8):
            kw = slice(k * BW, (k + 1) * BW)
            te.matmul(st_ps[:, 0:BW], onesc[:], xb[blk % 2][:, kw],
                      start=(k == 0), stop=(k == 7))
            i = te.matmul(st_ps[:, BW:2 * BW], onesc[:], xsq[:, kw],
                          start=(k == 0), stop=(k == 7))
        p.inc("s_pe", i)
        p.wait(ve, "s_pe")
        ve.tensor_copy(srow[:], st_ps[:])
        ln_rows(p, srow, a1r[:], b1r[:], scr[:])
        bcast(a1r[:], ab[:, 0:BW])
        bcast(b1r[:], ab[:, BW:2 * BW])
        a0b = ab[:, 0:BW].unsqueeze(1).broadcast_to([128, 8, BW])
        b0b = ab[:, BW:2 * BW].unsqueeze(1).broadcast_to([128, 8, BW])
        xbv = xb[blk % 2][:].rearrange("p (k w) -> p k w", k=8)
        xnv = xn[:].rearrange("p (k w) -> p k w", k=8)
        ve.tensor_mul(xnv, xbv, a0b)
        p.inc("s_ve", ve.tensor_add(xnv, xnv, b0b))

        p.wait(ac, "s_ve")
        p.inc("s_ac", ac.activation(xsq[:], xn[:], AF.Square))
        p.wait(te, "s_ac")
        p.wait(te, "s_ve")
        for k in range(8):
            kw = slice(k * BW, (k + 1) * BW)
            te.matmul(st_ps[:, 0:BW], onesc[:], xn[:, kw],
                      start=(k == 0), stop=(k == 7))
            i = te.matmul(st_ps[:, BW:2 * BW], onesc[:], xsq[:, kw],
                          start=(k == 0), stop=(k == 7))
        p.inc("s_pe", i)
        p.wait(ve, "s_pe")
        ve.tensor_copy(srow[:], st_ps[:])
        ln_rows(p, srow, a1r[:], b1r[:], scr[:])
        bcast(a1r[:], ab[:, 2 * BW:3 * BW])
        bcast(b1r[:], ab[:, 3 * BW:4 * BW])
        a1b = ab[:, 2 * BW:3 * BW].unsqueeze(1).broadcast_to([128, 8, BW])
        b1b = ab[:, 3 * BW:4 * BW].unsqueeze(1).broadcast_to([128, 8, BW])
        xn2v = xn2[:].rearrange("p (k w) -> p k w", k=8)
        ve.tensor_mul(xn2v, xnv, a1b)
        ve.tensor_add(xn2v, xn2v, b1b)
        for k in range(8):
            kw = slice(k * BW, (k + 1) * BW)
            ve.tensor_scalar(xn2[:, kw], xn2[:, kw],
                             lnp[:, 0 * 8 + k:0 * 8 + k + 1],
                             lnp[:, 1 * 8 + k:1 * 8 + k + 1], AL.mult, AL.add)
        p.inc("s_ve", ve.engine_nop())
        xn2_ve = p.semv["s_ve"]

        te.wait_ge(p.s_ve, xn2_ve)
        for m in range(8):
            bi = m % 2
            for oi, wname in enumerate(("ww_t", "wk_t", "wv_t", "wr_t")):
                pb = mm_ps[oi % 2]
                te.wait_ge(p.s_ve, p.semv["s_ve"])
                te.wait_ge(p.s_ac, p.semv["s_ac"])
                for k in range(8):
                    i = te.matmul(pb[:],
                                  wts[wname][:, k * D + m * 128:k * D + m * 128 + 128],
                                  xn2[:, k * BW:(k + 1) * BW],
                                  start=(k == 0), stop=(k == 7))
                p.inc("s_pe", i)
                mm_pe = p.semv["s_pe"]
                if wname == "ww_t":
                    ac.wait_ge(p.s_pe, mm_pe)
                    p.inc("s_ac", ac.activation(ew[:], pb[:], AF.Exp))
                elif wname == "wk_t":
                    ve.wait_ge(p.s_pe, mm_pe)
                    ve.wait_ge(p.s_ac, p.semv["s_ac"])
                    pm = st_spill_mark(p, blk, m, "e")
                    if pm is not None:
                        p.dma_wait(ve, pm)
                    p.inc("s_ve", ve.scalar_tensor_tensor(
                        eb[bi][:], pb[:], 1.0, ew[:], AL.mult, AL.subtract))
                    ac.wait_ge(p.s_ve, p.semv["s_ve"])
                    p.inc("s_ac", ac.activation(eb[bi][:], eb[bi][:], AF.Exp))
                elif wname == "wv_t":
                    ve.wait_ge(p.s_pe, mm_pe)
                    ve.wait_ge(p.s_ac, p.semv["s_ac"])
                    p.inc("s_ve", ve.tensor_mul(evb[bi][:], eb[bi][:], pb[:]))
                else:
                    ac.wait_ge(p.s_pe, mm_pe)
                    p.inc("s_ac", ac.activation(rb[bi][:], pb[:], AF.Sigmoid))
            sc.wait_ge(p.s_ve, p.semv["s_ve"])
            sc.wait_ge(p.s_ac, p.semv["s_ac"])
            stok = p.dma_open(sc)
            for nm, tile in (("e_sp", eb[bi]), ("ev_sp", evb[bi]),
                             ("r_sp", rb[bi])):
                p.dma_put(stok, sc.dma_start(out=dram[nm][m, :, tw],
                                             in_=tile[:]))
            _SPILLS[(id(p), blk, m)] = p.dma_close(stok)
        pe_mark.append(p.semv["s_pe"])


_SPILLS = {}


def st_spill_mark(p, blk, m, _tag):
    key = (id(p), blk, m - 1) if m >= 1 else (id(p), blk - 1, 7)
    return _SPILLS.get(key)
def ph3(p, st, *, decay):
    nc = p.nc
    sy, ve, sc = nc.sync, nc.vector, nc.scalar
    dram = p.dram

    ebuf = p.sb("p3_e", [128, T], stack=st)
    evbuf = p.sb("p3_ev", [128, T], stack=st)
    num = p.sb("p3_num", [128, T], stack=st)
    den = p.sb("p3_den", [128, T], stack=st)

    ve_mark = []
    st_tok = []
    for k in range(8):
        if k >= 1:
            sy.wait_ge(p.s_ve, ve_mark[k - 1])
        tok = p.dma_open(sy)
        p.dma_put(tok, sy.dma_start(out=ebuf[:], in_=dram["e_sp"][k]))
        p.dma_put(tok, sy.dma_start(out=evbuf[:], in_=dram["ev_sp"][k]))
        p.dma_close(tok)
        p.dma_wait(ve, tok)
        if k >= 1:
            p.dma_wait(ve, st_tok[k - 1])
        dk = decay[:, k:k + 1].broadcast_to([128, T])
        ve.tensor_tensor_scan(num[:], dk, evbuf[:], 0.0, AL.mult, AL.add)
        ve.tensor_tensor_scan(den[:], dk, ebuf[:], 0.0, AL.mult, AL.add)
        ve.tensor_scalar_add(den[:], den[:], WKV_EPS)
        p.inc("s_ve", ve.reciprocal(den[:], den[:]))
        p.wait(ve, "s_ve")
        p.inc("s_ve", ve.tensor_mul(num[:], num[:], den[:]))
        ve_mark.append(p.semv["s_ve"])
        sc.wait_ge(p.s_ve, p.semv["s_ve"])
        stok = p.dma_open(sc)
        p.dma_put(stok, sc.dma_start(out=dram["wkv_sp"][k], in_=num[:]))
        st_tok.append(p.dma_close(stok))


def phM(p, st, *, ident):
    nc = p.nc
    sy, te, ve, ac, gp = nc.sync, nc.tensor, nc.vector, nc.scalar, nc.gpsimd
    sc = nc.scalar
    dram = p.dram

    wg = [p.sb(f"pm_wg{i}", [128, 8 * 512], stack=st) for i in range(2)]
    wu = [p.sb(f"pm_wu{i}", [128, 8 * 512], stack=st) for i in range(2)]
    wd = [p.sb(f"pm_wd{i}", [128, 4 * 1024], stack=st) for i in range(2)]
    xt = p.sb("pm_xt", [128, NSL * WID], stack=st)
    eosb = xt[:, 0:8 * CAP]
    xsq = p.sb("pm_xsq", [128, D], stack=st)
    stats = p.sb("pm_stats", [128, 4 * NSL], stack=st)
    xtT = p.sb("pm_xtT", [128, 8 * CAP], stack=st)
    gel = p.sb("pm_gel", [128, 320], stack=st)
    hsb = p.sb("pm_hsb", [128, 8 * 320], stack=st)
    wcol = p.sb("pm_wcol", [128, NSL], stack=st)
    eoT = [p.sb(f"pm_eoT{i}", [128, D], stack=st) for i in range(2)]
    hofftt = p.sb("pm_hofft", [128, 2], stack=st)
    idc = p.sb("pm_idc", [128, NSL], I32, stack=st)
    idcf = p.sb("pm_idcf", [128, NSL], stack=st)
    idcm = p.sb("pm_idcm", [128, NSL], stack=st)

    trp = [p.ps(f"pm_trp{i}", [128, 128], stack=st) for i in range(2)]
    gps = p.ps("pm_gps", [128, BW], stack=st)
    ups = p.ps("pm_ups", [128, BW], stack=st)
    eop = p.ps("pm_eop", [128, BW], stack=st)
    top = p.ps("pm_top", [128, 1024], stack=st)

    xtv = xt[:].rearrange("p (n w) -> p n w", n=NSL)
    eoTr = dram["eoT_sp"].rearrange("(u p) d -> u p d", p=128)
    st_eoT = []

    htok = p.dma_open(sy)
    p.dma_put(htok, sy.dma_start(out=hofftt[:, 0:1], in_=dram["hofft1"]))
    p.dma_put(htok, sy.dma_start(out=hofftt[:, 1:2], in_=dram["hofft2"]))
    p.dma_close(htok)

    wq_tok = {}
    buf_mark = [0, 0]

    def issue_wq(gidx):
        if gidx >= 4 * E:
            return
        eq, q = divmod(gidx, 4)
        b = gidx % 2
        if buf_mark[b] > 0:
            sy.wait_ge(p.s_pe, buf_mark[b])
        fq = slice(q * 512, (q + 1) * 512)
        tok = p.dma_open(sy)
        p.dma_put(tok, sy.dma_start(
            out=wg[b][:].rearrange("p (k f) -> p k f", k=8),
            in_=dram["wg_t"][eq].rearrange("(k p) f -> p k f", p=128)[:, :, fq]))
        p.dma_put(tok, sy.dma_start(
            out=wu[b][:].rearrange("p (k f) -> p k f", k=8),
            in_=dram["wu_t"][eq].rearrange("(k p) f -> p k f", p=128)[:, :, fq]))
        p.dma_put(tok, sy.dma_start(
            out=wd[b][:].rearrange("p (k d) -> p k d", k=4),
            in_=dram["wd_t"][eq].rearrange(
                "(k p) d -> p k d", p=128)[:, q * 4:(q + 1) * 4]))
        wq_tok[gidx] = p.dma_close(tok)

    issue_wq(0)
    issue_wq(1)

    for e in range(E):
        if e > 0:
            sy.wait_ge(p.s_pe, p.semv["s_pe"])
            sy.wait_ge(p.s_ve, p.semv["s_ve"])
            for t_ in st_eoT[-2:]:
                p.dma_wait(sy, t_)
        xtok_t = p.dma_open(sy)
        p.dma_put(xtok_t, sy.dma_start(
            out=xtv, in_=dram[f"xtok{e}"].rearrange("(n p) w -> p n w", p=128)))
        p.dma_close(xtok_t)
        if e == 0:
            p.dma_wait(ve, htok)

        p.dma_wait(ve, xtok_t)
        p.dma_wait(ac, xtok_t)
        for sl in range(NSL):
            xs = xt[:, sl * WID:sl * WID + D]
            if sl > 0:
                ac.wait_ge(p.s_ve, p.semv["s_ve"])
            p.inc("s_ac", ac.activation(xsq[:], xs, AF.Square))
            ve.tensor_reduce(stats[:, sl:sl + 1], xs.unsqueeze(1), AX.X, AL.add)
            ve.wait_ge(p.s_ac, p.semv["s_ac"])
            p.inc("s_ve", ve.tensor_reduce(
                stats[:, NSL + sl:NSL + sl + 1], xsq[:].unsqueeze(1),
                AX.X, AL.add))
        p.wait(ve, "s_ve")
        ve.tensor_scalar_mul(stats[:, 2 * NSL:3 * NSL], stats[:, 0:NSL], 1.0 / D)
        p.vf(ve.tensor_scalar_mul(stats[:, 3 * NSL:4 * NSL],
                                  stats[:, NSL:2 * NSL], 1.0 / D))
        p.vf(ve.tensor_mul(stats[:, 0:NSL], stats[:, 2 * NSL:3 * NSL],
                           stats[:, 2 * NSL:3 * NSL]))
        p.vf(ve.tensor_sub(stats[:, 3 * NSL:4 * NSL], stats[:, 3 * NSL:4 * NSL],
                           stats[:, 0:NSL]))
        p.inc("s_ve", ve.tensor_scalar_add(stats[:, 3 * NSL:4 * NSL],
                                           stats[:, 3 * NSL:4 * NSL], LN_EPS))
        p.wait(ac, "s_ve")
        ac.activation(stats[:, 3 * NSL:4 * NSL],
                      stats[:, 3 * NSL:4 * NSL], AF.Sqrt)
        p.inc("s_ac", ac.activation(xsq[:, 0:512], xsq[:, 0:512], AF.Square))
        p.wait(ve, "s_ac")
        p.inc("s_ve", ve.reciprocal(stats[:, 3 * NSL:4 * NSL],
                                    stats[:, 3 * NSL:4 * NSL]))
        p.wait(ve, "s_ve")
        p.vf(ve.tensor_mul(stats[:, 2 * NSL:3 * NSL], stats[:, 2 * NSL:3 * NSL],
                           stats[:, 3 * NSL:4 * NSL]))
        p.vf(ve.tensor_scalar_mul(stats[:, 2 * NSL:3 * NSL],
                                  stats[:, 2 * NSL:3 * NSL], -1.0))
        for sl in range(NSL):
            ve.tensor_scalar(xt[:, sl * WID:sl * WID + D],
                             xt[:, sl * WID:sl * WID + D],
                             stats[:, 3 * NSL + sl:3 * NSL + sl + 1],
                             stats[:, 2 * NSL + sl:2 * NSL + sl + 1],
                             AL.mult, AL.add)
        ve.tensor_copy(wcol[:].unsqueeze(2), xtv[:, :, D + e:D + e + 1])
        p.vf(ve.tensor_copy(idcf[:].unsqueeze(2), xtv[:, :, WID - 1:WID]))
        p.vf(ve.tensor_scalar(idcf[:], idcf[:], hofftt[:, 0:1], None,
                              AL.subtract))
        p.vf(ve.tensor_single_scalar(idcm[:], idcf[:], 0.0, AL.is_lt))
        p.vf(ve.scalar_tensor_tensor(idcf[:], idcm[:], hofftt[:, 1:2], idcf[:],
                                     AL.mult, AL.add))
        p.inc("s_ve", ve.tensor_copy(idc[:], idcf[:]))
        rn_ve = p.semv["s_ve"]

        sc.wait_ge(p.s_ve, rn_ve)
        idtok = p.dma_open(sc)
        p.dma_put(idtok, sc.dma_start(
            out=dram["idT_sp"][e * NSL:(e + 1) * NSL, :].rearrange("s p -> p s"),
            in_=idc[:], allow_slow_non_contiguous=True))
        p.dma_close(idtok)

        te.wait_ge(p.s_ve, rn_ve)
        nt = 0
        tr_ve = p.semv["s_ve"]
        for dc in range(8):
            for sl in range(NSL):
                if nt >= 2:
                    te.wait_ge(p.s_ve, tr_ve + nt - 1)
                i = te.transpose(
                    trp[nt % 2][:],
                    xt[:, sl * WID + dc * 128:sl * WID + (dc + 1) * 128],
                    ident[:])
                p.inc("s_pe", i)
                ve.wait_ge(p.s_pe, p.semv["s_pe"])
                p.inc("s_ve", ve.tensor_copy(
                    xtT[:, dc * CAP + sl * 128:dc * CAP + (sl + 1) * 128],
                    trp[nt % 2][:]))
                nt += 1

        for q in range(4):
            gidx = e * 4 + q
            b = gidx % 2
            p.dma_wait(te, wq_tok[gidx])
            for nh in range(2):
                for fcl in range(4):
                    te.wait_ge(p.s_ve, p.semv["s_ve"])
                    for k in range(8):
                        te.matmul(gps[:, 0:320],
                                  wg[b][:, k * 512 + fcl * 128:k * 512 + (fcl + 1) * 128],
                                  xtT[:, k * CAP + nh * 320:k * CAP + nh * 320 + 320],
                                  start=(k == 0), stop=(k == 7))
                    for k in range(8):
                        i = te.matmul(ups[:, 0:320],
                                      wu[b][:, k * 512 + fcl * 128:k * 512 + (fcl + 1) * 128],
                                      xtT[:, k * CAP + nh * 320:k * CAP + nh * 320 + 320],
                                      start=(k == 0), stop=(k == 7))
                    p.inc("s_pe", i)
                    ac.wait_ge(p.s_pe, p.semv["s_pe"])
                    p.inc("s_ac", ac.activation(gel[:, 0:320], gps[:, 0:320],
                                                AF.Gelu))
                    ve.wait_ge(p.s_ac, p.semv["s_ac"])
                    p.inc("s_ve", ve.tensor_mul(
                        hsb[:, fcl * 320:fcl * 320 + 320],
                        gel[:, 0:320], ups[:, 0:320]))
                for m in range(8):
                    te.wait_ge(p.s_ve, p.semv["s_ve"])
                    for fcl in range(4):
                        i = te.matmul(eop[:, 0:320],
                                      wd[b][:, fcl * 1024 + m * 128:fcl * 1024 + (m + 1) * 128],
                                      hsb[:, fcl * 320:fcl * 320 + 320],
                                      start=(fcl == 0), stop=(fcl == 3))
                    p.inc("s_pe", i)
                    ve.wait_ge(p.s_pe, p.semv["s_pe"])
                    dst = eosb[:, m * CAP + nh * 320:m * CAP + nh * 320 + 320]
                    if q == 0:
                        p.inc("s_ve", ve.tensor_copy(dst, eop[:, 0:320]))
                    else:
                        p.inc("s_ve", ve.scalar_tensor_tensor(
                            dst, eop[:, 0:320], 1.0, dst, AL.mult, AL.add))
            buf_mark[b] = p.semv["s_pe"]
            issue_wq(gidx + 2)

        for sl in range(NSL):
            te.wait_ge(p.s_ve, p.semv["s_ve"])
            for dc in range(8):
                i = te.transpose(top[:, dc * 128:(dc + 1) * 128],
                                 eosb[:, dc * CAP + sl * 128:dc * CAP + (sl + 1) * 128],
                                 ident[:])
            p.inc("s_pe", i)
            ve.wait_ge(p.s_pe, p.semv["s_pe"])
            if len(st_eoT) >= 2:
                p.dma_wait(ve, st_eoT[-2])
            p.inc("s_ve", ve.tensor_scalar(
                eoT[sl % 2][:], top[:], wcol[:, sl:sl + 1], None, AL.mult))
            sc.wait_ge(p.s_ve, p.semv["s_ve"])
            etok = p.dma_open(sc)
            p.dma_put(etok, sc.dma_start(out=eoTr[e * NSL + sl],
                                         in_=eoT[sl % 2][:]))
            st_eoT.append(p.dma_close(etok))


def eoFori(p, st):
    nc = p.nc
    gp = nc.gpsimd
    dram = p.dram
    eob = p.sb("eof_eob", [128, D], stack=st)
    ixt = p.sb("eof_ixt", [128, 1], I32, stack=st)
    eoTr = dram["eoT_sp"]
    idTr = dram["idT_sp"].rearrange("u p -> p u")
    gbase = p.semv["s_gp"]
    PER = 3 * 16
    NU = E * NSL
    with gp.Fori(0, NU) as u:
        gp.wait_ge(p.s_gp, u * PER + gbase)
        gp.dma_start(out=eob[:],
                     in_=eoTr[DynSlice(u * 128, 128), :]).then_inc(p.s_gp, 16)
        gp.dma_start(out=ixt[:], in_=idTr[:, DynSlice(u, 1)],
                     allow_slow_non_contiguous=True).then_inc(p.s_gp, 16)
        gp.wait_ge(p.s_gp, u * PER + gbase + 32)
        gp.indirect_dma_start(
            dram["macc"], IndirectOffsetOnAxis(ixt[:], 0), eob[:], None,
            bounds_check=HALF, oob_is_err=False,
            compute_op=AL.add).then_inc(p.s_gp, 16)
    p.semv["s_gp"] += NU * PER
    gp.wait_ge(p.s_gp, p.semv["s_gp"])


def ph4a(p, st, *, onesr, onesc, lnp, hv):
    hoff_tok = hv * 128
    """xn (recompute), rw=r*wkv, tm=Wo@rw, y1, LN2->xn3; spills."""
    nc = p.nc
    sy, te, ve, ac, sc = nc.sync, nc.tensor, nc.vector, nc.scalar, nc.scalar
    dram = p.dram

    wo = p.sb("p4_wo", [128, 8 * D], stack=st)
    wtok = p.dma_open(sy)
    p.dma_put(wtok, sy.dma_start(
        out=wo[:].rearrange("p (k e) -> p k e", k=8),
        in_=dram["wo_t"].rearrange("(k p) e -> p k e", p=128)))
    w_ld = p.dma_close(wtok)

    xb = p.sb("p4_xb", [128, 8 * BW], stack=st)
    xsq = p.sb("p4_xsq", [128, 8 * BW], stack=st)
    xn = p.sb("p4_xn", [128, 8 * BW], stack=st)
    rbuf = p.sb("p4_rbuf", [128, 8 * BW], stack=st)
    wkvb = p.sb("p4_wkvb", [128, 8 * BW], stack=st)
    y1 = p.sb("p4_y1", [128, 8 * BW], stack=st)
    xn3 = p.sb("p4_xn3", [128, 8 * BW], stack=st)
    ab = p.sb("p4_ab", [128, 4 * BW], stack=st)
    srow = p.sb("p4_srow", [1, 2 * BW], stack=st)
    a1r = p.sb("p4_a1r", [1, BW], stack=st)
    b1r = p.sb("p4_b1r", [1, BW], stack=st)
    scr = p.sb("p4_scr", [1, BW], stack=st)

    bc_ps = p.ps("p4_bc", [128, BW], stack=st)
    st_ps = p.ps("p4_st", [1, 2 * BW], stack=st)
    mm_ps = [p.ps(f"p4_mm{i}", [128, BW], stack=st) for i in range(2)]

    xr = dram["x_own_fm"].rearrange("(k p) t -> k p t", p=128)

    def bcast(row_slice, dst_slice):
        p.wait(te, "s_ve")
        i = te.matmul(bc_ps[:], onesr[:], row_slice, start=True, stop=True)
        p.inc("s_pe", i)
        p.wait(ve, "s_pe")
        p.inc("s_ve", ve.tensor_copy(dst_slice, bc_ps[:]))

    def ln_full(src_tile, dst_tile, g_off, b_off):
        p.wait(ac, "s_ve")
        p.inc("s_ac", ac.activation(xsq[:], src_tile[:], AF.Square))
        p.wait(te, "s_ac")
        for k in range(8):
            kw = slice(k * BW, (k + 1) * BW)
            te.matmul(st_ps[:, 0:BW], onesc[:], src_tile[:, kw],
                      start=(k == 0), stop=(k == 7))
            i = te.matmul(st_ps[:, BW:2 * BW], onesc[:], xsq[:, kw],
                          start=(k == 0), stop=(k == 7))
        p.inc("s_pe", i)
        p.wait(ve, "s_pe")
        ve.tensor_copy(srow[:], st_ps[:])
        ln_rows(p, srow, a1r[:], b1r[:], scr[:])
        bcast(a1r[:], ab[:, 2 * BW:3 * BW])
        bcast(b1r[:], ab[:, 3 * BW:4 * BW])
        a1b = ab[:, 2 * BW:3 * BW].unsqueeze(1).broadcast_to([128, 8, BW])
        b1b = ab[:, 3 * BW:4 * BW].unsqueeze(1).broadcast_to([128, 8, BW])
        sv = src_tile[:].rearrange("p (k w) -> p k w", k=8)
        dv = dst_tile[:].rearrange("p (k w) -> p k w", k=8)
        ve.tensor_mul(dv, sv, a1b)
        ve.tensor_add(dv, dv, b1b)
        for k in range(8):
            kw = slice(k * BW, (k + 1) * BW)
            ve.tensor_scalar(dst_tile[:, kw], dst_tile[:, kw],
                             lnp[:, g_off + k:g_off + k + 1],
                             lnp[:, b_off + k:b_off + k + 1], AL.mult, AL.add)
        p.inc("s_ve", ve.engine_nop())

    st_mark = []
    for blk in range(NBO):
        tw = slice(blk * BW, (blk + 1) * BW)
        if blk >= 1:
            p.dma_wait(sy, st_mark[blk - 1])
            sy.wait_ge(p.s_ve, p.semv["s_ve"])
        tok = p.dma_open(sy)
        for k in range(8):
            p.dma_put(tok, sy.dma_start(out=xb[:, k * BW:(k + 1) * BW],
                                        in_=xr[k, :, tw]))
        p.dma_put(tok, sy.dma_start(
            out=rbuf[:].rearrange("p (k w) -> p k w", k=8),
            in_=dram["r_sp"][:, :, DynSlice(hoff_tok + blk * BW, BW)].rearrange(
                "k p w -> p k w")))
        p.dma_put(tok, sy.dma_start(
            out=wkvb[:].rearrange("p (k w) -> p k w", k=8),
            in_=dram["wkv_sp"][:, :, DynSlice(hoff_tok + blk * BW, BW)].rearrange(
                "k p w -> p k w")))
        ldm = p.dma_close(tok)
        if blk == 0:
            p.dma_wait(te, w_ld)

        p.dma_wait(ve, ldm)
        p.wait(ac, "s_ve")
        p.dma_wait(ac, ldm)
        p.inc("s_ac", ac.activation(xsq[:], xb[:], AF.Square))
        p.wait(te, "s_ac")
        for k in range(8):
            kw = slice(k * BW, (k + 1) * BW)
            te.matmul(st_ps[:, 0:BW], onesc[:], xb[:, kw],
                      start=(k == 0), stop=(k == 7))
            i = te.matmul(st_ps[:, BW:2 * BW], onesc[:], xsq[:, kw],
                          start=(k == 0), stop=(k == 7))
        p.inc("s_pe", i)
        p.wait(ve, "s_pe")
        ve.tensor_copy(srow[:], st_ps[:])
        ln_rows(p, srow, a1r[:], b1r[:], scr[:])
        bcast(a1r[:], ab[:, 0:BW])
        bcast(b1r[:], ab[:, BW:2 * BW])
        a0b = ab[:, 0:BW].unsqueeze(1).broadcast_to([128, 8, BW])
        b0b = ab[:, BW:2 * BW].unsqueeze(1).broadcast_to([128, 8, BW])
        xbv = xb[:].rearrange("p (k w) -> p k w", k=8)
        xnv = xn[:].rearrange("p (k w) -> p k w", k=8)
        ve.tensor_mul(xnv, xbv, a0b)
        ve.tensor_add(xnv, xnv, b0b)
        p.inc("s_ve", ve.tensor_mul(rbuf[:], rbuf[:], wkvb[:]))
        rw_ve = p.semv["s_ve"]

        te.wait_ge(p.s_ve, rw_ve)
        for m in range(8):
            mw = slice(m * BW, (m + 1) * BW)
            pb = mm_ps[m % 2]
            if m >= 2:
                te.wait_ge(p.s_ve, p.semv["s_ve"])
            for k in range(8):
                i = te.matmul(pb[:],
                              wo[:, k * D + m * 128:k * D + m * 128 + 128],
                              rbuf[:, k * BW:(k + 1) * BW],
                              start=(k == 0), stop=(k == 7))
            p.inc("s_pe", i)
            ve.wait_ge(p.s_pe, p.semv["s_pe"])
            p.inc("s_ve", ve.scalar_tensor_tensor(
                y1[:, mw], pb[:], 1.0, xn[:, mw], AL.mult, AL.add))

        ln_full(y1, xn3, 16, 24)

        sc.wait_ge(p.s_ve, p.semv["s_ve"])
        stok = p.dma_open(sc)
        for nm, tile in (("xn_sp", xn), ("y1_sp", y1), ("xn3_sp", xn3)):
            p.dma_put(stok, sc.dma_start(
                out=dram[nm][:, :, tw].rearrange("k p w -> p k w"),
                in_=tile[:].rearrange("p (k w) -> p k w", k=8)))
        st_mark.append(p.dma_close(stok))


def ph4sh(p, st):
    nc = p.nc
    sy, te, ve, ac, sc = nc.sync, nc.tensor, nc.vector, nc.scalar, nc.scalar
    dram = p.dram

    wsg = p.sb("psh_wsg", [128, 8 * 1024], stack=st)
    wsu = p.sb("psh_wsu", [128, 8 * 1024], stack=st)
    wsd = p.sb("psh_wsd", [128, 8 * 1024], stack=st)
    xnb = p.sb("psh_xn", [128, 8 * BW], stack=st)
    gel = p.sb("psh_gel", [128, BW], stack=st)
    hs = p.sb("psh_hs", [128, 8 * BW], stack=st)
    sh = p.sb("psh_sh", [128, 8 * BW], stack=st)

    gsp = p.ps("psh_gsp", [128, BW], stack=st)
    usp = p.ps("psh_usp", [128, BW], stack=st)
    dsp = p.ps("psh_dsp", [128, BW], stack=st)

    for fh in range(2):
        fw = slice(fh * 1024, (fh + 1) * 1024)
        sy.wait_ge(p.s_pe, p.semv["s_pe"])
        wtok = p.dma_open(sy)
        p.dma_put(wtok, sy.dma_start(
            out=wsg[:].rearrange("p (k f) -> p k f", k=8),
            in_=dram["wsg_t"].rearrange("(k p) f -> p k f", p=128)[:, :, fw]))
        p.dma_put(wtok, sy.dma_start(
            out=wsu[:].rearrange("p (k f) -> p k f", k=8),
            in_=dram["wsu_t"].rearrange("(k p) f -> p k f", p=128)[:, :, fw]))
        p.dma_put(wtok, sy.dma_start(
            out=wsd[:].rearrange("p (k e) -> p k e", k=8),
            in_=dram["wsd_t"].rearrange("(k p) e -> p k e", p=128)[:, fh * 8:(fh + 1) * 8]))
        w_ld = p.dma_close(wtok)

        st_mark = []
        for blk in range(NBO):
            tw = slice(blk * BW, (blk + 1) * BW)
            sy.wait_ge(p.s_pe, p.semv["s_pe"])
            if blk >= 1:
                p.dma_wait(sy, st_mark[blk - 1])
            tok = p.dma_open(sy)
            for k in range(8):
                p.dma_put(tok, sy.dma_start(
                    out=xnb[:, k * BW:(k + 1) * BW],
                    in_=dram["xn_sp"][k, :, tw]))
            if fh == 1:
                p.dma_put(tok, sy.dma_start(
                    out=sh[:].rearrange("p (k w) -> p k w", k=8),
                    in_=dram["sh_sp"][:, :, tw].rearrange("k p w -> p k w")))
            ldm = p.dma_close(tok)
            p.dma_wait(te, ldm)
            p.dma_wait(te, w_ld)
            for fc in range(8):
                te.wait_ge(p.s_ve, p.semv["s_ve"])
                for k in range(8):
                    te.matmul(gsp[:],
                              wsg[:, k * 1024 + fc * 128:k * 1024 + (fc + 1) * 128],
                              xnb[:, k * BW:(k + 1) * BW],
                              start=(k == 0), stop=(k == 7))
                for k in range(8):
                    i = te.matmul(usp[:],
                                  wsu[:, k * 1024 + fc * 128:k * 1024 + (fc + 1) * 128],
                                  xnb[:, k * BW:(k + 1) * BW],
                                  start=(k == 0), stop=(k == 7))
                p.inc("s_pe", i)
                ac.wait_ge(p.s_pe, p.semv["s_pe"])
                p.inc("s_ac", ac.activation(gel[:], gsp[:], AF.Gelu))
                ve.wait_ge(p.s_ac, p.semv["s_ac"])
                p.inc("s_ve", ve.tensor_mul(hs[:, fc * BW:(fc + 1) * BW],
                                            gel[:], usp[:]))
            for m in range(8):
                te.wait_ge(p.s_ve, p.semv["s_ve"])
                for fc in range(8):
                    i = te.matmul(dsp[:],
                                  wsd[:, fc * D + m * 128:fc * D + (m + 1) * 128],
                                  hs[:, fc * BW:(fc + 1) * BW],
                                  start=(fc == 0), stop=(fc == 7))
                p.inc("s_pe", i)
                ve.wait_ge(p.s_pe, p.semv["s_pe"])
                if fh == 1 and m == 0:
                    p.dma_wait(ve, ldm)
                mw = slice(m * BW, (m + 1) * BW)
                if fh == 0:
                    p.inc("s_ve", ve.tensor_copy(sh[:, mw], dsp[:]))
                else:
                    p.inc("s_ve", ve.scalar_tensor_tensor(
                        sh[:, mw], dsp[:], 1.0, sh[:, mw], AL.mult, AL.add))
            sc.wait_ge(p.s_ve, p.semv["s_ve"])
            stok = p.dma_open(sc)
            p.dma_put(stok, sc.dma_start(
                out=dram["sh_sp"][:, :, tw].rearrange("k p w -> p k w"),
                in_=sh[:].rearrange("p (k w) -> p k w", k=8)))
            st_mark.append(p.dma_close(stok))


def ph4b(p, st):
    nc = p.nc
    sy, te, ve, ac, sc = nc.sync, nc.tensor, nc.vector, nc.scalar, nc.scalar
    dram = p.dram

    wkey = p.sb("pb_wkey", [128, 8 * FCM // 2], stack=st)
    wval = p.sb("pb_wval", [128, 16 * D], stack=st)
    xnb = p.sb("pb_xn", [128, 8 * BW], stack=st)
    relu = p.sb("pb_relu", [128, BW], stack=st)
    kk = p.sb("pb_kk", [128, 16 * BW], stack=st)
    vv = p.sb("pb_vv", [128, 8 * BW], stack=st)

    kkp = p.ps("pb_kkp", [128, BW], stack=st)
    vvp = p.ps("pb_vvp", [128, BW], stack=st)

    for fh in range(2):
        sy.wait_ge(p.s_pe, p.semv["s_pe"])
        wtok = p.dma_open(sy)
        p.dma_put(wtok, sy.dma_start(
            out=wkey[:].rearrange("p (k f) -> p k f", k=8),
            in_=dram["wkey_t"].rearrange("(k p) f -> p k f", p=128)[
                :, :, fh * 2048:(fh + 1) * 2048]))
        p.dma_put(wtok, sy.dma_start(
            out=wval[:].rearrange("p (k e) -> p k e", k=16),
            in_=dram["wval_t"].rearrange("(k p) e -> p k e", p=128)[
                :, fh * 16:(fh + 1) * 16]))
        w_ld = p.dma_close(wtok)

        st_mark = []
        for blk in range(NBO):
            tw = slice(blk * BW, (blk + 1) * BW)
            sy.wait_ge(p.s_pe, p.semv["s_pe"])
            if blk >= 1:
                p.dma_wait(sy, st_mark[blk - 1])
            tok = p.dma_open(sy)
            for k in range(8):
                p.dma_put(tok, sy.dma_start(
                    out=xnb[:, k * BW:(k + 1) * BW],
                    in_=dram["xn3_sp"][k, :, tw]))
            if fh == 1:
                p.dma_put(tok, sy.dma_start(
                    out=vv[:].rearrange("p (k w) -> p k w", k=8),
                    in_=dram["vvh_sp"][:, :, tw].rearrange("k p w -> p k w")))
            ldm = p.dma_close(tok)
            p.dma_wait(te, ldm)
            p.dma_wait(te, w_ld)
            for fc in range(16):
                te.wait_ge(p.s_ve, p.semv["s_ve"])
                for k in range(8):
                    i = te.matmul(kkp[:],
                                  wkey[:, k * 2048 + fc * 128:k * 2048 + (fc + 1) * 128],
                                  xnb[:, k * BW:(k + 1) * BW],
                                  start=(k == 0), stop=(k == 7))
                p.inc("s_pe", i)
                ac.wait_ge(p.s_pe, p.semv["s_pe"])
                p.inc("s_ac", ac.activation(relu[:], kkp[:], AF.Relu))
                ve.wait_ge(p.s_ac, p.semv["s_ac"])
                p.inc("s_ve", ve.tensor_mul(kk[:, fc * BW:(fc + 1) * BW],
                                            relu[:], relu[:]))
            for m in range(8):
                te.wait_ge(p.s_ve, p.semv["s_ve"])
                for fc in range(16):
                    i = te.matmul(vvp[:],
                                  wval[:, fc * D + m * 128:fc * D + (m + 1) * 128],
                                  kk[:, fc * BW:(fc + 1) * BW],
                                  start=(fc == 0), stop=(fc == 15))
                p.inc("s_pe", i)
                ve.wait_ge(p.s_pe, p.semv["s_pe"])
                if m == 0:
                    p.dma_wait(ve, ldm)
                    if blk >= 1:
                        p.dma_wait(ve, st_mark[blk - 1])
                mw = slice(m * BW, (m + 1) * BW)
                if fh == 0:
                    p.inc("s_ve", ve.tensor_copy(vv[:, mw], vvp[:]))
                else:
                    p.inc("s_ve", ve.scalar_tensor_tensor(
                        vv[:, mw], vvp[:], 1.0, vv[:, mw], AL.mult, AL.add))
            sc.wait_ge(p.s_ve, p.semv["s_ve"])
            stok = p.dma_open(sc)
            p.dma_put(stok, sc.dma_start(
                out=dram["vvh_sp"][:, :, tw].rearrange("k p w -> p k w"),
                in_=vv[:].rearrange("p (k w) -> p k w", k=8)))
            st_mark.append(p.dma_close(stok))


def ph4c(p, st, *, ident):
    nc = p.nc
    sy, te, ve, ac, sc = nc.sync, nc.tensor, nc.vector, nc.scalar, nc.scalar
    dram = p.dram

    wrec = p.sb("pc_wrec", [128, 8 * D], stack=st)
    wtok = p.dma_open(sy)
    p.dma_put(wtok, sy.dma_start(
        out=wrec[:].rearrange("p (k e) -> p k e", k=8),
        in_=dram["wrec_t"].rearrange("(k p) e -> p k e", p=128)))
    w_ld = p.dma_close(wtok)

    xn3b = p.sb("pc_xn3", [128, 8 * BW], stack=st)
    vvb = p.sb("pc_vv", [128, 8 * BW], stack=st)
    y1b = p.sb("pc_y1", [128, 8 * BW], stack=st)
    shb = p.sb("pc_sh", [128, 8 * BW], stack=st)
    xbb = p.sb("pc_xb", [128, 8 * BW], stack=st)
    rr = p.sb("pc_rr", [128, 8 * BW], stack=st)
    mrow = p.sb("pc_mrow", [128, D], stack=st)
    orow = p.sb("pc_orow", [128, D], stack=st)

    mm_ps = [p.ps(f"pc_mm{i}", [128, BW], stack=st) for i in range(2)]
    trp = p.ps("pc_trp", [128, 1024], stack=st)

    xr = dram["x_own_fm"].rearrange("(k p) t -> k p t", p=128)
    outr = dram["out"].rearrange("(n p) d -> n p d", p=128)
    maccr = dram["macc"][0:HALF].rearrange("(n p) d -> n p d", p=128)

    for blk in range(NBO):
        tw = slice(blk * BW, (blk + 1) * BW)
        if blk >= 1:
            sy.wait_ge(p.s_ve, p.semv["s_ve"])
        tok = p.dma_open(sy)
        for k in range(8):
            p.dma_put(tok, sy.dma_start(out=xbb[:, k * BW:(k + 1) * BW],
                                        in_=xr[k, :, tw]))
        for nm, tile in (("xn3_sp", xn3b), ("vvh_sp", vvb),
                         ("y1_sp", y1b), ("sh_sp", shb)):
            p.dma_put(tok, sy.dma_start(
                out=tile[:].rearrange("p (k w) -> p k w", k=8),
                in_=dram[nm][:, :, tw].rearrange("k p w -> p k w")))
        ldm = p.dma_close(tok)
        if blk == 0:
            p.dma_wait(te, w_ld)

        p.dma_wait(te, ldm)
        for m in range(8):
            mw = slice(m * BW, (m + 1) * BW)
            pb = mm_ps[m % 2]
            if m >= 2:
                te.wait_ge(p.s_ac, p.semv["s_ac"])
            for k in range(8):
                i = te.matmul(pb[:],
                              wrec[:, k * D + m * 128:k * D + m * 128 + 128],
                              xn3b[:, k * BW:(k + 1) * BW],
                              start=(k == 0), stop=(k == 7))
            p.inc("s_pe", i)
            ac.wait_ge(p.s_pe, p.semv["s_pe"])
            p.inc("s_ac", ac.activation(rr[:, mw], pb[:], AF.Sigmoid))

        ve.wait_ge(p.s_ac, p.semv["s_ac"])
        p.dma_wait(ve, ldm)
        ve.tensor_mul(rr[:], rr[:], vvb[:])
        ve.tensor_add(rr[:], rr[:], y1b[:])
        ve.tensor_add(rr[:], rr[:], shb[:])
        p.inc("s_ve", ve.tensor_add(rr[:], rr[:], xbb[:]))
        base_ve = p.semv["s_ve"]

        te.wait_ge(p.s_ve, base_ve)
        ot_prev = None
        for t4 in range(4):
            n = blk * 4 + t4
            sy.wait_ge(p.s_ve, p.semv["s_ve"])
            mtok = p.dma_open(sy)
            p.dma_put(mtok, sy.dma_start(out=mrow[:], in_=maccr[n]))
            p.dma_close(mtok)
            if t4 > 0:
                te.wait_ge(p.s_ve, p.semv["s_ve"])
            for k in range(8):
                i = te.transpose(trp[:, k * 128:(k + 1) * 128],
                                 rr[:, k * BW + t4 * 128:k * BW + t4 * 128 + 128],
                                 ident[:])
            p.inc("s_pe", i)
            ve.wait_ge(p.s_pe, p.semv["s_pe"])
            p.dma_wait(ve, mtok)
            if ot_prev is not None:
                p.dma_wait(ve, ot_prev)
            p.inc("s_ve", ve.scalar_tensor_tensor(
                orow[:], trp[:], 1.0, mrow[:], AL.mult, AL.add))
            sc.wait_ge(p.s_ve, p.semv["s_ve"])
            otok = p.dma_open(sc)
            p.dma_put(otok, sc.dma_start(out=outr[n], in_=orow[:]))
            ot_prev = p.dma_close(otok)

    p.dma_wait_all(sc)
    p.dma_wait_all(sy)
    p.wait(nc.gpsimd, "s_gp")


def make_inmaps(inputs):
    x = np.ascontiguousarray(inputs["L0_attn_x"], np.float32)

    def tr(a):
        return np.ascontiguousarray(np.asarray(a, np.float32).T)

    def chunks8(v):
        return np.ascontiguousarray(np.asarray(v, np.float32).reshape(8, 128).T)

    shared = {
        "wr_t": tr(inputs["L0_attn_Wr"]),
        "ww_t": tr(inputs["L0_attn_Ww"]),
        "wk_t": tr(inputs["L0_attn_Wk"]),
        "wv_t": tr(inputs["L0_attn_Wv"]),
        "wo_t": tr(inputs["L0_attn_Wo"]),
        "wrec_t": tr(inputs["L0_attn_Wrec"]),
        "wkey_t": tr(inputs["L0_attn_Wkey"]),
        "wval_t": tr(inputs["L0_attn_Wval"]),
        "wsg_t": tr(inputs["L0_ffn_Ws_g"]),
        "wsu_t": tr(inputs["L0_ffn_Ws_u"]),
        "wsd_t": tr(inputs["L0_ffn_Ws_d"]),
        "gate_t": tr(inputs["L0_ffn_gate_w"]),
        "wg_t": np.ascontiguousarray(
            np.asarray(inputs["L0_ffn_Wg"], np.float32).transpose(0, 2, 1)),
        "wu_t": np.ascontiguousarray(
            np.asarray(inputs["L0_ffn_Wu"], np.float32).transpose(0, 2, 1)),
        "wd_t": np.ascontiguousarray(
            np.asarray(inputs["L0_ffn_Wd"], np.float32).transpose(0, 2, 1)),
        "ln1g": chunks8(inputs["L0_attn_ln1_g"]),
        "ln1b": chunks8(inputs["L0_attn_ln1_b"]),
        "ln2g": chunks8(inputs["L0_attn_ln2_g"]),
        "ln2b": chunks8(inputs["L0_attn_ln2_b"]),
        "td": chunks8(inputs["L0_attn_time_decay"]),
        "ident": np.eye(128, dtype=np.float32),
        "tri": np.ascontiguousarray(
            np.triu(np.ones((128, 128), np.float32), 1)),
        "onesc": np.ones((128, 1), np.float32),
        "onesr": np.ones((1, 128), np.float32),
        "tokidG": np.ascontiguousarray(
            (np.arange(CC)[None, :] * 128 + np.arange(128)[:, None] + 1
             ).astype(np.float32)),
    }
    maps = []
    for c in range(8):
        b, h = c // 2, c % 2
        xb = x[b]
        own = xb[h * HALF:(h + 1) * HALF]
        m = dict(shared)
        m["x_fm"] = np.ascontiguousarray(xb.T)
        m["x_own_tm"] = np.ascontiguousarray(own)
        m["x_own_fm"] = np.ascontiguousarray(own.T)
        mc = np.zeros((128, CC), np.float32)
        mc[:, h * CCH:(h + 1) * CCH] = 1.0
        m["maskc"] = mc
        m["hseg"] = np.array([[h * CCH]], np.int32)
        m["selinit"] = np.concatenate([np.zeros((128, E), np.float32),
                                       np.ones((128, E), np.float32)], 1)
        m["hofft1"] = np.full((128, 1), h * HALF + 1, np.float32)
        m["hofft2"] = np.full((128, 1), HALF + h * HALF + 1, np.float32)
        maps.append(m)
    return maps


def kernel(**inputs):
    prog = build()
    maps = make_inmaps(inputs)
    res = run_bass_kernel_spmd(prog.nc, maps, core_ids=list(range(8)))
    out = np.empty((4, T, D), np.float32)
    for c in range(8):
        b, h = c // 2, c % 2
        out[b, h * HALF:(h + 1) * HALF] = res.results[c]["out"]
    return out


# version 03
# speedup vs baseline: 1.0971x; 1.0971x over previous
import contextlib
import numpy as np
import concourse.bass as bass
import concourse.mybir as mybir
from concourse.bass_utils import run_bass_kernel_spmd
from concourse.bass_types import DynSlice
from concourse.bass_core_rust import IndirectOffsetOnAxis

F32 = mybir.dt.float32
I32 = mybir.dt.int32
AL = mybir.AluOpType
AF = mybir.ActivationFunctionType
AX = mybir.AxisListType

T = 8192
D = 1024
E = 16
HALF = 4096
CC = 64
CCH = 32
CAP = 640
NSL = CAP // 128
WID = D + E + 1
SENT = 1 << 20
LN_EPS = 1e-5
WKV_EPS = 1e-8
NB = T // 512
NBO = HALF // 512
BW = 512
FEXP = 2048
FSH = 2048
FCM = 4096
TH_ITERS = 38
TH_PAD = 3e-4


class Prog:

    def __init__(self, debug_taps=()):
        self.nc = bass.Bass("TRN2", target_bir_lowering=False, debug=False,
                            detect_race_conditions=False)
        self.dram = {}
        self.taps = {}
        self.debug_taps = set(debug_taps)
        self.stack = contextlib.ExitStack()
        self.semv = {}

    def din(self, name, shape, dtype=F32):
        ap = self.nc.dram_tensor(name, shape, dtype, kind="ExternalInput").ap()
        self.dram[name] = ap
        return ap

    def dout(self, name, shape, dtype=F32):
        ap = self.nc.dram_tensor(name, shape, dtype, kind="ExternalOutput").ap()
        self.dram[name] = ap
        return ap

    def dscr(self, name, shape, dtype=F32):
        ap = self.nc.dram_tensor(name, shape, dtype, kind="Internal").ap()
        self.dram[name] = ap
        return ap

    def sb(self, name, shape, dtype=F32, stack=None):
        return (stack or self.stack).enter_context(
            self.nc.sbuf_tensor(name, shape, dtype))

    def ps(self, name, shape, dtype=F32, stack=None):
        return (stack or self.stack).enter_context(
            self.nc.psum_tensor(name, shape, dtype))

    def sem(self, name):
        h = self.stack.enter_context(self.nc.semaphore(name))
        self.semv[name] = 0
        return h

    def inc(self, name, instr, n=1):
        instr.then_inc(getattr(self, name), n)
        self.semv[name] += n
        return self.semv[name]

    def dinc(self, name, instr):
        return self.inc(name, instr, 16)

    def wait(self, eng, name, v=None):
        eng.wait_ge(getattr(self, name), self.semv[name] if v is None else v)

    def vf(self, instr):
        self.inc("s_ve", instr)
        self.nc.vector.wait_ge(self.s_ve, self.semv["s_ve"])

    def dma_open(self, eng):
        j = self._rr
        self._rr = (self._rr + 1) % 8
        if self._dtarget[j] > 0:
            eng.wait_ge(self.dsem[j], self._dtarget[j])
        return [j, None]

    def dma_put(self, tok, instr):
        j = tok[0]
        instr.then_inc(self.dsem[j], 16)
        self.semv[f"s_d{j}"] += 16

    def dma_close(self, tok):
        j = tok[0]
        tok[1] = self.semv[f"s_d{j}"]
        self._dtarget[j] = tok[1]
        return tok

    def dma_batch(self, eng, emits):
        tok = self.dma_open(eng)
        for fn in emits:
            self.dma_put(tok, fn())
        return self.dma_close(tok)

    def dma_wait(self, eng, tok):
        eng.wait_ge(self.dsem[tok[0]], tok[1])

    def dma_wait_all(self, eng):
        for j in range(8):
            if self._dtarget[j] > 0:
                eng.wait_ge(self.dsem[j], self._dtarget[j])

    def tap(self, name, src_tile, shape=None):
        if name not in self.debug_taps:
            return
        sc = self.nc.scalar
        self.wait(sc, "s_ve")
        self.wait(sc, "s_ac")
        self.wait(sc, "s_pe")
        shape = shape or list(src_tile.shape)
        ap = self.dout("tap_" + name, shape)
        self.taps[name] = ap
        tok = self.dma_open(sc)
        self.dma_put(tok, sc.dma_start(out=ap, in_=src_tile))
        self.dma_close(tok)
        nc = self.nc
        for eng in (nc.vector, nc.scalar, nc.tensor, nc.gpsimd, nc.sync):
            self.dma_wait(eng, tok)

    def barrier(self, skip_gp=False):
        nc = self.nc
        self.dma_wait_all(nc.sync)
        self.dma_wait_all(nc.scalar)
        engines = (nc.sync, nc.tensor, nc.vector, nc.scalar)
        if not skip_gp:
            self.wait(nc.gpsimd, "s_gp")
            engines = engines + (nc.gpsimd,)
        for eng in engines:
            self.inc("s_bar", eng.nop())
        tgt = self.semv["s_bar"]
        for eng in engines:
            eng.wait_ge(self.s_bar, tgt)


def build(debug_taps=()):
    p = Prog(debug_taps)
    nc = p.nc

    p.din("x_fm", [D, T])
    p.din("x_own_tm", [HALF, D])
    p.din("x_own_fm", [D, HALF])
    for nm in ("wr_t", "ww_t", "wk_t", "wv_t", "wo_t", "wrec_t"):
        p.din(nm, [D, D])
    p.din("wkey_t", [D, FCM])
    p.din("wval_t", [FCM, D])
    p.din("wsg_t", [D, FSH])
    p.din("wsu_t", [D, FSH])
    p.din("wsd_t", [FSH, D])
    p.din("wg_t", [E, D, FEXP])
    p.din("wu_t", [E, D, FEXP])
    p.din("wd_t", [E, FEXP, D])
    p.din("gate_t", [D, E])
    for nm in ("ln1g", "ln1b", "ln2g", "ln2b", "td"):
        p.din(nm, [128, 8])
    p.din("ident", [128, 128])
    p.din("tri", [128, 128])
    p.din("maskc", [128, CC])
    p.din("tokidG", [128, CC])
    p.din("selinit", [128, 2 * E])
    p.din("hofft1", [128, 1])
    p.din("hofft2", [128, 1])
    p.din("onesc", [128, 1])
    p.din("onesr", [1, 128])
    p.din("hseg", [1, 1], I32)

    p.dscr("e_sp", [8, 128, T])
    p.dscr("ev_sp", [8, 128, T])
    p.dscr("r_sp", [8, 128, T])
    p.dscr("wkv_sp", [8, 128, T])
    p.dscr("xn_sp", [8, 128, HALF])
    p.dscr("xn3_sp", [8, 128, HALF])
    p.dscr("y1_sp", [8, 128, HALF])
    p.dscr("sh_sp", [8, 128, HALF])
    p.dscr("vvh_sp", [8, 128, HALF])
    for e in range(E):
        p.dscr(f"xtok{e}", [CAP, WID])
    p.dscr("wid_sp", [128, CC * (E + 1)])
    p.dscr("idxe_sp", [128, CC * E], I32)
    p.dscr("eoT_sp", [E * NSL * 128, D])
    p.dscr("idT_sp", [E * NSL, 128], I32)
    p.dscr("ab0_sp", [2, T])
    p.dscr("macc", [HALF + 1, D])
    p.dout("out", [HALF, D])

    for s in ("s_gp", "s_pe", "s_ve", "s_ac", "s_bar"):
        setattr(p, s, p.sem(s))
    p.dsem = [p.sem(f"s_d{i}") for i in range(8)]
    p._rr = 0
    p._dtarget = [0] * 8

    with nc.Block() as _block:
        emit(p)

    p.stack.close()
    return p


def emit(p):
    nc = p.nc
    sy, te, ve, ac, gp = nc.sync, nc.tensor, nc.vector, nc.scalar, nc.gpsimd

    ident = p.sb("c_ident", [128, 128])
    onesc = p.sb("c_onesc", [128, 1])
    onesr = p.sb("c_onesr", [1, 128])
    lnp = p.sb("c_lnp", [128, 4 * 8])
    decay = p.sb("c_decay", [128, 8])
    hsegt = p.sb("c_hseg", [1, 1], I32)

    dram = p.dram

    tok0 = p.dma_open(sy)
    for nm, dst in (("ident", ident), ("onesc", onesc),
                    ("onesr", onesr), ("hseg", hsegt)):
        p.dma_put(tok0, sy.dma_start(out=dst[:], in_=dram[nm]))
    for i, nm in enumerate(("ln1g", "ln1b", "ln2g", "ln2b")):
        p.dma_put(tok0, sy.dma_start(out=lnp[:, i * 8:(i + 1) * 8], in_=dram[nm]))
    p.dma_put(tok0, sy.dma_start(out=decay[:], in_=dram["td"]))
    p.dma_close(tok0)
    p.dma_wait(ac, tok0)
    p.inc("s_ac", ac.activation(decay[:], decay[:], AF.Exp))

    p.dma_wait(sy, tok0)
    hv = sy.value_load(hsegt[0:1, 0:1])
    p.dma_wait(gp, tok0)
    hvg = gp.value_load(hsegt[0:1, 0:1])

    with contextlib.ExitStack() as sel_st:
        tri = p.sb("m_tri", [128, 128], stack=sel_st)
        gatet = p.sb("m_gate", [128, 8 * E], stack=sel_st)
        maskc = p.sb("m_maskc", [128, CC], stack=sel_st)
        tokidG = p.sb("m_tokidG", [128, CC], stack=sel_st)
        scores3 = p.sb("m_scores3", [128, E * CC], stack=sel_st)
        theta = p.sb("m_theta", [128, E], stack=sel_st)
        gsumP = p.sb("m_gsumP", [128, E], stack=sel_st)
        idxL = p.sb("m_idxL", [128, CC * E], I32, stack=sel_st)
        zeros = p.sb("m_zeros", [128, WID], stack=sel_st)

        tokc = p.dma_open(sy)
        for nm, dst in (("tri", tri), ("maskc", maskc), ("tokidG", tokidG)):
            p.dma_put(tokc, sy.dma_start(out=dst[:], in_=dram[nm]))
        p.dma_put(tokc, sy.dma_start(
            out=gatet[:].rearrange("p (k e) -> p k e", k=8),
            in_=dram["gate_t"].rearrange("(k p) e -> p k e", p=128)))
        p.dma_close(tokc)
        p.inc("s_ve", ve.memset(zeros[:], 0.0))

        p.wait(gp, "s_ve")
        mr = dram["macc"][0:HALF].rearrange("(n p) d -> n p d", p=128)
        for i in range(HALF // 128):
            p.dinc("s_gp", gp.dma_start(out=mr[i], in_=zeros[:, :D]))
        p.dinc("s_gp", gp.dma_start(out=dram["macc"][HALF:HALF + 1, :],
                                    in_=zeros[0:1, 0:D]))
        for e in range(E):
            xr = dram[f"xtok{e}"].rearrange("(n p) w -> n p w", p=128)
            for i in range(NSL):
                p.dinc("s_gp", gp.dma_start(out=xr[i], in_=zeros[:, :WID]))

        p.barrier(skip_gp=True)

        with contextlib.ExitStack() as st:
            ph1(p, st, ident=ident, onesc=onesc, onesr=onesr, gatet=gatet,
                scores3=scores3, gsumP=gsumP)
            p.barrier(skip_gp=True)

        with contextlib.ExitStack() as st:
            selection(p, st, ident=ident, tri=tri, onesc=onesc, onesr=onesr,
                      maskc=maskc, scores3=scores3, theta=theta, gsumP=gsumP,
                      tokidG=tokidG, idxL=idxL)
            p.barrier()

    with contextlib.ExitStack() as st:
        phMprep(p, st, hvg=hvg)
        ph2(p, st, onesr=onesr, onesc=onesc, lnp=lnp)
        p.barrier()

    with contextlib.ExitStack() as st:
        ph3(p, st, decay=decay)
        p.barrier()

    with contextlib.ExitStack() as st:
        phM(p, st, ident=ident)
        p.barrier()

    with contextlib.ExitStack() as st:
        eoFori(p, st)
        ph4a(p, st, onesr=onesr, onesc=onesc, lnp=lnp, hv=hv)
        p.barrier()
    with contextlib.ExitStack() as st:
        ph4sh(p, st)
        p.barrier()
    with contextlib.ExitStack() as st:
        ph4b(p, st)
        p.barrier()
    with contextlib.ExitStack() as st:
        ph4c(p, st, ident=ident)
        p.barrier()


def ln_rows(p, stats_ps, a_slice, b_slice, scr):
    nc = p.nc
    ve, ac = nc.vector, nc.scalar
    W = a_slice.shape[1]
    ve.tensor_scalar_mul(a_slice, stats_ps[:, 0:W], 1.0 / D)
    ve.tensor_scalar_mul(b_slice, stats_ps[:, W:2 * W], 1.0 / D)
    ve.tensor_mul(scr, a_slice, a_slice)
    ve.tensor_sub(b_slice, b_slice, scr[:])
    p.inc("s_ve", ve.tensor_scalar_add(b_slice, b_slice, LN_EPS))
    p.wait(ac, "s_ve")
    p.inc("s_ac", ac.activation(b_slice, b_slice, AF.Sqrt))
    p.wait(ve, "s_ac")
    p.inc("s_ve", ve.reciprocal(b_slice, b_slice))
    p.wait(ve, "s_ve")
    ve.tensor_mul(scr, a_slice, b_slice)
    ve.tensor_copy(a_slice, b_slice)
    p.inc("s_ve", ve.tensor_scalar_mul(b_slice, scr, -1.0))


PH1_PARTS = {"blocks", "gsum", "staging", "transpose", "logits"}


def ph1(p, st, *, ident, onesc, onesr, gatet, scores3, gsumP):
    nc = p.nc
    sy, te, ve, ac = nc.sync, nc.tensor, nc.vector, nc.scalar
    dram = p.dram

    xb = [p.sb(f"p1_xb{i}", [128, 8 * BW], stack=st) for i in range(2)]
    xsq = p.sb("p1_xsq", [128, 8 * BW], stack=st)
    arow = p.sb("p1_arow", [1, T], stack=st)
    brow = p.sb("p1_brow", [1, T], stack=st)
    st18 = p.sb("p1_st18", [18, T], stack=st)
    srow = p.sb("p1_srow", [1, 2 * BW], stack=st)
    scr = p.sb("p1_scr", [1, BW], stack=st)
    a3 = p.sb("p1_a3", [128, CC], stack=st)
    b3 = p.sb("p1_b3", [128, CC], stack=st)
    gx3 = p.sb("p1_gx3", [128, E * CC], stack=st)
    grow = p.sb("p1_grow", [1, E], stack=st)
    l3 = p.sb("p1_l3", [128, E * CC], stack=st)
    t3 = p.sb("p1_t3", [128, E * CC], stack=st)
    sume = p.sb("p1_sume", [128, 8 * CC], stack=st)

    gxp = p.ps("p1_gxp", [16, BW], stack=st)
    stp = p.ps("p1_stp", [1, 2 * BW], stack=st)
    trp = [p.ps(f"p1_trp{i}", [128, 32], stack=st) for i in range(2)]
    bcp = p.ps("p1_bcp", [128, E], stack=st)
    gsp = p.ps("p1_gsp", [1, E], stack=st)

    xr = dram["x_fm"].rearrange("(k p) t -> k p t", p=128)

    ld_mark = []
    ac_mark = []
    pe_mark = []
    ve_mark = []

    for blk in range(NB if "blocks" in PH1_PARTS else 0):
        if blk >= 2:
            sy.wait_ge(p.s_pe, pe_mark[blk - 2])
        tok = p.dma_open(sy)
        for k in range(8):
            p.dma_put(tok, sy.dma_start(
                out=xb[blk % 2][:, k * BW:(k + 1) * BW],
                in_=xr[k, :, blk * BW:(blk + 1) * BW]))
        ld_mark.append(p.dma_close(tok))

        p.dma_wait(ac, ld_mark[blk])
        if blk >= 1:
            ac.wait_ge(p.s_pe, pe_mark[blk - 1])
        p.inc("s_ac", ac.activation(xsq[:], xb[blk % 2][:], AF.Square))
        ac_mark.append(p.semv["s_ac"])

        te.wait_ge(p.s_ac, ac_mark[blk])
        if blk >= 1:
            te.wait_ge(p.s_ve, ve_mark[blk - 1])
        for k in range(8):
            kw = slice(k * BW, (k + 1) * BW)
            te.matmul(gxp[:], gatet[:, k * E:(k + 1) * E], xb[blk % 2][:, kw],
                      start=(k == 0), stop=(k == 7))
            te.matmul(stp[:, 0:BW], onesc[:], xb[blk % 2][:, kw],
                      start=(k == 0), stop=(k == 7))
            i = te.matmul(stp[:, BW:2 * BW], onesc[:], xsq[:, kw],
                          start=(k == 0), stop=(k == 7))
        p.inc("s_pe", i)
        pe_mark.append(p.semv["s_pe"])

        ve.wait_ge(p.s_pe, pe_mark[blk])
        ve.tensor_copy(st18[0:16, blk * BW:(blk + 1) * BW], gxp[:])
        ve.tensor_copy(srow[:], stp[:])
        ln_rows(p, srow,
                arow[:, blk * BW:(blk + 1) * BW],
                brow[:, blk * BW:(blk + 1) * BW], scr[:])
        ve_mark.append(p.semv["s_ve"])

    p.tap("arow", arow[:])
    p.tap("brow", brow[:])

    if "gsum" not in PH1_PARTS:
        return
    p.dma_wait_all(te)
    for k in range(8):
        i = te.matmul(gsp[:], onesc[:], gatet[:, k * E:(k + 1) * E],
                      start=(k == 0), stop=(k == 7))
    p.inc("s_pe", i)
    p.wait(ve, "s_pe")
    p.inc("s_ve", ve.tensor_copy(grow[:], gsp[:]))
    p.wait(te, "s_ve")
    i = te.matmul(bcp[:], onesr[:], grow[:], start=True, stop=True)
    p.inc("s_pe", i)
    p.wait(ve, "s_pe")
    p.inc("s_ve", ve.tensor_copy(gsumP[:], bcp[:]))

    if "staging" not in PH1_PARTS:
        return
    p.wait(sy, "s_ve")
    stok = p.dma_open(sy)
    p.dma_put(stok, sy.dma_start(out=st18[16:17, :], in_=arow[:]))
    p.dma_put(stok, sy.dma_start(out=st18[17:18, :], in_=brow[:]))
    p.dma_put(stok, sy.dma_start(out=dram["ab0_sp"][0:1, :], in_=arow[:]))
    p.dma_put(stok, sy.dma_start(out=dram["ab0_sp"][1:2, :], in_=brow[:]))
    stg_ld = p.dma_close(stok)

    if "transpose" not in PH1_PARTS:
        return
    p.dma_wait(te, stg_ld)
    ve_evac_base = p.semv["s_ve"]
    for tb in range(CC):
        if tb >= 2:
            te.wait_ge(p.s_ve, ve_evac_base + tb - 1)
        i = te.transpose(trp[tb % 2][:, 0:18],
                         st18[:, tb * 128:(tb + 1) * 128], ident[0:18, 0:18])
        p.inc("s_pe", i)
        ve.wait_ge(p.s_pe, p.semv["s_pe"])
        gx3v = gx3[:].rearrange("p (e c) -> p e c", c=CC)
        ve.tensor_copy(gx3v[:, :, tb:tb + 1],
                       trp[tb % 2][:, 0:16].rearrange("p (e c) -> p e c", c=1))
        ve.tensor_copy(a3[:, tb:tb + 1], trp[tb % 2][:, 16:17])
        p.inc("s_ve", ve.tensor_copy(b3[:, tb:tb + 1], trp[tb % 2][:, 17:18]))

    if "logits" not in PH1_PARTS:
        return
    a3bc = a3[:].unsqueeze(1).broadcast_to([128, E, CC])
    b3bc = b3[:].unsqueeze(1).broadcast_to([128, E, CC])
    gsbc = gsumP[:].unsqueeze(2).broadcast_to([128, E, CC])
    l3v = l3[:].rearrange("p (e c) -> p e c", c=CC)
    t3v = t3[:].rearrange("p (e c) -> p e c", c=CC)
    p.wait(ve, "s_ve")
    ve.tensor_mul(l3v, gx3[:].rearrange("p (e c) -> p e c", c=CC), a3bc)
    ve.tensor_mul(t3v, b3bc, gsbc)
    p.inc("s_ve", ve.tensor_add(l3[:], l3[:], t3[:]))
    p.tap("logits3", l3[:])
    p.tap("gx3", gx3[:])
    p.tap("a3", a3[:])
    p.tap("b3", b3[:])
    p.tap("gsumP", gsumP[:])
    p.wait(ac, "s_ve")
    p.inc("s_ac", ac.activation(l3[:], l3[:], AF.Exp))
    p.wait(ve, "s_ac")
    l3e = l3[:].rearrange("p (e c) -> p e c", c=CC)
    s8 = sume[:].rearrange("p (e c) -> p e c", c=CC)
    ve.tensor_add(s8[:, 0:8, :], l3e[:, 0:8, :], l3e[:, 8:16, :])
    ve.tensor_add(s8[:, 0:4, :], s8[:, 0:4, :], s8[:, 4:8, :])
    ve.tensor_add(s8[:, 0:2, :], s8[:, 0:2, :], s8[:, 2:4, :])
    p.vf(ve.tensor_add(s8[:, 0:1, :], s8[:, 0:1, :], s8[:, 1:2, :]))
    p.inc("s_ve", ve.reciprocal(sume[:, 0:CC], sume[:, 0:CC]))
    p.wait(ve, "s_ve")
    p.tap("sumr", sume[:, 0:CC])
    rbc = sume[:, 0:CC].unsqueeze(1).broadcast_to([128, E, CC])
    p.inc("s_ve", ve.tensor_mul(
        scores3[:].rearrange("p (e c) -> p e c", c=CC), l3e, rbc))
    p.tap("scores3", scores3[:])


def selection(p, st, *, ident, tri, onesc, onesr, maskc, scores3, theta,
              gsumP, tokidG, idxL):
    dram = p.dram
    """Binary-search theta per expert; build w (e-minor) + scatter slots."""
    nc = p.nc
    sy, te, ve, ac = nc.sync, nc.tensor, nc.vector, nc.scalar

    lohit = p.sb("sl_lohi", [128, 2 * E], stack=st)
    lo = lohit[:, 0:E]
    hi = lohit[:, E:2 * E]
    mid = p.sb("sl_mid", [128, E], stack=st)
    ge = p.sb("sl_ge", [128, E * CC], stack=st)
    red = p.sb("sl_red", [128, E], stack=st)
    crow = p.sb("sl_crow", [1, E], stack=st)
    cntP = p.sb("sl_cntP", [128, E], stack=st)
    cntN = p.sb("sl_cntN", [128, E], stack=st)
    tmp = p.sb("sl_tmp", [128, E], stack=st)
    tmp2 = p.sb("sl_tmp2", [128, E], stack=st)
    w3 = p.sb("sl_w3", [128, E * CC], stack=st)
    cand = p.sb("sl_cand", [128, E * CC], stack=st)
    jj = p.sb("sl_jj", [128, E * CC], stack=st)
    rtot = p.sb("sl_rtot", [128, E], stack=st)
    offp = p.sb("sl_offp", [128, E], stack=st)
    idxf = p.sb("sl_idxf", [128, E * CC], stack=st)
    idx3 = p.sb("sl_idx3", [128, E * CC], I32, stack=st)
    wm2 = p.sb("sl_wm2", [128, CC * (E + 1)], stack=st)

    cp = p.ps("sl_cp", [1, E], stack=st)
    bp = p.ps("sl_bp", [128, E], stack=st)
    op = p.ps("sl_op", [128, E], stack=st)

    s3v = scores3[:].rearrange("p (e c) -> p e c", c=CC)
    gev = ge[:].rearrange("p (e c) -> p e c", c=CC)

    itok = p.dma_open(sy)
    p.dma_put(itok, sy.dma_start(out=lohit[:], in_=dram["selinit"]))
    p.dma_close(itok)
    p.dma_wait(ve, itok)

    for it in range(TH_ITERS):
        p.vf(ve.tensor_add(mid[:], lo, hi))
        p.vf(ve.tensor_scalar_mul(mid[:], mid[:], 0.5))
        midbc = mid[:].unsqueeze(2).broadcast_to([128, E, CC])
        ve.tensor_tensor(gev, s3v, midbc, AL.is_ge)
        p.inc("s_ve", ve.tensor_reduce(
            red[:].unsqueeze(2), gev, AX.X, AL.add))
        if it == 0:
            p.tap("red0", red[:])
            p.tap("mid0", mid[:])
        p.wait(te, "s_ve")
        i = te.matmul(cp[:], onesc[:], red[:], start=True, stop=True)
        p.inc("s_pe", i)
        p.wait(ve, "s_pe")
        p.inc("s_ve", ve.tensor_copy(crow[:], cp[:]))
        p.wait(te, "s_ve")
        i = te.matmul(bp[:], onesr[:], crow[:], start=True, stop=True)
        p.inc("s_pe", i)
        p.wait(ve, "s_pe")
        if it == 0 and "cnt0" in p.debug_taps:
            p.vf(ve.tensor_copy(tmp[:], bp[:]))
            p.tap("cnt0", tmp[:])
        p.vf(ve.tensor_single_scalar(cntP[:], bp[:], 1024.0, AL.is_ge))
        p.vf(ve.tensor_sub(tmp[:], mid[:], lo))
        p.vf(ve.tensor_mul(tmp[:], tmp[:], cntP[:]))
        p.vf(ve.tensor_sub(tmp2[:], mid[:], hi))
        p.vf(ve.tensor_add(lo, lo, tmp[:]))
        p.vf(ve.tensor_scalar(cntN[:], cntP[:], -1.0, 1.0, AL.mult, AL.add))
        p.vf(ve.tensor_mul(tmp2[:], tmp2[:], cntN[:]))
        p.vf(ve.tensor_add(hi, hi, tmp2[:]))

    p.vf(ve.tensor_copy(theta[:], lo))
    p.tap("theta", theta[:])

    thbc = theta[:].unsqueeze(2).broadcast_to([128, E, CC])
    w3v = w3[:].rearrange("p (e c) -> p e c", c=CC)
    ve.tensor_tensor(w3v, s3v, thbc, AL.is_ge)
    ve.tensor_mul(w3[:], w3[:], scores3[:])
    p.vf(ve.tensor_scalar_mul(tmp[:], theta[:], 1.0 - TH_PAD))
    tpbc = tmp[:].unsqueeze(2).broadcast_to([128, E, CC])
    cndv = cand[:].rearrange("p (e c) -> p e c", c=CC)
    ve.tensor_tensor(cndv, s3v, tpbc, AL.is_ge)
    mbc = maskc[:].unsqueeze(1).broadcast_to([128, E, CC])
    p.inc("s_ve", ve.tensor_mul(cndv, cndv, mbc))
    p.tap("cand", cand[:])

    onesbc = onesc[:].broadcast_to([128, CC])
    for e in range(E):
        i = ve.tensor_tensor_scan(jj[:, e * CC:(e + 1) * CC],
                                  onesbc, cand[:, e * CC:(e + 1) * CC],
                                  0.0, AL.mult, AL.add)
        if e == E - 1:
            p.vf(i)
    p.inc("s_ve", ve.tensor_copy(rtot[:], jj[:].rearrange(
        "p (e c) -> p e c", c=CC)[:, :, CC - 1:CC]))
    p.wait(te, "s_ve")
    i = te.matmul(op[:], tri[:], rtot[:], start=True, stop=True)
    p.inc("s_pe", i)
    p.wait(ve, "s_pe")
    p.vf(ve.tensor_copy(offp[:], op[:]))
    offbc = offp[:].unsqueeze(2).broadcast_to([128, E, CC])
    jv = jj[:].rearrange("p (e c) -> p e c", c=CC)
    ve.tensor_add(jv, jv, offbc)
    ve.tensor_single_scalar(ge[:], jj[:], float(CAP), AL.is_le)
    ve.tensor_mul(ge[:], ge[:], cand[:])
    ve.tensor_scalar_add(idxf[:], jj[:], float(-1 - SENT))
    ve.tensor_mul(idxf[:], idxf[:], ge[:])
    ve.tensor_scalar_add(idxf[:], idxf[:], float(SENT))
    p.inc("s_ve", ve.tensor_copy(idx3[:], idxf[:]))
    p.tap("jj", jj[:])

    widv = wm2[:].rearrange("p (c w) -> p c w", w=E + 1)
    w3p = w3[:].rearrange("p (e c) -> p c e", c=CC, e=E)
    ve.tensor_copy(widv[:, :, 0:E], w3p)
    ve.tensor_copy(widv[:, :, E:E + 1], tokidG[:].unsqueeze(2))
    idxev = idxL[:].rearrange("p (c e) -> p c e", e=E)
    idx3p = idx3[:].rearrange("p (e c) -> p c e", c=CC, e=E)
    p.inc("s_ve", ve.tensor_copy(idxev, idx3p))
    p.wait(sy, "s_ve")
    sptok = p.dma_open(sy)
    p.dma_put(sptok, sy.dma_start(out=dram["wid_sp"], in_=wm2[:, 0:CC * (E + 1)]))
    p.dma_put(sptok, sy.dma_start(out=dram["idxe_sp"], in_=idxL[:]))
    p.dma_close(sptok)
    p.tap("widsp", wm2[:, 0:CC * (E + 1)])
    p.tap("idxLf", idxf[:])


def phMprep(p, st, *, hvg):
    nc = p.nc
    gp = nc.gpsimd
    dram = p.dram

    wide = p.sb("mp_wide", [128, WID], stack=st)
    ixt = p.sb("mp_ixt", [128, E], I32, stack=st)

    xtm = dram["x_own_tm"]
    widr = dram["wid_sp"]
    idxr = dram["idxe_sp"]

    gbase = p.semv["s_gp"]
    PER = 3 * 16 + E * 16
    with gp.Fori(0, CCH) as i:
        gp.wait_ge(p.s_gp, i * PER + gbase)
        gp.dma_start(out=wide[:, 0:D],
                     in_=xtm[DynSlice(i * 128, 128), :]).then_inc(p.s_gp, 16)
        gp.dma_start(out=wide[:, D:WID],
                     in_=widr[:, DynSlice((hvg + i) * (E + 1), E + 1)]
                     ).then_inc(p.s_gp, 16)
        gp.dma_start(out=ixt[:],
                     in_=idxr[:, DynSlice((hvg + i) * E, E)]).then_inc(p.s_gp, 16)
        gp.wait_ge(p.s_gp, i * PER + gbase + 48)
        for e in range(E):
            gp.indirect_dma_start(
                dram[f"xtok{e}"],
                IndirectOffsetOnAxis(ixt[:, e:e + 1], 0),
                wide[:], None,
                bounds_check=CAP - 1, oob_is_err=False).then_inc(p.s_gp, 16)
    p.semv["s_gp"] += CCH * PER
    gp.wait_ge(p.s_gp, p.semv["s_gp"])


def ph2(p, st, *, onesr, onesc, lnp):
    nc = p.nc
    sy, te, ve, ac = nc.sync, nc.tensor, nc.vector, nc.scalar
    sc = nc.scalar
    dram = p.dram

    wts = {}
    wtok = p.dma_open(sy)
    for nm in ("wr_t", "ww_t", "wk_t", "wv_t"):
        wts[nm] = p.sb("p2_" + nm, [128, 8 * D], stack=st)
        p.dma_put(wtok, sy.dma_start(
            out=wts[nm][:].rearrange("p (k e) -> p k e", k=8),
            in_=dram[nm].rearrange("(k p) e -> p k e", p=128)))
    w_ld = p.dma_close(wtok)

    xb = [p.sb("p2_xb0", [128, 8 * BW], stack=st)] * 2
    xsq = p.sb("p2_xsq", [128, 4 * BW], stack=st)
    xn = p.sb("p2_xn", [128, 8 * BW], stack=st)
    xn2 = xb[0]
    ab = p.sb("p2_ab", [128, 2 * BW], stack=st)
    srow = p.sb("p2_srow", [1, 2 * BW], stack=st)
    a0r = p.sb("p2_a0r", [1, BW], stack=st)
    b0r = p.sb("p2_b0r", [1, BW], stack=st)
    a1r = p.sb("p2_a1r", [1, BW], stack=st)
    b1r = p.sb("p2_b1r", [1, BW], stack=st)
    scr = p.sb("p2_scr", [1, BW], stack=st)
    ew = [p.sb(f"p2_ew{i}", [128, BW], stack=st) for i in range(2)]
    eb = [p.sb(f"p2_eb{i}", [128, BW], stack=st) for i in range(2)]
    evb = [p.sb(f"p2_evb{i}", [128, BW], stack=st) for i in range(2)]
    rb = [p.sb(f"p2_rb{i}", [128, BW], stack=st) for i in range(2)]

    bc_ps = p.ps("p2_bc", [128, BW], stack=st)
    st_ps = p.ps("p2_st", [1, 2 * BW], stack=st)
    mm_ps = [p.ps(f"p2_mm{i}", [128, BW], stack=st) for i in range(4)]

    xr = dram["x_fm"].rearrange("(k p) t -> k p t", p=128)

    def bcast(row_slice, dst_slice):
        p.wait(te, "s_ve")
        i = te.matmul(bc_ps[:], onesr[:], row_slice, start=True, stop=True)
        p.inc("s_pe", i)
        p.wait(ve, "s_pe")
        p.inc("s_ve", ve.tensor_copy(dst_slice, bc_ps[:]))

    pe_mark = [0]
    for blk in range(NB):
        tw = slice(blk * BW, (blk + 1) * BW)
        if blk >= 1:
            sy.wait_ge(p.s_pe, pe_mark[-1])
            sy.wait_ge(p.s_ve, p.semv["s_ve"])
        if blk >= 1:
            p.dma_wait(ve, _SPILLS[(id(p), blk - 1, 7)])
            p.dma_wait(ac, _SPILLS[(id(p), blk - 1, 7)])
        tok = p.dma_open(sy)
        for k in range(8):
            p.dma_put(tok, sy.dma_start(
                out=xb[blk % 2][:, k * BW:(k + 1) * BW], in_=xr[k, :, tw]))
        p.dma_put(tok, sy.dma_start(out=a0r[:], in_=dram["ab0_sp"][0:1, tw]))
        p.dma_put(tok, sy.dma_start(out=b0r[:], in_=dram["ab0_sp"][1:2, tw]))
        ldm = p.dma_close(tok)

        p.dma_wait(te, ldm)
        if blk == 0:
            p.dma_wait(te, w_ld)
        bcast(a0r[:], ab[:, 0:BW])
        bcast(b0r[:], ab[:, BW:2 * BW])
        a0b = ab[:, 0:BW].unsqueeze(1).broadcast_to([128, 8, BW])
        b0b = ab[:, BW:2 * BW].unsqueeze(1).broadcast_to([128, 8, BW])
        xbv = xb[blk % 2][:].rearrange("p (k w) -> p k w", k=8)
        xnv = xn[:].rearrange("p (k w) -> p k w", k=8)
        p.dma_wait(ve, ldm)
        ve.tensor_mul(xnv, xbv, a0b)
        p.inc("s_ve", ve.tensor_add(xnv, xnv, b0b))

        p.wait(ac, "s_ve")
        p.inc("s_ac", ac.activation(xsq[:], xn[:, 0:4 * BW], AF.Square))
        p.wait(te, "s_ac")
        p.wait(te, "s_ve")
        i = None
        for k in range(8):
            if k == 4:
                p.inc("s_pe", i)
                ac.wait_ge(p.s_pe, p.semv["s_pe"])
                p.inc("s_ac", ac.activation(xsq[:], xn[:, 4 * BW:8 * BW],
                                            AF.Square))
                p.wait(te, "s_ac")
            kw = slice(k * BW, (k + 1) * BW)
            te.matmul(st_ps[:, 0:BW], onesc[:], xn[:, kw],
                      start=(k == 0), stop=(k == 7))
            i = te.matmul(st_ps[:, BW:2 * BW], onesc[:],
                          xsq[:, (k % 4) * BW:(k % 4 + 1) * BW],
                          start=(k == 0), stop=(k == 7))
        p.inc("s_pe", i)
        p.wait(ve, "s_pe")
        ve.tensor_copy(srow[:], st_ps[:])
        ln_rows(p, srow, a1r[:], b1r[:], scr[:])
        bcast(a1r[:], ab[:, 0:BW])
        bcast(b1r[:], ab[:, BW:2 * BW])
        a1b = ab[:, 0:BW].unsqueeze(1).broadcast_to([128, 8, BW])
        b1b = ab[:, BW:2 * BW].unsqueeze(1).broadcast_to([128, 8, BW])
        xn2v = xn2[:].rearrange("p (k w) -> p k w", k=8)
        ve.tensor_mul(xn2v, xnv, a1b)
        ve.tensor_add(xn2v, xn2v, b1b)
        for k in range(8):
            kw = slice(k * BW, (k + 1) * BW)
            ve.tensor_scalar(xn2[:, kw], xn2[:, kw],
                             lnp[:, 0 * 8 + k:0 * 8 + k + 1],
                             lnp[:, 1 * 8 + k:1 * 8 + k + 1], AL.mult, AL.add)
        p.inc("s_ve", ve.engine_nop())
        xn2_ve = p.semv["s_ve"]

        te.wait_ge(p.s_ve, xn2_ve)
        cons = [None] * 4
        for m in range(8):
            bi = m % 2
            pm = st_spill_mark(p, blk, m, 2)
            if pm is not None:
                p.dma_wait(ve, pm)
                p.dma_wait(ac, pm)
            for oi, wname in enumerate(("ww_t", "wk_t", "wv_t", "wr_t")):
                pb = mm_ps[oi]
                if cons[oi] is not None:
                    te.wait_ge(getattr(p, cons[oi][0]), cons[oi][1])
                for k in range(8):
                    i = te.matmul(pb[:],
                                  wts[wname][:, k * D + m * 128:k * D + m * 128 + 128],
                                  xn2[:, k * BW:(k + 1) * BW],
                                  start=(k == 0), stop=(k == 7))
                p.inc("s_pe", i)
                mm_pe = p.semv["s_pe"]
                if wname == "ww_t":
                    ac.wait_ge(p.s_pe, mm_pe)
                    p.inc("s_ac", ac.activation(ew[bi][:], pb[:], AF.Exp))
                    cons[0] = ("s_ac", p.semv["s_ac"])
                elif wname == "wk_t":
                    ve.wait_ge(p.s_pe, mm_pe)
                    ve.wait_ge(p.s_ac, p.semv["s_ac"])
                    p.inc("s_ve", ve.scalar_tensor_tensor(
                        eb[bi][:], pb[:], 1.0, ew[bi][:], AL.mult, AL.subtract))
                    cons[1] = ("s_ve", p.semv["s_ve"])
                    ac.wait_ge(p.s_ve, p.semv["s_ve"])
                    p.inc("s_ac", ac.activation(eb[bi][:], eb[bi][:], AF.Exp))
                elif wname == "wv_t":
                    ve.wait_ge(p.s_pe, mm_pe)
                    ve.wait_ge(p.s_ac, p.semv["s_ac"])
                    p.inc("s_ve", ve.tensor_mul(evb[bi][:], eb[bi][:], pb[:]))
                    cons[2] = ("s_ve", p.semv["s_ve"])
                else:
                    ac.wait_ge(p.s_pe, mm_pe)
                    p.inc("s_ac", ac.activation(rb[bi][:], pb[:], AF.Sigmoid))
                    cons[3] = ("s_ac", p.semv["s_ac"])
            sc.wait_ge(p.s_ve, p.semv["s_ve"])
            sc.wait_ge(p.s_ac, p.semv["s_ac"])
            stok = p.dma_open(sc)
            for nm, tile in (("e_sp", eb[bi]), ("ev_sp", evb[bi]),
                             ("r_sp", rb[bi])):
                p.dma_put(stok, sc.dma_start(out=dram[nm][m, :, tw],
                                             in_=tile[:]))
            _SPILLS[(id(p), blk, m)] = p.dma_close(stok)
        pe_mark.append(p.semv["s_pe"])


_SPILLS = {}


def st_spill_mark(p, blk, m, dist):
    mm = m - dist
    key = (id(p), blk, mm) if mm >= 0 else (id(p), blk - 1, 8 + mm)
    return _SPILLS.get(key)
def ph3(p, st, *, decay):
    nc = p.nc
    sy, ve, sc = nc.sync, nc.vector, nc.scalar
    dram = p.dram

    ebuf = p.sb("p3_e", [128, T], stack=st)
    evbuf = p.sb("p3_ev", [128, T], stack=st)
    num = p.sb("p3_num", [128, T], stack=st)
    den = p.sb("p3_den", [128, T], stack=st)

    ve_mark = []
    st_tok = []
    for k in range(8):
        if k >= 1:
            sy.wait_ge(p.s_ve, ve_mark[k - 1])
        tok = p.dma_open(sy)
        p.dma_put(tok, sy.dma_start(out=ebuf[:], in_=dram["e_sp"][k]))
        p.dma_put(tok, sy.dma_start(out=evbuf[:], in_=dram["ev_sp"][k]))
        p.dma_close(tok)
        p.dma_wait(ve, tok)
        if k >= 1:
            p.dma_wait(ve, st_tok[k - 1])
        dk = decay[:, k:k + 1].broadcast_to([128, T])
        ve.tensor_tensor_scan(num[:], dk, evbuf[:], 0.0, AL.mult, AL.add)
        ve.tensor_tensor_scan(den[:], dk, ebuf[:], 0.0, AL.mult, AL.add)
        ve.tensor_scalar_add(den[:], den[:], WKV_EPS)
        p.inc("s_ve", ve.reciprocal(den[:], den[:]))
        p.wait(ve, "s_ve")
        p.inc("s_ve", ve.tensor_mul(num[:], num[:], den[:]))
        ve_mark.append(p.semv["s_ve"])
        sc.wait_ge(p.s_ve, p.semv["s_ve"])
        stok = p.dma_open(sc)
        p.dma_put(stok, sc.dma_start(out=dram["wkv_sp"][k], in_=num[:]))
        st_tok.append(p.dma_close(stok))


def phM(p, st, *, ident):
    nc = p.nc
    sy, te, ve, ac, gp = nc.sync, nc.tensor, nc.vector, nc.scalar, nc.gpsimd
    sc = nc.scalar
    dram = p.dram

    wg = [p.sb(f"pm_wg{i}", [128, 8 * 512], stack=st) for i in range(2)]
    wu = [p.sb(f"pm_wu{i}", [128, 8 * 512], stack=st) for i in range(2)]
    wd = [p.sb(f"pm_wd{i}", [128, 4 * 1024], stack=st) for i in range(2)]
    xt = p.sb("pm_xt", [128, NSL * WID], stack=st)
    eosb = xt[:, 0:8 * CAP]
    xsq = p.sb("pm_xsq", [128, D], stack=st)
    stats = p.sb("pm_stats", [128, 4 * NSL], stack=st)
    xtT = p.sb("pm_xtT", [128, 8 * CAP], stack=st)
    gel = p.sb("pm_gel", [128, 320], stack=st)
    hsb = p.sb("pm_hsb", [128, 8 * 320], stack=st)
    wcol = p.sb("pm_wcol", [128, NSL], stack=st)
    eoT = [p.sb(f"pm_eoT{i}", [128, D], stack=st) for i in range(2)]
    hofftt = p.sb("pm_hofft", [128, 2], stack=st)
    idc = p.sb("pm_idc", [128, NSL], I32, stack=st)
    idcf = p.sb("pm_idcf", [128, NSL], stack=st)
    idcm = p.sb("pm_idcm", [128, NSL], stack=st)

    trp = [p.ps(f"pm_trp{i}", [128, 128], stack=st) for i in range(2)]
    gps = p.ps("pm_gps", [128, BW], stack=st)
    ups = p.ps("pm_ups", [128, BW], stack=st)
    eop = p.ps("pm_eop", [128, BW], stack=st)
    top = p.ps("pm_top", [128, 1024], stack=st)

    xtv = xt[:].rearrange("p (n w) -> p n w", n=NSL)
    eoTr = dram["eoT_sp"].rearrange("(u p) d -> u p d", p=128)
    st_eoT = []

    htok = p.dma_open(sy)
    p.dma_put(htok, sy.dma_start(out=hofftt[:, 0:1], in_=dram["hofft1"]))
    p.dma_put(htok, sy.dma_start(out=hofftt[:, 1:2], in_=dram["hofft2"]))
    p.dma_close(htok)

    wq_tok = {}
    buf_mark = [0, 0]

    def issue_wq(gidx):
        if gidx >= 4 * E:
            return
        eq, q = divmod(gidx, 4)
        b = gidx % 2
        if buf_mark[b] > 0:
            sy.wait_ge(p.s_pe, buf_mark[b])
        fq = slice(q * 512, (q + 1) * 512)
        tok = p.dma_open(sy)
        p.dma_put(tok, sy.dma_start(
            out=wg[b][:].rearrange("p (k f) -> p k f", k=8),
            in_=dram["wg_t"][eq].rearrange("(k p) f -> p k f", p=128)[:, :, fq]))
        p.dma_put(tok, sy.dma_start(
            out=wu[b][:].rearrange("p (k f) -> p k f", k=8),
            in_=dram["wu_t"][eq].rearrange("(k p) f -> p k f", p=128)[:, :, fq]))
        p.dma_put(tok, sy.dma_start(
            out=wd[b][:].rearrange("p (k d) -> p k d", k=4),
            in_=dram["wd_t"][eq].rearrange(
                "(k p) d -> p k d", p=128)[:, q * 4:(q + 1) * 4]))
        wq_tok[gidx] = p.dma_close(tok)

    issue_wq(0)
    issue_wq(1)

    for e in range(E):
        if e > 0:
            sy.wait_ge(p.s_pe, p.semv["s_pe"])
            sy.wait_ge(p.s_ve, p.semv["s_ve"])
            for t_ in st_eoT[-2:]:
                p.dma_wait(sy, t_)
        xtok_t = p.dma_open(sy)
        p.dma_put(xtok_t, sy.dma_start(
            out=xtv, in_=dram[f"xtok{e}"].rearrange("(n p) w -> p n w", p=128)))
        p.dma_close(xtok_t)
        if e == 0:
            p.dma_wait(ve, htok)

        p.dma_wait(ve, xtok_t)
        p.dma_wait(ac, xtok_t)
        for sl in range(NSL):
            xs = xt[:, sl * WID:sl * WID + D]
            if sl > 0:
                ac.wait_ge(p.s_ve, p.semv["s_ve"])
            p.inc("s_ac", ac.activation(xsq[:], xs, AF.Square))
            ve.tensor_reduce(stats[:, sl:sl + 1], xs.unsqueeze(1), AX.X, AL.add)
            ve.wait_ge(p.s_ac, p.semv["s_ac"])
            p.inc("s_ve", ve.tensor_reduce(
                stats[:, NSL + sl:NSL + sl + 1], xsq[:].unsqueeze(1),
                AX.X, AL.add))
        p.wait(ve, "s_ve")
        ve.tensor_scalar_mul(stats[:, 2 * NSL:3 * NSL], stats[:, 0:NSL], 1.0 / D)
        p.vf(ve.tensor_scalar_mul(stats[:, 3 * NSL:4 * NSL],
                                  stats[:, NSL:2 * NSL], 1.0 / D))
        p.vf(ve.tensor_mul(stats[:, 0:NSL], stats[:, 2 * NSL:3 * NSL],
                           stats[:, 2 * NSL:3 * NSL]))
        p.vf(ve.tensor_sub(stats[:, 3 * NSL:4 * NSL], stats[:, 3 * NSL:4 * NSL],
                           stats[:, 0:NSL]))
        p.inc("s_ve", ve.tensor_scalar_add(stats[:, 3 * NSL:4 * NSL],
                                           stats[:, 3 * NSL:4 * NSL], LN_EPS))
        p.wait(ac, "s_ve")
        ac.activation(stats[:, 3 * NSL:4 * NSL],
                      stats[:, 3 * NSL:4 * NSL], AF.Sqrt)
        p.inc("s_ac", ac.activation(xsq[:, 0:512], xsq[:, 0:512], AF.Square))
        p.wait(ve, "s_ac")
        p.inc("s_ve", ve.reciprocal(stats[:, 3 * NSL:4 * NSL],
                                    stats[:, 3 * NSL:4 * NSL]))
        p.wait(ve, "s_ve")
        p.vf(ve.tensor_mul(stats[:, 2 * NSL:3 * NSL], stats[:, 2 * NSL:3 * NSL],
                           stats[:, 3 * NSL:4 * NSL]))
        p.vf(ve.tensor_scalar_mul(stats[:, 2 * NSL:3 * NSL],
                                  stats[:, 2 * NSL:3 * NSL], -1.0))
        for sl in range(NSL):
            ve.tensor_scalar(xt[:, sl * WID:sl * WID + D],
                             xt[:, sl * WID:sl * WID + D],
                             stats[:, 3 * NSL + sl:3 * NSL + sl + 1],
                             stats[:, 2 * NSL + sl:2 * NSL + sl + 1],
                             AL.mult, AL.add)
        ve.tensor_copy(wcol[:].unsqueeze(2), xtv[:, :, D + e:D + e + 1])
        p.vf(ve.tensor_copy(idcf[:].unsqueeze(2), xtv[:, :, WID - 1:WID]))
        p.vf(ve.tensor_scalar(idcf[:], idcf[:], hofftt[:, 0:1], None,
                              AL.subtract))
        p.vf(ve.tensor_single_scalar(idcm[:], idcf[:], 0.0, AL.is_lt))
        p.vf(ve.scalar_tensor_tensor(idcf[:], idcm[:], hofftt[:, 1:2], idcf[:],
                                     AL.mult, AL.add))
        p.inc("s_ve", ve.tensor_copy(idc[:], idcf[:]))
        rn_ve = p.semv["s_ve"]

        sc.wait_ge(p.s_ve, rn_ve)
        idtok = p.dma_open(sc)
        p.dma_put(idtok, sc.dma_start(
            out=dram["idT_sp"][e * NSL:(e + 1) * NSL, :].rearrange("s p -> p s"),
            in_=idc[:], allow_slow_non_contiguous=True))
        p.dma_close(idtok)

        te.wait_ge(p.s_ve, rn_ve)
        nt = 0
        tr_ve = p.semv["s_ve"]
        for dc in range(8):
            for sl in range(NSL):
                if nt >= 2:
                    te.wait_ge(p.s_ve, tr_ve + nt - 1)
                i = te.transpose(
                    trp[nt % 2][:],
                    xt[:, sl * WID + dc * 128:sl * WID + (dc + 1) * 128],
                    ident[:])
                p.inc("s_pe", i)
                ve.wait_ge(p.s_pe, p.semv["s_pe"])
                p.inc("s_ve", ve.tensor_copy(
                    xtT[:, dc * CAP + sl * 128:dc * CAP + (sl + 1) * 128],
                    trp[nt % 2][:]))
                nt += 1

        for q in range(4):
            gidx = e * 4 + q
            b = gidx % 2
            p.dma_wait(te, wq_tok[gidx])
            for nh in range(2):
                for fcl in range(4):
                    te.wait_ge(p.s_ve, p.semv["s_ve"])
                    for k in range(8):
                        te.matmul(gps[:, 0:320],
                                  wg[b][:, k * 512 + fcl * 128:k * 512 + (fcl + 1) * 128],
                                  xtT[:, k * CAP + nh * 320:k * CAP + nh * 320 + 320],
                                  start=(k == 0), stop=(k == 7))
                    for k in range(8):
                        i = te.matmul(ups[:, 0:320],
                                      wu[b][:, k * 512 + fcl * 128:k * 512 + (fcl + 1) * 128],
                                      xtT[:, k * CAP + nh * 320:k * CAP + nh * 320 + 320],
                                      start=(k == 0), stop=(k == 7))
                    p.inc("s_pe", i)
                    ac.wait_ge(p.s_pe, p.semv["s_pe"])
                    p.inc("s_ac", ac.activation(gel[:, 0:320], gps[:, 0:320],
                                                AF.Gelu))
                    ve.wait_ge(p.s_ac, p.semv["s_ac"])
                    p.inc("s_ve", ve.tensor_mul(
                        hsb[:, fcl * 320:fcl * 320 + 320],
                        gel[:, 0:320], ups[:, 0:320]))
                for m in range(8):
                    te.wait_ge(p.s_ve, p.semv["s_ve"])
                    for fcl in range(4):
                        i = te.matmul(eop[:, 0:320],
                                      wd[b][:, fcl * 1024 + m * 128:fcl * 1024 + (m + 1) * 128],
                                      hsb[:, fcl * 320:fcl * 320 + 320],
                                      start=(fcl == 0), stop=(fcl == 3))
                    p.inc("s_pe", i)
                    ve.wait_ge(p.s_pe, p.semv["s_pe"])
                    dst = eosb[:, m * CAP + nh * 320:m * CAP + nh * 320 + 320]
                    if q == 0:
                        p.inc("s_ve", ve.tensor_copy(dst, eop[:, 0:320]))
                    else:
                        p.inc("s_ve", ve.scalar_tensor_tensor(
                            dst, eop[:, 0:320], 1.0, dst, AL.mult, AL.add))
            buf_mark[b] = p.semv["s_pe"]
            issue_wq(gidx + 2)

        for sl in range(NSL):
            te.wait_ge(p.s_ve, p.semv["s_ve"])
            for dc in range(8):
                i = te.transpose(top[:, dc * 128:(dc + 1) * 128],
                                 eosb[:, dc * CAP + sl * 128:dc * CAP + (sl + 1) * 128],
                                 ident[:])
            p.inc("s_pe", i)
            ve.wait_ge(p.s_pe, p.semv["s_pe"])
            if len(st_eoT) >= 2:
                p.dma_wait(ve, st_eoT[-2])
            p.inc("s_ve", ve.tensor_scalar(
                eoT[sl % 2][:], top[:], wcol[:, sl:sl + 1], None, AL.mult))
            sc.wait_ge(p.s_ve, p.semv["s_ve"])
            etok = p.dma_open(sc)
            p.dma_put(etok, sc.dma_start(out=eoTr[e * NSL + sl],
                                         in_=eoT[sl % 2][:]))
            st_eoT.append(p.dma_close(etok))


def eoFori(p, st):
    nc = p.nc
    gp = nc.gpsimd
    dram = p.dram
    eob = p.sb("eof_eob", [128, D], stack=st)
    ixt = p.sb("eof_ixt", [128, 1], I32, stack=st)
    eoTr = dram["eoT_sp"]
    idTr = dram["idT_sp"].rearrange("u p -> p u")
    gbase = p.semv["s_gp"]
    PER = 3 * 16
    NU = E * NSL
    with gp.Fori(0, NU) as u:
        gp.wait_ge(p.s_gp, u * PER + gbase)
        gp.dma_start(out=eob[:],
                     in_=eoTr[DynSlice(u * 128, 128), :]).then_inc(p.s_gp, 16)
        gp.dma_start(out=ixt[:], in_=idTr[:, DynSlice(u, 1)],
                     allow_slow_non_contiguous=True).then_inc(p.s_gp, 16)
        gp.wait_ge(p.s_gp, u * PER + gbase + 32)
        gp.indirect_dma_start(
            dram["macc"], IndirectOffsetOnAxis(ixt[:], 0), eob[:], None,
            bounds_check=HALF, oob_is_err=False,
            compute_op=AL.add).then_inc(p.s_gp, 16)
    p.semv["s_gp"] += NU * PER
    gp.wait_ge(p.s_gp, p.semv["s_gp"])


def ph4a(p, st, *, onesr, onesc, lnp, hv):
    hoff_tok = hv * 128
    """xn (recompute), rw=r*wkv, tm=Wo@rw, y1, LN2->xn3; spills."""
    nc = p.nc
    sy, te, ve, ac, sc = nc.sync, nc.tensor, nc.vector, nc.scalar, nc.scalar
    dram = p.dram

    wo = p.sb("p4_wo", [128, 8 * D], stack=st)
    wtok = p.dma_open(sy)
    p.dma_put(wtok, sy.dma_start(
        out=wo[:].rearrange("p (k e) -> p k e", k=8),
        in_=dram["wo_t"].rearrange("(k p) e -> p k e", p=128)))
    w_ld = p.dma_close(wtok)

    xb = p.sb("p4_xb", [128, 8 * BW], stack=st)
    xsq = p.sb("p4_xsq", [128, 8 * BW], stack=st)
    xn = p.sb("p4_xn", [128, 8 * BW], stack=st)
    rbuf = p.sb("p4_rbuf", [128, 8 * BW], stack=st)
    wkvb = p.sb("p4_wkvb", [128, 8 * BW], stack=st)
    y1 = p.sb("p4_y1", [128, 8 * BW], stack=st)
    xn3 = p.sb("p4_xn3", [128, 8 * BW], stack=st)
    ab = p.sb("p4_ab", [128, 4 * BW], stack=st)
    srow = p.sb("p4_srow", [1, 2 * BW], stack=st)
    a0r = p.sb("p4_a0r", [1, BW], stack=st)
    b0r = p.sb("p4_b0r", [1, BW], stack=st)
    a1r = p.sb("p4_a1r", [1, BW], stack=st)
    b1r = p.sb("p4_b1r", [1, BW], stack=st)
    scr = p.sb("p4_scr", [1, BW], stack=st)

    bc_ps = p.ps("p4_bc", [128, BW], stack=st)
    st_ps = p.ps("p4_st", [1, 2 * BW], stack=st)
    mm_ps = [p.ps(f"p4_mm{i}", [128, BW], stack=st) for i in range(2)]

    xr = dram["x_own_fm"].rearrange("(k p) t -> k p t", p=128)

    def bcast(row_slice, dst_slice):
        p.wait(te, "s_ve")
        i = te.matmul(bc_ps[:], onesr[:], row_slice, start=True, stop=True)
        p.inc("s_pe", i)
        p.wait(ve, "s_pe")
        p.inc("s_ve", ve.tensor_copy(dst_slice, bc_ps[:]))

    def ln_full(src_tile, dst_tile, g_off, b_off):
        p.wait(ac, "s_ve")
        p.inc("s_ac", ac.activation(xsq[:], src_tile[:], AF.Square))
        p.wait(te, "s_ac")
        for k in range(8):
            kw = slice(k * BW, (k + 1) * BW)
            te.matmul(st_ps[:, 0:BW], onesc[:], src_tile[:, kw],
                      start=(k == 0), stop=(k == 7))
            i = te.matmul(st_ps[:, BW:2 * BW], onesc[:], xsq[:, kw],
                          start=(k == 0), stop=(k == 7))
        p.inc("s_pe", i)
        p.wait(ve, "s_pe")
        ve.tensor_copy(srow[:], st_ps[:])
        ln_rows(p, srow, a1r[:], b1r[:], scr[:])
        bcast(a1r[:], ab[:, 2 * BW:3 * BW])
        bcast(b1r[:], ab[:, 3 * BW:4 * BW])
        a1b = ab[:, 2 * BW:3 * BW].unsqueeze(1).broadcast_to([128, 8, BW])
        b1b = ab[:, 3 * BW:4 * BW].unsqueeze(1).broadcast_to([128, 8, BW])
        sv = src_tile[:].rearrange("p (k w) -> p k w", k=8)
        dv = dst_tile[:].rearrange("p (k w) -> p k w", k=8)
        ve.tensor_mul(dv, sv, a1b)
        ve.tensor_add(dv, dv, b1b)
        for k in range(8):
            kw = slice(k * BW, (k + 1) * BW)
            ve.tensor_scalar(dst_tile[:, kw], dst_tile[:, kw],
                             lnp[:, g_off + k:g_off + k + 1],
                             lnp[:, b_off + k:b_off + k + 1], AL.mult, AL.add)
        p.inc("s_ve", ve.engine_nop())

    st_mark = []
    for blk in range(NBO):
        tw = slice(blk * BW, (blk + 1) * BW)
        if blk >= 1:
            p.dma_wait(sy, st_mark[blk - 1])
            sy.wait_ge(p.s_ve, p.semv["s_ve"])
        tok = p.dma_open(sy)
        for k in range(8):
            p.dma_put(tok, sy.dma_start(out=xb[:, k * BW:(k + 1) * BW],
                                        in_=xr[k, :, tw]))
        p.dma_put(tok, sy.dma_start(
            out=rbuf[:].rearrange("p (k w) -> p k w", k=8),
            in_=dram["r_sp"][:, :, DynSlice(hoff_tok + blk * BW, BW)].rearrange(
                "k p w -> p k w")))
        p.dma_put(tok, sy.dma_start(
            out=wkvb[:].rearrange("p (k w) -> p k w", k=8),
            in_=dram["wkv_sp"][:, :, DynSlice(hoff_tok + blk * BW, BW)].rearrange(
                "k p w -> p k w")))
        p.dma_put(tok, sy.dma_start(
            out=a0r[:], in_=dram["ab0_sp"][0:1, DynSlice(hoff_tok + blk * BW, BW)]))
        p.dma_put(tok, sy.dma_start(
            out=b0r[:], in_=dram["ab0_sp"][1:2, DynSlice(hoff_tok + blk * BW, BW)]))
        ldm = p.dma_close(tok)
        if blk == 0:
            p.dma_wait(te, w_ld)

        p.dma_wait(ve, ldm)
        p.dma_wait(te, ldm)
        bcast(a0r[:], ab[:, 0:BW])
        bcast(b0r[:], ab[:, BW:2 * BW])
        a0b = ab[:, 0:BW].unsqueeze(1).broadcast_to([128, 8, BW])
        b0b = ab[:, BW:2 * BW].unsqueeze(1).broadcast_to([128, 8, BW])
        xbv = xb[:].rearrange("p (k w) -> p k w", k=8)
        xnv = xn[:].rearrange("p (k w) -> p k w", k=8)
        ve.tensor_mul(xnv, xbv, a0b)
        ve.tensor_add(xnv, xnv, b0b)
        p.inc("s_ve", ve.tensor_mul(rbuf[:], rbuf[:], wkvb[:]))
        rw_ve = p.semv["s_ve"]

        te.wait_ge(p.s_ve, rw_ve)
        for m in range(8):
            mw = slice(m * BW, (m + 1) * BW)
            pb = mm_ps[m % 2]
            if m >= 2:
                te.wait_ge(p.s_ve, p.semv["s_ve"])
            for k in range(8):
                i = te.matmul(pb[:],
                              wo[:, k * D + m * 128:k * D + m * 128 + 128],
                              rbuf[:, k * BW:(k + 1) * BW],
                              start=(k == 0), stop=(k == 7))
            p.inc("s_pe", i)
            ve.wait_ge(p.s_pe, p.semv["s_pe"])
            p.inc("s_ve", ve.scalar_tensor_tensor(
                y1[:, mw], pb[:], 1.0, xn[:, mw], AL.mult, AL.add))

        ln_full(y1, xn3, 16, 24)

        sc.wait_ge(p.s_ve, p.semv["s_ve"])
        stok = p.dma_open(sc)
        for nm, tile in (("xn_sp", xn), ("y1_sp", y1), ("xn3_sp", xn3)):
            p.dma_put(stok, sc.dma_start(
                out=dram[nm][:, :, tw].rearrange("k p w -> p k w"),
                in_=tile[:].rearrange("p (k w) -> p k w", k=8)))
        st_mark.append(p.dma_close(stok))


def ph4sh(p, st):
    nc = p.nc
    sy, te, ve, ac, sc = nc.sync, nc.tensor, nc.vector, nc.scalar, nc.scalar
    dram = p.dram

    wsg = p.sb("psh_wsg", [128, 8 * 1024], stack=st)
    wsu = p.sb("psh_wsu", [128, 8 * 1024], stack=st)
    wsd = p.sb("psh_wsd", [128, 8 * 1024], stack=st)
    xnb = p.sb("psh_xn", [128, 8 * BW], stack=st)
    gel = p.sb("psh_gel", [128, BW], stack=st)
    hs = p.sb("psh_hs", [128, 8 * BW], stack=st)
    sh = p.sb("psh_sh", [128, 8 * BW], stack=st)

    gsp = p.ps("psh_gsp", [128, BW], stack=st)
    usp = p.ps("psh_usp", [128, BW], stack=st)
    dsp = p.ps("psh_dsp", [128, BW], stack=st)

    for fh in range(2):
        fw = slice(fh * 1024, (fh + 1) * 1024)
        sy.wait_ge(p.s_pe, p.semv["s_pe"])
        wtok = p.dma_open(sy)
        p.dma_put(wtok, sy.dma_start(
            out=wsg[:].rearrange("p (k f) -> p k f", k=8),
            in_=dram["wsg_t"].rearrange("(k p) f -> p k f", p=128)[:, :, fw]))
        p.dma_put(wtok, sy.dma_start(
            out=wsu[:].rearrange("p (k f) -> p k f", k=8),
            in_=dram["wsu_t"].rearrange("(k p) f -> p k f", p=128)[:, :, fw]))
        p.dma_put(wtok, sy.dma_start(
            out=wsd[:].rearrange("p (k e) -> p k e", k=8),
            in_=dram["wsd_t"].rearrange("(k p) e -> p k e", p=128)[:, fh * 8:(fh + 1) * 8]))
        w_ld = p.dma_close(wtok)

        st_mark = []
        for blk in range(NBO):
            tw = slice(blk * BW, (blk + 1) * BW)
            sy.wait_ge(p.s_pe, p.semv["s_pe"])
            if blk >= 1:
                p.dma_wait(sy, st_mark[blk - 1])
            tok = p.dma_open(sy)
            for k in range(8):
                p.dma_put(tok, sy.dma_start(
                    out=xnb[:, k * BW:(k + 1) * BW],
                    in_=dram["xn_sp"][k, :, tw]))
            if fh == 1:
                p.dma_put(tok, sy.dma_start(
                    out=sh[:].rearrange("p (k w) -> p k w", k=8),
                    in_=dram["sh_sp"][:, :, tw].rearrange("k p w -> p k w")))
            ldm = p.dma_close(tok)
            p.dma_wait(te, ldm)
            p.dma_wait(te, w_ld)
            for fc in range(8):
                te.wait_ge(p.s_ve, p.semv["s_ve"])
                for k in range(8):
                    te.matmul(gsp[:],
                              wsg[:, k * 1024 + fc * 128:k * 1024 + (fc + 1) * 128],
                              xnb[:, k * BW:(k + 1) * BW],
                              start=(k == 0), stop=(k == 7))
                for k in range(8):
                    i = te.matmul(usp[:],
                                  wsu[:, k * 1024 + fc * 128:k * 1024 + (fc + 1) * 128],
                                  xnb[:, k * BW:(k + 1) * BW],
                                  start=(k == 0), stop=(k == 7))
                p.inc("s_pe", i)
                ac.wait_ge(p.s_pe, p.semv["s_pe"])
                p.inc("s_ac", ac.activation(gel[:], gsp[:], AF.Gelu))
                ve.wait_ge(p.s_ac, p.semv["s_ac"])
                p.inc("s_ve", ve.tensor_mul(hs[:, fc * BW:(fc + 1) * BW],
                                            gel[:], usp[:]))
            for m in range(8):
                te.wait_ge(p.s_ve, p.semv["s_ve"])
                for fc in range(8):
                    i = te.matmul(dsp[:],
                                  wsd[:, fc * D + m * 128:fc * D + (m + 1) * 128],
                                  hs[:, fc * BW:(fc + 1) * BW],
                                  start=(fc == 0), stop=(fc == 7))
                p.inc("s_pe", i)
                ve.wait_ge(p.s_pe, p.semv["s_pe"])
                if fh == 1 and m == 0:
                    p.dma_wait(ve, ldm)
                mw = slice(m * BW, (m + 1) * BW)
                if fh == 0:
                    p.inc("s_ve", ve.tensor_copy(sh[:, mw], dsp[:]))
                else:
                    p.inc("s_ve", ve.scalar_tensor_tensor(
                        sh[:, mw], dsp[:], 1.0, sh[:, mw], AL.mult, AL.add))
            sc.wait_ge(p.s_ve, p.semv["s_ve"])
            stok = p.dma_open(sc)
            p.dma_put(stok, sc.dma_start(
                out=dram["sh_sp"][:, :, tw].rearrange("k p w -> p k w"),
                in_=sh[:].rearrange("p (k w) -> p k w", k=8)))
            st_mark.append(p.dma_close(stok))


def ph4b(p, st):
    nc = p.nc
    sy, te, ve, ac, sc = nc.sync, nc.tensor, nc.vector, nc.scalar, nc.scalar
    dram = p.dram

    wkey = p.sb("pb_wkey", [128, 8 * FCM // 2], stack=st)
    wval = p.sb("pb_wval", [128, 16 * D], stack=st)
    xnb = p.sb("pb_xn", [128, 8 * BW], stack=st)
    relu = p.sb("pb_relu", [128, BW], stack=st)
    kk = p.sb("pb_kk", [128, 16 * BW], stack=st)
    vv = p.sb("pb_vv", [128, 8 * BW], stack=st)

    kkp = p.ps("pb_kkp", [128, BW], stack=st)
    vvp = p.ps("pb_vvp", [128, BW], stack=st)

    for fh in range(2):
        sy.wait_ge(p.s_pe, p.semv["s_pe"])
        wtok = p.dma_open(sy)
        p.dma_put(wtok, sy.dma_start(
            out=wkey[:].rearrange("p (k f) -> p k f", k=8),
            in_=dram["wkey_t"].rearrange("(k p) f -> p k f", p=128)[
                :, :, fh * 2048:(fh + 1) * 2048]))
        p.dma_put(wtok, sy.dma_start(
            out=wval[:].rearrange("p (k e) -> p k e", k=16),
            in_=dram["wval_t"].rearrange("(k p) e -> p k e", p=128)[
                :, fh * 16:(fh + 1) * 16]))
        w_ld = p.dma_close(wtok)

        st_mark = []
        for blk in range(NBO):
            tw = slice(blk * BW, (blk + 1) * BW)
            sy.wait_ge(p.s_pe, p.semv["s_pe"])
            if blk >= 1:
                p.dma_wait(sy, st_mark[blk - 1])
            tok = p.dma_open(sy)
            for k in range(8):
                p.dma_put(tok, sy.dma_start(
                    out=xnb[:, k * BW:(k + 1) * BW],
                    in_=dram["xn3_sp"][k, :, tw]))
            if fh == 1:
                p.dma_put(tok, sy.dma_start(
                    out=vv[:].rearrange("p (k w) -> p k w", k=8),
                    in_=dram["vvh_sp"][:, :, tw].rearrange("k p w -> p k w")))
            ldm = p.dma_close(tok)
            p.dma_wait(te, ldm)
            p.dma_wait(te, w_ld)
            for fc in range(16):
                te.wait_ge(p.s_ve, p.semv["s_ve"])
                for k in range(8):
                    i = te.matmul(kkp[:],
                                  wkey[:, k * 2048 + fc * 128:k * 2048 + (fc + 1) * 128],
                                  xnb[:, k * BW:(k + 1) * BW],
                                  start=(k == 0), stop=(k == 7))
                p.inc("s_pe", i)
                ac.wait_ge(p.s_pe, p.semv["s_pe"])
                p.inc("s_ac", ac.activation(relu[:], kkp[:], AF.Relu))
                ve.wait_ge(p.s_ac, p.semv["s_ac"])
                p.inc("s_ve", ve.tensor_mul(kk[:, fc * BW:(fc + 1) * BW],
                                            relu[:], relu[:]))
            for m in range(8):
                te.wait_ge(p.s_ve, p.semv["s_ve"])
                for fc in range(16):
                    i = te.matmul(vvp[:],
                                  wval[:, fc * D + m * 128:fc * D + (m + 1) * 128],
                                  kk[:, fc * BW:(fc + 1) * BW],
                                  start=(fc == 0), stop=(fc == 15))
                p.inc("s_pe", i)
                ve.wait_ge(p.s_pe, p.semv["s_pe"])
                if m == 0:
                    p.dma_wait(ve, ldm)
                    if blk >= 1:
                        p.dma_wait(ve, st_mark[blk - 1])
                mw = slice(m * BW, (m + 1) * BW)
                if fh == 0:
                    p.inc("s_ve", ve.tensor_copy(vv[:, mw], vvp[:]))
                else:
                    p.inc("s_ve", ve.scalar_tensor_tensor(
                        vv[:, mw], vvp[:], 1.0, vv[:, mw], AL.mult, AL.add))
            sc.wait_ge(p.s_ve, p.semv["s_ve"])
            stok = p.dma_open(sc)
            p.dma_put(stok, sc.dma_start(
                out=dram["vvh_sp"][:, :, tw].rearrange("k p w -> p k w"),
                in_=vv[:].rearrange("p (k w) -> p k w", k=8)))
            st_mark.append(p.dma_close(stok))


def ph4c(p, st, *, ident):
    nc = p.nc
    sy, te, ve, ac, sc = nc.sync, nc.tensor, nc.vector, nc.scalar, nc.scalar
    dram = p.dram

    wrec = p.sb("pc_wrec", [128, 8 * D], stack=st)
    wtok = p.dma_open(sy)
    p.dma_put(wtok, sy.dma_start(
        out=wrec[:].rearrange("p (k e) -> p k e", k=8),
        in_=dram["wrec_t"].rearrange("(k p) e -> p k e", p=128)))
    w_ld = p.dma_close(wtok)

    xn3b = p.sb("pc_xn3", [128, 8 * BW], stack=st)
    vvb = p.sb("pc_vv", [128, 8 * BW], stack=st)
    y1b = p.sb("pc_y1", [128, 8 * BW], stack=st)
    shb = p.sb("pc_sh", [128, 8 * BW], stack=st)
    xbb = p.sb("pc_xb", [128, 8 * BW], stack=st)
    rr = p.sb("pc_rr", [128, 8 * BW], stack=st)
    mrow = p.sb("pc_mrow", [128, D], stack=st)
    orow = p.sb("pc_orow", [128, D], stack=st)

    mm_ps = [p.ps(f"pc_mm{i}", [128, BW], stack=st) for i in range(2)]
    trp = p.ps("pc_trp", [128, 1024], stack=st)

    xr = dram["x_own_fm"].rearrange("(k p) t -> k p t", p=128)
    outr = dram["out"].rearrange("(n p) d -> n p d", p=128)
    maccr = dram["macc"][0:HALF].rearrange("(n p) d -> n p d", p=128)

    for blk in range(NBO):
        tw = slice(blk * BW, (blk + 1) * BW)
        if blk >= 1:
            sy.wait_ge(p.s_ve, p.semv["s_ve"])
        tok = p.dma_open(sy)
        for k in range(8):
            p.dma_put(tok, sy.dma_start(out=xbb[:, k * BW:(k + 1) * BW],
                                        in_=xr[k, :, tw]))
        for nm, tile in (("xn3_sp", xn3b), ("vvh_sp", vvb),
                         ("y1_sp", y1b), ("sh_sp", shb)):
            p.dma_put(tok, sy.dma_start(
                out=tile[:].rearrange("p (k w) -> p k w", k=8),
                in_=dram[nm][:, :, tw].rearrange("k p w -> p k w")))
        ldm = p.dma_close(tok)
        if blk == 0:
            p.dma_wait(te, w_ld)

        p.dma_wait(te, ldm)
        for m in range(8):
            mw = slice(m * BW, (m + 1) * BW)
            pb = mm_ps[m % 2]
            if m >= 2:
                te.wait_ge(p.s_ac, p.semv["s_ac"])
            for k in range(8):
                i = te.matmul(pb[:],
                              wrec[:, k * D + m * 128:k * D + m * 128 + 128],
                              xn3b[:, k * BW:(k + 1) * BW],
                              start=(k == 0), stop=(k == 7))
            p.inc("s_pe", i)
            ac.wait_ge(p.s_pe, p.semv["s_pe"])
            p.inc("s_ac", ac.activation(rr[:, mw], pb[:], AF.Sigmoid))

        ve.wait_ge(p.s_ac, p.semv["s_ac"])
        p.dma_wait(ve, ldm)
        ve.tensor_mul(rr[:], rr[:], vvb[:])
        ve.tensor_add(rr[:], rr[:], y1b[:])
        ve.tensor_add(rr[:], rr[:], shb[:])
        p.inc("s_ve", ve.tensor_add(rr[:], rr[:], xbb[:]))
        base_ve = p.semv["s_ve"]

        te.wait_ge(p.s_ve, base_ve)
        ot_prev = None
        for t4 in range(4):
            n = blk * 4 + t4
            sy.wait_ge(p.s_ve, p.semv["s_ve"])
            mtok = p.dma_open(sy)
            p.dma_put(mtok, sy.dma_start(out=mrow[:], in_=maccr[n]))
            p.dma_close(mtok)
            if t4 > 0:
                te.wait_ge(p.s_ve, p.semv["s_ve"])
            for k in range(8):
                i = te.transpose(trp[:, k * 128:(k + 1) * 128],
                                 rr[:, k * BW + t4 * 128:k * BW + t4 * 128 + 128],
                                 ident[:])
            p.inc("s_pe", i)
            ve.wait_ge(p.s_pe, p.semv["s_pe"])
            p.dma_wait(ve, mtok)
            if ot_prev is not None:
                p.dma_wait(ve, ot_prev)
            p.inc("s_ve", ve.scalar_tensor_tensor(
                orow[:], trp[:], 1.0, mrow[:], AL.mult, AL.add))
            sc.wait_ge(p.s_ve, p.semv["s_ve"])
            otok = p.dma_open(sc)
            p.dma_put(otok, sc.dma_start(out=outr[n], in_=orow[:]))
            ot_prev = p.dma_close(otok)

    p.dma_wait_all(sc)
    p.dma_wait_all(sy)
    p.wait(nc.gpsimd, "s_gp")


def make_inmaps(inputs):
    x = np.ascontiguousarray(inputs["L0_attn_x"], np.float32)

    def tr(a):
        return np.ascontiguousarray(np.asarray(a, np.float32).T)

    def chunks8(v):
        return np.ascontiguousarray(np.asarray(v, np.float32).reshape(8, 128).T)

    shared = {
        "wr_t": tr(inputs["L0_attn_Wr"]),
        "ww_t": tr(inputs["L0_attn_Ww"]),
        "wk_t": tr(inputs["L0_attn_Wk"]),
        "wv_t": tr(inputs["L0_attn_Wv"]),
        "wo_t": tr(inputs["L0_attn_Wo"]),
        "wrec_t": tr(inputs["L0_attn_Wrec"]),
        "wkey_t": tr(inputs["L0_attn_Wkey"]),
        "wval_t": tr(inputs["L0_attn_Wval"]),
        "wsg_t": tr(inputs["L0_ffn_Ws_g"]),
        "wsu_t": tr(inputs["L0_ffn_Ws_u"]),
        "wsd_t": tr(inputs["L0_ffn_Ws_d"]),
        "gate_t": tr(inputs["L0_ffn_gate_w"]),
        "wg_t": np.ascontiguousarray(
            np.asarray(inputs["L0_ffn_Wg"], np.float32).transpose(0, 2, 1)),
        "wu_t": np.ascontiguousarray(
            np.asarray(inputs["L0_ffn_Wu"], np.float32).transpose(0, 2, 1)),
        "wd_t": np.ascontiguousarray(
            np.asarray(inputs["L0_ffn_Wd"], np.float32).transpose(0, 2, 1)),
        "ln1g": chunks8(inputs["L0_attn_ln1_g"]),
        "ln1b": chunks8(inputs["L0_attn_ln1_b"]),
        "ln2g": chunks8(inputs["L0_attn_ln2_g"]),
        "ln2b": chunks8(inputs["L0_attn_ln2_b"]),
        "td": chunks8(inputs["L0_attn_time_decay"]),
        "ident": np.eye(128, dtype=np.float32),
        "tri": np.ascontiguousarray(
            np.triu(np.ones((128, 128), np.float32), 1)),
        "onesc": np.ones((128, 1), np.float32),
        "onesr": np.ones((1, 128), np.float32),
        "tokidG": np.ascontiguousarray(
            (np.arange(CC)[None, :] * 128 + np.arange(128)[:, None] + 1
             ).astype(np.float32)),
    }
    maps = []
    for c in range(8):
        b, h = c // 2, c % 2
        xb = x[b]
        own = xb[h * HALF:(h + 1) * HALF]
        m = dict(shared)
        m["x_fm"] = np.ascontiguousarray(xb.T)
        m["x_own_tm"] = np.ascontiguousarray(own)
        m["x_own_fm"] = np.ascontiguousarray(own.T)
        mc = np.zeros((128, CC), np.float32)
        mc[:, h * CCH:(h + 1) * CCH] = 1.0
        m["maskc"] = mc
        m["hseg"] = np.array([[h * CCH]], np.int32)
        m["selinit"] = np.concatenate([np.zeros((128, E), np.float32),
                                       np.ones((128, E), np.float32)], 1)
        m["hofft1"] = np.full((128, 1), h * HALF + 1, np.float32)
        m["hofft2"] = np.full((128, 1), HALF + h * HALF + 1, np.float32)
        maps.append(m)
    return maps


def kernel(**inputs):
    prog = build()
    maps = make_inmaps(inputs)
    res = run_bass_kernel_spmd(prog.nc, maps, core_ids=list(range(8)))
    out = np.empty((4, T, D), np.float32)
    for c in range(8):
        b, h = c // 2, c % 2
        out[b, h * HALF:(h + 1) * HALF] = res.results[c]["out"]
    return out


# version 04
# speedup vs baseline: 1.1394x; 1.0385x over previous
import contextlib
import numpy as np
import concourse.bass as bass
import concourse.mybir as mybir
from concourse.bass_utils import run_bass_kernel_spmd
from concourse.bass_types import DynSlice
from concourse.bass_core_rust import IndirectOffsetOnAxis

F32 = mybir.dt.float32
I32 = mybir.dt.int32
AL = mybir.AluOpType
AF = mybir.ActivationFunctionType
AX = mybir.AxisListType

T = 8192
D = 1024
E = 16
HALF = 4096
CC = 64
CCH = 32
CAP = 640
NSL = CAP // 128
WID = D + E + 1
SENT = 1 << 20
LN_EPS = 1e-5
WKV_EPS = 1e-8
NB = T // 512
NBO = HALF // 512
BW = 512
FEXP = 2048
FSH = 2048
FCM = 4096
TH_ITERS = 38
TH_PAD = 3e-4


class Prog:

    def __init__(self, debug_taps=()):
        self.nc = bass.Bass("TRN2", target_bir_lowering=False, debug=False,
                            detect_race_conditions=False)
        self.dram = {}
        self.taps = {}
        self.debug_taps = set(debug_taps)
        self.stack = contextlib.ExitStack()
        self.semv = {}

    def din(self, name, shape, dtype=F32):
        ap = self.nc.dram_tensor(name, shape, dtype, kind="ExternalInput").ap()
        self.dram[name] = ap
        return ap

    def dout(self, name, shape, dtype=F32):
        ap = self.nc.dram_tensor(name, shape, dtype, kind="ExternalOutput").ap()
        self.dram[name] = ap
        return ap

    def dscr(self, name, shape, dtype=F32):
        ap = self.nc.dram_tensor(name, shape, dtype, kind="Internal").ap()
        self.dram[name] = ap
        return ap

    def sb(self, name, shape, dtype=F32, stack=None):
        return (stack or self.stack).enter_context(
            self.nc.sbuf_tensor(name, shape, dtype))

    def ps(self, name, shape, dtype=F32, stack=None):
        return (stack or self.stack).enter_context(
            self.nc.psum_tensor(name, shape, dtype))

    def sem(self, name):
        h = self.stack.enter_context(self.nc.semaphore(name))
        self.semv[name] = 0
        return h

    def inc(self, name, instr, n=1):
        instr.then_inc(getattr(self, name), n)
        self.semv[name] += n
        return self.semv[name]

    def dinc(self, name, instr):
        return self.inc(name, instr, 16)

    def wait(self, eng, name, v=None):
        eng.wait_ge(getattr(self, name), self.semv[name] if v is None else v)

    def vf(self, instr):
        self.inc("s_ve", instr)
        self.nc.vector.wait_ge(self.s_ve, self.semv["s_ve"])

    def dma_open(self, eng):
        j = self._rr
        self._rr = (self._rr + 1) % 8
        if self._dtarget[j] > 0:
            eng.wait_ge(self.dsem[j], self._dtarget[j])
        return [j, None]

    def dma_put(self, tok, instr):
        j = tok[0]
        instr.then_inc(self.dsem[j], 16)
        self.semv[f"s_d{j}"] += 16

    def dma_close(self, tok):
        j = tok[0]
        tok[1] = self.semv[f"s_d{j}"]
        self._dtarget[j] = tok[1]
        return tok

    def dma_batch(self, eng, emits):
        tok = self.dma_open(eng)
        for fn in emits:
            self.dma_put(tok, fn())
        return self.dma_close(tok)

    def dma_wait(self, eng, tok):
        eng.wait_ge(self.dsem[tok[0]], tok[1])

    def dma_wait_all(self, eng):
        for j in range(8):
            if self._dtarget[j] > 0:
                eng.wait_ge(self.dsem[j], self._dtarget[j])

    def tap(self, name, src_tile, shape=None):
        if name not in self.debug_taps:
            return
        sc = self.nc.scalar
        self.wait(sc, "s_ve")
        self.wait(sc, "s_ac")
        self.wait(sc, "s_pe")
        shape = shape or list(src_tile.shape)
        ap = self.dout("tap_" + name, shape)
        self.taps[name] = ap
        tok = self.dma_open(sc)
        self.dma_put(tok, sc.dma_start(out=ap, in_=src_tile))
        self.dma_close(tok)
        nc = self.nc
        for eng in (nc.vector, nc.scalar, nc.tensor, nc.gpsimd, nc.sync):
            self.dma_wait(eng, tok)

    def barrier(self, skip_gp=False):
        nc = self.nc
        self.dma_wait_all(nc.sync)
        self.dma_wait_all(nc.scalar)
        engines = (nc.sync, nc.tensor, nc.vector, nc.scalar)
        if not skip_gp:
            self.wait(nc.gpsimd, "s_gp")
            engines = engines + (nc.gpsimd,)
        for eng in engines:
            self.inc("s_bar", eng.nop())
        tgt = self.semv["s_bar"]
        for eng in engines:
            eng.wait_ge(self.s_bar, tgt)


def build(debug_taps=()):
    p = Prog(debug_taps)
    nc = p.nc

    p.din("x_fm", [D, T])
    p.din("x_own_tm", [HALF, D])
    p.din("x_own_fm", [D, HALF])
    for nm in ("wr_t", "ww_t", "wk_t", "wv_t", "wo_t", "wrec_t"):
        p.din(nm, [D, D])
    p.din("wkey_t", [D, FCM])
    p.din("wval_t", [FCM, D])
    p.din("wsg_t", [D, FSH])
    p.din("wsu_t", [D, FSH])
    p.din("wsd_t", [FSH, D])
    p.din("wg_t", [E, D, FEXP])
    p.din("wu_t", [E, D, FEXP])
    p.din("wd_t", [E, FEXP, D])
    p.din("gate_t", [D, E])
    for nm in ("ln1g", "ln1b", "ln2g", "ln2b", "td"):
        p.din(nm, [128, 8])
    p.din("ident", [128, 128])
    p.din("tri", [128, 128])
    p.din("maskc", [128, CC])
    p.din("tokidG", [128, CC])
    p.din("selinit", [128, 2 * E])
    p.din("hofft1", [128, 1])
    p.din("hofft2", [128, 1])
    p.din("onesc", [128, 1])
    p.din("onesr", [1, 128])
    p.din("hseg", [1, 1], I32)

    p.dscr("e_sp", [8, 128, T])
    p.dscr("ev_sp", [8, 128, T])
    p.dscr("r_sp", [8, 128, T])
    p.dscr("wkv_sp", [8, 128, T])
    p.dscr("xn_sp", [8, 128, HALF])
    p.dscr("xn3_sp", [8, 128, HALF])
    p.dscr("y1_sp", [8, 128, HALF])
    p.dscr("sh_sp", [8, 128, HALF])
    p.dscr("vvh_sp", [8, 128, HALF])
    for e in range(E):
        p.dscr(f"xtok{e}", [CAP, WID])
    p.dscr("wid_sp", [128, CC * (E + 1)])
    p.dscr("idxe_sp", [128, CC * E], I32)
    p.dscr("eoT_sp", [E * NSL * 128, D])
    p.dscr("idT_sp", [E * NSL, 128], I32)
    p.dscr("ab0_sp", [2, T])
    p.dscr("macc", [HALF + 1, D])
    p.dout("out", [HALF, D])

    for s in ("s_gp", "s_pe", "s_ve", "s_ac", "s_bar"):
        setattr(p, s, p.sem(s))
    p.dsem = [p.sem(f"s_d{i}") for i in range(8)]
    p._rr = 0
    p._dtarget = [0] * 8

    with nc.Block() as _block:
        emit(p)

    p.stack.close()
    return p


def emit(p):
    nc = p.nc
    sy, te, ve, ac, gp = nc.sync, nc.tensor, nc.vector, nc.scalar, nc.gpsimd

    ident = p.sb("c_ident", [128, 128])
    onesc = p.sb("c_onesc", [128, 1])
    onesr = p.sb("c_onesr", [1, 128])
    lnp = p.sb("c_lnp", [128, 4 * 8])
    decay = p.sb("c_decay", [128, 8])
    hsegt = p.sb("c_hseg", [1, 1], I32)

    dram = p.dram

    tok0 = p.dma_open(sy)
    for nm, dst in (("ident", ident), ("onesc", onesc),
                    ("onesr", onesr), ("hseg", hsegt)):
        p.dma_put(tok0, sy.dma_start(out=dst[:], in_=dram[nm]))
    for i, nm in enumerate(("ln1g", "ln1b", "ln2g", "ln2b")):
        p.dma_put(tok0, sy.dma_start(out=lnp[:, i * 8:(i + 1) * 8], in_=dram[nm]))
    p.dma_put(tok0, sy.dma_start(out=decay[:], in_=dram["td"]))
    p.dma_close(tok0)
    p.dma_wait(ac, tok0)
    p.inc("s_ac", ac.activation(decay[:], decay[:], AF.Exp))

    p.dma_wait(sy, tok0)
    hv = sy.value_load(hsegt[0:1, 0:1])
    p.dma_wait(gp, tok0)
    hvg = gp.value_load(hsegt[0:1, 0:1])

    with contextlib.ExitStack() as sel_st:
        tri = p.sb("m_tri", [128, 128], stack=sel_st)
        gatet = p.sb("m_gate", [128, 8 * E], stack=sel_st)
        maskc = p.sb("m_maskc", [128, CC], stack=sel_st)
        tokidG = p.sb("m_tokidG", [128, CC], stack=sel_st)
        scores3 = p.sb("m_scores3", [128, E * CC], stack=sel_st)
        theta = p.sb("m_theta", [128, E], stack=sel_st)
        gsumP = p.sb("m_gsumP", [128, E], stack=sel_st)
        idxL = p.sb("m_idxL", [128, CC * E], I32, stack=sel_st)
        zeros = p.sb("m_zeros", [128, WID], stack=sel_st)

        tokc = p.dma_open(sy)
        for nm, dst in (("tri", tri), ("maskc", maskc), ("tokidG", tokidG)):
            p.dma_put(tokc, sy.dma_start(out=dst[:], in_=dram[nm]))
        p.dma_put(tokc, sy.dma_start(
            out=gatet[:].rearrange("p (k e) -> p k e", k=8),
            in_=dram["gate_t"].rearrange("(k p) e -> p k e", p=128)))
        p.dma_close(tokc)
        p.inc("s_ve", ve.memset(zeros[:], 0.0))

        p.wait(gp, "s_ve")
        mr = dram["macc"][0:HALF].rearrange("(n p) d -> n p d", p=128)
        for i in range(HALF // 128):
            p.dinc("s_gp", gp.dma_start(out=mr[i], in_=zeros[:, :D]))
        p.dinc("s_gp", gp.dma_start(out=dram["macc"][HALF:HALF + 1, :],
                                    in_=zeros[0:1, 0:D]))
        for e in range(E):
            xr = dram[f"xtok{e}"].rearrange("(n p) w -> n p w", p=128)
            for i in range(NSL):
                p.dinc("s_gp", gp.dma_start(out=xr[i], in_=zeros[:, :WID]))

        p.barrier(skip_gp=True)

        with contextlib.ExitStack() as st:
            ph1(p, st, ident=ident, onesc=onesc, onesr=onesr, gatet=gatet,
                scores3=scores3, gsumP=gsumP)
            p.barrier(skip_gp=True)

        with contextlib.ExitStack() as st:
            selection(p, st, ident=ident, tri=tri, onesc=onesc, onesr=onesr,
                      maskc=maskc, scores3=scores3, theta=theta, gsumP=gsumP,
                      tokidG=tokidG, idxL=idxL)
            p.barrier()

    with contextlib.ExitStack() as st:
        phMprep(p, st, hvg=hvg)
        ph2(p, st, onesr=onesr, onesc=onesc, lnp=lnp)
        p.barrier()

    with contextlib.ExitStack() as st:
        ph3(p, st, decay=decay)
        p.barrier()

    with contextlib.ExitStack() as st4a:
        wo = p.sb("p4_wo", [128, 8 * D], stack=st4a)
        wotok = p.dma_open(sy)
        p.dma_put(wotok, sy.dma_start(
            out=wo[:].rearrange("p (k e) -> p k e", k=8),
            in_=dram["wo_t"].rearrange("(k p) e -> p k e", p=128)))
        wo_ld = p.dma_close(wotok)
        with contextlib.ExitStack() as st:
            phM(p, st, ident=ident)
            p.barrier()

        eoFori(p, st4a)
        ph4a(p, st4a, onesr=onesr, onesc=onesc, lnp=lnp, hv=hv,
             wo=wo, w_ld=wo_ld)
        p.barrier()
    with contextlib.ExitStack() as st:
        ph4sh(p, st)
        p.barrier()
    with contextlib.ExitStack() as st:
        ph4b(p, st)
        p.barrier()
    with contextlib.ExitStack() as st:
        ph4c(p, st, ident=ident)
        p.barrier()


def ln_rows(p, stats_ps, a_slice, b_slice, scr):
    nc = p.nc
    ve, ac = nc.vector, nc.scalar
    W = a_slice.shape[1]
    ve.tensor_scalar_mul(a_slice, stats_ps[:, 0:W], 1.0 / D)
    ve.tensor_scalar_mul(b_slice, stats_ps[:, W:2 * W], 1.0 / D)
    ve.tensor_mul(scr, a_slice, a_slice)
    ve.tensor_sub(b_slice, b_slice, scr[:])
    p.inc("s_ve", ve.tensor_scalar_add(b_slice, b_slice, LN_EPS))
    p.wait(ac, "s_ve")
    p.inc("s_ac", ac.activation(b_slice, b_slice, AF.Sqrt))
    p.wait(ve, "s_ac")
    p.inc("s_ve", ve.reciprocal(b_slice, b_slice))
    p.wait(ve, "s_ve")
    ve.tensor_mul(scr, a_slice, b_slice)
    ve.tensor_copy(a_slice, b_slice)
    p.inc("s_ve", ve.tensor_scalar_mul(b_slice, scr, -1.0))


PH1_PARTS = {"blocks", "gsum", "staging", "transpose", "logits"}


def ph1(p, st, *, ident, onesc, onesr, gatet, scores3, gsumP):
    nc = p.nc
    sy, te, ve, ac = nc.sync, nc.tensor, nc.vector, nc.scalar
    dram = p.dram

    xb = [p.sb(f"p1_xb{i}", [128, 8 * BW], stack=st) for i in range(2)]
    xsq = p.sb("p1_xsq", [128, 8 * BW], stack=st)
    arow = p.sb("p1_arow", [1, T], stack=st)
    brow = p.sb("p1_brow", [1, T], stack=st)
    st18 = p.sb("p1_st18", [18, T], stack=st)
    srow = p.sb("p1_srow", [1, 2 * BW], stack=st)
    scr = p.sb("p1_scr", [1, BW], stack=st)
    a3 = p.sb("p1_a3", [128, CC], stack=st)
    b3 = p.sb("p1_b3", [128, CC], stack=st)
    gx3 = p.sb("p1_gx3", [128, E * CC], stack=st)
    grow = p.sb("p1_grow", [1, E], stack=st)
    l3 = p.sb("p1_l3", [128, E * CC], stack=st)
    t3 = p.sb("p1_t3", [128, E * CC], stack=st)
    sume = p.sb("p1_sume", [128, 8 * CC], stack=st)

    gxp = p.ps("p1_gxp", [16, BW], stack=st)
    stp = p.ps("p1_stp", [1, 2 * BW], stack=st)
    trp = [p.ps(f"p1_trp{i}", [128, 32], stack=st) for i in range(2)]
    bcp = p.ps("p1_bcp", [128, E], stack=st)
    gsp = p.ps("p1_gsp", [1, E], stack=st)

    xr = dram["x_fm"].rearrange("(k p) t -> k p t", p=128)

    ld_mark = []
    ac_mark = []
    pe_mark = []
    ve_mark = []

    for blk in range(NB if "blocks" in PH1_PARTS else 0):
        if blk >= 2:
            sy.wait_ge(p.s_pe, pe_mark[blk - 2])
        tok = p.dma_open(sy)
        for k in range(8):
            p.dma_put(tok, sy.dma_start(
                out=xb[blk % 2][:, k * BW:(k + 1) * BW],
                in_=xr[k, :, blk * BW:(blk + 1) * BW]))
        ld_mark.append(p.dma_close(tok))

        p.dma_wait(ac, ld_mark[blk])
        if blk >= 1:
            ac.wait_ge(p.s_pe, pe_mark[blk - 1])
        p.inc("s_ac", ac.activation(xsq[:], xb[blk % 2][:], AF.Square))
        ac_mark.append(p.semv["s_ac"])

        te.wait_ge(p.s_ac, ac_mark[blk])
        if blk >= 1:
            te.wait_ge(p.s_ve, ve_mark[blk - 1])
        for k in range(8):
            kw = slice(k * BW, (k + 1) * BW)
            te.matmul(gxp[:], gatet[:, k * E:(k + 1) * E], xb[blk % 2][:, kw],
                      start=(k == 0), stop=(k == 7))
            te.matmul(stp[:, 0:BW], onesc[:], xb[blk % 2][:, kw],
                      start=(k == 0), stop=(k == 7))
            i = te.matmul(stp[:, BW:2 * BW], onesc[:], xsq[:, kw],
                          start=(k == 0), stop=(k == 7))
        p.inc("s_pe", i)
        pe_mark.append(p.semv["s_pe"])

        ve.wait_ge(p.s_pe, pe_mark[blk])
        ve.tensor_copy(st18[0:16, blk * BW:(blk + 1) * BW], gxp[:])
        ve.tensor_copy(srow[:], stp[:])
        ln_rows(p, srow,
                arow[:, blk * BW:(blk + 1) * BW],
                brow[:, blk * BW:(blk + 1) * BW], scr[:])
        ve_mark.append(p.semv["s_ve"])

    p.tap("arow", arow[:])
    p.tap("brow", brow[:])

    if "gsum" not in PH1_PARTS:
        return
    p.dma_wait_all(te)
    for k in range(8):
        i = te.matmul(gsp[:], onesc[:], gatet[:, k * E:(k + 1) * E],
                      start=(k == 0), stop=(k == 7))
    p.inc("s_pe", i)
    p.wait(ve, "s_pe")
    p.inc("s_ve", ve.tensor_copy(grow[:], gsp[:]))
    p.wait(te, "s_ve")
    i = te.matmul(bcp[:], onesr[:], grow[:], start=True, stop=True)
    p.inc("s_pe", i)
    p.wait(ve, "s_pe")
    p.inc("s_ve", ve.tensor_copy(gsumP[:], bcp[:]))

    if "staging" not in PH1_PARTS:
        return
    p.wait(sy, "s_ve")
    stok = p.dma_open(sy)
    p.dma_put(stok, sy.dma_start(out=st18[16:17, :], in_=arow[:]))
    p.dma_put(stok, sy.dma_start(out=st18[17:18, :], in_=brow[:]))
    p.dma_put(stok, sy.dma_start(out=dram["ab0_sp"][0:1, :], in_=arow[:]))
    p.dma_put(stok, sy.dma_start(out=dram["ab0_sp"][1:2, :], in_=brow[:]))
    stg_ld = p.dma_close(stok)

    if "transpose" not in PH1_PARTS:
        return
    p.dma_wait(te, stg_ld)
    ve_evac_base = p.semv["s_ve"]
    for tb in range(CC):
        if tb >= 2:
            te.wait_ge(p.s_ve, ve_evac_base + tb - 1)
        i = te.transpose(trp[tb % 2][:, 0:18],
                         st18[:, tb * 128:(tb + 1) * 128], ident[0:18, 0:18])
        p.inc("s_pe", i)
        ve.wait_ge(p.s_pe, p.semv["s_pe"])
        gx3v = gx3[:].rearrange("p (e c) -> p e c", c=CC)
        ve.tensor_copy(gx3v[:, :, tb:tb + 1],
                       trp[tb % 2][:, 0:16].rearrange("p (e c) -> p e c", c=1))
        ve.tensor_copy(a3[:, tb:tb + 1], trp[tb % 2][:, 16:17])
        p.inc("s_ve", ve.tensor_copy(b3[:, tb:tb + 1], trp[tb % 2][:, 17:18]))

    if "logits" not in PH1_PARTS:
        return
    a3bc = a3[:].unsqueeze(1).broadcast_to([128, E, CC])
    b3bc = b3[:].unsqueeze(1).broadcast_to([128, E, CC])
    gsbc = gsumP[:].unsqueeze(2).broadcast_to([128, E, CC])
    l3v = l3[:].rearrange("p (e c) -> p e c", c=CC)
    t3v = t3[:].rearrange("p (e c) -> p e c", c=CC)
    p.wait(ve, "s_ve")
    ve.tensor_mul(l3v, gx3[:].rearrange("p (e c) -> p e c", c=CC), a3bc)
    ve.tensor_mul(t3v, b3bc, gsbc)
    p.inc("s_ve", ve.tensor_add(l3[:], l3[:], t3[:]))
    p.tap("logits3", l3[:])
    p.tap("gx3", gx3[:])
    p.tap("a3", a3[:])
    p.tap("b3", b3[:])
    p.tap("gsumP", gsumP[:])
    p.wait(ac, "s_ve")
    p.inc("s_ac", ac.activation(l3[:], l3[:], AF.Exp))
    p.wait(ve, "s_ac")
    l3e = l3[:].rearrange("p (e c) -> p e c", c=CC)
    s8 = sume[:].rearrange("p (e c) -> p e c", c=CC)
    ve.tensor_add(s8[:, 0:8, :], l3e[:, 0:8, :], l3e[:, 8:16, :])
    ve.tensor_add(s8[:, 0:4, :], s8[:, 0:4, :], s8[:, 4:8, :])
    ve.tensor_add(s8[:, 0:2, :], s8[:, 0:2, :], s8[:, 2:4, :])
    p.vf(ve.tensor_add(s8[:, 0:1, :], s8[:, 0:1, :], s8[:, 1:2, :]))
    p.inc("s_ve", ve.reciprocal(sume[:, 0:CC], sume[:, 0:CC]))
    p.wait(ve, "s_ve")
    p.tap("sumr", sume[:, 0:CC])
    rbc = sume[:, 0:CC].unsqueeze(1).broadcast_to([128, E, CC])
    p.inc("s_ve", ve.tensor_mul(
        scores3[:].rearrange("p (e c) -> p e c", c=CC), l3e, rbc))
    p.tap("scores3", scores3[:])


def selection(p, st, *, ident, tri, onesc, onesr, maskc, scores3, theta,
              gsumP, tokidG, idxL):
    dram = p.dram
    """Binary-search theta per expert; build w (e-minor) + scatter slots."""
    nc = p.nc
    sy, te, ve, ac = nc.sync, nc.tensor, nc.vector, nc.scalar

    lohit = p.sb("sl_lohi", [128, 2 * E], stack=st)
    lo = lohit[:, 0:E]
    hi = lohit[:, E:2 * E]
    mid = p.sb("sl_mid", [128, E], stack=st)
    ge = p.sb("sl_ge", [128, E * CC], stack=st)
    red = p.sb("sl_red", [128, E], stack=st)
    crow = p.sb("sl_crow", [1, E], stack=st)
    cntP = p.sb("sl_cntP", [128, E], stack=st)
    cntN = p.sb("sl_cntN", [128, E], stack=st)
    tmp = p.sb("sl_tmp", [128, E], stack=st)
    tmp2 = p.sb("sl_tmp2", [128, E], stack=st)
    w3 = p.sb("sl_w3", [128, E * CC], stack=st)
    cand = p.sb("sl_cand", [128, E * CC], stack=st)
    jj = p.sb("sl_jj", [128, E * CC], stack=st)
    rtot = p.sb("sl_rtot", [128, E], stack=st)
    offp = p.sb("sl_offp", [128, E], stack=st)
    idxf = p.sb("sl_idxf", [128, E * CC], stack=st)
    idx3 = p.sb("sl_idx3", [128, E * CC], I32, stack=st)
    wm2 = p.sb("sl_wm2", [128, CC * (E + 1)], stack=st)

    cp = p.ps("sl_cp", [1, E], stack=st)
    bp = p.ps("sl_bp", [128, E], stack=st)
    op = p.ps("sl_op", [128, E], stack=st)

    s3v = scores3[:].rearrange("p (e c) -> p e c", c=CC)
    gev = ge[:].rearrange("p (e c) -> p e c", c=CC)

    itok = p.dma_open(sy)
    p.dma_put(itok, sy.dma_start(out=lohit[:], in_=dram["selinit"]))
    p.dma_close(itok)
    p.dma_wait(ve, itok)

    for it in range(TH_ITERS):
        p.vf(ve.tensor_add(mid[:], lo, hi))
        p.vf(ve.tensor_scalar_mul(mid[:], mid[:], 0.5))
        midbc = mid[:].unsqueeze(2).broadcast_to([128, E, CC])
        ve.tensor_tensor(gev, s3v, midbc, AL.is_ge)
        p.inc("s_ve", ve.tensor_reduce(
            red[:].unsqueeze(2), gev, AX.X, AL.add))
        if it == 0:
            p.tap("red0", red[:])
            p.tap("mid0", mid[:])
        p.wait(te, "s_ve")
        i = te.matmul(cp[:], onesc[:], red[:], start=True, stop=True)
        p.inc("s_pe", i)
        p.wait(ve, "s_pe")
        p.inc("s_ve", ve.tensor_copy(crow[:], cp[:]))
        p.wait(te, "s_ve")
        i = te.matmul(bp[:], onesr[:], crow[:], start=True, stop=True)
        p.inc("s_pe", i)
        p.wait(ve, "s_pe")
        if it == 0 and "cnt0" in p.debug_taps:
            p.vf(ve.tensor_copy(tmp[:], bp[:]))
            p.tap("cnt0", tmp[:])
        p.vf(ve.tensor_single_scalar(cntP[:], bp[:], 1024.0, AL.is_ge))
        p.vf(ve.tensor_sub(tmp[:], mid[:], lo))
        p.vf(ve.tensor_mul(tmp[:], tmp[:], cntP[:]))
        p.vf(ve.tensor_sub(tmp2[:], mid[:], hi))
        p.vf(ve.tensor_add(lo, lo, tmp[:]))
        p.vf(ve.tensor_scalar(cntN[:], cntP[:], -1.0, 1.0, AL.mult, AL.add))
        p.vf(ve.tensor_mul(tmp2[:], tmp2[:], cntN[:]))
        p.vf(ve.tensor_add(hi, hi, tmp2[:]))

    p.vf(ve.tensor_copy(theta[:], lo))
    p.tap("theta", theta[:])

    thbc = theta[:].unsqueeze(2).broadcast_to([128, E, CC])
    w3v = w3[:].rearrange("p (e c) -> p e c", c=CC)
    ve.tensor_tensor(w3v, s3v, thbc, AL.is_ge)
    ve.tensor_mul(w3[:], w3[:], scores3[:])
    p.vf(ve.tensor_scalar_mul(tmp[:], theta[:], 1.0 - TH_PAD))
    tpbc = tmp[:].unsqueeze(2).broadcast_to([128, E, CC])
    cndv = cand[:].rearrange("p (e c) -> p e c", c=CC)
    ve.tensor_tensor(cndv, s3v, tpbc, AL.is_ge)
    mbc = maskc[:].unsqueeze(1).broadcast_to([128, E, CC])
    p.inc("s_ve", ve.tensor_mul(cndv, cndv, mbc))
    p.tap("cand", cand[:])

    onesbc = onesc[:].broadcast_to([128, CC])
    for e in range(E):
        i = ve.tensor_tensor_scan(jj[:, e * CC:(e + 1) * CC],
                                  onesbc, cand[:, e * CC:(e + 1) * CC],
                                  0.0, AL.mult, AL.add)
        if e == E - 1:
            p.vf(i)
    p.inc("s_ve", ve.tensor_copy(rtot[:], jj[:].rearrange(
        "p (e c) -> p e c", c=CC)[:, :, CC - 1:CC]))
    p.wait(te, "s_ve")
    i = te.matmul(op[:], tri[:], rtot[:], start=True, stop=True)
    p.inc("s_pe", i)
    p.wait(ve, "s_pe")
    p.vf(ve.tensor_copy(offp[:], op[:]))
    offbc = offp[:].unsqueeze(2).broadcast_to([128, E, CC])
    jv = jj[:].rearrange("p (e c) -> p e c", c=CC)
    ve.tensor_add(jv, jv, offbc)
    ve.tensor_single_scalar(ge[:], jj[:], float(CAP), AL.is_le)
    ve.tensor_mul(ge[:], ge[:], cand[:])
    ve.tensor_scalar_add(idxf[:], jj[:], float(-1 - SENT))
    ve.tensor_mul(idxf[:], idxf[:], ge[:])
    ve.tensor_scalar_add(idxf[:], idxf[:], float(SENT))
    p.inc("s_ve", ve.tensor_copy(idx3[:], idxf[:]))
    p.tap("jj", jj[:])

    widv = wm2[:].rearrange("p (c w) -> p c w", w=E + 1)
    w3p = w3[:].rearrange("p (e c) -> p c e", c=CC, e=E)
    ve.tensor_copy(widv[:, :, 0:E], w3p)
    ve.tensor_copy(widv[:, :, E:E + 1], tokidG[:].unsqueeze(2))
    idxev = idxL[:].rearrange("p (c e) -> p c e", e=E)
    idx3p = idx3[:].rearrange("p (e c) -> p c e", c=CC, e=E)
    p.inc("s_ve", ve.tensor_copy(idxev, idx3p))
    p.wait(sy, "s_ve")
    sptok = p.dma_open(sy)
    p.dma_put(sptok, sy.dma_start(out=dram["wid_sp"], in_=wm2[:, 0:CC * (E + 1)]))
    p.dma_put(sptok, sy.dma_start(out=dram["idxe_sp"], in_=idxL[:]))
    p.dma_close(sptok)
    p.tap("widsp", wm2[:, 0:CC * (E + 1)])
    p.tap("idxLf", idxf[:])


def phMprep(p, st, *, hvg):
    nc = p.nc
    gp = nc.gpsimd
    dram = p.dram

    wide = p.sb("mp_wide", [128, WID], stack=st)
    ixt = p.sb("mp_ixt", [128, E], I32, stack=st)

    xtm = dram["x_own_tm"]
    widr = dram["wid_sp"]
    idxr = dram["idxe_sp"]

    gbase = p.semv["s_gp"]
    PER = 3 * 16 + E * 16
    with gp.Fori(0, CCH) as i:
        gp.wait_ge(p.s_gp, i * PER + gbase)
        gp.dma_start(out=wide[:, 0:D],
                     in_=xtm[DynSlice(i * 128, 128), :]).then_inc(p.s_gp, 16)
        gp.dma_start(out=wide[:, D:WID],
                     in_=widr[:, DynSlice((hvg + i) * (E + 1), E + 1)]
                     ).then_inc(p.s_gp, 16)
        gp.dma_start(out=ixt[:],
                     in_=idxr[:, DynSlice((hvg + i) * E, E)]).then_inc(p.s_gp, 16)
        gp.wait_ge(p.s_gp, i * PER + gbase + 48)
        for e in range(E):
            gp.indirect_dma_start(
                dram[f"xtok{e}"],
                IndirectOffsetOnAxis(ixt[:, e:e + 1], 0),
                wide[:], None,
                bounds_check=CAP - 1, oob_is_err=False).then_inc(p.s_gp, 16)
    p.semv["s_gp"] += CCH * PER
    gp.wait_ge(p.s_gp, p.semv["s_gp"])


def ph2(p, st, *, onesr, onesc, lnp):
    nc = p.nc
    sy, te, ve, ac = nc.sync, nc.tensor, nc.vector, nc.scalar
    sc = nc.scalar
    dram = p.dram

    wts = {}
    wtok = p.dma_open(sy)
    for nm in ("wr_t", "ww_t", "wk_t", "wv_t"):
        wts[nm] = p.sb("p2_" + nm, [128, 8 * D], stack=st)
        p.dma_put(wtok, sy.dma_start(
            out=wts[nm][:].rearrange("p (k e) -> p k e", k=8),
            in_=dram[nm].rearrange("(k p) e -> p k e", p=128)))
    w_ld = p.dma_close(wtok)

    xb = [p.sb("p2_xb0", [128, 8 * BW], stack=st)] * 2
    xsq = p.sb("p2_xsq", [128, 4 * BW], stack=st)
    xn = p.sb("p2_xn", [128, 8 * BW], stack=st)
    xn2 = xb[0]
    ab = p.sb("p2_ab", [128, 2 * BW], stack=st)
    srow = p.sb("p2_srow", [1, 2 * BW], stack=st)
    a0r = p.sb("p2_a0r", [1, BW], stack=st)
    b0r = p.sb("p2_b0r", [1, BW], stack=st)
    a1r = p.sb("p2_a1r", [1, BW], stack=st)
    b1r = p.sb("p2_b1r", [1, BW], stack=st)
    scr = p.sb("p2_scr", [1, BW], stack=st)
    ew = [p.sb(f"p2_ew{i}", [128, BW], stack=st) for i in range(2)]
    eb = [p.sb(f"p2_eb{i}", [128, BW], stack=st) for i in range(2)]
    evb = [p.sb(f"p2_evb{i}", [128, BW], stack=st) for i in range(2)]
    rb = [p.sb(f"p2_rb{i}", [128, BW], stack=st) for i in range(2)]

    bc_ps = p.ps("p2_bc", [128, BW], stack=st)
    st_ps = p.ps("p2_st", [1, 2 * BW], stack=st)
    mm_ps = [p.ps(f"p2_mm{i}", [128, BW], stack=st) for i in range(4)]

    xr = dram["x_fm"].rearrange("(k p) t -> k p t", p=128)

    def bcast(row_slice, dst_slice):
        p.wait(te, "s_ve")
        i = te.matmul(bc_ps[:], onesr[:], row_slice, start=True, stop=True)
        p.inc("s_pe", i)
        p.wait(ve, "s_pe")
        p.inc("s_ve", ve.tensor_copy(dst_slice, bc_ps[:]))

    pe_mark = [0]
    for blk in range(NB):
        tw = slice(blk * BW, (blk + 1) * BW)
        if blk >= 1:
            sy.wait_ge(p.s_pe, pe_mark[-1])
            sy.wait_ge(p.s_ve, p.semv["s_ve"])
        if blk >= 1:
            p.dma_wait(ve, _SPILLS[(id(p), blk - 1, 7)])
            p.dma_wait(ac, _SPILLS[(id(p), blk - 1, 7)])
        tok = p.dma_open(sy)
        for k in range(8):
            p.dma_put(tok, sy.dma_start(
                out=xb[blk % 2][:, k * BW:(k + 1) * BW], in_=xr[k, :, tw]))
        p.dma_put(tok, sy.dma_start(out=a0r[:], in_=dram["ab0_sp"][0:1, tw]))
        p.dma_put(tok, sy.dma_start(out=b0r[:], in_=dram["ab0_sp"][1:2, tw]))
        ldm = p.dma_close(tok)

        p.dma_wait(te, ldm)
        if blk == 0:
            p.dma_wait(te, w_ld)
        bcast(a0r[:], ab[:, 0:BW])
        bcast(b0r[:], ab[:, BW:2 * BW])
        a0b = ab[:, 0:BW].unsqueeze(1).broadcast_to([128, 8, BW])
        b0b = ab[:, BW:2 * BW].unsqueeze(1).broadcast_to([128, 8, BW])
        xbv = xb[blk % 2][:].rearrange("p (k w) -> p k w", k=8)
        xnv = xn[:].rearrange("p (k w) -> p k w", k=8)
        p.dma_wait(ve, ldm)
        ve.tensor_mul(xnv, xbv, a0b)
        p.inc("s_ve", ve.tensor_add(xnv, xnv, b0b))

        p.wait(ac, "s_ve")
        p.inc("s_ac", ac.activation(xsq[:], xn[:, 0:4 * BW], AF.Square))
        p.wait(te, "s_ac")
        p.wait(te, "s_ve")
        i = None
        for k in range(8):
            if k == 4:
                p.inc("s_pe", i)
                ac.wait_ge(p.s_pe, p.semv["s_pe"])
                p.inc("s_ac", ac.activation(xsq[:], xn[:, 4 * BW:8 * BW],
                                            AF.Square))
                p.wait(te, "s_ac")
            kw = slice(k * BW, (k + 1) * BW)
            te.matmul(st_ps[:, 0:BW], onesc[:], xn[:, kw],
                      start=(k == 0), stop=(k == 7))
            i = te.matmul(st_ps[:, BW:2 * BW], onesc[:],
                          xsq[:, (k % 4) * BW:(k % 4 + 1) * BW],
                          start=(k == 0), stop=(k == 7))
        p.inc("s_pe", i)
        p.wait(ve, "s_pe")
        ve.tensor_copy(srow[:], st_ps[:])
        ln_rows(p, srow, a1r[:], b1r[:], scr[:])
        bcast(a1r[:], ab[:, 0:BW])
        bcast(b1r[:], ab[:, BW:2 * BW])
        a1b = ab[:, 0:BW].unsqueeze(1).broadcast_to([128, 8, BW])
        b1b = ab[:, BW:2 * BW].unsqueeze(1).broadcast_to([128, 8, BW])
        xn2v = xn2[:].rearrange("p (k w) -> p k w", k=8)
        ve.tensor_mul(xn2v, xnv, a1b)
        ve.tensor_add(xn2v, xn2v, b1b)
        for k in range(8):
            kw = slice(k * BW, (k + 1) * BW)
            ve.tensor_scalar(xn2[:, kw], xn2[:, kw],
                             lnp[:, 0 * 8 + k:0 * 8 + k + 1],
                             lnp[:, 1 * 8 + k:1 * 8 + k + 1], AL.mult, AL.add)
        p.inc("s_ve", ve.engine_nop())
        xn2_ve = p.semv["s_ve"]

        te.wait_ge(p.s_ve, xn2_ve)
        cons = [None] * 4
        for m in range(8):
            bi = m % 2
            pm = st_spill_mark(p, blk, m, 2)
            if pm is not None:
                p.dma_wait(ve, pm)
                p.dma_wait(ac, pm)
            for oi, wname in enumerate(("ww_t", "wk_t", "wv_t", "wr_t")):
                pb = mm_ps[oi]
                if cons[oi] is not None:
                    te.wait_ge(getattr(p, cons[oi][0]), cons[oi][1])
                for k in range(8):
                    i = te.matmul(pb[:],
                                  wts[wname][:, k * D + m * 128:k * D + m * 128 + 128],
                                  xn2[:, k * BW:(k + 1) * BW],
                                  start=(k == 0), stop=(k == 7))
                p.inc("s_pe", i)
                mm_pe = p.semv["s_pe"]
                if wname == "ww_t":
                    ac.wait_ge(p.s_pe, mm_pe)
                    p.inc("s_ac", ac.activation(ew[bi][:], pb[:], AF.Exp))
                    cons[0] = ("s_ac", p.semv["s_ac"])
                elif wname == "wk_t":
                    ve.wait_ge(p.s_pe, mm_pe)
                    ve.wait_ge(p.s_ac, p.semv["s_ac"])
                    p.inc("s_ve", ve.scalar_tensor_tensor(
                        eb[bi][:], pb[:], 1.0, ew[bi][:], AL.mult, AL.subtract))
                    cons[1] = ("s_ve", p.semv["s_ve"])
                    ac.wait_ge(p.s_ve, p.semv["s_ve"])
                    p.inc("s_ac", ac.activation(eb[bi][:], eb[bi][:], AF.Exp))
                elif wname == "wv_t":
                    ve.wait_ge(p.s_pe, mm_pe)
                    ve.wait_ge(p.s_ac, p.semv["s_ac"])
                    p.inc("s_ve", ve.tensor_mul(evb[bi][:], eb[bi][:], pb[:]))
                    cons[2] = ("s_ve", p.semv["s_ve"])
                else:
                    ac.wait_ge(p.s_pe, mm_pe)
                    p.inc("s_ac", ac.activation(rb[bi][:], pb[:], AF.Sigmoid))
                    cons[3] = ("s_ac", p.semv["s_ac"])
            sc.wait_ge(p.s_ve, p.semv["s_ve"])
            sc.wait_ge(p.s_ac, p.semv["s_ac"])
            stok = p.dma_open(sc)
            for nm, tile in (("e_sp", eb[bi]), ("ev_sp", evb[bi]),
                             ("r_sp", rb[bi])):
                p.dma_put(stok, sc.dma_start(out=dram[nm][m, :, tw],
                                             in_=tile[:]))
            _SPILLS[(id(p), blk, m)] = p.dma_close(stok)
        pe_mark.append(p.semv["s_pe"])


_SPILLS = {}


def st_spill_mark(p, blk, m, dist):
    mm = m - dist
    key = (id(p), blk, mm) if mm >= 0 else (id(p), blk - 1, 8 + mm)
    return _SPILLS.get(key)
def ph3(p, st, *, decay):
    nc = p.nc
    sy, ve, sc = nc.sync, nc.vector, nc.scalar
    dram = p.dram

    ebuf = p.sb("p3_e", [128, T], stack=st)
    evbuf = p.sb("p3_ev", [128, T], stack=st)
    num = p.sb("p3_num", [128, T], stack=st)
    den = p.sb("p3_den", [128, T], stack=st)

    ve_mark = []
    st_tok = []
    for k in range(8):
        if k >= 1:
            sy.wait_ge(p.s_ve, ve_mark[k - 1])
        tok = p.dma_open(sy)
        p.dma_put(tok, sy.dma_start(out=ebuf[:], in_=dram["e_sp"][k]))
        p.dma_put(tok, sy.dma_start(out=evbuf[:], in_=dram["ev_sp"][k]))
        p.dma_close(tok)
        p.dma_wait(ve, tok)
        if k >= 1:
            p.dma_wait(ve, st_tok[k - 1])
        dk = decay[:, k:k + 1].broadcast_to([128, T])
        ve.tensor_tensor_scan(num[:], dk, evbuf[:], 0.0, AL.mult, AL.add)
        ve.tensor_tensor_scan(den[:], dk, ebuf[:], 0.0, AL.mult, AL.add)
        ve.tensor_scalar_add(den[:], den[:], WKV_EPS)
        p.inc("s_ve", ve.reciprocal(den[:], den[:]))
        p.wait(ve, "s_ve")
        p.inc("s_ve", ve.tensor_mul(num[:], num[:], den[:]))
        ve_mark.append(p.semv["s_ve"])
        sc.wait_ge(p.s_ve, p.semv["s_ve"])
        stok = p.dma_open(sc)
        p.dma_put(stok, sc.dma_start(out=dram["wkv_sp"][k], in_=num[:]))
        st_tok.append(p.dma_close(stok))


def phM(p, st, *, ident):
    nc = p.nc
    sy, te, ve, ac, gp = nc.sync, nc.tensor, nc.vector, nc.scalar, nc.gpsimd
    sc = nc.scalar
    dram = p.dram

    wg = [p.sb(f"pm_wg{i}", [128, 8 * 512], stack=st) for i in range(2)]
    wu = [p.sb(f"pm_wu{i}", [128, 8 * 512], stack=st) for i in range(2)]
    wd = [p.sb(f"pm_wd{i}", [128, 4 * 1024], stack=st) for i in range(2)]
    xt = p.sb("pm_xt", [128, NSL * WID], stack=st)
    eosb = xt[:, 0:8 * CAP]
    xsq = p.sb("pm_xsq", [128, D], stack=st)
    stats = p.sb("pm_stats", [128, 4 * NSL], stack=st)
    xtT = p.sb("pm_xtT", [128, 8 * CAP], stack=st)
    gel = p.sb("pm_gel", [128, 320], stack=st)
    hsb = p.sb("pm_hsb", [128, 8 * 320], stack=st)
    wcol = p.sb("pm_wcol", [128, NSL], stack=st)
    eoT = [p.sb(f"pm_eoT{i}", [128, D], stack=st) for i in range(2)]
    hofftt = p.sb("pm_hofft", [128, 2], stack=st)
    idc = p.sb("pm_idc", [128, NSL], I32, stack=st)
    idcf = p.sb("pm_idcf", [128, NSL], stack=st)
    idcm = p.sb("pm_idcm", [128, NSL], stack=st)

    trp = [p.ps(f"pm_trp{i}", [128, 128], stack=st) for i in range(2)]
    gps = p.ps("pm_gps", [128, BW], stack=st)
    ups = p.ps("pm_ups", [128, BW], stack=st)
    eop = p.ps("pm_eop", [128, BW], stack=st)
    top = p.ps("pm_top", [128, 1024], stack=st)

    xtv = xt[:].rearrange("p (n w) -> p n w", n=NSL)
    eoTr = dram["eoT_sp"].rearrange("(u p) d -> u p d", p=128)
    st_eoT = []

    htok = p.dma_open(sy)
    p.dma_put(htok, sy.dma_start(out=hofftt[:, 0:1], in_=dram["hofft1"]))
    p.dma_put(htok, sy.dma_start(out=hofftt[:, 1:2], in_=dram["hofft2"]))
    p.dma_close(htok)

    wq_tok = {}
    buf_mark = [0, 0]

    def issue_wq(gidx):
        if gidx >= 4 * E:
            return
        eq, q = divmod(gidx, 4)
        b = gidx % 2
        if buf_mark[b] > 0:
            sy.wait_ge(p.s_pe, buf_mark[b])
        fq = slice(q * 512, (q + 1) * 512)
        tok = p.dma_open(sy)
        p.dma_put(tok, sy.dma_start(
            out=wg[b][:].rearrange("p (k f) -> p k f", k=8),
            in_=dram["wg_t"][eq].rearrange("(k p) f -> p k f", p=128)[:, :, fq]))
        p.dma_put(tok, sy.dma_start(
            out=wu[b][:].rearrange("p (k f) -> p k f", k=8),
            in_=dram["wu_t"][eq].rearrange("(k p) f -> p k f", p=128)[:, :, fq]))
        p.dma_put(tok, sy.dma_start(
            out=wd[b][:].rearrange("p (k d) -> p k d", k=4),
            in_=dram["wd_t"][eq].rearrange(
                "(k p) d -> p k d", p=128)[:, q * 4:(q + 1) * 4]))
        wq_tok[gidx] = p.dma_close(tok)

    issue_wq(0)
    issue_wq(1)

    for e in range(E):
        if e > 0:
            sy.wait_ge(p.s_pe, p.semv["s_pe"])
            sy.wait_ge(p.s_ve, p.semv["s_ve"])
            for t_ in st_eoT[-2:]:
                p.dma_wait(sy, t_)
        xtok_t = p.dma_open(sy)
        p.dma_put(xtok_t, sy.dma_start(
            out=xtv, in_=dram[f"xtok{e}"].rearrange("(n p) w -> p n w", p=128)))
        p.dma_close(xtok_t)
        if e == 0:
            p.dma_wait(ve, htok)

        p.dma_wait(ve, xtok_t)
        p.dma_wait(ac, xtok_t)
        for sl in range(NSL):
            xs = xt[:, sl * WID:sl * WID + D]
            if sl > 0:
                ac.wait_ge(p.s_ve, p.semv["s_ve"])
            p.inc("s_ac", ac.activation(xsq[:], xs, AF.Square))
            ve.tensor_reduce(stats[:, sl:sl + 1], xs.unsqueeze(1), AX.X, AL.add)
            ve.wait_ge(p.s_ac, p.semv["s_ac"])
            p.inc("s_ve", ve.tensor_reduce(
                stats[:, NSL + sl:NSL + sl + 1], xsq[:].unsqueeze(1),
                AX.X, AL.add))
        p.wait(ve, "s_ve")
        ve.tensor_scalar_mul(stats[:, 2 * NSL:3 * NSL], stats[:, 0:NSL], 1.0 / D)
        p.vf(ve.tensor_scalar_mul(stats[:, 3 * NSL:4 * NSL],
                                  stats[:, NSL:2 * NSL], 1.0 / D))
        p.vf(ve.tensor_mul(stats[:, 0:NSL], stats[:, 2 * NSL:3 * NSL],
                           stats[:, 2 * NSL:3 * NSL]))
        p.vf(ve.tensor_sub(stats[:, 3 * NSL:4 * NSL], stats[:, 3 * NSL:4 * NSL],
                           stats[:, 0:NSL]))
        p.inc("s_ve", ve.tensor_scalar_add(stats[:, 3 * NSL:4 * NSL],
                                           stats[:, 3 * NSL:4 * NSL], LN_EPS))
        p.wait(ac, "s_ve")
        ac.activation(stats[:, 3 * NSL:4 * NSL],
                      stats[:, 3 * NSL:4 * NSL], AF.Sqrt)
        p.inc("s_ac", ac.activation(xsq[:, 0:512], xsq[:, 0:512], AF.Square))
        p.wait(ve, "s_ac")
        p.inc("s_ve", ve.reciprocal(stats[:, 3 * NSL:4 * NSL],
                                    stats[:, 3 * NSL:4 * NSL]))
        p.wait(ve, "s_ve")
        p.vf(ve.tensor_mul(stats[:, 2 * NSL:3 * NSL], stats[:, 2 * NSL:3 * NSL],
                           stats[:, 3 * NSL:4 * NSL]))
        p.vf(ve.tensor_scalar_mul(stats[:, 2 * NSL:3 * NSL],
                                  stats[:, 2 * NSL:3 * NSL], -1.0))
        for sl in range(NSL):
            ve.tensor_scalar(xt[:, sl * WID:sl * WID + D],
                             xt[:, sl * WID:sl * WID + D],
                             stats[:, 3 * NSL + sl:3 * NSL + sl + 1],
                             stats[:, 2 * NSL + sl:2 * NSL + sl + 1],
                             AL.mult, AL.add)
        ve.tensor_copy(wcol[:].unsqueeze(2), xtv[:, :, D + e:D + e + 1])
        p.vf(ve.tensor_copy(idcf[:].unsqueeze(2), xtv[:, :, WID - 1:WID]))
        p.vf(ve.tensor_scalar(idcf[:], idcf[:], hofftt[:, 0:1], None,
                              AL.subtract))
        p.vf(ve.tensor_single_scalar(idcm[:], idcf[:], 0.0, AL.is_lt))
        p.vf(ve.scalar_tensor_tensor(idcf[:], idcm[:], hofftt[:, 1:2], idcf[:],
                                     AL.mult, AL.add))
        p.inc("s_ve", ve.tensor_copy(idc[:], idcf[:]))
        rn_ve = p.semv["s_ve"]

        sc.wait_ge(p.s_ve, rn_ve)
        idtok = p.dma_open(sc)
        p.dma_put(idtok, sc.dma_start(
            out=dram["idT_sp"][e * NSL:(e + 1) * NSL, :].rearrange("s p -> p s"),
            in_=idc[:], allow_slow_non_contiguous=True))
        p.dma_close(idtok)

        te.wait_ge(p.s_ve, rn_ve)
        nt = 0
        tr_ve = p.semv["s_ve"]
        for dc in range(8):
            for sl in range(NSL):
                if nt >= 2:
                    te.wait_ge(p.s_ve, tr_ve + nt - 1)
                i = te.transpose(
                    trp[nt % 2][:],
                    xt[:, sl * WID + dc * 128:sl * WID + (dc + 1) * 128],
                    ident[:])
                p.inc("s_pe", i)
                ve.wait_ge(p.s_pe, p.semv["s_pe"])
                p.inc("s_ve", ve.tensor_copy(
                    xtT[:, dc * CAP + sl * 128:dc * CAP + (sl + 1) * 128],
                    trp[nt % 2][:]))
                nt += 1

        for q in range(4):
            gidx = e * 4 + q
            b = gidx % 2
            p.dma_wait(te, wq_tok[gidx])
            for nh in range(2):
                for fcl in range(4):
                    te.wait_ge(p.s_ve, p.semv["s_ve"])
                    for k in range(8):
                        te.matmul(gps[:, 0:320],
                                  wg[b][:, k * 512 + fcl * 128:k * 512 + (fcl + 1) * 128],
                                  xtT[:, k * CAP + nh * 320:k * CAP + nh * 320 + 320],
                                  start=(k == 0), stop=(k == 7))
                    for k in range(8):
                        i = te.matmul(ups[:, 0:320],
                                      wu[b][:, k * 512 + fcl * 128:k * 512 + (fcl + 1) * 128],
                                      xtT[:, k * CAP + nh * 320:k * CAP + nh * 320 + 320],
                                      start=(k == 0), stop=(k == 7))
                    p.inc("s_pe", i)
                    ac.wait_ge(p.s_pe, p.semv["s_pe"])
                    p.inc("s_ac", ac.activation(gel[:, 0:320], gps[:, 0:320],
                                                AF.Gelu))
                    ve.wait_ge(p.s_ac, p.semv["s_ac"])
                    p.inc("s_ve", ve.tensor_mul(
                        hsb[:, fcl * 320:fcl * 320 + 320],
                        gel[:, 0:320], ups[:, 0:320]))
                for m in range(8):
                    te.wait_ge(p.s_ve, p.semv["s_ve"])
                    for fcl in range(4):
                        i = te.matmul(eop[:, 0:320],
                                      wd[b][:, fcl * 1024 + m * 128:fcl * 1024 + (m + 1) * 128],
                                      hsb[:, fcl * 320:fcl * 320 + 320],
                                      start=(fcl == 0), stop=(fcl == 3))
                    p.inc("s_pe", i)
                    ve.wait_ge(p.s_pe, p.semv["s_pe"])
                    dst = eosb[:, m * CAP + nh * 320:m * CAP + nh * 320 + 320]
                    if q == 0:
                        p.inc("s_ve", ve.tensor_copy(dst, eop[:, 0:320]))
                    else:
                        p.inc("s_ve", ve.scalar_tensor_tensor(
                            dst, eop[:, 0:320], 1.0, dst, AL.mult, AL.add))
            buf_mark[b] = p.semv["s_pe"]
            issue_wq(gidx + 2)

        for sl in range(NSL):
            te.wait_ge(p.s_ve, p.semv["s_ve"])
            for dc in range(8):
                i = te.transpose(top[:, dc * 128:(dc + 1) * 128],
                                 eosb[:, dc * CAP + sl * 128:dc * CAP + (sl + 1) * 128],
                                 ident[:])
            p.inc("s_pe", i)
            ve.wait_ge(p.s_pe, p.semv["s_pe"])
            if len(st_eoT) >= 2:
                p.dma_wait(ve, st_eoT[-2])
            p.inc("s_ve", ve.tensor_scalar(
                eoT[sl % 2][:], top[:], wcol[:, sl:sl + 1], None, AL.mult))
            sc.wait_ge(p.s_ve, p.semv["s_ve"])
            etok = p.dma_open(sc)
            p.dma_put(etok, sc.dma_start(out=eoTr[e * NSL + sl],
                                         in_=eoT[sl % 2][:]))
            st_eoT.append(p.dma_close(etok))


def eoFori(p, st):
    nc = p.nc
    gp = nc.gpsimd
    dram = p.dram
    eob = p.sb("eof_eob", [128, D], stack=st)
    ixt = p.sb("eof_ixt", [128, 1], I32, stack=st)
    eoTr = dram["eoT_sp"]
    idTr = dram["idT_sp"].rearrange("u p -> p u")
    gbase = p.semv["s_gp"]
    PER = 3 * 16
    NU = E * NSL
    with gp.Fori(0, NU) as u:
        gp.wait_ge(p.s_gp, u * PER + gbase)
        gp.dma_start(out=eob[:],
                     in_=eoTr[DynSlice(u * 128, 128), :]).then_inc(p.s_gp, 16)
        gp.dma_start(out=ixt[:], in_=idTr[:, DynSlice(u, 1)],
                     allow_slow_non_contiguous=True).then_inc(p.s_gp, 16)
        gp.wait_ge(p.s_gp, u * PER + gbase + 32)
        gp.indirect_dma_start(
            dram["macc"], IndirectOffsetOnAxis(ixt[:], 0), eob[:], None,
            bounds_check=HALF, oob_is_err=False,
            compute_op=AL.add).then_inc(p.s_gp, 16)
    p.semv["s_gp"] += NU * PER
    gp.wait_ge(p.s_gp, p.semv["s_gp"])


def ph4a(p, st, *, onesr, onesc, lnp, hv, wo, w_ld):
    hoff_tok = hv * 128
    """xn (from ab0_sp), rw=r*wkv, tm=Wo@rw, y1, LN2->xn3; spills.
    wo/w_ld: prefetched during phM."""
    nc = p.nc
    sy, te, ve, ac, sc = nc.sync, nc.tensor, nc.vector, nc.scalar, nc.scalar
    dram = p.dram

    xb = p.sb("p4_xb", [128, 8 * BW], stack=st)
    xsq = p.sb("p4_xsq", [128, 8 * BW], stack=st)
    xn = p.sb("p4_xn", [128, 8 * BW], stack=st)
    rbuf = p.sb("p4_rbuf", [128, 8 * BW], stack=st)
    wkvb = p.sb("p4_wkvb", [128, 8 * BW], stack=st)
    y1 = p.sb("p4_y1", [128, 8 * BW], stack=st)
    xn3 = p.sb("p4_xn3", [128, 8 * BW], stack=st)
    ab = p.sb("p4_ab", [128, 4 * BW], stack=st)
    srow = p.sb("p4_srow", [1, 2 * BW], stack=st)
    a0r = p.sb("p4_a0r", [1, BW], stack=st)
    b0r = p.sb("p4_b0r", [1, BW], stack=st)
    a1r = p.sb("p4_a1r", [1, BW], stack=st)
    b1r = p.sb("p4_b1r", [1, BW], stack=st)
    scr = p.sb("p4_scr", [1, BW], stack=st)

    bc_ps = p.ps("p4_bc", [128, BW], stack=st)
    st_ps = p.ps("p4_st", [1, 2 * BW], stack=st)
    mm_ps = [p.ps(f"p4_mm{i}", [128, BW], stack=st) for i in range(2)]

    xr = dram["x_own_fm"].rearrange("(k p) t -> k p t", p=128)

    def bcast(row_slice, dst_slice):
        p.wait(te, "s_ve")
        i = te.matmul(bc_ps[:], onesr[:], row_slice, start=True, stop=True)
        p.inc("s_pe", i)
        p.wait(ve, "s_pe")
        p.inc("s_ve", ve.tensor_copy(dst_slice, bc_ps[:]))

    def ln_full(src_tile, dst_tile, g_off, b_off):
        p.wait(ac, "s_ve")
        p.inc("s_ac", ac.activation(xsq[:], src_tile[:], AF.Square))
        p.wait(te, "s_ac")
        for k in range(8):
            kw = slice(k * BW, (k + 1) * BW)
            te.matmul(st_ps[:, 0:BW], onesc[:], src_tile[:, kw],
                      start=(k == 0), stop=(k == 7))
            i = te.matmul(st_ps[:, BW:2 * BW], onesc[:], xsq[:, kw],
                          start=(k == 0), stop=(k == 7))
        p.inc("s_pe", i)
        p.wait(ve, "s_pe")
        ve.tensor_copy(srow[:], st_ps[:])
        ln_rows(p, srow, a1r[:], b1r[:], scr[:])
        bcast(a1r[:], ab[:, 2 * BW:3 * BW])
        bcast(b1r[:], ab[:, 3 * BW:4 * BW])
        a1b = ab[:, 2 * BW:3 * BW].unsqueeze(1).broadcast_to([128, 8, BW])
        b1b = ab[:, 3 * BW:4 * BW].unsqueeze(1).broadcast_to([128, 8, BW])
        sv = src_tile[:].rearrange("p (k w) -> p k w", k=8)
        dv = dst_tile[:].rearrange("p (k w) -> p k w", k=8)
        ve.tensor_mul(dv, sv, a1b)
        ve.tensor_add(dv, dv, b1b)
        for k in range(8):
            kw = slice(k * BW, (k + 1) * BW)
            ve.tensor_scalar(dst_tile[:, kw], dst_tile[:, kw],
                             lnp[:, g_off + k:g_off + k + 1],
                             lnp[:, b_off + k:b_off + k + 1], AL.mult, AL.add)
        p.inc("s_ve", ve.engine_nop())

    st_mark = []
    for blk in range(NBO):
        tw = slice(blk * BW, (blk + 1) * BW)
        if blk >= 1:
            p.dma_wait(sy, st_mark[blk - 1])
            sy.wait_ge(p.s_ve, p.semv["s_ve"])
        tok = p.dma_open(sy)
        for k in range(8):
            p.dma_put(tok, sy.dma_start(out=xb[:, k * BW:(k + 1) * BW],
                                        in_=xr[k, :, tw]))
        p.dma_put(tok, sy.dma_start(
            out=rbuf[:].rearrange("p (k w) -> p k w", k=8),
            in_=dram["r_sp"][:, :, DynSlice(hoff_tok + blk * BW, BW)].rearrange(
                "k p w -> p k w")))
        p.dma_put(tok, sy.dma_start(
            out=wkvb[:].rearrange("p (k w) -> p k w", k=8),
            in_=dram["wkv_sp"][:, :, DynSlice(hoff_tok + blk * BW, BW)].rearrange(
                "k p w -> p k w")))
        p.dma_put(tok, sy.dma_start(
            out=a0r[:], in_=dram["ab0_sp"][0:1, DynSlice(hoff_tok + blk * BW, BW)]))
        p.dma_put(tok, sy.dma_start(
            out=b0r[:], in_=dram["ab0_sp"][1:2, DynSlice(hoff_tok + blk * BW, BW)]))
        ldm = p.dma_close(tok)
        if blk == 0:
            p.dma_wait(te, w_ld)

        p.dma_wait(ve, ldm)
        p.dma_wait(te, ldm)
        bcast(a0r[:], ab[:, 0:BW])
        bcast(b0r[:], ab[:, BW:2 * BW])
        a0b = ab[:, 0:BW].unsqueeze(1).broadcast_to([128, 8, BW])
        b0b = ab[:, BW:2 * BW].unsqueeze(1).broadcast_to([128, 8, BW])
        xbv = xb[:].rearrange("p (k w) -> p k w", k=8)
        xnv = xn[:].rearrange("p (k w) -> p k w", k=8)
        ve.tensor_mul(xnv, xbv, a0b)
        ve.tensor_add(xnv, xnv, b0b)
        p.inc("s_ve", ve.tensor_mul(rbuf[:], rbuf[:], wkvb[:]))
        rw_ve = p.semv["s_ve"]

        te.wait_ge(p.s_ve, rw_ve)
        for m in range(8):
            mw = slice(m * BW, (m + 1) * BW)
            pb = mm_ps[m % 2]
            if m >= 2:
                te.wait_ge(p.s_ve, p.semv["s_ve"])
            for k in range(8):
                i = te.matmul(pb[:],
                              wo[:, k * D + m * 128:k * D + m * 128 + 128],
                              rbuf[:, k * BW:(k + 1) * BW],
                              start=(k == 0), stop=(k == 7))
            p.inc("s_pe", i)
            ve.wait_ge(p.s_pe, p.semv["s_pe"])
            p.inc("s_ve", ve.scalar_tensor_tensor(
                y1[:, mw], pb[:], 1.0, xn[:, mw], AL.mult, AL.add))

        ln_full(y1, xn3, 16, 24)

        sc.wait_ge(p.s_ve, p.semv["s_ve"])
        stok = p.dma_open(sc)
        for nm, tile in (("xn_sp", xn), ("y1_sp", y1), ("xn3_sp", xn3)):
            p.dma_put(stok, sc.dma_start(
                out=dram[nm][:, :, tw].rearrange("k p w -> p k w"),
                in_=tile[:].rearrange("p (k w) -> p k w", k=8)))
        st_mark.append(p.dma_close(stok))


def ph4sh(p, st):
    nc = p.nc
    sy, te, ve, ac, sc = nc.sync, nc.tensor, nc.vector, nc.scalar, nc.scalar
    dram = p.dram

    wsg = [p.sb(f"psh_wsg{i}", [128, 8 * 512], stack=st) for i in range(2)]
    wsu = [p.sb(f"psh_wsu{i}", [128, 8 * 512], stack=st) for i in range(2)]
    wsd = [p.sb(f"psh_wsd{i}", [128, 4 * 1024], stack=st) for i in range(2)]
    xnb = p.sb("psh_xn", [128, 8 * BW], stack=st)
    gel = p.sb("psh_gel", [128, BW], stack=st)
    hs = p.sb("psh_hs", [128, 4 * BW], stack=st)
    sh = p.sb("psh_sh", [128, 8 * BW], stack=st)

    gsp = p.ps("psh_gsp", [128, BW], stack=st)
    usp = p.ps("psh_usp", [128, BW], stack=st)
    dsp = p.ps("psh_dsp", [128, BW], stack=st)

    wq_tok = {}
    buf_mark = [0, 0]

    def issue_wq(q):
        if q >= 4:
            return
        b = q % 2
        if buf_mark[b] > 0:
            sy.wait_ge(p.s_pe, buf_mark[b])
        fq = slice(q * 512, (q + 1) * 512)
        tok = p.dma_open(sy)
        p.dma_put(tok, sy.dma_start(
            out=wsg[b][:].rearrange("p (k f) -> p k f", k=8),
            in_=dram["wsg_t"].rearrange("(k p) f -> p k f", p=128)[:, :, fq]))
        p.dma_put(tok, sy.dma_start(
            out=wsu[b][:].rearrange("p (k f) -> p k f", k=8),
            in_=dram["wsu_t"].rearrange("(k p) f -> p k f", p=128)[:, :, fq]))
        p.dma_put(tok, sy.dma_start(
            out=wsd[b][:].rearrange("p (k e) -> p k e", k=4),
            in_=dram["wsd_t"].rearrange(
                "(k p) e -> p k e", p=128)[:, q * 4:(q + 1) * 4]))
        wq_tok[q] = p.dma_close(tok)

    issue_wq(0)
    issue_wq(1)

    for q in range(4):
        b = q % 2
        st_mark = []
        for blk in range(NBO):
            tw = slice(blk * BW, (blk + 1) * BW)
            sy.wait_ge(p.s_pe, p.semv["s_pe"])
            if blk >= 1:
                p.dma_wait(sy, st_mark[blk - 1])
            tok = p.dma_open(sy)
            for k in range(8):
                p.dma_put(tok, sy.dma_start(
                    out=xnb[:, k * BW:(k + 1) * BW],
                    in_=dram["xn_sp"][k, :, tw]))
            if q >= 1:
                p.dma_put(tok, sy.dma_start(
                    out=sh[:].rearrange("p (k w) -> p k w", k=8),
                    in_=dram["sh_sp"][:, :, tw].rearrange("k p w -> p k w")))
            ldm = p.dma_close(tok)
            p.dma_wait(te, ldm)
            p.dma_wait(te, wq_tok[q])
            for fc in range(4):
                te.wait_ge(p.s_ve, p.semv["s_ve"])
                for k in range(8):
                    te.matmul(gsp[:],
                              wsg[b][:, k * 512 + fc * 128:k * 512 + (fc + 1) * 128],
                              xnb[:, k * BW:(k + 1) * BW],
                              start=(k == 0), stop=(k == 7))
                for k in range(8):
                    i = te.matmul(usp[:],
                                  wsu[b][:, k * 512 + fc * 128:k * 512 + (fc + 1) * 128],
                                  xnb[:, k * BW:(k + 1) * BW],
                                  start=(k == 0), stop=(k == 7))
                p.inc("s_pe", i)
                ac.wait_ge(p.s_pe, p.semv["s_pe"])
                p.inc("s_ac", ac.activation(gel[:], gsp[:], AF.Gelu))
                ve.wait_ge(p.s_ac, p.semv["s_ac"])
                p.inc("s_ve", ve.tensor_mul(hs[:, fc * BW:(fc + 1) * BW],
                                            gel[:], usp[:]))
            for m in range(8):
                te.wait_ge(p.s_ve, p.semv["s_ve"])
                for fc in range(4):
                    i = te.matmul(dsp[:],
                                  wsd[b][:, fc * D + m * 128:fc * D + (m + 1) * 128],
                                  hs[:, fc * BW:(fc + 1) * BW],
                                  start=(fc == 0), stop=(fc == 3))
                p.inc("s_pe", i)
                ve.wait_ge(p.s_pe, p.semv["s_pe"])
                if q >= 1 and m == 0:
                    p.dma_wait(ve, ldm)
                if q == 0 and m == 0 and blk >= 1:
                    p.dma_wait(ve, st_mark[blk - 1])
                mw = slice(m * BW, (m + 1) * BW)
                if q == 0:
                    p.inc("s_ve", ve.tensor_copy(sh[:, mw], dsp[:]))
                else:
                    p.inc("s_ve", ve.scalar_tensor_tensor(
                        sh[:, mw], dsp[:], 1.0, sh[:, mw], AL.mult, AL.add))
            sc.wait_ge(p.s_ve, p.semv["s_ve"])
            stok = p.dma_open(sc)
            p.dma_put(stok, sc.dma_start(
                out=dram["sh_sp"][:, :, tw].rearrange("k p w -> p k w"),
                in_=sh[:].rearrange("p (k w) -> p k w", k=8)))
            st_mark.append(p.dma_close(stok))
        buf_mark[b] = p.semv["s_pe"]
        issue_wq(q + 2)


def ph4b(p, st):
    nc = p.nc
    sy, te, ve, ac, sc = nc.sync, nc.tensor, nc.vector, nc.scalar, nc.scalar
    dram = p.dram

    wkey = [p.sb(f"pb_wkey{i}", [128, 8 * 1024], stack=st) for i in range(2)]
    wval = [p.sb(f"pb_wval{i}", [128, 8 * 1024], stack=st) for i in range(2)]
    xnb = p.sb("pb_xn", [128, 8 * BW], stack=st)
    relu = p.sb("pb_relu", [128, BW], stack=st)
    kk = p.sb("pb_kk", [128, 8 * BW], stack=st)
    vv = p.sb("pb_vv", [128, 8 * BW], stack=st)

    kkp = p.ps("pb_kkp", [128, BW], stack=st)
    vvp = p.ps("pb_vvp", [128, BW], stack=st)

    wq_tok = {}
    buf_mark = [0, 0]

    def issue_wq(q):
        if q >= 4:
            return
        b = q % 2
        if buf_mark[b] > 0:
            sy.wait_ge(p.s_pe, buf_mark[b])
        tok = p.dma_open(sy)
        p.dma_put(tok, sy.dma_start(
            out=wkey[b][:].rearrange("p (k f) -> p k f", k=8),
            in_=dram["wkey_t"].rearrange("(k p) f -> p k f", p=128)[
                :, :, q * 1024:(q + 1) * 1024]))
        p.dma_put(tok, sy.dma_start(
            out=wval[b][:].rearrange("p (k e) -> p k e", k=8),
            in_=dram["wval_t"].rearrange("(k p) e -> p k e", p=128)[
                :, q * 8:(q + 1) * 8]))
        wq_tok[q] = p.dma_close(tok)

    issue_wq(0)
    issue_wq(1)

    for q in range(4):
        b = q % 2
        st_mark = []
        for blk in range(NBO):
            tw = slice(blk * BW, (blk + 1) * BW)
            sy.wait_ge(p.s_pe, p.semv["s_pe"])
            if blk >= 1:
                p.dma_wait(sy, st_mark[blk - 1])
            tok = p.dma_open(sy)
            for k in range(8):
                p.dma_put(tok, sy.dma_start(
                    out=xnb[:, k * BW:(k + 1) * BW],
                    in_=dram["xn3_sp"][k, :, tw]))
            if q >= 1:
                p.dma_put(tok, sy.dma_start(
                    out=vv[:].rearrange("p (k w) -> p k w", k=8),
                    in_=dram["vvh_sp"][:, :, tw].rearrange("k p w -> p k w")))
            ldm = p.dma_close(tok)
            p.dma_wait(te, ldm)
            p.dma_wait(te, wq_tok[q])
            for fc in range(8):
                te.wait_ge(p.s_ve, p.semv["s_ve"])
                for k in range(8):
                    i = te.matmul(kkp[:],
                                  wkey[b][:, k * 1024 + fc * 128:k * 1024 + (fc + 1) * 128],
                                  xnb[:, k * BW:(k + 1) * BW],
                                  start=(k == 0), stop=(k == 7))
                p.inc("s_pe", i)
                ac.wait_ge(p.s_pe, p.semv["s_pe"])
                p.inc("s_ac", ac.activation(relu[:], kkp[:], AF.Relu))
                ve.wait_ge(p.s_ac, p.semv["s_ac"])
                p.inc("s_ve", ve.tensor_mul(kk[:, fc * BW:(fc + 1) * BW],
                                            relu[:], relu[:]))
            for m in range(8):
                te.wait_ge(p.s_ve, p.semv["s_ve"])
                for fc in range(8):
                    i = te.matmul(vvp[:],
                                  wval[b][:, fc * D + m * 128:fc * D + (m + 1) * 128],
                                  kk[:, fc * BW:(fc + 1) * BW],
                                  start=(fc == 0), stop=(fc == 7))
                p.inc("s_pe", i)
                ve.wait_ge(p.s_pe, p.semv["s_pe"])
                if m == 0:
                    p.dma_wait(ve, ldm)
                    if blk >= 1:
                        p.dma_wait(ve, st_mark[blk - 1])
                mw = slice(m * BW, (m + 1) * BW)
                if q == 0:
                    p.inc("s_ve", ve.tensor_copy(vv[:, mw], vvp[:]))
                else:
                    p.inc("s_ve", ve.scalar_tensor_tensor(
                        vv[:, mw], vvp[:], 1.0, vv[:, mw], AL.mult, AL.add))
            sc.wait_ge(p.s_ve, p.semv["s_ve"])
            stok = p.dma_open(sc)
            p.dma_put(stok, sc.dma_start(
                out=dram["vvh_sp"][:, :, tw].rearrange("k p w -> p k w"),
                in_=vv[:].rearrange("p (k w) -> p k w", k=8)))
            st_mark.append(p.dma_close(stok))
        buf_mark[b] = p.semv["s_pe"]
        issue_wq(q + 2)


def ph4c(p, st, *, ident):
    nc = p.nc
    sy, te, ve, ac, sc = nc.sync, nc.tensor, nc.vector, nc.scalar, nc.scalar
    dram = p.dram

    wrec = p.sb("pc_wrec", [128, 8 * D], stack=st)
    wtok = p.dma_open(sy)
    p.dma_put(wtok, sy.dma_start(
        out=wrec[:].rearrange("p (k e) -> p k e", k=8),
        in_=dram["wrec_t"].rearrange("(k p) e -> p k e", p=128)))
    w_ld = p.dma_close(wtok)

    xn3b = p.sb("pc_xn3", [128, 8 * BW], stack=st)
    vvb = p.sb("pc_vv", [128, 8 * BW], stack=st)
    y1b = p.sb("pc_y1", [128, 8 * BW], stack=st)
    shb = p.sb("pc_sh", [128, 8 * BW], stack=st)
    xbb = p.sb("pc_xb", [128, 8 * BW], stack=st)
    rr = p.sb("pc_rr", [128, 8 * BW], stack=st)
    mrow = p.sb("pc_mrow", [128, D], stack=st)
    orow = p.sb("pc_orow", [128, D], stack=st)

    mm_ps = [p.ps(f"pc_mm{i}", [128, BW], stack=st) for i in range(2)]
    trp = p.ps("pc_trp", [128, 1024], stack=st)

    xr = dram["x_own_fm"].rearrange("(k p) t -> k p t", p=128)
    outr = dram["out"].rearrange("(n p) d -> n p d", p=128)
    maccr = dram["macc"][0:HALF].rearrange("(n p) d -> n p d", p=128)

    for blk in range(NBO):
        tw = slice(blk * BW, (blk + 1) * BW)
        if blk >= 1:
            sy.wait_ge(p.s_ve, p.semv["s_ve"])
        tok = p.dma_open(sy)
        for k in range(8):
            p.dma_put(tok, sy.dma_start(out=xbb[:, k * BW:(k + 1) * BW],
                                        in_=xr[k, :, tw]))
        for nm, tile in (("xn3_sp", xn3b), ("vvh_sp", vvb),
                         ("y1_sp", y1b), ("sh_sp", shb)):
            p.dma_put(tok, sy.dma_start(
                out=tile[:].rearrange("p (k w) -> p k w", k=8),
                in_=dram[nm][:, :, tw].rearrange("k p w -> p k w")))
        ldm = p.dma_close(tok)
        if blk == 0:
            p.dma_wait(te, w_ld)

        p.dma_wait(te, ldm)
        for m in range(8):
            mw = slice(m * BW, (m + 1) * BW)
            pb = mm_ps[m % 2]
            if m >= 2:
                te.wait_ge(p.s_ac, p.semv["s_ac"])
            for k in range(8):
                i = te.matmul(pb[:],
                              wrec[:, k * D + m * 128:k * D + m * 128 + 128],
                              xn3b[:, k * BW:(k + 1) * BW],
                              start=(k == 0), stop=(k == 7))
            p.inc("s_pe", i)
            ac.wait_ge(p.s_pe, p.semv["s_pe"])
            p.inc("s_ac", ac.activation(rr[:, mw], pb[:], AF.Sigmoid))

        ve.wait_ge(p.s_ac, p.semv["s_ac"])
        p.dma_wait(ve, ldm)
        ve.tensor_mul(rr[:], rr[:], vvb[:])
        ve.tensor_add(rr[:], rr[:], y1b[:])
        ve.tensor_add(rr[:], rr[:], shb[:])
        p.inc("s_ve", ve.tensor_add(rr[:], rr[:], xbb[:]))
        base_ve = p.semv["s_ve"]

        te.wait_ge(p.s_ve, base_ve)
        ot_prev = None
        for t4 in range(4):
            n = blk * 4 + t4
            sy.wait_ge(p.s_ve, p.semv["s_ve"])
            mtok = p.dma_open(sy)
            p.dma_put(mtok, sy.dma_start(out=mrow[:], in_=maccr[n]))
            p.dma_close(mtok)
            if t4 > 0:
                te.wait_ge(p.s_ve, p.semv["s_ve"])
            for k in range(8):
                i = te.transpose(trp[:, k * 128:(k + 1) * 128],
                                 rr[:, k * BW + t4 * 128:k * BW + t4 * 128 + 128],
                                 ident[:])
            p.inc("s_pe", i)
            ve.wait_ge(p.s_pe, p.semv["s_pe"])
            p.dma_wait(ve, mtok)
            if ot_prev is not None:
                p.dma_wait(ve, ot_prev)
            p.inc("s_ve", ve.scalar_tensor_tensor(
                orow[:], trp[:], 1.0, mrow[:], AL.mult, AL.add))
            sc.wait_ge(p.s_ve, p.semv["s_ve"])
            otok = p.dma_open(sc)
            p.dma_put(otok, sc.dma_start(out=outr[n], in_=orow[:]))
            ot_prev = p.dma_close(otok)

    p.dma_wait_all(sc)
    p.dma_wait_all(sy)
    p.wait(nc.gpsimd, "s_gp")


def make_inmaps(inputs):
    x = np.ascontiguousarray(inputs["L0_attn_x"], np.float32)

    def tr(a):
        return np.ascontiguousarray(np.asarray(a, np.float32).T)

    def chunks8(v):
        return np.ascontiguousarray(np.asarray(v, np.float32).reshape(8, 128).T)

    shared = {
        "wr_t": tr(inputs["L0_attn_Wr"]),
        "ww_t": tr(inputs["L0_attn_Ww"]),
        "wk_t": tr(inputs["L0_attn_Wk"]),
        "wv_t": tr(inputs["L0_attn_Wv"]),
        "wo_t": tr(inputs["L0_attn_Wo"]),
        "wrec_t": tr(inputs["L0_attn_Wrec"]),
        "wkey_t": tr(inputs["L0_attn_Wkey"]),
        "wval_t": tr(inputs["L0_attn_Wval"]),
        "wsg_t": tr(inputs["L0_ffn_Ws_g"]),
        "wsu_t": tr(inputs["L0_ffn_Ws_u"]),
        "wsd_t": tr(inputs["L0_ffn_Ws_d"]),
        "gate_t": tr(inputs["L0_ffn_gate_w"]),
        "wg_t": np.ascontiguousarray(
            np.asarray(inputs["L0_ffn_Wg"], np.float32).transpose(0, 2, 1)),
        "wu_t": np.ascontiguousarray(
            np.asarray(inputs["L0_ffn_Wu"], np.float32).transpose(0, 2, 1)),
        "wd_t": np.ascontiguousarray(
            np.asarray(inputs["L0_ffn_Wd"], np.float32).transpose(0, 2, 1)),
        "ln1g": chunks8(inputs["L0_attn_ln1_g"]),
        "ln1b": chunks8(inputs["L0_attn_ln1_b"]),
        "ln2g": chunks8(inputs["L0_attn_ln2_g"]),
        "ln2b": chunks8(inputs["L0_attn_ln2_b"]),
        "td": chunks8(inputs["L0_attn_time_decay"]),
        "ident": np.eye(128, dtype=np.float32),
        "tri": np.ascontiguousarray(
            np.triu(np.ones((128, 128), np.float32), 1)),
        "onesc": np.ones((128, 1), np.float32),
        "onesr": np.ones((1, 128), np.float32),
        "tokidG": np.ascontiguousarray(
            (np.arange(CC)[None, :] * 128 + np.arange(128)[:, None] + 1
             ).astype(np.float32)),
    }
    maps = []
    for c in range(8):
        b, h = c // 2, c % 2
        xb = x[b]
        own = xb[h * HALF:(h + 1) * HALF]
        m = dict(shared)
        m["x_fm"] = np.ascontiguousarray(xb.T)
        m["x_own_tm"] = np.ascontiguousarray(own)
        m["x_own_fm"] = np.ascontiguousarray(own.T)
        mc = np.zeros((128, CC), np.float32)
        mc[:, h * CCH:(h + 1) * CCH] = 1.0
        m["maskc"] = mc
        m["hseg"] = np.array([[h * CCH]], np.int32)
        m["selinit"] = np.concatenate([np.zeros((128, E), np.float32),
                                       np.ones((128, E), np.float32)], 1)
        m["hofft1"] = np.full((128, 1), h * HALF + 1, np.float32)
        m["hofft2"] = np.full((128, 1), HALF + h * HALF + 1, np.float32)
        maps.append(m)
    return maps


def kernel(**inputs):
    prog = build()
    maps = make_inmaps(inputs)
    res = run_bass_kernel_spmd(prog.nc, maps, core_ids=list(range(8)))
    out = np.empty((4, T, D), np.float32)
    for c in range(8):
        b, h = c // 2, c % 2
        out[b, h * HALF:(h + 1) * HALF] = res.results[c]["out"]
    return out


# version 05
# speedup vs baseline: 1.1406x; 1.0010x over previous
import contextlib
import numpy as np
import concourse.bass as bass
import concourse.mybir as mybir
from concourse.bass_utils import run_bass_kernel_spmd
from concourse.bass_types import DynSlice
from concourse.bass_core_rust import IndirectOffsetOnAxis

F32 = mybir.dt.float32
I32 = mybir.dt.int32
AL = mybir.AluOpType
AF = mybir.ActivationFunctionType
AX = mybir.AxisListType

T = 8192
D = 1024
E = 16
HALF = 4096
CC = 64
CCH = 32
CAP = 640
NSL = CAP // 128
WID = D + E + 1
SENT = 1 << 20
LN_EPS = 1e-5
WKV_EPS = 1e-8
NB = T // 512
NBO = HALF // 512
BW = 512
FEXP = 2048
FSH = 2048
FCM = 4096
TH_ITERS = 38
TH_PAD = 3e-4


class Prog:

    def __init__(self, debug_taps=()):
        self.nc = bass.Bass("TRN2", target_bir_lowering=False, debug=False,
                            detect_race_conditions=False)
        self.dram = {}
        self.taps = {}
        self.debug_taps = set(debug_taps)
        self.stack = contextlib.ExitStack()
        self.semv = {}

    def din(self, name, shape, dtype=F32):
        ap = self.nc.dram_tensor(name, shape, dtype, kind="ExternalInput").ap()
        self.dram[name] = ap
        return ap

    def dout(self, name, shape, dtype=F32):
        ap = self.nc.dram_tensor(name, shape, dtype, kind="ExternalOutput").ap()
        self.dram[name] = ap
        return ap

    def dscr(self, name, shape, dtype=F32):
        ap = self.nc.dram_tensor(name, shape, dtype, kind="Internal").ap()
        self.dram[name] = ap
        return ap

    def sb(self, name, shape, dtype=F32, stack=None):
        return (stack or self.stack).enter_context(
            self.nc.sbuf_tensor(name, shape, dtype))

    def ps(self, name, shape, dtype=F32, stack=None):
        return (stack or self.stack).enter_context(
            self.nc.psum_tensor(name, shape, dtype))

    def sem(self, name):
        h = self.stack.enter_context(self.nc.semaphore(name))
        self.semv[name] = 0
        return h

    def inc(self, name, instr, n=1):
        instr.then_inc(getattr(self, name), n)
        self.semv[name] += n
        return self.semv[name]

    def dinc(self, name, instr):
        return self.inc(name, instr, 16)

    def wait(self, eng, name, v=None):
        eng.wait_ge(getattr(self, name), self.semv[name] if v is None else v)

    def vf(self, instr):
        self.inc("s_ve", instr)
        self.nc.vector.wait_ge(self.s_ve, self.semv["s_ve"])

    def dma_open(self, eng):
        j = self._rr
        self._rr = (self._rr + 1) % 8
        if self._dtarget[j] > 0:
            eng.wait_ge(self.dsem[j], self._dtarget[j])
        return [j, None]

    def dma_put(self, tok, instr):
        j = tok[0]
        instr.then_inc(self.dsem[j], 16)
        self.semv[f"s_d{j}"] += 16

    def dma_close(self, tok):
        j = tok[0]
        tok[1] = self.semv[f"s_d{j}"]
        self._dtarget[j] = tok[1]
        return tok

    def dma_batch(self, eng, emits):
        tok = self.dma_open(eng)
        for fn in emits:
            self.dma_put(tok, fn())
        return self.dma_close(tok)

    def dma_wait(self, eng, tok):
        eng.wait_ge(self.dsem[tok[0]], tok[1])

    def dma_wait_all(self, eng):
        for j in range(8):
            if self._dtarget[j] > 0:
                eng.wait_ge(self.dsem[j], self._dtarget[j])

    def tap(self, name, src_tile, shape=None):
        if name not in self.debug_taps:
            return
        sc = self.nc.scalar
        self.wait(sc, "s_ve")
        self.wait(sc, "s_ac")
        self.wait(sc, "s_pe")
        shape = shape or list(src_tile.shape)
        ap = self.dout("tap_" + name, shape)
        self.taps[name] = ap
        tok = self.dma_open(sc)
        self.dma_put(tok, sc.dma_start(out=ap, in_=src_tile))
        self.dma_close(tok)
        nc = self.nc
        for eng in (nc.vector, nc.scalar, nc.tensor, nc.gpsimd, nc.sync):
            self.dma_wait(eng, tok)

    def barrier(self, skip_gp=False):
        nc = self.nc
        self.dma_wait_all(nc.sync)
        self.dma_wait_all(nc.scalar)
        engines = (nc.sync, nc.tensor, nc.vector, nc.scalar)
        if not skip_gp:
            self.wait(nc.gpsimd, "s_gp")
            engines = engines + (nc.gpsimd,)
        for eng in engines:
            self.inc("s_bar", eng.nop())
        tgt = self.semv["s_bar"]
        for eng in engines:
            eng.wait_ge(self.s_bar, tgt)


def build(debug_taps=()):
    p = Prog(debug_taps)
    nc = p.nc

    p.din("x_fm", [D, T])
    p.din("x_own_tm", [HALF, D])
    p.din("x_own_fm", [D, HALF])
    for nm in ("wr_t", "ww_t", "wk_t", "wv_t", "wo_t", "wrec_t"):
        p.din(nm, [D, D])
    p.din("wkey_t", [D, FCM])
    p.din("wval_t", [FCM, D])
    p.din("wsg_t", [D, FSH])
    p.din("wsu_t", [D, FSH])
    p.din("wsd_t", [FSH, D])
    p.din("wg_t", [E, D, FEXP])
    p.din("wu_t", [E, D, FEXP])
    p.din("wd_t", [E, FEXP, D])
    p.din("gate_t", [D, E])
    for nm in ("ln1g", "ln1b", "ln2g", "ln2b", "td"):
        p.din(nm, [128, 8])
    p.din("ident", [128, 128])
    p.din("tri", [128, 128])
    p.din("maskc", [128, CC])
    p.din("tokidG", [128, CC])
    p.din("selinit", [128, 2 * E])
    p.din("hofft1", [128, 1])
    p.din("hofft2", [128, 1])
    p.din("onesc", [128, 1])
    p.din("onesr", [1, 128])
    p.din("hseg", [1, 1], I32)

    p.dscr("e_sp", [8, 128, T])
    p.dscr("ev_sp", [8, 128, T])
    p.dscr("r_sp", [8, 128, T])
    p.dscr("wkv_sp", [8, 128, T])
    p.dscr("xn_sp", [8, 128, HALF])
    p.dscr("xn3_sp", [8, 128, HALF])
    p.dscr("y1_sp", [8, 128, HALF])
    p.dscr("sh_sp", [8, 128, HALF])
    p.dscr("vvh_sp", [8, 128, HALF])
    for e in range(E):
        p.dscr(f"xtok{e}", [CAP, WID])
    p.dscr("wid_sp", [128, CC * (E + 1)])
    p.dscr("idxe_sp", [128, CC * E], I32)
    p.dscr("eoT_sp", [E * NSL * 128, D])
    p.dscr("idT_sp", [E * NSL, 128], I32)
    p.dscr("ab0_sp", [2, T])
    p.dscr("macc", [HALF + 1, D])
    p.dout("out", [HALF, D])

    for s in ("s_gp", "s_pe", "s_ve", "s_ac", "s_bar"):
        setattr(p, s, p.sem(s))
    p.dsem = [p.sem(f"s_d{i}") for i in range(8)]
    p._rr = 0
    p._dtarget = [0] * 8

    with nc.Block() as _block:
        emit(p)

    p.stack.close()
    return p


def emit(p):
    nc = p.nc
    sy, te, ve, ac, gp = nc.sync, nc.tensor, nc.vector, nc.scalar, nc.gpsimd

    ident = p.sb("c_ident", [128, 128])
    onesc = p.sb("c_onesc", [128, 1])
    onesr = p.sb("c_onesr", [1, 128])
    lnp = p.sb("c_lnp", [128, 4 * 8])
    decay = p.sb("c_decay", [128, 8])
    hsegt = p.sb("c_hseg", [1, 1], I32)

    dram = p.dram

    tok0 = p.dma_open(sy)
    for nm, dst in (("ident", ident), ("onesc", onesc),
                    ("onesr", onesr), ("hseg", hsegt)):
        p.dma_put(tok0, sy.dma_start(out=dst[:], in_=dram[nm]))
    for i, nm in enumerate(("ln1g", "ln1b", "ln2g", "ln2b")):
        p.dma_put(tok0, sy.dma_start(out=lnp[:, i * 8:(i + 1) * 8], in_=dram[nm]))
    p.dma_put(tok0, sy.dma_start(out=decay[:], in_=dram["td"]))
    p.dma_close(tok0)
    p.dma_wait(ac, tok0)
    p.inc("s_ac", ac.activation(decay[:], decay[:], AF.Exp))

    p.dma_wait(sy, tok0)
    hv = sy.value_load(hsegt[0:1, 0:1])
    p.dma_wait(gp, tok0)
    hvg = gp.value_load(hsegt[0:1, 0:1])

    with contextlib.ExitStack() as sel_st:
        tri = p.sb("m_tri", [128, 128], stack=sel_st)
        gatet = p.sb("m_gate", [128, 8 * E], stack=sel_st)
        maskc = p.sb("m_maskc", [128, CC], stack=sel_st)
        tokidG = p.sb("m_tokidG", [128, CC], stack=sel_st)
        scores3 = p.sb("m_scores3", [128, E * CC], stack=sel_st)
        theta = p.sb("m_theta", [128, E], stack=sel_st)
        gsumP = p.sb("m_gsumP", [128, E], stack=sel_st)
        idxL = p.sb("m_idxL", [128, CC * E], I32, stack=sel_st)
        zeros = p.sb("m_zeros", [128, WID], stack=sel_st)

        tokc = p.dma_open(sy)
        for nm, dst in (("tri", tri), ("maskc", maskc), ("tokidG", tokidG)):
            p.dma_put(tokc, sy.dma_start(out=dst[:], in_=dram[nm]))
        p.dma_put(tokc, sy.dma_start(
            out=gatet[:].rearrange("p (k e) -> p k e", k=8),
            in_=dram["gate_t"].rearrange("(k p) e -> p k e", p=128)))
        p.dma_close(tokc)
        p.inc("s_ve", ve.memset(zeros[:], 0.0))

        p.wait(gp, "s_ve")
        mr = dram["macc"][0:HALF].rearrange("(n p) d -> n p d", p=128)
        for i in range(HALF // 128):
            p.dinc("s_gp", gp.dma_start(out=mr[i], in_=zeros[:, :D]))
        p.dinc("s_gp", gp.dma_start(out=dram["macc"][HALF:HALF + 1, :],
                                    in_=zeros[0:1, 0:D]))
        for e in range(E):
            xr = dram[f"xtok{e}"].rearrange("(n p) w -> n p w", p=128)
            for i in range(NSL):
                p.dinc("s_gp", gp.dma_start(out=xr[i], in_=zeros[:, :WID]))

        p.barrier(skip_gp=True)

        with contextlib.ExitStack() as st:
            ph1(p, st, ident=ident, onesc=onesc, onesr=onesr, gatet=gatet,
                scores3=scores3, gsumP=gsumP)
            p.barrier(skip_gp=True)

        with contextlib.ExitStack() as st:
            selection(p, st, ident=ident, tri=tri, onesc=onesc, onesr=onesr,
                      maskc=maskc, scores3=scores3, theta=theta, gsumP=gsumP,
                      tokidG=tokidG, idxL=idxL)
            p.barrier()

    with contextlib.ExitStack() as st:
        phMprep(p, st, hvg=hvg)
        ph2(p, st, onesr=onesr, onesc=onesc, lnp=lnp)
        p.barrier()

    with contextlib.ExitStack() as st:
        ph3(p, st, decay=decay)
        p.barrier()

    with contextlib.ExitStack() as st4a:
        wo = p.sb("p4_wo", [128, 8 * D], stack=st4a)
        wotok = p.dma_open(sy)
        p.dma_put(wotok, sy.dma_start(
            out=wo[:].rearrange("p (k e) -> p k e", k=8),
            in_=dram["wo_t"].rearrange("(k p) e -> p k e", p=128)))
        wo_ld = p.dma_close(wotok)
        with contextlib.ExitStack() as st:
            phM(p, st, ident=ident)
            p.barrier()

        eoFori(p, st4a)
        ph4a(p, st4a, onesr=onesr, onesc=onesc, lnp=lnp, hv=hv,
             wo=wo, w_ld=wo_ld)
        p.barrier()
    with contextlib.ExitStack() as st:
        ph4sh(p, st)
        p.barrier()
    with contextlib.ExitStack() as st:
        ph4b(p, st)
        p.barrier()
    with contextlib.ExitStack() as st:
        ph4c(p, st, ident=ident)
        p.barrier()


def ln_rows(p, stats_ps, a_slice, b_slice, scr):
    nc = p.nc
    ve, ac = nc.vector, nc.scalar
    W = a_slice.shape[1]
    ve.tensor_scalar_mul(a_slice, stats_ps[:, 0:W], 1.0 / D)
    ve.tensor_scalar_mul(b_slice, stats_ps[:, W:2 * W], 1.0 / D)
    ve.tensor_mul(scr, a_slice, a_slice)
    ve.tensor_sub(b_slice, b_slice, scr[:])
    p.inc("s_ve", ve.tensor_scalar_add(b_slice, b_slice, LN_EPS))
    p.wait(ac, "s_ve")
    p.inc("s_ac", ac.activation(b_slice, b_slice, AF.Sqrt))
    p.wait(ve, "s_ac")
    p.inc("s_ve", ve.reciprocal(b_slice, b_slice))
    p.wait(ve, "s_ve")
    ve.tensor_mul(scr, a_slice, b_slice)
    ve.tensor_copy(a_slice, b_slice)
    p.inc("s_ve", ve.tensor_scalar_mul(b_slice, scr, -1.0))


PH1_PARTS = {"blocks", "gsum", "staging", "transpose", "logits"}


def ph1(p, st, *, ident, onesc, onesr, gatet, scores3, gsumP):
    nc = p.nc
    sy, te, ve, ac = nc.sync, nc.tensor, nc.vector, nc.scalar
    dram = p.dram

    xb = [p.sb(f"p1_xb{i}", [128, 8 * BW], stack=st) for i in range(2)]
    xsq = p.sb("p1_xsq", [128, 8 * BW], stack=st)
    arow = p.sb("p1_arow", [1, T], stack=st)
    brow = p.sb("p1_brow", [1, T], stack=st)
    st18 = p.sb("p1_st18", [18, T], stack=st)
    srow = p.sb("p1_srow", [1, 2 * BW], stack=st)
    scr = p.sb("p1_scr", [1, BW], stack=st)
    a3 = p.sb("p1_a3", [128, CC], stack=st)
    b3 = p.sb("p1_b3", [128, CC], stack=st)
    gx3 = p.sb("p1_gx3", [128, E * CC], stack=st)
    grow = p.sb("p1_grow", [1, E], stack=st)
    l3 = p.sb("p1_l3", [128, E * CC], stack=st)
    t3 = p.sb("p1_t3", [128, E * CC], stack=st)
    sume = p.sb("p1_sume", [128, 8 * CC], stack=st)

    gxp = p.ps("p1_gxp", [16, BW], stack=st)
    stp = p.ps("p1_stp", [1, 2 * BW], stack=st)
    trp = [p.ps(f"p1_trp{i}", [128, 32], stack=st) for i in range(2)]
    bcp = p.ps("p1_bcp", [128, E], stack=st)
    gsp = p.ps("p1_gsp", [1, E], stack=st)

    xr = dram["x_fm"].rearrange("(k p) t -> k p t", p=128)

    ld_mark = []
    ac_mark = []
    pe_mark = []
    ve_mark = []

    for blk in range(NB if "blocks" in PH1_PARTS else 0):
        if blk >= 2:
            sy.wait_ge(p.s_pe, pe_mark[blk - 2])
        tok = p.dma_open(sy)
        for k in range(8):
            p.dma_put(tok, sy.dma_start(
                out=xb[blk % 2][:, k * BW:(k + 1) * BW],
                in_=xr[k, :, blk * BW:(blk + 1) * BW]))
        ld_mark.append(p.dma_close(tok))

        p.dma_wait(ac, ld_mark[blk])
        if blk >= 1:
            ac.wait_ge(p.s_pe, pe_mark[blk - 1])
        p.inc("s_ac", ac.activation(xsq[:], xb[blk % 2][:], AF.Square))
        ac_mark.append(p.semv["s_ac"])

        te.wait_ge(p.s_ac, ac_mark[blk])
        if blk >= 1:
            te.wait_ge(p.s_ve, ve_mark[blk - 1])
        for k in range(8):
            kw = slice(k * BW, (k + 1) * BW)
            te.matmul(gxp[:], gatet[:, k * E:(k + 1) * E], xb[blk % 2][:, kw],
                      start=(k == 0), stop=(k == 7))
            te.matmul(stp[:, 0:BW], onesc[:], xb[blk % 2][:, kw],
                      start=(k == 0), stop=(k == 7))
            i = te.matmul(stp[:, BW:2 * BW], onesc[:], xsq[:, kw],
                          start=(k == 0), stop=(k == 7))
        p.inc("s_pe", i)
        pe_mark.append(p.semv["s_pe"])

        ve.wait_ge(p.s_pe, pe_mark[blk])
        ve.tensor_copy(st18[0:16, blk * BW:(blk + 1) * BW], gxp[:])
        ve.tensor_copy(srow[:], stp[:])
        ln_rows(p, srow,
                arow[:, blk * BW:(blk + 1) * BW],
                brow[:, blk * BW:(blk + 1) * BW], scr[:])
        ve_mark.append(p.semv["s_ve"])

    p.tap("arow", arow[:])
    p.tap("brow", brow[:])

    if "gsum" not in PH1_PARTS:
        return
    p.dma_wait_all(te)
    for k in range(8):
        i = te.matmul(gsp[:], onesc[:], gatet[:, k * E:(k + 1) * E],
                      start=(k == 0), stop=(k == 7))
    p.inc("s_pe", i)
    p.wait(ve, "s_pe")
    p.inc("s_ve", ve.tensor_copy(grow[:], gsp[:]))
    p.wait(te, "s_ve")
    i = te.matmul(bcp[:], onesr[:], grow[:], start=True, stop=True)
    p.inc("s_pe", i)
    p.wait(ve, "s_pe")
    p.inc("s_ve", ve.tensor_copy(gsumP[:], bcp[:]))

    if "staging" not in PH1_PARTS:
        return
    p.wait(sy, "s_ve")
    stok = p.dma_open(sy)
    p.dma_put(stok, sy.dma_start(out=st18[16:17, :], in_=arow[:]))
    p.dma_put(stok, sy.dma_start(out=st18[17:18, :], in_=brow[:]))
    p.dma_put(stok, sy.dma_start(out=dram["ab0_sp"][0:1, :], in_=arow[:]))
    p.dma_put(stok, sy.dma_start(out=dram["ab0_sp"][1:2, :], in_=brow[:]))
    stg_ld = p.dma_close(stok)

    if "transpose" not in PH1_PARTS:
        return
    p.dma_wait(te, stg_ld)
    ve_evac_base = p.semv["s_ve"]
    for tb in range(CC):
        if tb >= 2:
            te.wait_ge(p.s_ve, ve_evac_base + tb - 1)
        i = te.transpose(trp[tb % 2][:, 0:18],
                         st18[:, tb * 128:(tb + 1) * 128], ident[0:18, 0:18])
        p.inc("s_pe", i)
        ve.wait_ge(p.s_pe, p.semv["s_pe"])
        gx3v = gx3[:].rearrange("p (e c) -> p e c", c=CC)
        ve.tensor_copy(gx3v[:, :, tb:tb + 1],
                       trp[tb % 2][:, 0:16].rearrange("p (e c) -> p e c", c=1))
        ve.tensor_copy(a3[:, tb:tb + 1], trp[tb % 2][:, 16:17])
        p.inc("s_ve", ve.tensor_copy(b3[:, tb:tb + 1], trp[tb % 2][:, 17:18]))

    if "logits" not in PH1_PARTS:
        return
    a3bc = a3[:].unsqueeze(1).broadcast_to([128, E, CC])
    b3bc = b3[:].unsqueeze(1).broadcast_to([128, E, CC])
    gsbc = gsumP[:].unsqueeze(2).broadcast_to([128, E, CC])
    l3v = l3[:].rearrange("p (e c) -> p e c", c=CC)
    t3v = t3[:].rearrange("p (e c) -> p e c", c=CC)
    p.wait(ve, "s_ve")
    ve.tensor_mul(l3v, gx3[:].rearrange("p (e c) -> p e c", c=CC), a3bc)
    ve.tensor_mul(t3v, b3bc, gsbc)
    p.inc("s_ve", ve.tensor_add(l3[:], l3[:], t3[:]))
    p.tap("logits3", l3[:])
    p.tap("gx3", gx3[:])
    p.tap("a3", a3[:])
    p.tap("b3", b3[:])
    p.tap("gsumP", gsumP[:])
    p.wait(ac, "s_ve")
    p.inc("s_ac", ac.activation(l3[:], l3[:], AF.Exp))
    p.wait(ve, "s_ac")
    l3e = l3[:].rearrange("p (e c) -> p e c", c=CC)
    s8 = sume[:].rearrange("p (e c) -> p e c", c=CC)
    ve.tensor_add(s8[:, 0:8, :], l3e[:, 0:8, :], l3e[:, 8:16, :])
    ve.tensor_add(s8[:, 0:4, :], s8[:, 0:4, :], s8[:, 4:8, :])
    ve.tensor_add(s8[:, 0:2, :], s8[:, 0:2, :], s8[:, 2:4, :])
    p.vf(ve.tensor_add(s8[:, 0:1, :], s8[:, 0:1, :], s8[:, 1:2, :]))
    p.inc("s_ve", ve.reciprocal(sume[:, 0:CC], sume[:, 0:CC]))
    p.wait(ve, "s_ve")
    p.tap("sumr", sume[:, 0:CC])
    rbc = sume[:, 0:CC].unsqueeze(1).broadcast_to([128, E, CC])
    p.inc("s_ve", ve.tensor_mul(
        scores3[:].rearrange("p (e c) -> p e c", c=CC), l3e, rbc))
    p.tap("scores3", scores3[:])


def selection(p, st, *, ident, tri, onesc, onesr, maskc, scores3, theta,
              gsumP, tokidG, idxL):
    dram = p.dram
    """Binary-search theta per expert; build w (e-minor) + scatter slots."""
    nc = p.nc
    sy, te, ve, ac = nc.sync, nc.tensor, nc.vector, nc.scalar

    lohit = p.sb("sl_lohi", [128, 2 * E], stack=st)
    lo = lohit[:, 0:E]
    hi = lohit[:, E:2 * E]
    mid = p.sb("sl_mid", [128, E], stack=st)
    ge = p.sb("sl_ge", [128, E * CC], stack=st)
    red = p.sb("sl_red", [128, E], stack=st)
    crow = p.sb("sl_crow", [1, E], stack=st)
    cntP = p.sb("sl_cntP", [128, E], stack=st)
    cntN = p.sb("sl_cntN", [128, E], stack=st)
    tmp = p.sb("sl_tmp", [128, E], stack=st)
    tmp2 = p.sb("sl_tmp2", [128, E], stack=st)
    w3 = p.sb("sl_w3", [128, E * CC], stack=st)
    cand = p.sb("sl_cand", [128, E * CC], stack=st)
    jj = p.sb("sl_jj", [128, E * CC], stack=st)
    rtot = p.sb("sl_rtot", [128, E], stack=st)
    offp = p.sb("sl_offp", [128, E], stack=st)
    idxf = p.sb("sl_idxf", [128, E * CC], stack=st)
    idx3 = p.sb("sl_idx3", [128, E * CC], I32, stack=st)
    wm2 = p.sb("sl_wm2", [128, CC * (E + 1)], stack=st)

    cp = p.ps("sl_cp", [1, E], stack=st)
    bp = p.ps("sl_bp", [128, E], stack=st)
    op = p.ps("sl_op", [128, E], stack=st)

    s3v = scores3[:].rearrange("p (e c) -> p e c", c=CC)
    gev = ge[:].rearrange("p (e c) -> p e c", c=CC)

    itok = p.dma_open(sy)
    p.dma_put(itok, sy.dma_start(out=lohit[:], in_=dram["selinit"]))
    p.dma_close(itok)
    p.dma_wait(ve, itok)

    for it in range(TH_ITERS):
        p.vf(ve.tensor_add(mid[:], lo, hi))
        p.vf(ve.tensor_scalar_mul(mid[:], mid[:], 0.5))
        midbc = mid[:].unsqueeze(2).broadcast_to([128, E, CC])
        ve.tensor_tensor(gev, s3v, midbc, AL.is_ge)
        p.inc("s_ve", ve.tensor_reduce(
            red[:].unsqueeze(2), gev, AX.X, AL.add))
        if it == 0:
            p.tap("red0", red[:])
            p.tap("mid0", mid[:])
        p.wait(te, "s_ve")
        i = te.matmul(cp[:], onesc[:], red[:], start=True, stop=True)
        p.inc("s_pe", i)
        p.wait(ve, "s_pe")
        p.inc("s_ve", ve.tensor_copy(crow[:], cp[:]))
        p.wait(te, "s_ve")
        i = te.matmul(bp[:], onesr[:], crow[:], start=True, stop=True)
        p.inc("s_pe", i)
        p.wait(ve, "s_pe")
        if it == 0 and "cnt0" in p.debug_taps:
            p.vf(ve.tensor_copy(tmp[:], bp[:]))
            p.tap("cnt0", tmp[:])
        p.vf(ve.tensor_single_scalar(cntP[:], bp[:], 1024.0, AL.is_ge))
        p.vf(ve.tensor_sub(tmp[:], mid[:], lo))
        p.vf(ve.tensor_mul(tmp[:], tmp[:], cntP[:]))
        p.vf(ve.tensor_sub(tmp2[:], mid[:], hi))
        p.vf(ve.tensor_add(lo, lo, tmp[:]))
        p.vf(ve.tensor_scalar(cntN[:], cntP[:], -1.0, 1.0, AL.mult, AL.add))
        p.vf(ve.tensor_mul(tmp2[:], tmp2[:], cntN[:]))
        p.vf(ve.tensor_add(hi, hi, tmp2[:]))

    p.vf(ve.tensor_copy(theta[:], lo))
    p.tap("theta", theta[:])

    thbc = theta[:].unsqueeze(2).broadcast_to([128, E, CC])
    w3v = w3[:].rearrange("p (e c) -> p e c", c=CC)
    ve.tensor_tensor(w3v, s3v, thbc, AL.is_ge)
    ve.tensor_mul(w3[:], w3[:], scores3[:])
    p.vf(ve.tensor_scalar_mul(tmp[:], theta[:], 1.0 - TH_PAD))
    tpbc = tmp[:].unsqueeze(2).broadcast_to([128, E, CC])
    cndv = cand[:].rearrange("p (e c) -> p e c", c=CC)
    ve.tensor_tensor(cndv, s3v, tpbc, AL.is_ge)
    mbc = maskc[:].unsqueeze(1).broadcast_to([128, E, CC])
    p.inc("s_ve", ve.tensor_mul(cndv, cndv, mbc))
    p.tap("cand", cand[:])

    onesbc = onesc[:].broadcast_to([128, CC])
    for e in range(E):
        i = ve.tensor_tensor_scan(jj[:, e * CC:(e + 1) * CC],
                                  onesbc, cand[:, e * CC:(e + 1) * CC],
                                  0.0, AL.mult, AL.add)
        if e == E - 1:
            p.vf(i)
    p.inc("s_ve", ve.tensor_copy(rtot[:], jj[:].rearrange(
        "p (e c) -> p e c", c=CC)[:, :, CC - 1:CC]))
    p.wait(te, "s_ve")
    i = te.matmul(op[:], tri[:], rtot[:], start=True, stop=True)
    p.inc("s_pe", i)
    p.wait(ve, "s_pe")
    p.vf(ve.tensor_copy(offp[:], op[:]))
    offbc = offp[:].unsqueeze(2).broadcast_to([128, E, CC])
    jv = jj[:].rearrange("p (e c) -> p e c", c=CC)
    ve.tensor_add(jv, jv, offbc)
    ve.tensor_single_scalar(ge[:], jj[:], float(CAP), AL.is_le)
    ve.tensor_mul(ge[:], ge[:], cand[:])
    ve.tensor_scalar_add(idxf[:], jj[:], float(-1 - SENT))
    ve.tensor_mul(idxf[:], idxf[:], ge[:])
    ve.tensor_scalar_add(idxf[:], idxf[:], float(SENT))
    p.inc("s_ve", ve.tensor_copy(idx3[:], idxf[:]))
    p.tap("jj", jj[:])

    widv = wm2[:].rearrange("p (c w) -> p c w", w=E + 1)
    w3p = w3[:].rearrange("p (e c) -> p c e", c=CC, e=E)
    ve.tensor_copy(widv[:, :, 0:E], w3p)
    ve.tensor_copy(widv[:, :, E:E + 1], tokidG[:].unsqueeze(2))
    idxev = idxL[:].rearrange("p (c e) -> p c e", e=E)
    idx3p = idx3[:].rearrange("p (e c) -> p c e", c=CC, e=E)
    p.inc("s_ve", ve.tensor_copy(idxev, idx3p))
    p.wait(sy, "s_ve")
    sptok = p.dma_open(sy)
    p.dma_put(sptok, sy.dma_start(out=dram["wid_sp"], in_=wm2[:, 0:CC * (E + 1)]))
    p.dma_put(sptok, sy.dma_start(out=dram["idxe_sp"], in_=idxL[:]))
    p.dma_close(sptok)
    p.tap("widsp", wm2[:, 0:CC * (E + 1)])
    p.tap("idxLf", idxf[:])


def phMprep(p, st, *, hvg):
    nc = p.nc
    gp = nc.gpsimd
    dram = p.dram

    wide = p.sb("mp_wide", [128, WID], stack=st)
    ixt = p.sb("mp_ixt", [128, E], I32, stack=st)

    xtm = dram["x_own_tm"]
    widr = dram["wid_sp"]
    idxr = dram["idxe_sp"]

    gbase = p.semv["s_gp"]
    PER = 3 * 16 + E * 16
    with gp.Fori(0, CCH) as i:
        gp.wait_ge(p.s_gp, i * PER + gbase)
        gp.dma_start(out=wide[:, 0:D],
                     in_=xtm[DynSlice(i * 128, 128), :]).then_inc(p.s_gp, 16)
        gp.dma_start(out=wide[:, D:WID],
                     in_=widr[:, DynSlice((hvg + i) * (E + 1), E + 1)]
                     ).then_inc(p.s_gp, 16)
        gp.dma_start(out=ixt[:],
                     in_=idxr[:, DynSlice((hvg + i) * E, E)]).then_inc(p.s_gp, 16)
        gp.wait_ge(p.s_gp, i * PER + gbase + 48)
        for e in range(E):
            gp.indirect_dma_start(
                dram[f"xtok{e}"],
                IndirectOffsetOnAxis(ixt[:, e:e + 1], 0),
                wide[:], None,
                bounds_check=CAP - 1, oob_is_err=False).then_inc(p.s_gp, 16)
    p.semv["s_gp"] += CCH * PER
    gp.wait_ge(p.s_gp, p.semv["s_gp"])


def ph2(p, st, *, onesr, onesc, lnp):
    nc = p.nc
    sy, te, ve, ac = nc.sync, nc.tensor, nc.vector, nc.scalar
    sc = nc.scalar
    dram = p.dram

    wts = {}
    wtok = p.dma_open(sy)
    for nm in ("wr_t", "ww_t", "wk_t", "wv_t"):
        wts[nm] = p.sb("p2_" + nm, [128, 8 * D], stack=st)
        p.dma_put(wtok, sy.dma_start(
            out=wts[nm][:].rearrange("p (k e) -> p k e", k=8),
            in_=dram[nm].rearrange("(k p) e -> p k e", p=128)))
    w_ld = p.dma_close(wtok)

    xb = [p.sb("p2_xb0", [128, 8 * BW], stack=st)] * 2
    xsq = p.sb("p2_xsq", [128, 4 * BW], stack=st)
    xn = p.sb("p2_xn", [128, 8 * BW], stack=st)
    xn2 = xb[0]
    ab = p.sb("p2_ab", [128, 2 * BW], stack=st)
    srow = p.sb("p2_srow", [1, 2 * BW], stack=st)
    a0r = p.sb("p2_a0r", [1, BW], stack=st)
    b0r = p.sb("p2_b0r", [1, BW], stack=st)
    a1r = p.sb("p2_a1r", [1, BW], stack=st)
    b1r = p.sb("p2_b1r", [1, BW], stack=st)
    scr = p.sb("p2_scr", [1, BW], stack=st)
    ew = [p.sb(f"p2_ew{i}", [128, BW], stack=st) for i in range(2)]
    eb = [p.sb(f"p2_eb{i}", [128, BW], stack=st) for i in range(2)]
    evb = [p.sb(f"p2_evb{i}", [128, BW], stack=st) for i in range(2)]
    rb = [p.sb(f"p2_rb{i}", [128, BW], stack=st) for i in range(2)]

    bc_ps = p.ps("p2_bc", [128, BW], stack=st)
    st_ps = p.ps("p2_st", [1, 2 * BW], stack=st)
    mm_ps = [p.ps(f"p2_mm{i}", [128, BW], stack=st) for i in range(4)]

    xr = dram["x_fm"].rearrange("(k p) t -> k p t", p=128)

    def bcast(row_slice, dst_slice):
        p.wait(te, "s_ve")
        i = te.matmul(bc_ps[:], onesr[:], row_slice, start=True, stop=True)
        p.inc("s_pe", i)
        p.wait(ve, "s_pe")
        p.inc("s_ve", ve.tensor_copy(dst_slice, bc_ps[:]))

    pe_mark = [0]
    for blk in range(NB):
        tw = slice(blk * BW, (blk + 1) * BW)
        if blk >= 1:
            sy.wait_ge(p.s_pe, pe_mark[-1])
            sy.wait_ge(p.s_ve, p.semv["s_ve"])
        if blk >= 1:
            p.dma_wait(ve, _SPILLS[(id(p), blk - 1, 7)])
            p.dma_wait(ac, _SPILLS[(id(p), blk - 1, 7)])
        tok = p.dma_open(sy)
        for k in range(8):
            p.dma_put(tok, sy.dma_start(
                out=xb[blk % 2][:, k * BW:(k + 1) * BW], in_=xr[k, :, tw]))
        p.dma_put(tok, sy.dma_start(out=a0r[:], in_=dram["ab0_sp"][0:1, tw]))
        p.dma_put(tok, sy.dma_start(out=b0r[:], in_=dram["ab0_sp"][1:2, tw]))
        ldm = p.dma_close(tok)

        p.dma_wait(te, ldm)
        if blk == 0:
            p.dma_wait(te, w_ld)
        bcast(a0r[:], ab[:, 0:BW])
        bcast(b0r[:], ab[:, BW:2 * BW])
        a0b = ab[:, 0:BW].unsqueeze(1).broadcast_to([128, 8, BW])
        b0b = ab[:, BW:2 * BW].unsqueeze(1).broadcast_to([128, 8, BW])
        xbv = xb[blk % 2][:].rearrange("p (k w) -> p k w", k=8)
        xnv = xn[:].rearrange("p (k w) -> p k w", k=8)
        p.dma_wait(ve, ldm)
        ve.tensor_mul(xnv, xbv, a0b)
        p.inc("s_ve", ve.tensor_add(xnv, xnv, b0b))

        p.wait(ac, "s_ve")
        p.inc("s_ac", ac.activation(xsq[:], xn[:, 0:4 * BW], AF.Square))
        p.wait(te, "s_ac")
        p.wait(te, "s_ve")
        i = None
        for k in range(8):
            if k == 4:
                p.inc("s_pe", i)
                ac.wait_ge(p.s_pe, p.semv["s_pe"])
                p.inc("s_ac", ac.activation(xsq[:], xn[:, 4 * BW:8 * BW],
                                            AF.Square))
                p.wait(te, "s_ac")
            kw = slice(k * BW, (k + 1) * BW)
            te.matmul(st_ps[:, 0:BW], onesc[:], xn[:, kw],
                      start=(k == 0), stop=(k == 7))
            i = te.matmul(st_ps[:, BW:2 * BW], onesc[:],
                          xsq[:, (k % 4) * BW:(k % 4 + 1) * BW],
                          start=(k == 0), stop=(k == 7))
        p.inc("s_pe", i)
        p.wait(ve, "s_pe")
        ve.tensor_copy(srow[:], st_ps[:])
        ln_rows(p, srow, a1r[:], b1r[:], scr[:])
        bcast(a1r[:], ab[:, 0:BW])
        bcast(b1r[:], ab[:, BW:2 * BW])
        a1b = ab[:, 0:BW].unsqueeze(1).broadcast_to([128, 8, BW])
        b1b = ab[:, BW:2 * BW].unsqueeze(1).broadcast_to([128, 8, BW])
        xn2v = xn2[:].rearrange("p (k w) -> p k w", k=8)
        ve.tensor_mul(xn2v, xnv, a1b)
        ve.tensor_add(xn2v, xn2v, b1b)
        for k in range(8):
            kw = slice(k * BW, (k + 1) * BW)
            ve.tensor_scalar(xn2[:, kw], xn2[:, kw],
                             lnp[:, 0 * 8 + k:0 * 8 + k + 1],
                             lnp[:, 1 * 8 + k:1 * 8 + k + 1], AL.mult, AL.add)
        p.inc("s_ve", ve.engine_nop())
        xn2_ve = p.semv["s_ve"]

        te.wait_ge(p.s_ve, xn2_ve)
        cons = [None] * 4
        for m in range(8):
            bi = m % 2
            pm = st_spill_mark(p, blk, m, 2)
            if pm is not None:
                p.dma_wait(ve, pm)
                p.dma_wait(ac, pm)
            for oi, wname in enumerate(("ww_t", "wk_t", "wv_t", "wr_t")):
                pb = mm_ps[oi]
                if cons[oi] is not None:
                    te.wait_ge(getattr(p, cons[oi][0]), cons[oi][1])
                for k in range(8):
                    i = te.matmul(pb[:],
                                  wts[wname][:, k * D + m * 128:k * D + m * 128 + 128],
                                  xn2[:, k * BW:(k + 1) * BW],
                                  start=(k == 0), stop=(k == 7))
                p.inc("s_pe", i)
                mm_pe = p.semv["s_pe"]
                if wname == "ww_t":
                    ac.wait_ge(p.s_pe, mm_pe)
                    p.inc("s_ac", ac.activation(ew[bi][:], pb[:], AF.Exp))
                    cons[0] = ("s_ac", p.semv["s_ac"])
                elif wname == "wk_t":
                    ve.wait_ge(p.s_pe, mm_pe)
                    ve.wait_ge(p.s_ac, p.semv["s_ac"])
                    p.inc("s_ve", ve.scalar_tensor_tensor(
                        eb[bi][:], pb[:], 1.0, ew[bi][:], AL.mult, AL.subtract))
                    cons[1] = ("s_ve", p.semv["s_ve"])
                    ac.wait_ge(p.s_ve, p.semv["s_ve"])
                    p.inc("s_ac", ac.activation(eb[bi][:], eb[bi][:], AF.Exp))
                elif wname == "wv_t":
                    ve.wait_ge(p.s_pe, mm_pe)
                    ve.wait_ge(p.s_ac, p.semv["s_ac"])
                    p.inc("s_ve", ve.tensor_mul(evb[bi][:], eb[bi][:], pb[:]))
                    cons[2] = ("s_ve", p.semv["s_ve"])
                else:
                    ac.wait_ge(p.s_pe, mm_pe)
                    p.inc("s_ac", ac.activation(rb[bi][:], pb[:], AF.Sigmoid))
                    cons[3] = ("s_ac", p.semv["s_ac"])
            sc.wait_ge(p.s_ve, p.semv["s_ve"])
            sc.wait_ge(p.s_ac, p.semv["s_ac"])
            stok = p.dma_open(sc)
            for nm, tile in (("e_sp", eb[bi]), ("ev_sp", evb[bi]),
                             ("r_sp", rb[bi])):
                p.dma_put(stok, sc.dma_start(out=dram[nm][m, :, tw],
                                             in_=tile[:]))
            _SPILLS[(id(p), blk, m)] = p.dma_close(stok)
        pe_mark.append(p.semv["s_pe"])


_SPILLS = {}


def st_spill_mark(p, blk, m, dist):
    mm = m - dist
    key = (id(p), blk, mm) if mm >= 0 else (id(p), blk - 1, 8 + mm)
    return _SPILLS.get(key)
def ph3(p, st, *, decay):
    nc = p.nc
    sy, ve, sc = nc.sync, nc.vector, nc.scalar
    dram = p.dram

    ebuf = p.sb("p3_e", [128, T], stack=st)
    evbuf = p.sb("p3_ev", [128, T], stack=st)
    num = p.sb("p3_num", [128, T], stack=st)
    den = p.sb("p3_den", [128, T], stack=st)

    ve_mark = []
    st_tok = []
    for k in range(8):
        if k >= 1:
            sy.wait_ge(p.s_ve, ve_mark[k - 1])
        tok = p.dma_open(sy)
        p.dma_put(tok, sy.dma_start(out=ebuf[:], in_=dram["e_sp"][k]))
        p.dma_put(tok, sy.dma_start(out=evbuf[:], in_=dram["ev_sp"][k]))
        p.dma_close(tok)
        p.dma_wait(ve, tok)
        if k >= 1:
            p.dma_wait(ve, st_tok[k - 1])
        dk = decay[:, k:k + 1].broadcast_to([128, T])
        ve.tensor_tensor_scan(num[:], dk, evbuf[:], 0.0, AL.mult, AL.add)
        ve.tensor_tensor_scan(den[:], dk, ebuf[:], 0.0, AL.mult, AL.add)
        ve.tensor_scalar_add(den[:], den[:], WKV_EPS)
        p.inc("s_ve", ve.reciprocal(den[:], den[:]))
        p.wait(ve, "s_ve")
        p.inc("s_ve", ve.tensor_mul(num[:], num[:], den[:]))
        ve_mark.append(p.semv["s_ve"])
        sc.wait_ge(p.s_ve, p.semv["s_ve"])
        stok = p.dma_open(sc)
        p.dma_put(stok, sc.dma_start(out=dram["wkv_sp"][k], in_=num[:]))
        st_tok.append(p.dma_close(stok))


def phM(p, st, *, ident):
    nc = p.nc
    sy, te, ve, ac, gp = nc.sync, nc.tensor, nc.vector, nc.scalar, nc.gpsimd
    sc = nc.scalar
    dram = p.dram

    wg = [p.sb(f"pm_wg{i}", [128, 8 * 512], stack=st) for i in range(2)]
    wu = [p.sb(f"pm_wu{i}", [128, 8 * 512], stack=st) for i in range(2)]
    wd = [p.sb(f"pm_wd{i}", [128, 4 * 1024], stack=st) for i in range(2)]
    xt = p.sb("pm_xt", [128, NSL * WID], stack=st)
    eosb = xt[:, 0:8 * CAP]
    xsq = p.sb("pm_xsq", [128, D], stack=st)
    stats = p.sb("pm_stats", [128, 4 * NSL], stack=st)
    xtT = p.sb("pm_xtT", [128, 8 * CAP], stack=st)
    gel = p.sb("pm_gel", [128, 320], stack=st)
    hsb = p.sb("pm_hsb", [128, 8 * 320], stack=st)
    wcol = p.sb("pm_wcol", [128, NSL], stack=st)
    eoT = [p.sb(f"pm_eoT{i}", [128, D], stack=st) for i in range(2)]
    hofftt = p.sb("pm_hofft", [128, 2], stack=st)
    idc = p.sb("pm_idc", [128, NSL], I32, stack=st)
    idcf = p.sb("pm_idcf", [128, NSL], stack=st)
    idcm = p.sb("pm_idcm", [128, NSL], stack=st)

    trp = [p.ps(f"pm_trp{i}", [128, 128], stack=st) for i in range(2)]
    gps = p.ps("pm_gps", [128, BW], stack=st)
    ups = p.ps("pm_ups", [128, BW], stack=st)
    eop = p.ps("pm_eop", [128, BW], stack=st)
    top = p.ps("pm_top", [128, 1024], stack=st)

    xtv = xt[:].rearrange("p (n w) -> p n w", n=NSL)
    eoTr = dram["eoT_sp"].rearrange("(u p) d -> u p d", p=128)
    st_eoT = []

    htok = p.dma_open(sy)
    p.dma_put(htok, sy.dma_start(out=hofftt[:, 0:1], in_=dram["hofft1"]))
    p.dma_put(htok, sy.dma_start(out=hofftt[:, 1:2], in_=dram["hofft2"]))
    p.dma_close(htok)

    wq_tok = {}
    buf_mark = [0, 0]

    def issue_wq(gidx):
        if gidx >= 4 * E:
            return
        eq, q = divmod(gidx, 4)
        b = gidx % 2
        if buf_mark[b] > 0:
            sy.wait_ge(p.s_pe, buf_mark[b])
        fq = slice(q * 512, (q + 1) * 512)
        tok = p.dma_open(sy)
        p.dma_put(tok, sy.dma_start(
            out=wg[b][:].rearrange("p (k f) -> p k f", k=8),
            in_=dram["wg_t"][eq].rearrange("(k p) f -> p k f", p=128)[:, :, fq]))
        p.dma_put(tok, sy.dma_start(
            out=wu[b][:].rearrange("p (k f) -> p k f", k=8),
            in_=dram["wu_t"][eq].rearrange("(k p) f -> p k f", p=128)[:, :, fq]))
        p.dma_put(tok, sy.dma_start(
            out=wd[b][:].rearrange("p (k d) -> p k d", k=4),
            in_=dram["wd_t"][eq].rearrange(
                "(k p) d -> p k d", p=128)[:, q * 4:(q + 1) * 4]))
        wq_tok[gidx] = p.dma_close(tok)

    issue_wq(0)
    issue_wq(1)

    for e in range(E):
        if e > 0:
            sy.wait_ge(p.s_pe, p.semv["s_pe"])
            sy.wait_ge(p.s_ve, p.semv["s_ve"])
            for t_ in st_eoT[-2:]:
                p.dma_wait(sy, t_)
        xtok_t = p.dma_open(sy)
        p.dma_put(xtok_t, sy.dma_start(
            out=xtv, in_=dram[f"xtok{e}"].rearrange("(n p) w -> p n w", p=128)))
        p.dma_close(xtok_t)
        if e == 0:
            p.dma_wait(ve, htok)

        p.dma_wait(ve, xtok_t)
        p.dma_wait(ac, xtok_t)
        for sl in range(NSL):
            xs = xt[:, sl * WID:sl * WID + D]
            if sl > 0:
                ac.wait_ge(p.s_ve, p.semv["s_ve"])
            p.inc("s_ac", ac.activation(xsq[:], xs, AF.Square))
            ve.tensor_reduce(stats[:, sl:sl + 1], xs.unsqueeze(1), AX.X, AL.add)
            ve.wait_ge(p.s_ac, p.semv["s_ac"])
            p.inc("s_ve", ve.tensor_reduce(
                stats[:, NSL + sl:NSL + sl + 1], xsq[:].unsqueeze(1),
                AX.X, AL.add))
        p.wait(ve, "s_ve")
        ve.tensor_scalar_mul(stats[:, 2 * NSL:3 * NSL], stats[:, 0:NSL], 1.0 / D)
        p.vf(ve.tensor_scalar_mul(stats[:, 3 * NSL:4 * NSL],
                                  stats[:, NSL:2 * NSL], 1.0 / D))
        p.vf(ve.tensor_mul(stats[:, 0:NSL], stats[:, 2 * NSL:3 * NSL],
                           stats[:, 2 * NSL:3 * NSL]))
        p.vf(ve.tensor_sub(stats[:, 3 * NSL:4 * NSL], stats[:, 3 * NSL:4 * NSL],
                           stats[:, 0:NSL]))
        p.inc("s_ve", ve.tensor_scalar_add(stats[:, 3 * NSL:4 * NSL],
                                           stats[:, 3 * NSL:4 * NSL], LN_EPS))
        p.wait(ac, "s_ve")
        ac.activation(stats[:, 3 * NSL:4 * NSL],
                      stats[:, 3 * NSL:4 * NSL], AF.Sqrt)
        p.inc("s_ac", ac.activation(xsq[:, 0:512], xsq[:, 0:512], AF.Square))
        p.wait(ve, "s_ac")
        p.inc("s_ve", ve.reciprocal(stats[:, 3 * NSL:4 * NSL],
                                    stats[:, 3 * NSL:4 * NSL]))
        p.wait(ve, "s_ve")
        p.vf(ve.tensor_mul(stats[:, 2 * NSL:3 * NSL], stats[:, 2 * NSL:3 * NSL],
                           stats[:, 3 * NSL:4 * NSL]))
        p.vf(ve.tensor_scalar_mul(stats[:, 2 * NSL:3 * NSL],
                                  stats[:, 2 * NSL:3 * NSL], -1.0))
        for sl in range(NSL):
            ve.tensor_scalar(xt[:, sl * WID:sl * WID + D],
                             xt[:, sl * WID:sl * WID + D],
                             stats[:, 3 * NSL + sl:3 * NSL + sl + 1],
                             stats[:, 2 * NSL + sl:2 * NSL + sl + 1],
                             AL.mult, AL.add)
        ve.tensor_copy(wcol[:].unsqueeze(2), xtv[:, :, D + e:D + e + 1])
        p.vf(ve.tensor_copy(idcf[:].unsqueeze(2), xtv[:, :, WID - 1:WID]))
        p.vf(ve.tensor_scalar(idcf[:], idcf[:], hofftt[:, 0:1], None,
                              AL.subtract))
        p.vf(ve.tensor_single_scalar(idcm[:], idcf[:], 0.0, AL.is_lt))
        p.vf(ve.scalar_tensor_tensor(idcf[:], idcm[:], hofftt[:, 1:2], idcf[:],
                                     AL.mult, AL.add))
        p.inc("s_ve", ve.tensor_copy(idc[:], idcf[:]))
        rn_ve = p.semv["s_ve"]

        sc.wait_ge(p.s_ve, rn_ve)
        idtok = p.dma_open(sc)
        p.dma_put(idtok, sc.dma_start(
            out=dram["idT_sp"][e * NSL:(e + 1) * NSL, :].rearrange("s p -> p s"),
            in_=idc[:], allow_slow_non_contiguous=True))
        p.dma_close(idtok)

        te.wait_ge(p.s_ve, rn_ve)
        nt = 0
        tr_ve = p.semv["s_ve"]
        for dc in range(8):
            for sl in range(NSL):
                if nt >= 2:
                    te.wait_ge(p.s_ve, tr_ve + nt - 1)
                i = te.transpose(
                    trp[nt % 2][:],
                    xt[:, sl * WID + dc * 128:sl * WID + (dc + 1) * 128],
                    ident[:])
                p.inc("s_pe", i)
                ve.wait_ge(p.s_pe, p.semv["s_pe"])
                p.inc("s_ve", ve.tensor_copy(
                    xtT[:, dc * CAP + sl * 128:dc * CAP + (sl + 1) * 128],
                    trp[nt % 2][:]))
                nt += 1

        for q in range(4):
            gidx = e * 4 + q
            b = gidx % 2
            p.dma_wait(te, wq_tok[gidx])
            for nh in range(2):
                for fcl in range(4):
                    te.wait_ge(p.s_ve, p.semv["s_ve"])
                    for k in range(8):
                        te.matmul(gps[:, 0:320],
                                  wg[b][:, k * 512 + fcl * 128:k * 512 + (fcl + 1) * 128],
                                  xtT[:, k * CAP + nh * 320:k * CAP + nh * 320 + 320],
                                  start=(k == 0), stop=(k == 7))
                    for k in range(8):
                        i = te.matmul(ups[:, 0:320],
                                      wu[b][:, k * 512 + fcl * 128:k * 512 + (fcl + 1) * 128],
                                      xtT[:, k * CAP + nh * 320:k * CAP + nh * 320 + 320],
                                      start=(k == 0), stop=(k == 7))
                    p.inc("s_pe", i)
                    ac.wait_ge(p.s_pe, p.semv["s_pe"])
                    p.inc("s_ac", ac.activation(gel[:, 0:320], gps[:, 0:320],
                                                AF.Gelu))
                    ve.wait_ge(p.s_ac, p.semv["s_ac"])
                    p.inc("s_ve", ve.tensor_mul(
                        hsb[:, fcl * 320:fcl * 320 + 320],
                        gel[:, 0:320], ups[:, 0:320]))
                for m in range(8):
                    te.wait_ge(p.s_ve, p.semv["s_ve"])
                    for fcl in range(4):
                        i = te.matmul(eop[:, 0:320],
                                      wd[b][:, fcl * 1024 + m * 128:fcl * 1024 + (m + 1) * 128],
                                      hsb[:, fcl * 320:fcl * 320 + 320],
                                      start=(fcl == 0), stop=(fcl == 3))
                    p.inc("s_pe", i)
                    ve.wait_ge(p.s_pe, p.semv["s_pe"])
                    dst = eosb[:, m * CAP + nh * 320:m * CAP + nh * 320 + 320]
                    if q == 0:
                        p.inc("s_ve", ve.tensor_copy(dst, eop[:, 0:320]))
                    else:
                        p.inc("s_ve", ve.scalar_tensor_tensor(
                            dst, eop[:, 0:320], 1.0, dst, AL.mult, AL.add))
            buf_mark[b] = p.semv["s_pe"]
            issue_wq(gidx + 2)

        for sl in range(NSL):
            te.wait_ge(p.s_ve, p.semv["s_ve"])
            for dc in range(8):
                i = te.transpose(top[:, dc * 128:(dc + 1) * 128],
                                 eosb[:, dc * CAP + sl * 128:dc * CAP + (sl + 1) * 128],
                                 ident[:])
            p.inc("s_pe", i)
            ve.wait_ge(p.s_pe, p.semv["s_pe"])
            if len(st_eoT) >= 2:
                p.dma_wait(ve, st_eoT[-2])
            p.inc("s_ve", ve.tensor_scalar(
                eoT[sl % 2][:], top[:], wcol[:, sl:sl + 1], None, AL.mult))
            sc.wait_ge(p.s_ve, p.semv["s_ve"])
            etok = p.dma_open(sc)
            p.dma_put(etok, sc.dma_start(out=eoTr[e * NSL + sl],
                                         in_=eoT[sl % 2][:]))
            st_eoT.append(p.dma_close(etok))


def eoFori(p, st):
    nc = p.nc
    gp = nc.gpsimd
    dram = p.dram
    eob = p.sb("eof_eob", [128, D], stack=st)
    ixt = p.sb("eof_ixt", [128, 1], I32, stack=st)
    eoTr = dram["eoT_sp"]
    idTr = dram["idT_sp"].rearrange("u p -> p u")
    gbase = p.semv["s_gp"]
    PER = 3 * 16
    NU = E * NSL
    with gp.Fori(0, NU) as u:
        gp.wait_ge(p.s_gp, u * PER + gbase)
        gp.dma_start(out=eob[:],
                     in_=eoTr[DynSlice(u * 128, 128), :]).then_inc(p.s_gp, 16)
        gp.dma_start(out=ixt[:], in_=idTr[:, DynSlice(u, 1)],
                     allow_slow_non_contiguous=True).then_inc(p.s_gp, 16)
        gp.wait_ge(p.s_gp, u * PER + gbase + 32)
        gp.indirect_dma_start(
            dram["macc"], IndirectOffsetOnAxis(ixt[:], 0), eob[:], None,
            bounds_check=HALF, oob_is_err=False,
            compute_op=AL.add).then_inc(p.s_gp, 16)
    p.semv["s_gp"] += NU * PER
    gp.wait_ge(p.s_gp, p.semv["s_gp"])


def ph4a(p, st, *, onesr, onesc, lnp, hv, wo, w_ld):
    hoff_tok = hv * 128
    """xn (from ab0_sp), rw=r*wkv, tm=Wo@rw, y1, LN2->xn3; spills.
    wo/w_ld: prefetched during phM."""
    nc = p.nc
    sy, te, ve, ac, sc = nc.sync, nc.tensor, nc.vector, nc.scalar, nc.scalar
    dram = p.dram

    xb = p.sb("p4_xb", [128, 8 * BW], stack=st)
    xsq = p.sb("p4_xsq", [128, 8 * BW], stack=st)
    xn = p.sb("p4_xn", [128, 8 * BW], stack=st)
    rbuf = p.sb("p4_rbuf", [128, 8 * BW], stack=st)
    wkvb = p.sb("p4_wkvb", [128, 8 * BW], stack=st)
    y1 = p.sb("p4_y1", [128, 8 * BW], stack=st)
    xn3 = p.sb("p4_xn3", [128, 8 * BW], stack=st)
    ab = p.sb("p4_ab", [128, 4 * BW], stack=st)
    srow = p.sb("p4_srow", [1, 2 * BW], stack=st)
    a0r = p.sb("p4_a0r", [1, BW], stack=st)
    b0r = p.sb("p4_b0r", [1, BW], stack=st)
    a1r = p.sb("p4_a1r", [1, BW], stack=st)
    b1r = p.sb("p4_b1r", [1, BW], stack=st)
    scr = p.sb("p4_scr", [1, BW], stack=st)

    bc_ps = p.ps("p4_bc", [128, BW], stack=st)
    st_ps = p.ps("p4_st", [1, 2 * BW], stack=st)
    mm_ps = [p.ps(f"p4_mm{i}", [128, BW], stack=st) for i in range(2)]

    xr = dram["x_own_fm"].rearrange("(k p) t -> k p t", p=128)

    def bcast(row_slice, dst_slice):
        p.wait(te, "s_ve")
        i = te.matmul(bc_ps[:], onesr[:], row_slice, start=True, stop=True)
        p.inc("s_pe", i)
        p.wait(ve, "s_pe")
        p.inc("s_ve", ve.tensor_copy(dst_slice, bc_ps[:]))

    def ln_full(src_tile, dst_tile, g_off, b_off):
        p.wait(ac, "s_ve")
        p.inc("s_ac", ac.activation(xsq[:], src_tile[:], AF.Square))
        p.wait(te, "s_ac")
        for k in range(8):
            kw = slice(k * BW, (k + 1) * BW)
            te.matmul(st_ps[:, 0:BW], onesc[:], src_tile[:, kw],
                      start=(k == 0), stop=(k == 7))
            i = te.matmul(st_ps[:, BW:2 * BW], onesc[:], xsq[:, kw],
                          start=(k == 0), stop=(k == 7))
        p.inc("s_pe", i)
        p.wait(ve, "s_pe")
        ve.tensor_copy(srow[:], st_ps[:])
        ln_rows(p, srow, a1r[:], b1r[:], scr[:])
        bcast(a1r[:], ab[:, 2 * BW:3 * BW])
        bcast(b1r[:], ab[:, 3 * BW:4 * BW])
        a1b = ab[:, 2 * BW:3 * BW].unsqueeze(1).broadcast_to([128, 8, BW])
        b1b = ab[:, 3 * BW:4 * BW].unsqueeze(1).broadcast_to([128, 8, BW])
        sv = src_tile[:].rearrange("p (k w) -> p k w", k=8)
        dv = dst_tile[:].rearrange("p (k w) -> p k w", k=8)
        ve.tensor_mul(dv, sv, a1b)
        ve.tensor_add(dv, dv, b1b)
        for k in range(8):
            kw = slice(k * BW, (k + 1) * BW)
            ve.tensor_scalar(dst_tile[:, kw], dst_tile[:, kw],
                             lnp[:, g_off + k:g_off + k + 1],
                             lnp[:, b_off + k:b_off + k + 1], AL.mult, AL.add)
        p.inc("s_ve", ve.engine_nop())

    st_mark = []
    for blk in range(NBO):
        tw = slice(blk * BW, (blk + 1) * BW)
        if blk >= 1:
            p.dma_wait(sy, st_mark[blk - 1])
            sy.wait_ge(p.s_ve, p.semv["s_ve"])
        tok = p.dma_open(sy)
        for k in range(8):
            p.dma_put(tok, sy.dma_start(out=xb[:, k * BW:(k + 1) * BW],
                                        in_=xr[k, :, tw]))
        p.dma_put(tok, sy.dma_start(
            out=rbuf[:].rearrange("p (k w) -> p k w", k=8),
            in_=dram["r_sp"][:, :, DynSlice(hoff_tok + blk * BW, BW)].rearrange(
                "k p w -> p k w")))
        p.dma_put(tok, sy.dma_start(
            out=wkvb[:].rearrange("p (k w) -> p k w", k=8),
            in_=dram["wkv_sp"][:, :, DynSlice(hoff_tok + blk * BW, BW)].rearrange(
                "k p w -> p k w")))
        p.dma_put(tok, sy.dma_start(
            out=a0r[:], in_=dram["ab0_sp"][0:1, DynSlice(hoff_tok + blk * BW, BW)]))
        p.dma_put(tok, sy.dma_start(
            out=b0r[:], in_=dram["ab0_sp"][1:2, DynSlice(hoff_tok + blk * BW, BW)]))
        ldm = p.dma_close(tok)
        if blk == 0:
            p.dma_wait(te, w_ld)

        p.dma_wait(ve, ldm)
        p.dma_wait(te, ldm)
        bcast(a0r[:], ab[:, 0:BW])
        bcast(b0r[:], ab[:, BW:2 * BW])
        a0b = ab[:, 0:BW].unsqueeze(1).broadcast_to([128, 8, BW])
        b0b = ab[:, BW:2 * BW].unsqueeze(1).broadcast_to([128, 8, BW])
        xbv = xb[:].rearrange("p (k w) -> p k w", k=8)
        xnv = xn[:].rearrange("p (k w) -> p k w", k=8)
        ve.tensor_mul(xnv, xbv, a0b)
        ve.tensor_add(xnv, xnv, b0b)
        p.inc("s_ve", ve.tensor_mul(rbuf[:], rbuf[:], wkvb[:]))
        rw_ve = p.semv["s_ve"]

        te.wait_ge(p.s_ve, rw_ve)
        for m in range(8):
            mw = slice(m * BW, (m + 1) * BW)
            pb = mm_ps[m % 2]
            if m >= 2:
                te.wait_ge(p.s_ve, p.semv["s_ve"])
            for k in range(8):
                i = te.matmul(pb[:],
                              wo[:, k * D + m * 128:k * D + m * 128 + 128],
                              rbuf[:, k * BW:(k + 1) * BW],
                              start=(k == 0), stop=(k == 7))
            p.inc("s_pe", i)
            ve.wait_ge(p.s_pe, p.semv["s_pe"])
            p.inc("s_ve", ve.scalar_tensor_tensor(
                y1[:, mw], pb[:], 1.0, xn[:, mw], AL.mult, AL.add))

        ln_full(y1, xn3, 16, 24)

        sc.wait_ge(p.s_ve, p.semv["s_ve"])
        stok = p.dma_open(sc)
        for nm, tile in (("xn_sp", xn), ("y1_sp", y1), ("xn3_sp", xn3)):
            p.dma_put(stok, sc.dma_start(
                out=dram[nm][:, :, tw].rearrange("k p w -> p k w"),
                in_=tile[:].rearrange("p (k w) -> p k w", k=8)))
        st_mark.append(p.dma_close(stok))


def ph4sh(p, st):
    nc = p.nc
    sy, te, ve, ac, sc = nc.sync, nc.tensor, nc.vector, nc.scalar, nc.scalar
    dram = p.dram

    wsg = [p.sb(f"psh_wsg{i}", [128, 8 * 512], stack=st) for i in range(2)]
    wsu = [p.sb(f"psh_wsu{i}", [128, 8 * 512], stack=st) for i in range(2)]
    wsd = [p.sb(f"psh_wsd{i}", [128, 4 * 1024], stack=st) for i in range(2)]
    xnb = [p.sb(f"psh_xn{i}", [128, 8 * BW], stack=st) for i in range(2)]
    gel = p.sb("psh_gel", [128, BW], stack=st)
    hs = p.sb("psh_hs", [128, 4 * BW], stack=st)
    sh = [p.sb(f"psh_sh{i}", [128, 8 * BW], stack=st) for i in range(2)]

    gsp = p.ps("psh_gsp", [128, BW], stack=st)
    usp = p.ps("psh_usp", [128, BW], stack=st)
    dsp = p.ps("psh_dsp", [128, BW], stack=st)

    wq_tok = {}
    buf_mark = [0, 0]

    def issue_wq(q):
        if q >= 4:
            return
        b = q % 2
        if buf_mark[b] > 0:
            sy.wait_ge(p.s_pe, buf_mark[b])
        fq = slice(q * 512, (q + 1) * 512)
        tok = p.dma_open(sy)
        p.dma_put(tok, sy.dma_start(
            out=wsg[b][:].rearrange("p (k f) -> p k f", k=8),
            in_=dram["wsg_t"].rearrange("(k p) f -> p k f", p=128)[:, :, fq]))
        p.dma_put(tok, sy.dma_start(
            out=wsu[b][:].rearrange("p (k f) -> p k f", k=8),
            in_=dram["wsu_t"].rearrange("(k p) f -> p k f", p=128)[:, :, fq]))
        p.dma_put(tok, sy.dma_start(
            out=wsd[b][:].rearrange("p (k e) -> p k e", k=4),
            in_=dram["wsd_t"].rearrange(
                "(k p) e -> p k e", p=128)[:, q * 4:(q + 1) * 4]))
        wq_tok[q] = p.dma_close(tok)

    issue_wq(0)
    issue_wq(1)

    prev_pe = prev_st = None
    for q in range(4):
        b = q % 2
        st_mark = {}
        pe_mark = {}
        ldms = {}

        def issue_load(blk2, q=q, prev_pe=prev_pe, prev_st=prev_st):
            if blk2 >= NBO:
                return
            bi = blk2 % 2
            tw2 = slice(blk2 * BW, (blk2 + 1) * BW)
            if blk2 >= 2:
                sy.wait_ge(p.s_pe, pe_mark[blk2 - 2])
                p.dma_wait(sy, st_mark[blk2 - 2])
            elif prev_pe is not None:
                sy.wait_ge(p.s_pe, prev_pe[NBO - 2 + blk2])
                p.dma_wait(sy, prev_st[NBO - 2 + blk2])
            tok = p.dma_open(sy)
            for k in range(8):
                p.dma_put(tok, sy.dma_start(
                    out=xnb[bi][:, k * BW:(k + 1) * BW],
                    in_=dram["xn_sp"][k, :, tw2]))
            if q >= 1:
                p.dma_put(tok, sy.dma_start(
                    out=sh[bi][:].rearrange("p (k w) -> p k w", k=8),
                    in_=dram["sh_sp"][:, :, tw2].rearrange("k p w -> p k w")))
            ldms[blk2] = p.dma_close(tok)

        issue_load(0)
        issue_load(1)
        for blk in range(NBO):
            bi = blk % 2
            tw = slice(blk * BW, (blk + 1) * BW)
            p.dma_wait(te, ldms[blk])
            p.dma_wait(te, wq_tok[q])
            for fc in range(4):
                te.wait_ge(p.s_ve, p.semv["s_ve"])
                for k in range(8):
                    te.matmul(gsp[:],
                              wsg[b][:, k * 512 + fc * 128:k * 512 + (fc + 1) * 128],
                              xnb[bi][:, k * BW:(k + 1) * BW],
                              start=(k == 0), stop=(k == 7))
                for k in range(8):
                    i = te.matmul(usp[:],
                                  wsu[b][:, k * 512 + fc * 128:k * 512 + (fc + 1) * 128],
                                  xnb[bi][:, k * BW:(k + 1) * BW],
                                  start=(k == 0), stop=(k == 7))
                p.inc("s_pe", i)
                ac.wait_ge(p.s_pe, p.semv["s_pe"])
                p.inc("s_ac", ac.activation(gel[:], gsp[:], AF.Gelu))
                ve.wait_ge(p.s_ac, p.semv["s_ac"])
                p.inc("s_ve", ve.tensor_mul(hs[:, fc * BW:(fc + 1) * BW],
                                            gel[:], usp[:]))
            for m in range(8):
                te.wait_ge(p.s_ve, p.semv["s_ve"])
                for fc in range(4):
                    i = te.matmul(dsp[:],
                                  wsd[b][:, fc * D + m * 128:fc * D + (m + 1) * 128],
                                  hs[:, fc * BW:(fc + 1) * BW],
                                  start=(fc == 0), stop=(fc == 3))
                p.inc("s_pe", i)
                ve.wait_ge(p.s_pe, p.semv["s_pe"])
                if m == 0:
                    if q >= 1:
                        p.dma_wait(ve, ldms[blk])
                    elif blk >= 2:
                        p.dma_wait(ve, st_mark[blk - 2])
                mw = slice(m * BW, (m + 1) * BW)
                if q == 0:
                    p.inc("s_ve", ve.tensor_copy(sh[bi][:, mw], dsp[:]))
                else:
                    p.inc("s_ve", ve.scalar_tensor_tensor(
                        sh[bi][:, mw], dsp[:], 1.0, sh[bi][:, mw],
                        AL.mult, AL.add))
            pe_mark[blk] = p.semv["s_pe"]
            sc.wait_ge(p.s_ve, p.semv["s_ve"])
            stok = p.dma_open(sc)
            p.dma_put(stok, sc.dma_start(
                out=dram["sh_sp"][:, :, tw].rearrange("k p w -> p k w"),
                in_=sh[bi][:].rearrange("p (k w) -> p k w", k=8)))
            st_mark[blk] = p.dma_close(stok)
            issue_load(blk + 2)
        prev_pe, prev_st = pe_mark, st_mark
        buf_mark[b] = p.semv["s_pe"]
        issue_wq(q + 2)


def ph4b(p, st):
    nc = p.nc
    sy, te, ve, ac, sc = nc.sync, nc.tensor, nc.vector, nc.scalar, nc.scalar
    dram = p.dram

    wkey = [p.sb(f"pb_wkey{i}", [128, 8 * 1024], stack=st) for i in range(2)]
    wval = [p.sb(f"pb_wval{i}", [128, 8 * 1024], stack=st) for i in range(2)]
    xnb = [p.sb(f"pb_xn{i}", [128, 8 * BW], stack=st) for i in range(2)]
    relu = p.sb("pb_relu", [128, BW], stack=st)
    kk = p.sb("pb_kk", [128, 8 * BW], stack=st)
    vv = p.sb("pb_vv", [128, 8 * BW], stack=st)

    kkp = p.ps("pb_kkp", [128, BW], stack=st)
    vvp = p.ps("pb_vvp", [128, BW], stack=st)

    wq_tok = {}
    buf_mark = [0, 0]

    def issue_wq(q):
        if q >= 4:
            return
        b = q % 2
        if buf_mark[b] > 0:
            sy.wait_ge(p.s_pe, buf_mark[b])
        tok = p.dma_open(sy)
        p.dma_put(tok, sy.dma_start(
            out=wkey[b][:].rearrange("p (k f) -> p k f", k=8),
            in_=dram["wkey_t"].rearrange("(k p) f -> p k f", p=128)[
                :, :, q * 1024:(q + 1) * 1024]))
        p.dma_put(tok, sy.dma_start(
            out=wval[b][:].rearrange("p (k e) -> p k e", k=8),
            in_=dram["wval_t"].rearrange("(k p) e -> p k e", p=128)[
                :, q * 8:(q + 1) * 8]))
        wq_tok[q] = p.dma_close(tok)

    issue_wq(0)
    issue_wq(1)

    prev_pe = prev_st = None
    for q in range(4):
        b = q % 2
        st_mark = {}
        pe_mark = {}
        ldms = {}
        vvtok = {}

        def issue_load(blk2, q=q, prev_pe=prev_pe, prev_st=prev_st):
            if blk2 >= NBO:
                return
            bi = blk2 % 2
            tw2 = slice(blk2 * BW, (blk2 + 1) * BW)
            if blk2 >= 2:
                sy.wait_ge(p.s_pe, pe_mark[blk2 - 2])
            elif prev_pe is not None:
                sy.wait_ge(p.s_pe, prev_pe[NBO - 2 + blk2])
            tok = p.dma_open(sy)
            for k in range(8):
                p.dma_put(tok, sy.dma_start(
                    out=xnb[bi][:, k * BW:(k + 1) * BW],
                    in_=dram["xn3_sp"][k, :, tw2]))
            ldms[blk2] = p.dma_close(tok)

        def issue_vv(blk2, q=q, prev_st=prev_st):
            if blk2 >= NBO or q == 0:
                return
            tw2 = slice(blk2 * BW, (blk2 + 1) * BW)
            if blk2 >= 1:
                p.dma_wait(sy, st_mark[blk2 - 1])
            elif prev_st is not None:
                p.dma_wait(sy, prev_st[NBO - 1])
            tok = p.dma_open(sy)
            p.dma_put(tok, sy.dma_start(
                out=vv[:].rearrange("p (k w) -> p k w", k=8),
                in_=dram["vvh_sp"][:, :, tw2].rearrange("k p w -> p k w")))
            vvtok[blk2] = p.dma_close(tok)

        issue_load(0)
        issue_load(1)
        issue_vv(0)
        for blk in range(NBO):
            bi = blk % 2
            tw = slice(blk * BW, (blk + 1) * BW)
            p.dma_wait(te, ldms[blk])
            p.dma_wait(te, wq_tok[q])
            for fc in range(8):
                te.wait_ge(p.s_ve, p.semv["s_ve"])
                for k in range(8):
                    i = te.matmul(kkp[:],
                                  wkey[b][:, k * 1024 + fc * 128:k * 1024 + (fc + 1) * 128],
                                  xnb[bi][:, k * BW:(k + 1) * BW],
                                  start=(k == 0), stop=(k == 7))
                p.inc("s_pe", i)
                ac.wait_ge(p.s_pe, p.semv["s_pe"])
                p.inc("s_ac", ac.activation(relu[:], kkp[:], AF.Relu))
                ve.wait_ge(p.s_ac, p.semv["s_ac"])
                p.inc("s_ve", ve.tensor_mul(kk[:, fc * BW:(fc + 1) * BW],
                                            relu[:], relu[:]))
            for m in range(8):
                te.wait_ge(p.s_ve, p.semv["s_ve"])
                for fc in range(8):
                    i = te.matmul(vvp[:],
                                  wval[b][:, fc * D + m * 128:fc * D + (m + 1) * 128],
                                  kk[:, fc * BW:(fc + 1) * BW],
                                  start=(fc == 0), stop=(fc == 7))
                p.inc("s_pe", i)
                ve.wait_ge(p.s_pe, p.semv["s_pe"])
                if m == 0:
                    if q >= 1:
                        p.dma_wait(ve, vvtok[blk])
                    elif blk >= 1:
                        p.dma_wait(ve, st_mark[blk - 1])
                mw = slice(m * BW, (m + 1) * BW)
                if q == 0:
                    p.inc("s_ve", ve.tensor_copy(vv[:, mw], vvp[:]))
                else:
                    p.inc("s_ve", ve.scalar_tensor_tensor(
                        vv[:, mw], vvp[:], 1.0, vv[:, mw],
                        AL.mult, AL.add))
            pe_mark[blk] = p.semv["s_pe"]
            sc.wait_ge(p.s_ve, p.semv["s_ve"])
            stok = p.dma_open(sc)
            p.dma_put(stok, sc.dma_start(
                out=dram["vvh_sp"][:, :, tw].rearrange("k p w -> p k w"),
                in_=vv[:].rearrange("p (k w) -> p k w", k=8)))
            st_mark[blk] = p.dma_close(stok)
            issue_load(blk + 2)
            issue_vv(blk + 1)
        prev_pe, prev_st = pe_mark, st_mark
        buf_mark[b] = p.semv["s_pe"]
        issue_wq(q + 2)


def ph4c(p, st, *, ident):
    nc = p.nc
    sy, te, ve, ac, sc = nc.sync, nc.tensor, nc.vector, nc.scalar, nc.scalar
    dram = p.dram

    wrec = p.sb("pc_wrec", [128, 8 * D], stack=st)
    wtok = p.dma_open(sy)
    p.dma_put(wtok, sy.dma_start(
        out=wrec[:].rearrange("p (k e) -> p k e", k=8),
        in_=dram["wrec_t"].rearrange("(k p) e -> p k e", p=128)))
    w_ld = p.dma_close(wtok)

    xn3b = p.sb("pc_xn3", [128, 8 * BW], stack=st)
    vvb = p.sb("pc_vv", [128, 8 * BW], stack=st)
    y1b = p.sb("pc_y1", [128, 8 * BW], stack=st)
    shb = p.sb("pc_sh", [128, 8 * BW], stack=st)
    xbb = p.sb("pc_xb", [128, 8 * BW], stack=st)
    rr = p.sb("pc_rr", [128, 8 * BW], stack=st)
    mrow = p.sb("pc_mrow", [128, D], stack=st)
    orow = p.sb("pc_orow", [128, D], stack=st)

    mm_ps = [p.ps(f"pc_mm{i}", [128, BW], stack=st) for i in range(2)]
    trp = p.ps("pc_trp", [128, 1024], stack=st)

    xr = dram["x_own_fm"].rearrange("(k p) t -> k p t", p=128)
    outr = dram["out"].rearrange("(n p) d -> n p d", p=128)
    maccr = dram["macc"][0:HALF].rearrange("(n p) d -> n p d", p=128)

    for blk in range(NBO):
        tw = slice(blk * BW, (blk + 1) * BW)
        if blk >= 1:
            sy.wait_ge(p.s_ve, p.semv["s_ve"])
        tok = p.dma_open(sy)
        for k in range(8):
            p.dma_put(tok, sy.dma_start(out=xbb[:, k * BW:(k + 1) * BW],
                                        in_=xr[k, :, tw]))
        for nm, tile in (("xn3_sp", xn3b), ("vvh_sp", vvb),
                         ("y1_sp", y1b), ("sh_sp", shb)):
            p.dma_put(tok, sy.dma_start(
                out=tile[:].rearrange("p (k w) -> p k w", k=8),
                in_=dram[nm][:, :, tw].rearrange("k p w -> p k w")))
        ldm = p.dma_close(tok)
        if blk == 0:
            p.dma_wait(te, w_ld)

        p.dma_wait(te, ldm)
        for m in range(8):
            mw = slice(m * BW, (m + 1) * BW)
            pb = mm_ps[m % 2]
            if m >= 2:
                te.wait_ge(p.s_ac, p.semv["s_ac"])
            for k in range(8):
                i = te.matmul(pb[:],
                              wrec[:, k * D + m * 128:k * D + m * 128 + 128],
                              xn3b[:, k * BW:(k + 1) * BW],
                              start=(k == 0), stop=(k == 7))
            p.inc("s_pe", i)
            ac.wait_ge(p.s_pe, p.semv["s_pe"])
            p.inc("s_ac", ac.activation(rr[:, mw], pb[:], AF.Sigmoid))

        ve.wait_ge(p.s_ac, p.semv["s_ac"])
        p.dma_wait(ve, ldm)
        ve.tensor_mul(rr[:], rr[:], vvb[:])
        ve.tensor_add(rr[:], rr[:], y1b[:])
        ve.tensor_add(rr[:], rr[:], shb[:])
        p.inc("s_ve", ve.tensor_add(rr[:], rr[:], xbb[:]))
        base_ve = p.semv["s_ve"]

        te.wait_ge(p.s_ve, base_ve)
        ot_prev = None
        for t4 in range(4):
            n = blk * 4 + t4
            sy.wait_ge(p.s_ve, p.semv["s_ve"])
            mtok = p.dma_open(sy)
            p.dma_put(mtok, sy.dma_start(out=mrow[:], in_=maccr[n]))
            p.dma_close(mtok)
            if t4 > 0:
                te.wait_ge(p.s_ve, p.semv["s_ve"])
            for k in range(8):
                i = te.transpose(trp[:, k * 128:(k + 1) * 128],
                                 rr[:, k * BW + t4 * 128:k * BW + t4 * 128 + 128],
                                 ident[:])
            p.inc("s_pe", i)
            ve.wait_ge(p.s_pe, p.semv["s_pe"])
            p.dma_wait(ve, mtok)
            if ot_prev is not None:
                p.dma_wait(ve, ot_prev)
            p.inc("s_ve", ve.scalar_tensor_tensor(
                orow[:], trp[:], 1.0, mrow[:], AL.mult, AL.add))
            sc.wait_ge(p.s_ve, p.semv["s_ve"])
            otok = p.dma_open(sc)
            p.dma_put(otok, sc.dma_start(out=outr[n], in_=orow[:]))
            ot_prev = p.dma_close(otok)

    p.dma_wait_all(sc)
    p.dma_wait_all(sy)
    p.wait(nc.gpsimd, "s_gp")


def make_inmaps(inputs):
    x = np.ascontiguousarray(inputs["L0_attn_x"], np.float32)

    def tr(a):
        return np.ascontiguousarray(np.asarray(a, np.float32).T)

    def chunks8(v):
        return np.ascontiguousarray(np.asarray(v, np.float32).reshape(8, 128).T)

    shared = {
        "wr_t": tr(inputs["L0_attn_Wr"]),
        "ww_t": tr(inputs["L0_attn_Ww"]),
        "wk_t": tr(inputs["L0_attn_Wk"]),
        "wv_t": tr(inputs["L0_attn_Wv"]),
        "wo_t": tr(inputs["L0_attn_Wo"]),
        "wrec_t": tr(inputs["L0_attn_Wrec"]),
        "wkey_t": tr(inputs["L0_attn_Wkey"]),
        "wval_t": tr(inputs["L0_attn_Wval"]),
        "wsg_t": tr(inputs["L0_ffn_Ws_g"]),
        "wsu_t": tr(inputs["L0_ffn_Ws_u"]),
        "wsd_t": tr(inputs["L0_ffn_Ws_d"]),
        "gate_t": tr(inputs["L0_ffn_gate_w"]),
        "wg_t": np.ascontiguousarray(
            np.asarray(inputs["L0_ffn_Wg"], np.float32).transpose(0, 2, 1)),
        "wu_t": np.ascontiguousarray(
            np.asarray(inputs["L0_ffn_Wu"], np.float32).transpose(0, 2, 1)),
        "wd_t": np.ascontiguousarray(
            np.asarray(inputs["L0_ffn_Wd"], np.float32).transpose(0, 2, 1)),
        "ln1g": chunks8(inputs["L0_attn_ln1_g"]),
        "ln1b": chunks8(inputs["L0_attn_ln1_b"]),
        "ln2g": chunks8(inputs["L0_attn_ln2_g"]),
        "ln2b": chunks8(inputs["L0_attn_ln2_b"]),
        "td": chunks8(inputs["L0_attn_time_decay"]),
        "ident": np.eye(128, dtype=np.float32),
        "tri": np.ascontiguousarray(
            np.triu(np.ones((128, 128), np.float32), 1)),
        "onesc": np.ones((128, 1), np.float32),
        "onesr": np.ones((1, 128), np.float32),
        "tokidG": np.ascontiguousarray(
            (np.arange(CC)[None, :] * 128 + np.arange(128)[:, None] + 1
             ).astype(np.float32)),
    }
    maps = []
    for c in range(8):
        b, h = c // 2, c % 2
        xb = x[b]
        own = xb[h * HALF:(h + 1) * HALF]
        m = dict(shared)
        m["x_fm"] = np.ascontiguousarray(xb.T)
        m["x_own_tm"] = np.ascontiguousarray(own)
        m["x_own_fm"] = np.ascontiguousarray(own.T)
        mc = np.zeros((128, CC), np.float32)
        mc[:, h * CCH:(h + 1) * CCH] = 1.0
        m["maskc"] = mc
        m["hseg"] = np.array([[h * CCH]], np.int32)
        m["selinit"] = np.concatenate([np.zeros((128, E), np.float32),
                                       np.ones((128, E), np.float32)], 1)
        m["hofft1"] = np.full((128, 1), h * HALF + 1, np.float32)
        m["hofft2"] = np.full((128, 1), HALF + h * HALF + 1, np.float32)
        maps.append(m)
    return maps


def kernel(**inputs):
    prog = build()
    maps = make_inmaps(inputs)
    res = run_bass_kernel_spmd(prog.nc, maps, core_ids=list(range(8)))
    out = np.empty((4, T, D), np.float32)
    for c in range(8):
        b, h = c // 2, c % 2
        out[b, h * HALF:(h + 1) * HALF] = res.results[c]["out"]
    return out


# version 06 (final)
# speedup vs baseline: 1.1440x; 1.0030x over previous
import contextlib
import numpy as np
import concourse.bass as bass
import concourse.mybir as mybir
from concourse.bass_utils import run_bass_kernel_spmd
from concourse.bass_types import DynSlice
from concourse.bass_core_rust import IndirectOffsetOnAxis

F32 = mybir.dt.float32
I32 = mybir.dt.int32
AL = mybir.AluOpType
AF = mybir.ActivationFunctionType
AX = mybir.AxisListType

T = 8192
D = 1024
E = 16
HALF = 4096
CC = 64
CCH = 32
CAP = 640
NSL = CAP // 128
WID = D + E + 1
SENT = 1 << 20
LN_EPS = 1e-5
WKV_EPS = 1e-8
NB = T // 512
NBO = HALF // 512
BW = 512
FEXP = 2048
FSH = 2048
FCM = 4096
TH_ITERS = 38
TH_PAD = 3e-4


class Prog:

    def __init__(self, debug_taps=()):
        self.nc = bass.Bass("TRN2", target_bir_lowering=False, debug=False,
                            detect_race_conditions=False)
        self.dram = {}
        self.taps = {}
        self.debug_taps = set(debug_taps)
        self.stack = contextlib.ExitStack()
        self.semv = {}

    def din(self, name, shape, dtype=F32):
        ap = self.nc.dram_tensor(name, shape, dtype, kind="ExternalInput").ap()
        self.dram[name] = ap
        return ap

    def dout(self, name, shape, dtype=F32):
        ap = self.nc.dram_tensor(name, shape, dtype, kind="ExternalOutput").ap()
        self.dram[name] = ap
        return ap

    def dscr(self, name, shape, dtype=F32):
        ap = self.nc.dram_tensor(name, shape, dtype, kind="Internal").ap()
        self.dram[name] = ap
        return ap

    def sb(self, name, shape, dtype=F32, stack=None):
        return (stack or self.stack).enter_context(
            self.nc.sbuf_tensor(name, shape, dtype))

    def ps(self, name, shape, dtype=F32, stack=None):
        return (stack or self.stack).enter_context(
            self.nc.psum_tensor(name, shape, dtype))

    def sem(self, name):
        h = self.stack.enter_context(self.nc.semaphore(name))
        self.semv[name] = 0
        return h

    def inc(self, name, instr, n=1):
        instr.then_inc(getattr(self, name), n)
        self.semv[name] += n
        return self.semv[name]

    def dinc(self, name, instr):
        return self.inc(name, instr, 16)

    def wait(self, eng, name, v=None):
        eng.wait_ge(getattr(self, name), self.semv[name] if v is None else v)

    def vf(self, instr):
        self.inc("s_ve", instr)
        self.nc.vector.wait_ge(self.s_ve, self.semv["s_ve"])

    def dma_open(self, eng):
        j = self._rr
        self._rr = (self._rr + 1) % 8
        if self._dtarget[j] > 0:
            eng.wait_ge(self.dsem[j], self._dtarget[j])
        return [j, None]

    def dma_put(self, tok, instr):
        j = tok[0]
        instr.then_inc(self.dsem[j], 16)
        self.semv[f"s_d{j}"] += 16

    def dma_close(self, tok):
        j = tok[0]
        tok[1] = self.semv[f"s_d{j}"]
        self._dtarget[j] = tok[1]
        return tok

    def dma_batch(self, eng, emits):
        tok = self.dma_open(eng)
        for fn in emits:
            self.dma_put(tok, fn())
        return self.dma_close(tok)

    def dma_wait(self, eng, tok):
        eng.wait_ge(self.dsem[tok[0]], tok[1])

    def dma_wait_all(self, eng):
        for j in range(8):
            if self._dtarget[j] > 0:
                eng.wait_ge(self.dsem[j], self._dtarget[j])

    def tap(self, name, src_tile, shape=None):
        if name not in self.debug_taps:
            return
        sc = self.nc.scalar
        self.wait(sc, "s_ve")
        self.wait(sc, "s_ac")
        self.wait(sc, "s_pe")
        shape = shape or list(src_tile.shape)
        ap = self.dout("tap_" + name, shape)
        self.taps[name] = ap
        tok = self.dma_open(sc)
        self.dma_put(tok, sc.dma_start(out=ap, in_=src_tile))
        self.dma_close(tok)
        nc = self.nc
        for eng in (nc.vector, nc.scalar, nc.tensor, nc.gpsimd, nc.sync):
            self.dma_wait(eng, tok)

    def barrier(self, skip_gp=False):
        nc = self.nc
        self.dma_wait_all(nc.sync)
        self.dma_wait_all(nc.scalar)
        engines = (nc.sync, nc.tensor, nc.vector, nc.scalar)
        if not skip_gp:
            self.wait(nc.gpsimd, "s_gp")
            engines = engines + (nc.gpsimd,)
        for eng in engines:
            self.inc("s_bar", eng.nop())
        tgt = self.semv["s_bar"]
        for eng in engines:
            eng.wait_ge(self.s_bar, tgt)


def build(debug_taps=()):
    p = Prog(debug_taps)
    nc = p.nc

    p.din("x_fm", [D, T])
    p.din("x_own_tm", [HALF, D])
    p.din("x_own_fm", [D, HALF])
    for nm in ("wr_t", "ww_t", "wk_t", "wv_t", "wo_t", "wrec_t"):
        p.din(nm, [D, D])
    p.din("wkey_t", [D, FCM])
    p.din("wval_t", [FCM, D])
    p.din("wsg_t", [D, FSH])
    p.din("wsu_t", [D, FSH])
    p.din("wsd_t", [FSH, D])
    p.din("wg_t", [E, D, FEXP])
    p.din("wu_t", [E, D, FEXP])
    p.din("wd_t", [E, FEXP, D])
    p.din("gate_t", [D, E])
    for nm in ("ln1g", "ln1b", "ln2g", "ln2b", "td"):
        p.din(nm, [128, 8])
    p.din("ident", [128, 128])
    p.din("tri", [128, 128])
    p.din("maskc", [128, CC])
    p.din("tokidG", [128, CC])
    p.din("selinit", [128, 2 * E])
    p.din("hofft1", [128, 1])
    p.din("hofft2", [128, 1])
    p.din("onesc", [128, 1])
    p.din("onesr", [1, 128])
    p.din("hseg", [1, 1], I32)

    p.dscr("e_sp", [8, 128, T])
    p.dscr("ev_sp", [8, 128, T])
    p.dscr("r_sp", [8, 128, T])
    p.dscr("wkv_sp", [8, 128, T])
    p.dscr("xn_sp", [8, 128, HALF])
    p.dscr("xn3_sp", [8, 128, HALF])
    p.dscr("y1_sp", [8, 128, HALF])
    p.dscr("sh_sp", [8, 128, HALF])
    p.dscr("vvh_sp", [8, 128, HALF])
    for e in range(E):
        p.dscr(f"xtok{e}", [CAP, WID])
    p.dscr("wid_sp", [128, CC * (E + 1)])
    p.dscr("idxe_sp", [128, CC * E], I32)
    p.dscr("eoT_sp", [E * NSL * 128, D])
    p.dscr("idT_sp", [E * NSL, 128], I32)
    p.dscr("ab0_sp", [2, T])
    p.dscr("macc", [HALF + 1, D])
    p.dout("out", [HALF, D])

    for s in ("s_gp", "s_pe", "s_ve", "s_ac", "s_bar"):
        setattr(p, s, p.sem(s))
    p.dsem = [p.sem(f"s_d{i}") for i in range(8)]
    p._rr = 0
    p._dtarget = [0] * 8

    with nc.Block() as _block:
        emit(p)

    p.stack.close()
    return p


def emit(p):
    nc = p.nc
    sy, te, ve, ac, gp = nc.sync, nc.tensor, nc.vector, nc.scalar, nc.gpsimd

    ident = p.sb("c_ident", [128, 128])
    onesc = p.sb("c_onesc", [128, 1])
    onesr = p.sb("c_onesr", [1, 128])
    lnp = p.sb("c_lnp", [128, 4 * 8])
    decay = p.sb("c_decay", [128, 8])
    hsegt = p.sb("c_hseg", [1, 1], I32)

    dram = p.dram

    tok0 = p.dma_open(sy)
    for nm, dst in (("ident", ident), ("onesc", onesc),
                    ("onesr", onesr), ("hseg", hsegt)):
        p.dma_put(tok0, sy.dma_start(out=dst[:], in_=dram[nm]))
    for i, nm in enumerate(("ln1g", "ln1b", "ln2g", "ln2b")):
        p.dma_put(tok0, sy.dma_start(out=lnp[:, i * 8:(i + 1) * 8], in_=dram[nm]))
    p.dma_put(tok0, sy.dma_start(out=decay[:], in_=dram["td"]))
    p.dma_close(tok0)
    p.dma_wait(ac, tok0)
    p.inc("s_ac", ac.activation(decay[:], decay[:], AF.Exp))

    p.dma_wait(sy, tok0)
    hv = sy.value_load(hsegt[0:1, 0:1])
    p.dma_wait(gp, tok0)
    hvg = gp.value_load(hsegt[0:1, 0:1])

    with contextlib.ExitStack() as sel_st:
        tri = p.sb("m_tri", [128, 128], stack=sel_st)
        gatet = p.sb("m_gate", [128, 8 * E], stack=sel_st)
        maskc = p.sb("m_maskc", [128, CC], stack=sel_st)
        tokidG = p.sb("m_tokidG", [128, CC], stack=sel_st)
        scores3 = p.sb("m_scores3", [128, E * CC], stack=sel_st)
        theta = p.sb("m_theta", [128, E], stack=sel_st)
        gsumP = p.sb("m_gsumP", [128, E], stack=sel_st)
        idxL = p.sb("m_idxL", [128, CC * E], I32, stack=sel_st)
        zeros = p.sb("m_zeros", [128, WID], stack=sel_st)

        tokc = p.dma_open(sy)
        for nm, dst in (("tri", tri), ("maskc", maskc), ("tokidG", tokidG)):
            p.dma_put(tokc, sy.dma_start(out=dst[:], in_=dram[nm]))
        p.dma_put(tokc, sy.dma_start(
            out=gatet[:].rearrange("p (k e) -> p k e", k=8),
            in_=dram["gate_t"].rearrange("(k p) e -> p k e", p=128)))
        p.dma_close(tokc)
        p.inc("s_ve", ve.memset(zeros[:], 0.0))

        p.wait(gp, "s_ve")
        mr = dram["macc"][0:HALF].rearrange("(n p) d -> n p d", p=128)
        for i in range(HALF // 128):
            p.dinc("s_gp", gp.dma_start(out=mr[i], in_=zeros[:, :D]))
        p.dinc("s_gp", gp.dma_start(out=dram["macc"][HALF:HALF + 1, :],
                                    in_=zeros[0:1, 0:D]))
        for e in range(E):
            xr = dram[f"xtok{e}"].rearrange("(n p) w -> n p w", p=128)
            for i in range(NSL):
                p.dinc("s_gp", gp.dma_start(out=xr[i], in_=zeros[:, :WID]))

        p.barrier(skip_gp=True)

        with contextlib.ExitStack() as st:
            ph1(p, st, ident=ident, onesc=onesc, onesr=onesr, gatet=gatet,
                scores3=scores3, gsumP=gsumP)
            p.barrier(skip_gp=True)

        with contextlib.ExitStack() as st:
            selection(p, st, ident=ident, tri=tri, onesc=onesc, onesr=onesr,
                      maskc=maskc, scores3=scores3, theta=theta, gsumP=gsumP,
                      tokidG=tokidG, idxL=idxL)
            p.barrier()

    with contextlib.ExitStack() as st:
        phMprep(p, st, hvg=hvg)
        ph2(p, st, onesr=onesr, onesc=onesc, lnp=lnp, decay=decay)
        p.barrier()

    with contextlib.ExitStack() as st4a:
        wo = p.sb("p4_wo", [128, 8 * D], stack=st4a)
        wotok = p.dma_open(sy)
        p.dma_put(wotok, sy.dma_start(
            out=wo[:].rearrange("p (k e) -> p k e", k=8),
            in_=dram["wo_t"].rearrange("(k p) e -> p k e", p=128)))
        wo_ld = p.dma_close(wotok)
        with contextlib.ExitStack() as st:
            phM(p, st, ident=ident)
            p.barrier()

        eoFori(p, st4a)
        ph4a(p, st4a, onesr=onesr, onesc=onesc, lnp=lnp, hv=hv,
             wo=wo, w_ld=wo_ld)
        p.barrier()
    with contextlib.ExitStack() as st:
        ph4sh(p, st)
        p.barrier()
    with contextlib.ExitStack() as st:
        ph4b(p, st)
        p.barrier()
    with contextlib.ExitStack() as st:
        ph4c(p, st, ident=ident)
        p.barrier()


def ln_rows(p, stats_ps, a_slice, b_slice, scr):
    nc = p.nc
    ve, ac = nc.vector, nc.scalar
    W = a_slice.shape[1]
    ve.tensor_scalar_mul(a_slice, stats_ps[:, 0:W], 1.0 / D)
    ve.tensor_scalar_mul(b_slice, stats_ps[:, W:2 * W], 1.0 / D)
    ve.tensor_mul(scr, a_slice, a_slice)
    ve.tensor_sub(b_slice, b_slice, scr[:])
    p.inc("s_ve", ve.tensor_scalar_add(b_slice, b_slice, LN_EPS))
    p.wait(ac, "s_ve")
    p.inc("s_ac", ac.activation(b_slice, b_slice, AF.Sqrt))
    p.wait(ve, "s_ac")
    p.inc("s_ve", ve.reciprocal(b_slice, b_slice))
    p.wait(ve, "s_ve")
    ve.tensor_mul(scr, a_slice, b_slice)
    ve.tensor_copy(a_slice, b_slice)
    p.inc("s_ve", ve.tensor_scalar_mul(b_slice, scr, -1.0))


PH1_PARTS = {"blocks", "gsum", "staging", "transpose", "logits"}


def ph1(p, st, *, ident, onesc, onesr, gatet, scores3, gsumP):
    nc = p.nc
    sy, te, ve, ac = nc.sync, nc.tensor, nc.vector, nc.scalar
    dram = p.dram

    xb = [p.sb(f"p1_xb{i}", [128, 8 * BW], stack=st) for i in range(2)]
    xsq = p.sb("p1_xsq", [128, 8 * BW], stack=st)
    arow = p.sb("p1_arow", [1, T], stack=st)
    brow = p.sb("p1_brow", [1, T], stack=st)
    st18 = p.sb("p1_st18", [18, T], stack=st)
    srow = p.sb("p1_srow", [1, 2 * BW], stack=st)
    scr = p.sb("p1_scr", [1, BW], stack=st)
    a3 = p.sb("p1_a3", [128, CC], stack=st)
    b3 = p.sb("p1_b3", [128, CC], stack=st)
    gx3 = p.sb("p1_gx3", [128, E * CC], stack=st)
    grow = p.sb("p1_grow", [1, E], stack=st)
    l3 = p.sb("p1_l3", [128, E * CC], stack=st)
    t3 = p.sb("p1_t3", [128, E * CC], stack=st)
    sume = p.sb("p1_sume", [128, 8 * CC], stack=st)

    gxp = p.ps("p1_gxp", [16, BW], stack=st)
    stp = p.ps("p1_stp", [1, 2 * BW], stack=st)
    trp = [p.ps(f"p1_trp{i}", [128, 32], stack=st) for i in range(2)]
    bcp = p.ps("p1_bcp", [128, E], stack=st)
    gsp = p.ps("p1_gsp", [1, E], stack=st)

    xr = dram["x_fm"].rearrange("(k p) t -> k p t", p=128)

    ld_mark = []
    ac_mark = []
    pe_mark = []
    ve_mark = []

    for blk in range(NB if "blocks" in PH1_PARTS else 0):
        if blk >= 2:
            sy.wait_ge(p.s_pe, pe_mark[blk - 2])
        tok = p.dma_open(sy)
        for k in range(8):
            p.dma_put(tok, sy.dma_start(
                out=xb[blk % 2][:, k * BW:(k + 1) * BW],
                in_=xr[k, :, blk * BW:(blk + 1) * BW]))
        ld_mark.append(p.dma_close(tok))

        p.dma_wait(ac, ld_mark[blk])
        if blk >= 1:
            ac.wait_ge(p.s_pe, pe_mark[blk - 1])
        p.inc("s_ac", ac.activation(xsq[:], xb[blk % 2][:], AF.Square))
        ac_mark.append(p.semv["s_ac"])

        te.wait_ge(p.s_ac, ac_mark[blk])
        if blk >= 1:
            te.wait_ge(p.s_ve, ve_mark[blk - 1])
        for k in range(8):
            kw = slice(k * BW, (k + 1) * BW)
            te.matmul(gxp[:], gatet[:, k * E:(k + 1) * E], xb[blk % 2][:, kw],
                      start=(k == 0), stop=(k == 7))
            te.matmul(stp[:, 0:BW], onesc[:], xb[blk % 2][:, kw],
                      start=(k == 0), stop=(k == 7))
            i = te.matmul(stp[:, BW:2 * BW], onesc[:], xsq[:, kw],
                          start=(k == 0), stop=(k == 7))
        p.inc("s_pe", i)
        pe_mark.append(p.semv["s_pe"])

        ve.wait_ge(p.s_pe, pe_mark[blk])
        ve.tensor_copy(st18[0:16, blk * BW:(blk + 1) * BW], gxp[:])
        ve.tensor_copy(srow[:], stp[:])
        ln_rows(p, srow,
                arow[:, blk * BW:(blk + 1) * BW],
                brow[:, blk * BW:(blk + 1) * BW], scr[:])
        ve_mark.append(p.semv["s_ve"])

    p.tap("arow", arow[:])
    p.tap("brow", brow[:])

    if "gsum" not in PH1_PARTS:
        return
    p.dma_wait_all(te)
    for k in range(8):
        i = te.matmul(gsp[:], onesc[:], gatet[:, k * E:(k + 1) * E],
                      start=(k == 0), stop=(k == 7))
    p.inc("s_pe", i)
    p.wait(ve, "s_pe")
    p.inc("s_ve", ve.tensor_copy(grow[:], gsp[:]))
    p.wait(te, "s_ve")
    i = te.matmul(bcp[:], onesr[:], grow[:], start=True, stop=True)
    p.inc("s_pe", i)
    p.wait(ve, "s_pe")
    p.inc("s_ve", ve.tensor_copy(gsumP[:], bcp[:]))

    if "staging" not in PH1_PARTS:
        return
    p.wait(sy, "s_ve")
    stok = p.dma_open(sy)
    p.dma_put(stok, sy.dma_start(out=st18[16:17, :], in_=arow[:]))
    p.dma_put(stok, sy.dma_start(out=st18[17:18, :], in_=brow[:]))
    p.dma_put(stok, sy.dma_start(out=dram["ab0_sp"][0:1, :], in_=arow[:]))
    p.dma_put(stok, sy.dma_start(out=dram["ab0_sp"][1:2, :], in_=brow[:]))
    stg_ld = p.dma_close(stok)

    if "transpose" not in PH1_PARTS:
        return
    p.dma_wait(te, stg_ld)
    ve_evac_base = p.semv["s_ve"]
    for tb in range(CC):
        if tb >= 2:
            te.wait_ge(p.s_ve, ve_evac_base + tb - 1)
        i = te.transpose(trp[tb % 2][:, 0:18],
                         st18[:, tb * 128:(tb + 1) * 128], ident[0:18, 0:18])
        p.inc("s_pe", i)
        ve.wait_ge(p.s_pe, p.semv["s_pe"])
        gx3v = gx3[:].rearrange("p (e c) -> p e c", c=CC)
        ve.tensor_copy(gx3v[:, :, tb:tb + 1],
                       trp[tb % 2][:, 0:16].rearrange("p (e c) -> p e c", c=1))
        ve.tensor_copy(a3[:, tb:tb + 1], trp[tb % 2][:, 16:17])
        p.inc("s_ve", ve.tensor_copy(b3[:, tb:tb + 1], trp[tb % 2][:, 17:18]))

    if "logits" not in PH1_PARTS:
        return
    a3bc = a3[:].unsqueeze(1).broadcast_to([128, E, CC])
    b3bc = b3[:].unsqueeze(1).broadcast_to([128, E, CC])
    gsbc = gsumP[:].unsqueeze(2).broadcast_to([128, E, CC])
    l3v = l3[:].rearrange("p (e c) -> p e c", c=CC)
    t3v = t3[:].rearrange("p (e c) -> p e c", c=CC)
    p.wait(ve, "s_ve")
    ve.tensor_mul(l3v, gx3[:].rearrange("p (e c) -> p e c", c=CC), a3bc)
    ve.tensor_mul(t3v, b3bc, gsbc)
    p.inc("s_ve", ve.tensor_add(l3[:], l3[:], t3[:]))
    p.tap("logits3", l3[:])
    p.tap("gx3", gx3[:])
    p.tap("a3", a3[:])
    p.tap("b3", b3[:])
    p.tap("gsumP", gsumP[:])
    p.wait(ac, "s_ve")
    p.inc("s_ac", ac.activation(l3[:], l3[:], AF.Exp))
    p.wait(ve, "s_ac")
    l3e = l3[:].rearrange("p (e c) -> p e c", c=CC)
    s8 = sume[:].rearrange("p (e c) -> p e c", c=CC)
    ve.tensor_add(s8[:, 0:8, :], l3e[:, 0:8, :], l3e[:, 8:16, :])
    ve.tensor_add(s8[:, 0:4, :], s8[:, 0:4, :], s8[:, 4:8, :])
    ve.tensor_add(s8[:, 0:2, :], s8[:, 0:2, :], s8[:, 2:4, :])
    p.vf(ve.tensor_add(s8[:, 0:1, :], s8[:, 0:1, :], s8[:, 1:2, :]))
    p.inc("s_ve", ve.reciprocal(sume[:, 0:CC], sume[:, 0:CC]))
    p.wait(ve, "s_ve")
    p.tap("sumr", sume[:, 0:CC])
    rbc = sume[:, 0:CC].unsqueeze(1).broadcast_to([128, E, CC])
    p.inc("s_ve", ve.tensor_mul(
        scores3[:].rearrange("p (e c) -> p e c", c=CC), l3e, rbc))
    p.tap("scores3", scores3[:])


def selection(p, st, *, ident, tri, onesc, onesr, maskc, scores3, theta,
              gsumP, tokidG, idxL):
    dram = p.dram
    """Binary-search theta per expert; build w (e-minor) + scatter slots."""
    nc = p.nc
    sy, te, ve, ac = nc.sync, nc.tensor, nc.vector, nc.scalar

    lohit = p.sb("sl_lohi", [128, 2 * E], stack=st)
    lo = lohit[:, 0:E]
    hi = lohit[:, E:2 * E]
    mid = p.sb("sl_mid", [128, E], stack=st)
    ge = p.sb("sl_ge", [128, E * CC], stack=st)
    red = p.sb("sl_red", [128, E], stack=st)
    crow = p.sb("sl_crow", [1, E], stack=st)
    cntP = p.sb("sl_cntP", [128, E], stack=st)
    cntN = p.sb("sl_cntN", [128, E], stack=st)
    tmp = p.sb("sl_tmp", [128, E], stack=st)
    tmp2 = p.sb("sl_tmp2", [128, E], stack=st)
    w3 = p.sb("sl_w3", [128, E * CC], stack=st)
    cand = p.sb("sl_cand", [128, E * CC], stack=st)
    jj = p.sb("sl_jj", [128, E * CC], stack=st)
    rtot = p.sb("sl_rtot", [128, E], stack=st)
    offp = p.sb("sl_offp", [128, E], stack=st)
    idxf = p.sb("sl_idxf", [128, E * CC], stack=st)
    idx3 = p.sb("sl_idx3", [128, E * CC], I32, stack=st)
    wm2 = p.sb("sl_wm2", [128, CC * (E + 1)], stack=st)

    cp = p.ps("sl_cp", [1, E], stack=st)
    bp = p.ps("sl_bp", [128, E], stack=st)
    op = p.ps("sl_op", [128, E], stack=st)

    s3v = scores3[:].rearrange("p (e c) -> p e c", c=CC)
    gev = ge[:].rearrange("p (e c) -> p e c", c=CC)

    itok = p.dma_open(sy)
    p.dma_put(itok, sy.dma_start(out=lohit[:], in_=dram["selinit"]))
    p.dma_close(itok)
    p.dma_wait(ve, itok)

    for it in range(TH_ITERS):
        p.vf(ve.tensor_add(mid[:], lo, hi))
        p.vf(ve.tensor_scalar_mul(mid[:], mid[:], 0.5))
        midbc = mid[:].unsqueeze(2).broadcast_to([128, E, CC])
        ve.tensor_tensor(gev, s3v, midbc, AL.is_ge)
        p.inc("s_ve", ve.tensor_reduce(
            red[:].unsqueeze(2), gev, AX.X, AL.add))
        if it == 0:
            p.tap("red0", red[:])
            p.tap("mid0", mid[:])
        p.wait(te, "s_ve")
        i = te.matmul(cp[:], onesc[:], red[:], start=True, stop=True)
        p.inc("s_pe", i)
        p.wait(ve, "s_pe")
        p.inc("s_ve", ve.tensor_copy(crow[:], cp[:]))
        p.wait(te, "s_ve")
        i = te.matmul(bp[:], onesr[:], crow[:], start=True, stop=True)
        p.inc("s_pe", i)
        p.wait(ve, "s_pe")
        if it == 0 and "cnt0" in p.debug_taps:
            p.vf(ve.tensor_copy(tmp[:], bp[:]))
            p.tap("cnt0", tmp[:])
        p.vf(ve.tensor_single_scalar(cntP[:], bp[:], 1024.0, AL.is_ge))
        p.vf(ve.tensor_sub(tmp[:], mid[:], lo))
        p.vf(ve.tensor_mul(tmp[:], tmp[:], cntP[:]))
        p.vf(ve.tensor_sub(tmp2[:], mid[:], hi))
        p.vf(ve.tensor_add(lo, lo, tmp[:]))
        p.vf(ve.tensor_scalar(cntN[:], cntP[:], -1.0, 1.0, AL.mult, AL.add))
        p.vf(ve.tensor_mul(tmp2[:], tmp2[:], cntN[:]))
        p.vf(ve.tensor_add(hi, hi, tmp2[:]))

    p.vf(ve.tensor_copy(theta[:], lo))
    p.tap("theta", theta[:])

    thbc = theta[:].unsqueeze(2).broadcast_to([128, E, CC])
    w3v = w3[:].rearrange("p (e c) -> p e c", c=CC)
    ve.tensor_tensor(w3v, s3v, thbc, AL.is_ge)
    ve.tensor_mul(w3[:], w3[:], scores3[:])
    p.vf(ve.tensor_scalar_mul(tmp[:], theta[:], 1.0 - TH_PAD))
    tpbc = tmp[:].unsqueeze(2).broadcast_to([128, E, CC])
    cndv = cand[:].rearrange("p (e c) -> p e c", c=CC)
    ve.tensor_tensor(cndv, s3v, tpbc, AL.is_ge)
    mbc = maskc[:].unsqueeze(1).broadcast_to([128, E, CC])
    p.inc("s_ve", ve.tensor_mul(cndv, cndv, mbc))
    p.tap("cand", cand[:])

    onesbc = onesc[:].broadcast_to([128, CC])
    for e in range(E):
        i = ve.tensor_tensor_scan(jj[:, e * CC:(e + 1) * CC],
                                  onesbc, cand[:, e * CC:(e + 1) * CC],
                                  0.0, AL.mult, AL.add)
        if e == E - 1:
            p.vf(i)
    p.inc("s_ve", ve.tensor_copy(rtot[:], jj[:].rearrange(
        "p (e c) -> p e c", c=CC)[:, :, CC - 1:CC]))
    p.wait(te, "s_ve")
    i = te.matmul(op[:], tri[:], rtot[:], start=True, stop=True)
    p.inc("s_pe", i)
    p.wait(ve, "s_pe")
    p.vf(ve.tensor_copy(offp[:], op[:]))
    offbc = offp[:].unsqueeze(2).broadcast_to([128, E, CC])
    jv = jj[:].rearrange("p (e c) -> p e c", c=CC)
    ve.tensor_add(jv, jv, offbc)
    ve.tensor_single_scalar(ge[:], jj[:], float(CAP), AL.is_le)
    ve.tensor_mul(ge[:], ge[:], cand[:])
    ve.tensor_scalar_add(idxf[:], jj[:], float(-1 - SENT))
    ve.tensor_mul(idxf[:], idxf[:], ge[:])
    ve.tensor_scalar_add(idxf[:], idxf[:], float(SENT))
    p.inc("s_ve", ve.tensor_copy(idx3[:], idxf[:]))
    p.tap("jj", jj[:])

    widv = wm2[:].rearrange("p (c w) -> p c w", w=E + 1)
    w3p = w3[:].rearrange("p (e c) -> p c e", c=CC, e=E)
    ve.tensor_copy(widv[:, :, 0:E], w3p)
    ve.tensor_copy(widv[:, :, E:E + 1], tokidG[:].unsqueeze(2))
    idxev = idxL[:].rearrange("p (c e) -> p c e", e=E)
    idx3p = idx3[:].rearrange("p (e c) -> p c e", c=CC, e=E)
    p.inc("s_ve", ve.tensor_copy(idxev, idx3p))
    p.wait(sy, "s_ve")
    sptok = p.dma_open(sy)
    p.dma_put(sptok, sy.dma_start(out=dram["wid_sp"], in_=wm2[:, 0:CC * (E + 1)]))
    p.dma_put(sptok, sy.dma_start(out=dram["idxe_sp"], in_=idxL[:]))
    p.dma_close(sptok)
    p.tap("widsp", wm2[:, 0:CC * (E + 1)])
    p.tap("idxLf", idxf[:])


def phMprep(p, st, *, hvg):
    nc = p.nc
    gp = nc.gpsimd
    dram = p.dram

    wide = p.sb("mp_wide", [128, WID], stack=st)
    ixt = p.sb("mp_ixt", [128, E], I32, stack=st)

    xtm = dram["x_own_tm"]
    widr = dram["wid_sp"]
    idxr = dram["idxe_sp"]

    gbase = p.semv["s_gp"]
    PER = 3 * 16 + E * 16
    with gp.Fori(0, CCH) as i:
        gp.wait_ge(p.s_gp, i * PER + gbase)
        gp.dma_start(out=wide[:, 0:D],
                     in_=xtm[DynSlice(i * 128, 128), :]).then_inc(p.s_gp, 16)
        gp.dma_start(out=wide[:, D:WID],
                     in_=widr[:, DynSlice((hvg + i) * (E + 1), E + 1)]
                     ).then_inc(p.s_gp, 16)
        gp.dma_start(out=ixt[:],
                     in_=idxr[:, DynSlice((hvg + i) * E, E)]).then_inc(p.s_gp, 16)
        gp.wait_ge(p.s_gp, i * PER + gbase + 48)
        for e in range(E):
            gp.indirect_dma_start(
                dram[f"xtok{e}"],
                IndirectOffsetOnAxis(ixt[:, e:e + 1], 0),
                wide[:], None,
                bounds_check=CAP - 1, oob_is_err=False).then_inc(p.s_gp, 16)
    p.semv["s_gp"] += CCH * PER
    gp.wait_ge(p.s_gp, p.semv["s_gp"])


def ph2(p, st, *, onesr, onesc, lnp, decay):
    nc = p.nc
    sy, te, ve, ac = nc.sync, nc.tensor, nc.vector, nc.scalar
    sc = nc.scalar
    dram = p.dram

    wts = {}
    wtok = p.dma_open(sy)
    for nm in ("wr_t", "ww_t", "wk_t", "wv_t"):
        wts[nm] = p.sb("p2_" + nm, [128, 8 * D], stack=st)
        p.dma_put(wtok, sy.dma_start(
            out=wts[nm][:].rearrange("p (k e) -> p k e", k=8),
            in_=dram[nm].rearrange("(k p) e -> p k e", p=128)))
    w_ld = p.dma_close(wtok)

    xb = [p.sb("p2_xb0", [128, 8 * BW], stack=st)] * 2
    xsq = p.sb("p2_xsq", [128, 4 * BW], stack=st)
    xn = p.sb("p2_xn", [128, 8 * BW], stack=st)
    xn2 = xb[0]
    ab = p.sb("p2_ab", [128, 2 * BW], stack=st)
    srow = p.sb("p2_srow", [1, 2 * BW], stack=st)
    a0r = srow[0:1, 0:BW]
    b0r = srow[0:1, BW:2 * BW]
    a1r = p.sb("p2_a1r", [1, BW], stack=st)
    b1r = p.sb("p2_b1r", [1, BW], stack=st)
    scr = p.sb("p2_scr", [1, BW], stack=st)
    ew = [p.sb(f"p2_ew{i}", [128, BW], stack=st) for i in range(2)]
    eb = [p.sb("p2_eb", [128, BW], stack=st)] * 2
    evb = [p.sb("p2_evb", [128, BW], stack=st)] * 2
    rb = [p.sb(f"p2_rb{i}", [128, BW], stack=st) for i in range(2)]
    nb = [p.sb(f"p2_nb{i}", [128, BW], stack=st) for i in range(2)]
    db = [p.sb(f"p2_db{i}", [128, BW], stack=st) for i in range(2)]
    nst = p.sb("p2_nst", [128, 8], stack=st)
    dst = p.sb("p2_dst", [128, 8], stack=st)

    bc_ps = p.ps("p2_bc", [128, BW], stack=st)
    st_ps = p.ps("p2_st", [1, 2 * BW], stack=st)
    mm_ps = [p.ps(f"p2_mm{i}", [128, BW], stack=st) for i in range(4)]

    xr = dram["x_fm"].rearrange("(k p) t -> k p t", p=128)

    def bcast(row_slice, dst_slice):
        p.wait(te, "s_ve")
        i = te.matmul(bc_ps[:], onesr[:], row_slice, start=True, stop=True)
        p.inc("s_pe", i)
        p.wait(ve, "s_pe")
        p.inc("s_ve", ve.tensor_copy(dst_slice, bc_ps[:]))

    ve.memset(nst[:], 0.0)
    p.vf(ve.memset(dst[:], 0.0))

    pe_mark = [0]
    for blk in range(NB):
        tw = slice(blk * BW, (blk + 1) * BW)
        if blk >= 1:
            sy.wait_ge(p.s_pe, pe_mark[-1])
            sy.wait_ge(p.s_ve, p.semv["s_ve"])
        if blk >= 1:
            p.dma_wait(ve, _SPILLS[(id(p), blk - 1, 7)])
            p.dma_wait(ac, _SPILLS[(id(p), blk - 1, 7)])
        tok = p.dma_open(sy)
        for k in range(8):
            p.dma_put(tok, sy.dma_start(
                out=xb[blk % 2][:, k * BW:(k + 1) * BW], in_=xr[k, :, tw]))
        p.dma_put(tok, sy.dma_start(out=a0r, in_=dram["ab0_sp"][0:1, tw]))
        p.dma_put(tok, sy.dma_start(out=b0r, in_=dram["ab0_sp"][1:2, tw]))
        ldm = p.dma_close(tok)

        p.dma_wait(te, ldm)
        if blk == 0:
            p.dma_wait(te, w_ld)
        bcast(a0r, ab[:, 0:BW])
        bcast(b0r, ab[:, BW:2 * BW])
        a0b = ab[:, 0:BW].unsqueeze(1).broadcast_to([128, 8, BW])
        b0b = ab[:, BW:2 * BW].unsqueeze(1).broadcast_to([128, 8, BW])
        xbv = xb[blk % 2][:].rearrange("p (k w) -> p k w", k=8)
        xnv = xn[:].rearrange("p (k w) -> p k w", k=8)
        p.dma_wait(ve, ldm)
        ve.tensor_mul(xnv, xbv, a0b)
        p.inc("s_ve", ve.tensor_add(xnv, xnv, b0b))

        p.wait(ac, "s_ve")
        p.inc("s_ac", ac.activation(xsq[:], xn[:, 0:4 * BW], AF.Square))
        p.wait(te, "s_ac")
        p.wait(te, "s_ve")
        i = None
        for k in range(8):
            if k == 4:
                p.inc("s_pe", i)
                ac.wait_ge(p.s_pe, p.semv["s_pe"])
                p.inc("s_ac", ac.activation(xsq[:], xn[:, 4 * BW:8 * BW],
                                            AF.Square))
                p.wait(te, "s_ac")
            kw = slice(k * BW, (k + 1) * BW)
            te.matmul(st_ps[:, 0:BW], onesc[:], xn[:, kw],
                      start=(k == 0), stop=(k == 7))
            i = te.matmul(st_ps[:, BW:2 * BW], onesc[:],
                          xsq[:, (k % 4) * BW:(k % 4 + 1) * BW],
                          start=(k == 0), stop=(k == 7))
        p.inc("s_pe", i)
        p.wait(ve, "s_pe")
        ve.tensor_copy(srow[:], st_ps[:])
        ln_rows(p, srow, a1r[:], b1r[:], scr[:])
        bcast(a1r[:], ab[:, 0:BW])
        bcast(b1r[:], ab[:, BW:2 * BW])
        a1b = ab[:, 0:BW].unsqueeze(1).broadcast_to([128, 8, BW])
        b1b = ab[:, BW:2 * BW].unsqueeze(1).broadcast_to([128, 8, BW])
        xn2v = xn2[:].rearrange("p (k w) -> p k w", k=8)
        ve.tensor_mul(xn2v, xnv, a1b)
        ve.tensor_add(xn2v, xn2v, b1b)
        for k in range(8):
            kw = slice(k * BW, (k + 1) * BW)
            ve.tensor_scalar(xn2[:, kw], xn2[:, kw],
                             lnp[:, 0 * 8 + k:0 * 8 + k + 1],
                             lnp[:, 1 * 8 + k:1 * 8 + k + 1], AL.mult, AL.add)
        p.inc("s_ve", ve.engine_nop())
        xn2_ve = p.semv["s_ve"]

        te.wait_ge(p.s_ve, xn2_ve)
        cons = [None] * 4
        for m in range(8):
            bi = m % 2
            pm = st_spill_mark(p, blk, m, 2)
            if pm is not None:
                p.dma_wait(ve, pm)
                p.dma_wait(ac, pm)
            for oi, wname in enumerate(("ww_t", "wk_t", "wv_t", "wr_t")):
                pb = mm_ps[oi]
                if cons[oi] is not None:
                    te.wait_ge(getattr(p, cons[oi][0]), cons[oi][1])
                for k in range(8):
                    i = te.matmul(pb[:],
                                  wts[wname][:, k * D + m * 128:k * D + m * 128 + 128],
                                  xn2[:, k * BW:(k + 1) * BW],
                                  start=(k == 0), stop=(k == 7))
                p.inc("s_pe", i)
                mm_pe = p.semv["s_pe"]
                if wname == "ww_t":
                    ac.wait_ge(p.s_pe, mm_pe)
                    p.inc("s_ac", ac.activation(ew[bi][:], pb[:], AF.Exp))
                    cons[0] = ("s_ac", p.semv["s_ac"])
                elif wname == "wk_t":
                    ve.wait_ge(p.s_pe, mm_pe)
                    ve.wait_ge(p.s_ac, p.semv["s_ac"])
                    p.inc("s_ve", ve.scalar_tensor_tensor(
                        eb[bi][:], pb[:], 1.0, ew[bi][:], AL.mult, AL.subtract))
                    cons[1] = ("s_ve", p.semv["s_ve"])
                    ac.wait_ge(p.s_ve, p.semv["s_ve"])
                    p.inc("s_ac", ac.activation(eb[bi][:], eb[bi][:], AF.Exp))
                elif wname == "wv_t":
                    ve.wait_ge(p.s_pe, mm_pe)
                    ve.wait_ge(p.s_ac, p.semv["s_ac"])
                    p.inc("s_ve", ve.tensor_mul(evb[bi][:], eb[bi][:], pb[:]))
                    cons[2] = ("s_ve", p.semv["s_ve"])
                    dkb = decay[:, m:m + 1].broadcast_to([128, BW])
                    ve.tensor_tensor_scan(nb[bi][:], dkb, evb[bi][:],
                                          nst[:, m:m + 1], AL.mult, AL.add)
                    ve.tensor_tensor_scan(db[bi][:], dkb, eb[bi][:],
                                          dst[:, m:m + 1], AL.mult, AL.add)
                    p.vf(ve.tensor_copy(nst[:, m:m + 1], nb[bi][:, BW - 1:BW]))
                    p.inc("s_ve", ve.tensor_copy(dst[:, m:m + 1],
                                                 db[bi][:, BW - 1:BW]))
                else:
                    ac.wait_ge(p.s_pe, mm_pe)
                    p.inc("s_ac", ac.activation(rb[bi][:], pb[:], AF.Sigmoid))
                    cons[3] = ("s_ac", p.semv["s_ac"])
            sc.wait_ge(p.s_ve, p.semv["s_ve"])
            sc.wait_ge(p.s_ac, p.semv["s_ac"])
            stok = p.dma_open(sc)
            for nm, tile in (("e_sp", nb[bi]), ("ev_sp", db[bi]),
                             ("r_sp", rb[bi])):
                p.dma_put(stok, sc.dma_start(out=dram[nm][m, :, tw],
                                             in_=tile[:]))
            _SPILLS[(id(p), blk, m)] = p.dma_close(stok)
        pe_mark.append(p.semv["s_pe"])


_SPILLS = {}


def st_spill_mark(p, blk, m, dist):
    mm = m - dist
    key = (id(p), blk, mm) if mm >= 0 else (id(p), blk - 1, 8 + mm)
    return _SPILLS.get(key)
def phM(p, st, *, ident):
    nc = p.nc
    sy, te, ve, ac, gp = nc.sync, nc.tensor, nc.vector, nc.scalar, nc.gpsimd
    sc = nc.scalar
    dram = p.dram

    wg = [p.sb(f"pm_wg{i}", [128, 8 * 512], stack=st) for i in range(2)]
    wu = [p.sb(f"pm_wu{i}", [128, 8 * 512], stack=st) for i in range(2)]
    wd = [p.sb(f"pm_wd{i}", [128, 4 * 1024], stack=st) for i in range(2)]
    xt = p.sb("pm_xt", [128, NSL * WID], stack=st)
    eosb = p.sb("pm_eosb", [128, 8 * CAP], stack=st)
    stats = p.sb("pm_stats", [128, 4 * NSL], stack=st)
    xtT = p.sb("pm_xtT", [128, 8 * CAP], stack=st)
    gel = p.sb("pm_gel", [128, 320], stack=st)
    hsb = p.sb("pm_hsb", [128, 4 * 320], stack=st)
    wcol = p.sb("pm_wcol", [128, NSL], stack=st)
    eoT = [p.sb(f"pm_eoT{i}", [128, D], stack=st) for i in range(2)]
    hofftt = p.sb("pm_hofft", [128, 2], stack=st)
    idc = p.sb("pm_idc", [128, NSL], I32, stack=st)
    idcf = p.sb("pm_idcf", [128, NSL], stack=st)
    idcm = p.sb("pm_idcm", [128, NSL], stack=st)

    trp = [p.ps(f"pm_trp{i}", [128, 128], stack=st) for i in range(2)]
    gps = p.ps("pm_gps", [128, BW], stack=st)
    ups = p.ps("pm_ups", [128, BW], stack=st)
    eop = p.ps("pm_eop", [128, BW], stack=st)
    top = p.ps("pm_top", [128, 1024], stack=st)

    xtv = xt[:].rearrange("p (n w) -> p n w", n=NSL)
    eoTr = dram["eoT_sp"].rearrange("(u p) d -> u p d", p=128)
    st_eoT = []

    htok = p.dma_open(sy)
    p.dma_put(htok, sy.dma_start(out=hofftt[:, 0:1], in_=dram["hofft1"]))
    p.dma_put(htok, sy.dma_start(out=hofftt[:, 1:2], in_=dram["hofft2"]))
    p.dma_close(htok)

    wq_tok = {}
    buf_mark = [0, 0]

    def issue_wq(gidx):
        if gidx >= 4 * E:
            return
        eq, q = divmod(gidx, 4)
        b = gidx % 2
        if buf_mark[b] > 0:
            sy.wait_ge(p.s_pe, buf_mark[b])
        fq = slice(q * 512, (q + 1) * 512)
        tok = p.dma_open(sy)
        p.dma_put(tok, sy.dma_start(
            out=wg[b][:].rearrange("p (k f) -> p k f", k=8),
            in_=dram["wg_t"][eq].rearrange("(k p) f -> p k f", p=128)[:, :, fq]))
        p.dma_put(tok, sy.dma_start(
            out=wu[b][:].rearrange("p (k f) -> p k f", k=8),
            in_=dram["wu_t"][eq].rearrange("(k p) f -> p k f", p=128)[:, :, fq]))
        p.dma_put(tok, sy.dma_start(
            out=wd[b][:].rearrange("p (k d) -> p k d", k=4),
            in_=dram["wd_t"][eq].rearrange(
                "(k p) d -> p k d", p=128)[:, q * 4:(q + 1) * 4]))
        wq_tok[gidx] = p.dma_close(tok)

    issue_wq(0)
    issue_wq(1)

    xtok_toks = {}

    def issue_xtok(e2):
        tok = p.dma_open(sy)
        p.dma_put(tok, sy.dma_start(
            out=xtv, in_=dram[f"xtok{e2}"].rearrange("(n p) w -> p n w", p=128)))
        xtok_toks[e2] = p.dma_close(tok)

    issue_xtok(0)
    for e in range(E):
        xtok_t = xtok_toks[e]
        if e == 0:
            p.dma_wait(ve, htok)

        p.dma_wait(ve, xtok_t)
        p.dma_wait(ac, xtok_t)
        ac.wait_ge(p.s_pe, p.semv["s_pe"])
        xsq = eosb[:, 0:D]
        for sl in range(NSL):
            xs = xt[:, sl * WID:sl * WID + D]
            if sl > 0:
                ac.wait_ge(p.s_ve, p.semv["s_ve"])
            p.inc("s_ac", ac.activation(xsq, xs, AF.Square))
            ve.tensor_reduce(stats[:, sl:sl + 1], xs.unsqueeze(1), AX.X, AL.add)
            ve.wait_ge(p.s_ac, p.semv["s_ac"])
            p.inc("s_ve", ve.tensor_reduce(
                stats[:, NSL + sl:NSL + sl + 1], xsq.unsqueeze(1),
                AX.X, AL.add))
        p.wait(ve, "s_ve")
        ve.tensor_scalar_mul(stats[:, 2 * NSL:3 * NSL], stats[:, 0:NSL], 1.0 / D)
        p.vf(ve.tensor_scalar_mul(stats[:, 3 * NSL:4 * NSL],
                                  stats[:, NSL:2 * NSL], 1.0 / D))
        p.vf(ve.tensor_mul(stats[:, 0:NSL], stats[:, 2 * NSL:3 * NSL],
                           stats[:, 2 * NSL:3 * NSL]))
        p.vf(ve.tensor_sub(stats[:, 3 * NSL:4 * NSL], stats[:, 3 * NSL:4 * NSL],
                           stats[:, 0:NSL]))
        p.inc("s_ve", ve.tensor_scalar_add(stats[:, 3 * NSL:4 * NSL],
                                           stats[:, 3 * NSL:4 * NSL], LN_EPS))
        p.wait(ac, "s_ve")
        ac.activation(stats[:, 3 * NSL:4 * NSL],
                      stats[:, 3 * NSL:4 * NSL], AF.Sqrt)
        p.inc("s_ac", ac.activation(eosb[:, 0:512], eosb[:, 0:512],
                                    AF.Square))
        p.wait(ve, "s_ac")
        p.inc("s_ve", ve.reciprocal(stats[:, 3 * NSL:4 * NSL],
                                    stats[:, 3 * NSL:4 * NSL]))
        p.wait(ve, "s_ve")
        p.vf(ve.tensor_mul(stats[:, 2 * NSL:3 * NSL], stats[:, 2 * NSL:3 * NSL],
                           stats[:, 3 * NSL:4 * NSL]))
        p.vf(ve.tensor_scalar_mul(stats[:, 2 * NSL:3 * NSL],
                                  stats[:, 2 * NSL:3 * NSL], -1.0))
        for sl in range(NSL):
            ve.tensor_scalar(xt[:, sl * WID:sl * WID + D],
                             xt[:, sl * WID:sl * WID + D],
                             stats[:, 3 * NSL + sl:3 * NSL + sl + 1],
                             stats[:, 2 * NSL + sl:2 * NSL + sl + 1],
                             AL.mult, AL.add)
        ve.tensor_copy(wcol[:].unsqueeze(2), xtv[:, :, D + e:D + e + 1])
        p.vf(ve.tensor_copy(idcf[:].unsqueeze(2), xtv[:, :, WID - 1:WID]))
        p.vf(ve.tensor_scalar(idcf[:], idcf[:], hofftt[:, 0:1], None,
                              AL.subtract))
        p.vf(ve.tensor_single_scalar(idcm[:], idcf[:], 0.0, AL.is_lt))
        p.vf(ve.scalar_tensor_tensor(idcf[:], idcm[:], hofftt[:, 1:2], idcf[:],
                                     AL.mult, AL.add))
        p.inc("s_ve", ve.tensor_copy(idc[:], idcf[:]))
        rn_ve = p.semv["s_ve"]

        sc.wait_ge(p.s_ve, rn_ve)
        idtok = p.dma_open(sc)
        p.dma_put(idtok, sc.dma_start(
            out=dram["idT_sp"][e * NSL:(e + 1) * NSL, :].rearrange("s p -> p s"),
            in_=idc[:], allow_slow_non_contiguous=True))
        p.dma_close(idtok)

        te.wait_ge(p.s_ve, rn_ve)
        nt = 0
        tr_ve = p.semv["s_ve"]
        for dc in range(8):
            for sl in range(NSL):
                if nt >= 2:
                    te.wait_ge(p.s_ve, tr_ve + nt - 1)
                i = te.transpose(
                    trp[nt % 2][:],
                    xt[:, sl * WID + dc * 128:sl * WID + (dc + 1) * 128],
                    ident[:])
                p.inc("s_pe", i)
                ve.wait_ge(p.s_pe, p.semv["s_pe"])
                p.inc("s_ve", ve.tensor_copy(
                    xtT[:, dc * CAP + sl * 128:dc * CAP + (sl + 1) * 128],
                    trp[nt % 2][:]))
                nt += 1

        if e + 1 < E:
            sy.wait_ge(p.s_pe, p.semv["s_pe"])
            sy.wait_ge(p.s_ve, p.semv["s_ve"])
            issue_xtok(e + 1)

        for q in range(4):
            gidx = e * 4 + q
            b = gidx % 2
            p.dma_wait(te, wq_tok[gidx])
            for nh in range(2):
                for fcl in range(4):
                    te.wait_ge(p.s_ve, p.semv["s_ve"])
                    for k in range(8):
                        te.matmul(gps[:, 0:320],
                                  wg[b][:, k * 512 + fcl * 128:k * 512 + (fcl + 1) * 128],
                                  xtT[:, k * CAP + nh * 320:k * CAP + nh * 320 + 320],
                                  start=(k == 0), stop=(k == 7))
                    for k in range(8):
                        i = te.matmul(ups[:, 0:320],
                                      wu[b][:, k * 512 + fcl * 128:k * 512 + (fcl + 1) * 128],
                                      xtT[:, k * CAP + nh * 320:k * CAP + nh * 320 + 320],
                                      start=(k == 0), stop=(k == 7))
                    p.inc("s_pe", i)
                    ac.wait_ge(p.s_pe, p.semv["s_pe"])
                    p.inc("s_ac", ac.activation(gel[:, 0:320], gps[:, 0:320],
                                                AF.Gelu))
                    ve.wait_ge(p.s_ac, p.semv["s_ac"])
                    p.inc("s_ve", ve.tensor_mul(
                        hsb[:, fcl * 320:fcl * 320 + 320],
                        gel[:, 0:320], ups[:, 0:320]))
                for m in range(8):
                    te.wait_ge(p.s_ve, p.semv["s_ve"])
                    for fcl in range(4):
                        i = te.matmul(eop[:, 0:320],
                                      wd[b][:, fcl * 1024 + m * 128:fcl * 1024 + (m + 1) * 128],
                                      hsb[:, fcl * 320:fcl * 320 + 320],
                                      start=(fcl == 0), stop=(fcl == 3))
                    p.inc("s_pe", i)
                    ve.wait_ge(p.s_pe, p.semv["s_pe"])
                    dst = eosb[:, m * CAP + nh * 320:m * CAP + nh * 320 + 320]
                    if q == 0:
                        p.inc("s_ve", ve.tensor_copy(dst, eop[:, 0:320]))
                    else:
                        p.inc("s_ve", ve.scalar_tensor_tensor(
                            dst, eop[:, 0:320], 1.0, dst, AL.mult, AL.add))
            buf_mark[b] = p.semv["s_pe"]
            issue_wq(gidx + 2)

        for sl in range(NSL):
            te.wait_ge(p.s_ve, p.semv["s_ve"])
            for dc in range(8):
                i = te.transpose(top[:, dc * 128:(dc + 1) * 128],
                                 eosb[:, dc * CAP + sl * 128:dc * CAP + (sl + 1) * 128],
                                 ident[:])
            p.inc("s_pe", i)
            ve.wait_ge(p.s_pe, p.semv["s_pe"])
            if len(st_eoT) >= 2:
                p.dma_wait(ve, st_eoT[-2])
            p.inc("s_ve", ve.tensor_scalar(
                eoT[sl % 2][:], top[:], wcol[:, sl:sl + 1], None, AL.mult))
            sc.wait_ge(p.s_ve, p.semv["s_ve"])
            etok = p.dma_open(sc)
            p.dma_put(etok, sc.dma_start(out=eoTr[e * NSL + sl],
                                         in_=eoT[sl % 2][:]))
            st_eoT.append(p.dma_close(etok))


def eoFori(p, st):
    nc = p.nc
    gp = nc.gpsimd
    dram = p.dram
    eob = p.sb("eof_eob", [128, D], stack=st)
    ixt = p.sb("eof_ixt", [128, 1], I32, stack=st)
    eoTr = dram["eoT_sp"]
    idTr = dram["idT_sp"].rearrange("u p -> p u")
    gbase = p.semv["s_gp"]
    PER = 3 * 16
    NU = E * NSL
    with gp.Fori(0, NU) as u:
        gp.wait_ge(p.s_gp, u * PER + gbase)
        gp.dma_start(out=eob[:],
                     in_=eoTr[DynSlice(u * 128, 128), :]).then_inc(p.s_gp, 16)
        gp.dma_start(out=ixt[:], in_=idTr[:, DynSlice(u, 1)],
                     allow_slow_non_contiguous=True).then_inc(p.s_gp, 16)
        gp.wait_ge(p.s_gp, u * PER + gbase + 32)
        gp.indirect_dma_start(
            dram["macc"], IndirectOffsetOnAxis(ixt[:], 0), eob[:], None,
            bounds_check=HALF, oob_is_err=False,
            compute_op=AL.add).then_inc(p.s_gp, 16)
    p.semv["s_gp"] += NU * PER
    gp.wait_ge(p.s_gp, p.semv["s_gp"])


def ph4a(p, st, *, onesr, onesc, lnp, hv, wo, w_ld):
    hoff_tok = hv * 128
    """xn (from ab0_sp), rw=r*wkv, tm=Wo@rw, y1, LN2->xn3; spills.
    wo/w_ld: prefetched during phM."""
    nc = p.nc
    sy, te, ve, ac, sc = nc.sync, nc.tensor, nc.vector, nc.scalar, nc.scalar
    dram = p.dram

    xb = p.sb("p4_xb", [128, 8 * BW], stack=st)
    xsq = p.sb("p4_xsq", [128, 8 * BW], stack=st)
    xn = p.sb("p4_xn", [128, 8 * BW], stack=st)
    rbuf = p.sb("p4_rbuf", [128, 8 * BW], stack=st)
    wkvb = p.sb("p4_wkvb", [128, 8 * BW], stack=st)
    y1 = p.sb("p4_y1", [128, 8 * BW], stack=st)
    xn3 = p.sb("p4_xn3", [128, 8 * BW], stack=st)
    ab = p.sb("p4_ab", [128, 4 * BW], stack=st)
    srow = p.sb("p4_srow", [1, 2 * BW], stack=st)
    a0r = p.sb("p4_a0r", [1, BW], stack=st)
    b0r = p.sb("p4_b0r", [1, BW], stack=st)
    a1r = p.sb("p4_a1r", [1, BW], stack=st)
    b1r = p.sb("p4_b1r", [1, BW], stack=st)
    scr = p.sb("p4_scr", [1, BW], stack=st)

    bc_ps = p.ps("p4_bc", [128, BW], stack=st)
    st_ps = p.ps("p4_st", [1, 2 * BW], stack=st)
    mm_ps = [p.ps(f"p4_mm{i}", [128, BW], stack=st) for i in range(2)]

    xr = dram["x_own_fm"].rearrange("(k p) t -> k p t", p=128)

    def bcast(row_slice, dst_slice):
        p.wait(te, "s_ve")
        i = te.matmul(bc_ps[:], onesr[:], row_slice, start=True, stop=True)
        p.inc("s_pe", i)
        p.wait(ve, "s_pe")
        p.inc("s_ve", ve.tensor_copy(dst_slice, bc_ps[:]))

    def ln_full(src_tile, dst_tile, g_off, b_off):
        p.wait(ac, "s_ve")
        p.inc("s_ac", ac.activation(xsq[:], src_tile[:], AF.Square))
        p.wait(te, "s_ac")
        for k in range(8):
            kw = slice(k * BW, (k + 1) * BW)
            te.matmul(st_ps[:, 0:BW], onesc[:], src_tile[:, kw],
                      start=(k == 0), stop=(k == 7))
            i = te.matmul(st_ps[:, BW:2 * BW], onesc[:], xsq[:, kw],
                          start=(k == 0), stop=(k == 7))
        p.inc("s_pe", i)
        p.wait(ve, "s_pe")
        ve.tensor_copy(srow[:], st_ps[:])
        ln_rows(p, srow, a1r[:], b1r[:], scr[:])
        bcast(a1r[:], ab[:, 2 * BW:3 * BW])
        bcast(b1r[:], ab[:, 3 * BW:4 * BW])
        a1b = ab[:, 2 * BW:3 * BW].unsqueeze(1).broadcast_to([128, 8, BW])
        b1b = ab[:, 3 * BW:4 * BW].unsqueeze(1).broadcast_to([128, 8, BW])
        sv = src_tile[:].rearrange("p (k w) -> p k w", k=8)
        dv = dst_tile[:].rearrange("p (k w) -> p k w", k=8)
        ve.tensor_mul(dv, sv, a1b)
        ve.tensor_add(dv, dv, b1b)
        for k in range(8):
            kw = slice(k * BW, (k + 1) * BW)
            ve.tensor_scalar(dst_tile[:, kw], dst_tile[:, kw],
                             lnp[:, g_off + k:g_off + k + 1],
                             lnp[:, b_off + k:b_off + k + 1], AL.mult, AL.add)
        p.inc("s_ve", ve.engine_nop())

    st_mark = []
    for blk in range(NBO):
        tw = slice(blk * BW, (blk + 1) * BW)
        if blk >= 1:
            p.dma_wait(sy, st_mark[blk - 1])
            sy.wait_ge(p.s_ve, p.semv["s_ve"])
        tok = p.dma_open(sy)
        for k in range(8):
            p.dma_put(tok, sy.dma_start(out=xb[:, k * BW:(k + 1) * BW],
                                        in_=xr[k, :, tw]))
        p.dma_put(tok, sy.dma_start(
            out=rbuf[:].rearrange("p (k w) -> p k w", k=8),
            in_=dram["r_sp"][:, :, DynSlice(hoff_tok + blk * BW, BW)].rearrange(
                "k p w -> p k w")))
        p.dma_put(tok, sy.dma_start(
            out=xsq[:].rearrange("p (k w) -> p k w", k=8),
            in_=dram["e_sp"][:, :, DynSlice(hoff_tok + blk * BW, BW)].rearrange(
                "k p w -> p k w")))
        p.dma_put(tok, sy.dma_start(
            out=wkvb[:].rearrange("p (k w) -> p k w", k=8),
            in_=dram["ev_sp"][:, :, DynSlice(hoff_tok + blk * BW, BW)].rearrange(
                "k p w -> p k w")))
        p.dma_put(tok, sy.dma_start(
            out=a0r[:], in_=dram["ab0_sp"][0:1, DynSlice(hoff_tok + blk * BW, BW)]))
        p.dma_put(tok, sy.dma_start(
            out=b0r[:], in_=dram["ab0_sp"][1:2, DynSlice(hoff_tok + blk * BW, BW)]))
        ldm = p.dma_close(tok)
        if blk == 0:
            p.dma_wait(te, w_ld)

        p.dma_wait(ve, ldm)
        p.dma_wait(te, ldm)
        bcast(a0r[:], ab[:, 0:BW])
        bcast(b0r[:], ab[:, BW:2 * BW])
        a0b = ab[:, 0:BW].unsqueeze(1).broadcast_to([128, 8, BW])
        b0b = ab[:, BW:2 * BW].unsqueeze(1).broadcast_to([128, 8, BW])
        xbv = xb[:].rearrange("p (k w) -> p k w", k=8)
        xnv = xn[:].rearrange("p (k w) -> p k w", k=8)
        ve.tensor_mul(xnv, xbv, a0b)
        ve.tensor_add(xnv, xnv, b0b)
        ve.tensor_scalar_add(wkvb[:], wkvb[:], WKV_EPS)
        p.inc("s_ve", ve.reciprocal(wkvb[:], wkvb[:]))
        p.wait(ve, "s_ve")
        ve.tensor_mul(wkvb[:], xsq[:], wkvb[:])
        p.inc("s_ve", ve.tensor_mul(rbuf[:], rbuf[:], wkvb[:]))
        rw_ve = p.semv["s_ve"]

        te.wait_ge(p.s_ve, rw_ve)
        for m in range(8):
            mw = slice(m * BW, (m + 1) * BW)
            pb = mm_ps[m % 2]
            if m >= 2:
                te.wait_ge(p.s_ve, p.semv["s_ve"])
            for k in range(8):
                i = te.matmul(pb[:],
                              wo[:, k * D + m * 128:k * D + m * 128 + 128],
                              rbuf[:, k * BW:(k + 1) * BW],
                              start=(k == 0), stop=(k == 7))
            p.inc("s_pe", i)
            ve.wait_ge(p.s_pe, p.semv["s_pe"])
            p.inc("s_ve", ve.scalar_tensor_tensor(
                y1[:, mw], pb[:], 1.0, xn[:, mw], AL.mult, AL.add))

        ln_full(y1, xn3, 16, 24)

        sc.wait_ge(p.s_ve, p.semv["s_ve"])
        stok = p.dma_open(sc)
        for nm, tile in (("xn_sp", xn), ("y1_sp", y1), ("xn3_sp", xn3)):
            p.dma_put(stok, sc.dma_start(
                out=dram[nm][:, :, tw].rearrange("k p w -> p k w"),
                in_=tile[:].rearrange("p (k w) -> p k w", k=8)))
        st_mark.append(p.dma_close(stok))


def ph4sh(p, st):
    nc = p.nc
    sy, te, ve, ac, sc = nc.sync, nc.tensor, nc.vector, nc.scalar, nc.scalar
    dram = p.dram

    wsg = [p.sb(f"psh_wsg{i}", [128, 8 * 512], stack=st) for i in range(2)]
    wsu = [p.sb(f"psh_wsu{i}", [128, 8 * 512], stack=st) for i in range(2)]
    wsd = [p.sb(f"psh_wsd{i}", [128, 4 * 1024], stack=st) for i in range(2)]
    xnb = [p.sb(f"psh_xn{i}", [128, 8 * BW], stack=st) for i in range(2)]
    gel = p.sb("psh_gel", [128, BW], stack=st)
    hs = p.sb("psh_hs", [128, 4 * BW], stack=st)
    sh = [p.sb(f"psh_sh{i}", [128, 8 * BW], stack=st) for i in range(2)]

    gsp = p.ps("psh_gsp", [128, BW], stack=st)
    usp = p.ps("psh_usp", [128, BW], stack=st)
    dsp = p.ps("psh_dsp", [128, BW], stack=st)

    wq_tok = {}
    buf_mark = [0, 0]

    def issue_wq(q):
        if q >= 4:
            return
        b = q % 2
        if buf_mark[b] > 0:
            sy.wait_ge(p.s_pe, buf_mark[b])
        fq = slice(q * 512, (q + 1) * 512)
        tok = p.dma_open(sy)
        p.dma_put(tok, sy.dma_start(
            out=wsg[b][:].rearrange("p (k f) -> p k f", k=8),
            in_=dram["wsg_t"].rearrange("(k p) f -> p k f", p=128)[:, :, fq]))
        p.dma_put(tok, sy.dma_start(
            out=wsu[b][:].rearrange("p (k f) -> p k f", k=8),
            in_=dram["wsu_t"].rearrange("(k p) f -> p k f", p=128)[:, :, fq]))
        p.dma_put(tok, sy.dma_start(
            out=wsd[b][:].rearrange("p (k e) -> p k e", k=4),
            in_=dram["wsd_t"].rearrange(
                "(k p) e -> p k e", p=128)[:, q * 4:(q + 1) * 4]))
        wq_tok[q] = p.dma_close(tok)

    issue_wq(0)
    issue_wq(1)

    prev_pe = prev_st = None
    for q in range(4):
        b = q % 2
        st_mark = {}
        pe_mark = {}
        ldms = {}

        def issue_load(blk2, q=q, prev_pe=prev_pe, prev_st=prev_st):
            if blk2 >= NBO:
                return
            bi = blk2 % 2
            tw2 = slice(blk2 * BW, (blk2 + 1) * BW)
            if blk2 >= 2:
                sy.wait_ge(p.s_pe, pe_mark[blk2 - 2])
                p.dma_wait(sy, st_mark[blk2 - 2])
            elif prev_pe is not None:
                sy.wait_ge(p.s_pe, prev_pe[NBO - 2 + blk2])
                p.dma_wait(sy, prev_st[NBO - 2 + blk2])
            tok = p.dma_open(sy)
            for k in range(8):
                p.dma_put(tok, sy.dma_start(
                    out=xnb[bi][:, k * BW:(k + 1) * BW],
                    in_=dram["xn_sp"][k, :, tw2]))
            if q >= 1:
                p.dma_put(tok, sy.dma_start(
                    out=sh[bi][:].rearrange("p (k w) -> p k w", k=8),
                    in_=dram["sh_sp"][:, :, tw2].rearrange("k p w -> p k w")))
            ldms[blk2] = p.dma_close(tok)

        issue_load(0)
        issue_load(1)
        for blk in range(NBO):
            bi = blk % 2
            tw = slice(blk * BW, (blk + 1) * BW)
            p.dma_wait(te, ldms[blk])
            p.dma_wait(te, wq_tok[q])
            for fc in range(4):
                te.wait_ge(p.s_ve, p.semv["s_ve"])
                for k in range(8):
                    te.matmul(gsp[:],
                              wsg[b][:, k * 512 + fc * 128:k * 512 + (fc + 1) * 128],
                              xnb[bi][:, k * BW:(k + 1) * BW],
                              start=(k == 0), stop=(k == 7))
                for k in range(8):
                    i = te.matmul(usp[:],
                                  wsu[b][:, k * 512 + fc * 128:k * 512 + (fc + 1) * 128],
                                  xnb[bi][:, k * BW:(k + 1) * BW],
                                  start=(k == 0), stop=(k == 7))
                p.inc("s_pe", i)
                ac.wait_ge(p.s_pe, p.semv["s_pe"])
                p.inc("s_ac", ac.activation(gel[:], gsp[:], AF.Gelu))
                ve.wait_ge(p.s_ac, p.semv["s_ac"])
                p.inc("s_ve", ve.tensor_mul(hs[:, fc * BW:(fc + 1) * BW],
                                            gel[:], usp[:]))
            for m in range(8):
                te.wait_ge(p.s_ve, p.semv["s_ve"])
                for fc in range(4):
                    i = te.matmul(dsp[:],
                                  wsd[b][:, fc * D + m * 128:fc * D + (m + 1) * 128],
                                  hs[:, fc * BW:(fc + 1) * BW],
                                  start=(fc == 0), stop=(fc == 3))
                p.inc("s_pe", i)
                ve.wait_ge(p.s_pe, p.semv["s_pe"])
                if m == 0:
                    if q >= 1:
                        p.dma_wait(ve, ldms[blk])
                    elif blk >= 2:
                        p.dma_wait(ve, st_mark[blk - 2])
                mw = slice(m * BW, (m + 1) * BW)
                if q == 0:
                    p.inc("s_ve", ve.tensor_copy(sh[bi][:, mw], dsp[:]))
                else:
                    p.inc("s_ve", ve.scalar_tensor_tensor(
                        sh[bi][:, mw], dsp[:], 1.0, sh[bi][:, mw],
                        AL.mult, AL.add))
            pe_mark[blk] = p.semv["s_pe"]
            sc.wait_ge(p.s_ve, p.semv["s_ve"])
            stok = p.dma_open(sc)
            p.dma_put(stok, sc.dma_start(
                out=dram["sh_sp"][:, :, tw].rearrange("k p w -> p k w"),
                in_=sh[bi][:].rearrange("p (k w) -> p k w", k=8)))
            st_mark[blk] = p.dma_close(stok)
            issue_load(blk + 2)
        prev_pe, prev_st = pe_mark, st_mark
        buf_mark[b] = p.semv["s_pe"]
        issue_wq(q + 2)


def ph4b(p, st):
    nc = p.nc
    sy, te, ve, ac, sc = nc.sync, nc.tensor, nc.vector, nc.scalar, nc.scalar
    dram = p.dram

    wkey = [p.sb(f"pb_wkey{i}", [128, 8 * 1024], stack=st) for i in range(2)]
    wval = [p.sb(f"pb_wval{i}", [128, 8 * 1024], stack=st) for i in range(2)]
    xnb = [p.sb(f"pb_xn{i}", [128, 8 * BW], stack=st) for i in range(2)]
    relu = p.sb("pb_relu", [128, BW], stack=st)
    kk = p.sb("pb_kk", [128, 8 * BW], stack=st)
    vv = p.sb("pb_vv", [128, 8 * BW], stack=st)

    kkp = p.ps("pb_kkp", [128, BW], stack=st)
    vvp = p.ps("pb_vvp", [128, BW], stack=st)

    wq_tok = {}
    buf_mark = [0, 0]

    def issue_wq(q):
        if q >= 4:
            return
        b = q % 2
        if buf_mark[b] > 0:
            sy.wait_ge(p.s_pe, buf_mark[b])
        tok = p.dma_open(sy)
        p.dma_put(tok, sy.dma_start(
            out=wkey[b][:].rearrange("p (k f) -> p k f", k=8),
            in_=dram["wkey_t"].rearrange("(k p) f -> p k f", p=128)[
                :, :, q * 1024:(q + 1) * 1024]))
        p.dma_put(tok, sy.dma_start(
            out=wval[b][:].rearrange("p (k e) -> p k e", k=8),
            in_=dram["wval_t"].rearrange("(k p) e -> p k e", p=128)[
                :, q * 8:(q + 1) * 8]))
        wq_tok[q] = p.dma_close(tok)

    issue_wq(0)
    issue_wq(1)

    prev_pe = prev_st = None
    for q in range(4):
        b = q % 2
        st_mark = {}
        pe_mark = {}
        ldms = {}
        vvtok = {}

        def issue_load(blk2, q=q, prev_pe=prev_pe, prev_st=prev_st):
            if blk2 >= NBO:
                return
            bi = blk2 % 2
            tw2 = slice(blk2 * BW, (blk2 + 1) * BW)
            if blk2 >= 2:
                sy.wait_ge(p.s_pe, pe_mark[blk2 - 2])
            elif prev_pe is not None:
                sy.wait_ge(p.s_pe, prev_pe[NBO - 2 + blk2])
            tok = p.dma_open(sy)
            for k in range(8):
                p.dma_put(tok, sy.dma_start(
                    out=xnb[bi][:, k * BW:(k + 1) * BW],
                    in_=dram["xn3_sp"][k, :, tw2]))
            ldms[blk2] = p.dma_close(tok)

        def issue_vv(blk2, q=q, prev_st=prev_st):
            if blk2 >= NBO or q == 0:
                return
            tw2 = slice(blk2 * BW, (blk2 + 1) * BW)
            if blk2 >= 1:
                p.dma_wait(sy, st_mark[blk2 - 1])
            elif prev_st is not None:
                p.dma_wait(sy, prev_st[NBO - 1])
            tok = p.dma_open(sy)
            p.dma_put(tok, sy.dma_start(
                out=vv[:].rearrange("p (k w) -> p k w", k=8),
                in_=dram["vvh_sp"][:, :, tw2].rearrange("k p w -> p k w")))
            vvtok[blk2] = p.dma_close(tok)

        issue_load(0)
        issue_load(1)
        issue_vv(0)
        for blk in range(NBO):
            bi = blk % 2
            tw = slice(blk * BW, (blk + 1) * BW)
            p.dma_wait(te, ldms[blk])
            p.dma_wait(te, wq_tok[q])
            for fc in range(8):
                te.wait_ge(p.s_ve, p.semv["s_ve"])
                for k in range(8):
                    i = te.matmul(kkp[:],
                                  wkey[b][:, k * 1024 + fc * 128:k * 1024 + (fc + 1) * 128],
                                  xnb[bi][:, k * BW:(k + 1) * BW],
                                  start=(k == 0), stop=(k == 7))
                p.inc("s_pe", i)
                ac.wait_ge(p.s_pe, p.semv["s_pe"])
                p.inc("s_ac", ac.activation(relu[:], kkp[:], AF.Relu))
                ve.wait_ge(p.s_ac, p.semv["s_ac"])
                p.inc("s_ve", ve.tensor_mul(kk[:, fc * BW:(fc + 1) * BW],
                                            relu[:], relu[:]))
            for m in range(8):
                te.wait_ge(p.s_ve, p.semv["s_ve"])
                for fc in range(8):
                    i = te.matmul(vvp[:],
                                  wval[b][:, fc * D + m * 128:fc * D + (m + 1) * 128],
                                  kk[:, fc * BW:(fc + 1) * BW],
                                  start=(fc == 0), stop=(fc == 7))
                p.inc("s_pe", i)
                ve.wait_ge(p.s_pe, p.semv["s_pe"])
                if m == 0:
                    if q >= 1:
                        p.dma_wait(ve, vvtok[blk])
                    elif blk >= 1:
                        p.dma_wait(ve, st_mark[blk - 1])
                mw = slice(m * BW, (m + 1) * BW)
                if q == 0:
                    p.inc("s_ve", ve.tensor_copy(vv[:, mw], vvp[:]))
                else:
                    p.inc("s_ve", ve.scalar_tensor_tensor(
                        vv[:, mw], vvp[:], 1.0, vv[:, mw],
                        AL.mult, AL.add))
            pe_mark[blk] = p.semv["s_pe"]
            sc.wait_ge(p.s_ve, p.semv["s_ve"])
            stok = p.dma_open(sc)
            p.dma_put(stok, sc.dma_start(
                out=dram["vvh_sp"][:, :, tw].rearrange("k p w -> p k w"),
                in_=vv[:].rearrange("p (k w) -> p k w", k=8)))
            st_mark[blk] = p.dma_close(stok)
            issue_load(blk + 2)
            issue_vv(blk + 1)
        prev_pe, prev_st = pe_mark, st_mark
        buf_mark[b] = p.semv["s_pe"]
        issue_wq(q + 2)


def ph4c(p, st, *, ident):
    nc = p.nc
    sy, te, ve, ac, sc = nc.sync, nc.tensor, nc.vector, nc.scalar, nc.scalar
    dram = p.dram

    wrec = p.sb("pc_wrec", [128, 8 * D], stack=st)
    wtok = p.dma_open(sy)
    p.dma_put(wtok, sy.dma_start(
        out=wrec[:].rearrange("p (k e) -> p k e", k=8),
        in_=dram["wrec_t"].rearrange("(k p) e -> p k e", p=128)))
    w_ld = p.dma_close(wtok)

    xn3b = p.sb("pc_xn3", [128, 8 * BW], stack=st)
    vvb = p.sb("pc_vv", [128, 8 * BW], stack=st)
    y1b = p.sb("pc_y1", [128, 8 * BW], stack=st)
    shb = p.sb("pc_sh", [128, 8 * BW], stack=st)
    xbb = p.sb("pc_xb", [128, 8 * BW], stack=st)
    rr = p.sb("pc_rr", [128, 8 * BW], stack=st)
    mrow = p.sb("pc_mrow", [128, D], stack=st)
    orow = p.sb("pc_orow", [128, D], stack=st)

    mm_ps = [p.ps(f"pc_mm{i}", [128, BW], stack=st) for i in range(2)]
    trp = p.ps("pc_trp", [128, 1024], stack=st)

    xr = dram["x_own_fm"].rearrange("(k p) t -> k p t", p=128)
    outr = dram["out"].rearrange("(n p) d -> n p d", p=128)
    maccr = dram["macc"][0:HALF].rearrange("(n p) d -> n p d", p=128)

    for blk in range(NBO):
        tw = slice(blk * BW, (blk + 1) * BW)
        if blk >= 1:
            sy.wait_ge(p.s_ve, p.semv["s_ve"])
        tok = p.dma_open(sy)
        for k in range(8):
            p.dma_put(tok, sy.dma_start(out=xbb[:, k * BW:(k + 1) * BW],
                                        in_=xr[k, :, tw]))
        for nm, tile in (("xn3_sp", xn3b), ("vvh_sp", vvb),
                         ("y1_sp", y1b), ("sh_sp", shb)):
            p.dma_put(tok, sy.dma_start(
                out=tile[:].rearrange("p (k w) -> p k w", k=8),
                in_=dram[nm][:, :, tw].rearrange("k p w -> p k w")))
        ldm = p.dma_close(tok)
        if blk == 0:
            p.dma_wait(te, w_ld)

        p.dma_wait(te, ldm)
        for m in range(8):
            mw = slice(m * BW, (m + 1) * BW)
            pb = mm_ps[m % 2]
            if m >= 2:
                te.wait_ge(p.s_ac, p.semv["s_ac"])
            for k in range(8):
                i = te.matmul(pb[:],
                              wrec[:, k * D + m * 128:k * D + m * 128 + 128],
                              xn3b[:, k * BW:(k + 1) * BW],
                              start=(k == 0), stop=(k == 7))
            p.inc("s_pe", i)
            ac.wait_ge(p.s_pe, p.semv["s_pe"])
            p.inc("s_ac", ac.activation(rr[:, mw], pb[:], AF.Sigmoid))

        ve.wait_ge(p.s_ac, p.semv["s_ac"])
        p.dma_wait(ve, ldm)
        ve.tensor_mul(rr[:], rr[:], vvb[:])
        ve.tensor_add(rr[:], rr[:], y1b[:])
        ve.tensor_add(rr[:], rr[:], shb[:])
        p.inc("s_ve", ve.tensor_add(rr[:], rr[:], xbb[:]))
        base_ve = p.semv["s_ve"]

        te.wait_ge(p.s_ve, base_ve)
        ot_prev = None
        for t4 in range(4):
            n = blk * 4 + t4
            sy.wait_ge(p.s_ve, p.semv["s_ve"])
            mtok = p.dma_open(sy)
            p.dma_put(mtok, sy.dma_start(out=mrow[:], in_=maccr[n]))
            p.dma_close(mtok)
            if t4 > 0:
                te.wait_ge(p.s_ve, p.semv["s_ve"])
            for k in range(8):
                i = te.transpose(trp[:, k * 128:(k + 1) * 128],
                                 rr[:, k * BW + t4 * 128:k * BW + t4 * 128 + 128],
                                 ident[:])
            p.inc("s_pe", i)
            ve.wait_ge(p.s_pe, p.semv["s_pe"])
            p.dma_wait(ve, mtok)
            if ot_prev is not None:
                p.dma_wait(ve, ot_prev)
            p.inc("s_ve", ve.scalar_tensor_tensor(
                orow[:], trp[:], 1.0, mrow[:], AL.mult, AL.add))
            sc.wait_ge(p.s_ve, p.semv["s_ve"])
            otok = p.dma_open(sc)
            p.dma_put(otok, sc.dma_start(out=outr[n], in_=orow[:]))
            ot_prev = p.dma_close(otok)

    p.dma_wait_all(sc)
    p.dma_wait_all(sy)
    p.wait(nc.gpsimd, "s_gp")


def make_inmaps(inputs):
    x = np.ascontiguousarray(inputs["L0_attn_x"], np.float32)

    def tr(a):
        return np.ascontiguousarray(np.asarray(a, np.float32).T)

    def chunks8(v):
        return np.ascontiguousarray(np.asarray(v, np.float32).reshape(8, 128).T)

    shared = {
        "wr_t": tr(inputs["L0_attn_Wr"]),
        "ww_t": tr(inputs["L0_attn_Ww"]),
        "wk_t": tr(inputs["L0_attn_Wk"]),
        "wv_t": tr(inputs["L0_attn_Wv"]),
        "wo_t": tr(inputs["L0_attn_Wo"]),
        "wrec_t": tr(inputs["L0_attn_Wrec"]),
        "wkey_t": tr(inputs["L0_attn_Wkey"]),
        "wval_t": tr(inputs["L0_attn_Wval"]),
        "wsg_t": tr(inputs["L0_ffn_Ws_g"]),
        "wsu_t": tr(inputs["L0_ffn_Ws_u"]),
        "wsd_t": tr(inputs["L0_ffn_Ws_d"]),
        "gate_t": tr(inputs["L0_ffn_gate_w"]),
        "wg_t": np.ascontiguousarray(
            np.asarray(inputs["L0_ffn_Wg"], np.float32).transpose(0, 2, 1)),
        "wu_t": np.ascontiguousarray(
            np.asarray(inputs["L0_ffn_Wu"], np.float32).transpose(0, 2, 1)),
        "wd_t": np.ascontiguousarray(
            np.asarray(inputs["L0_ffn_Wd"], np.float32).transpose(0, 2, 1)),
        "ln1g": chunks8(inputs["L0_attn_ln1_g"]),
        "ln1b": chunks8(inputs["L0_attn_ln1_b"]),
        "ln2g": chunks8(inputs["L0_attn_ln2_g"]),
        "ln2b": chunks8(inputs["L0_attn_ln2_b"]),
        "td": chunks8(inputs["L0_attn_time_decay"]),
        "ident": np.eye(128, dtype=np.float32),
        "tri": np.ascontiguousarray(
            np.triu(np.ones((128, 128), np.float32), 1)),
        "onesc": np.ones((128, 1), np.float32),
        "onesr": np.ones((1, 128), np.float32),
        "tokidG": np.ascontiguousarray(
            (np.arange(CC)[None, :] * 128 + np.arange(128)[:, None] + 1
             ).astype(np.float32)),
    }
    maps = []
    for c in range(8):
        b, h = c // 2, c % 2
        xb = x[b]
        own = xb[h * HALF:(h + 1) * HALF]
        m = dict(shared)
        m["x_fm"] = np.ascontiguousarray(xb.T)
        m["x_own_tm"] = np.ascontiguousarray(own)
        m["x_own_fm"] = np.ascontiguousarray(own.T)
        mc = np.zeros((128, CC), np.float32)
        mc[:, h * CCH:(h + 1) * CCH] = 1.0
        m["maskc"] = mc
        m["hseg"] = np.array([[h * CCH]], np.int32)
        m["selinit"] = np.concatenate([np.zeros((128, E), np.float32),
                                       np.ones((128, E), np.float32)], 1)
        m["hofft1"] = np.full((128, 1), h * HALF + 1, np.float32)
        m["hofft2"] = np.full((128, 1), HALF + h * HALF + 1, np.float32)
        maps.append(m)
    return maps


def kernel(**inputs):
    prog = build()
    maps = make_inmaps(inputs)
    res = run_bass_kernel_spmd(prog.nc, maps, core_ids=list(range(8)))
    out = np.empty((4, T, D), np.float32)
    for c in range(8):
        b, h = c // 2, c % 2
        out[b, h * HALF:(h + 1) * HALF] = res.results[c]["out"]
    return out


# version 07
# speedup vs baseline: 1.2017x; 1.0505x over previous
import contextlib
import numpy as np
import concourse.bass as bass
import concourse.mybir as mybir
from concourse.bass_utils import run_bass_kernel_spmd
from concourse.bass_types import DynSlice
from concourse.bass_core_rust import IndirectOffsetOnAxis

F32 = mybir.dt.float32
I32 = mybir.dt.int32
AL = mybir.AluOpType
AF = mybir.ActivationFunctionType
AX = mybir.AxisListType

T = 8192
D = 1024
E = 16
HALF = 4096
CC = 64
CCH = 32
CAP = 640
NSL = CAP // 128
WID = D + E + 1
SENT = 1 << 20
LN_EPS = 1e-5
WKV_EPS = 1e-8
NB = T // 512
NBO = HALF // 512
BW = 512
FEXP = 2048
FSH = 2048
FCM = 4096
TH_ITERS = 38
TH_PAD = 3e-4


class Prog:

    def __init__(self, debug_taps=()):
        self.nc = bass.Bass("TRN2", target_bir_lowering=False, debug=False,
                            detect_race_conditions=False)
        self.dram = {}
        self.taps = {}
        self.debug_taps = set(debug_taps)
        self.stack = contextlib.ExitStack()
        self.semv = {}

    def din(self, name, shape, dtype=F32):
        ap = self.nc.dram_tensor(name, shape, dtype, kind="ExternalInput").ap()
        self.dram[name] = ap
        return ap

    def dout(self, name, shape, dtype=F32):
        ap = self.nc.dram_tensor(name, shape, dtype, kind="ExternalOutput").ap()
        self.dram[name] = ap
        return ap

    def dscr(self, name, shape, dtype=F32):
        ap = self.nc.dram_tensor(name, shape, dtype, kind="Internal").ap()
        self.dram[name] = ap
        return ap

    def sb(self, name, shape, dtype=F32, stack=None):
        return (stack or self.stack).enter_context(
            self.nc.sbuf_tensor(name, shape, dtype))

    def ps(self, name, shape, dtype=F32, stack=None):
        return (stack or self.stack).enter_context(
            self.nc.psum_tensor(name, shape, dtype))

    def sem(self, name):
        h = self.stack.enter_context(self.nc.semaphore(name))
        self.semv[name] = 0
        return h

    def inc(self, name, instr, n=1):
        instr.then_inc(getattr(self, name), n)
        self.semv[name] += n
        return self.semv[name]

    def dinc(self, name, instr):
        return self.inc(name, instr, 16)

    def wait(self, eng, name, v=None):
        eng.wait_ge(getattr(self, name), self.semv[name] if v is None else v)

    def vf(self, instr):
        self.inc("s_ve", instr)
        self.nc.vector.wait_ge(self.s_ve, self.semv["s_ve"])

    def dma_open(self, eng):
        j = self._rr
        self._rr = (self._rr + 1) % 8
        if self._dtarget[j] > 0:
            eng.wait_ge(self.dsem[j], self._dtarget[j])
        return [j, None]

    def dma_put(self, tok, instr):
        j = tok[0]
        instr.then_inc(self.dsem[j], 16)
        self.semv[f"s_d{j}"] += 16

    def dma_close(self, tok):
        j = tok[0]
        tok[1] = self.semv[f"s_d{j}"]
        self._dtarget[j] = tok[1]
        return tok

    def dma_batch(self, eng, emits):
        tok = self.dma_open(eng)
        for fn in emits:
            self.dma_put(tok, fn())
        return self.dma_close(tok)

    def dma_wait(self, eng, tok):
        eng.wait_ge(self.dsem[tok[0]], tok[1])

    def dma_wait_all(self, eng):
        for j in range(8):
            if self._dtarget[j] > 0:
                eng.wait_ge(self.dsem[j], self._dtarget[j])

    def tap(self, name, src_tile, shape=None):
        if name not in self.debug_taps:
            return
        sc = self.nc.scalar
        self.wait(sc, "s_ve")
        self.wait(sc, "s_ac")
        self.wait(sc, "s_pe")
        shape = shape or list(src_tile.shape)
        ap = self.dout("tap_" + name, shape)
        self.taps[name] = ap
        tok = self.dma_open(sc)
        self.dma_put(tok, sc.dma_start(out=ap, in_=src_tile))
        self.dma_close(tok)
        nc = self.nc
        for eng in (nc.vector, nc.scalar, nc.tensor, nc.gpsimd, nc.sync):
            self.dma_wait(eng, tok)

    def barrier(self, skip_gp=False):
        nc = self.nc
        self.dma_wait_all(nc.sync)
        self.dma_wait_all(nc.scalar)
        engines = (nc.sync, nc.tensor, nc.vector, nc.scalar)
        if not skip_gp:
            self.wait(nc.gpsimd, "s_gp")
            engines = engines + (nc.gpsimd,)
        for eng in engines:
            self.inc("s_bar", eng.nop())
        tgt = self.semv["s_bar"]
        for eng in engines:
            eng.wait_ge(self.s_bar, tgt)


def build(debug_taps=()):
    p = Prog(debug_taps)
    nc = p.nc

    p.din("x_fm", [D, T])
    p.din("x_own_tm", [HALF, D])
    p.din("x_own_fm", [D, HALF])
    for nm in ("wr_t", "ww_t", "wk_t", "wv_t", "wo_t", "wrec_t"):
        p.din(nm, [D, D])
    p.din("wkey_t", [D, FCM])
    p.din("wval_t", [FCM, D])
    p.din("wsg_t", [D, FSH])
    p.din("wsu_t", [D, FSH])
    p.din("wsd_t", [FSH, D])
    p.din("wg_t", [E, D, FEXP])
    p.din("wu_t", [E, D, FEXP])
    p.din("wd_t", [E, FEXP, D])
    p.din("gate_t", [D, E])
    for nm in ("ln1g", "ln1b", "ln2g", "ln2b", "td"):
        p.din(nm, [128, 8])
    p.din("ident", [128, 128])
    p.din("tri", [128, 128])
    p.din("maskc", [128, CC])
    p.din("tokidG", [128, CC])
    p.din("selinit", [128, 2 * E])
    p.din("hofft1", [128, 1])
    p.din("hofft2", [128, 1])
    p.din("onesc", [128, 1])
    p.din("onesr", [1, 128])
    p.din("hseg", [1, 1], I32)

    p.dscr("e_sp", [8, 128, T])
    p.dscr("ev_sp", [8, 128, T])
    p.dscr("r_sp", [8, 128, T])
    p.dscr("wkv_sp", [8, 128, T])
    p.dscr("xn_sp", [8, 128, HALF])
    p.dscr("xn3_sp", [8, 128, HALF])
    p.dscr("y1_sp", [8, 128, HALF])
    p.dscr("sh_sp", [8, 128, HALF])
    p.dscr("vvh_sp", [8, 128, HALF])
    for e in range(E):
        p.dscr(f"xtok{e}", [CAP, WID])
    p.dscr("wid_sp", [128, CC * (E + 1)])
    p.dscr("idxe_sp", [128, CC * E], I32)
    p.dscr("eoT_sp", [E * NSL * 128, D])
    p.dscr("idT_sp", [E * NSL, 128], I32)
    p.dscr("ab0_sp", [2, T])
    p.dscr("macc", [HALF + 1, D])
    p.dout("out", [HALF, D])

    for s in ("s_gp", "s_pe", "s_ve", "s_ac", "s_bar"):
        setattr(p, s, p.sem(s))
    p.dsem = [p.sem(f"s_d{i}") for i in range(8)]
    p._rr = 0
    p._dtarget = [0] * 8

    with nc.Block() as _block:
        emit(p)

    p.stack.close()
    return p


def emit(p):
    nc = p.nc
    sy, te, ve, ac, gp = nc.sync, nc.tensor, nc.vector, nc.scalar, nc.gpsimd

    ident = p.sb("c_ident", [128, 128])
    onesc = p.sb("c_onesc", [128, 1])
    onesr = p.sb("c_onesr", [1, 128])
    lnp = p.sb("c_lnp", [128, 4 * 8])
    decay = p.sb("c_decay", [128, 8])
    hsegt = p.sb("c_hseg", [1, 1], I32)

    dram = p.dram

    tok0 = p.dma_open(sy)
    for nm, dst in (("ident", ident), ("onesc", onesc),
                    ("onesr", onesr), ("hseg", hsegt)):
        p.dma_put(tok0, sy.dma_start(out=dst[:], in_=dram[nm]))
    for i, nm in enumerate(("ln1g", "ln1b", "ln2g", "ln2b")):
        p.dma_put(tok0, sy.dma_start(out=lnp[:, i * 8:(i + 1) * 8], in_=dram[nm]))
    p.dma_put(tok0, sy.dma_start(out=decay[:], in_=dram["td"]))
    p.dma_close(tok0)
    p.dma_wait(ac, tok0)
    p.inc("s_ac", ac.activation(decay[:], decay[:], AF.Exp))

    p.dma_wait(sy, tok0)
    hv = sy.value_load(hsegt[0:1, 0:1])
    p.dma_wait(gp, tok0)
    hvg = gp.value_load(hsegt[0:1, 0:1])

    with contextlib.ExitStack() as sel_st:
        tri = p.sb("m_tri", [128, 128], stack=sel_st)
        gatet = p.sb("m_gate", [128, 8 * E], stack=sel_st)
        maskc = p.sb("m_maskc", [128, CC], stack=sel_st)
        tokidG = p.sb("m_tokidG", [128, CC], stack=sel_st)
        scores3 = p.sb("m_scores3", [128, E * CC], stack=sel_st)
        theta = p.sb("m_theta", [128, E], stack=sel_st)
        gsumP = p.sb("m_gsumP", [128, E], stack=sel_st)
        idxL = p.sb("m_idxL", [128, CC * E], I32, stack=sel_st)
        zeros = p.sb("m_zeros", [128, WID], stack=sel_st)

        tokc = p.dma_open(sy)
        for nm, dst in (("tri", tri), ("maskc", maskc), ("tokidG", tokidG)):
            p.dma_put(tokc, sy.dma_start(out=dst[:], in_=dram[nm]))
        p.dma_put(tokc, sy.dma_start(
            out=gatet[:].rearrange("p (k e) -> p k e", k=8),
            in_=dram["gate_t"].rearrange("(k p) e -> p k e", p=128)))
        p.dma_close(tokc)
        p.inc("s_ve", ve.memset(zeros[:], 0.0))

        p.wait(gp, "s_ve")
        mr = dram["macc"][0:HALF].rearrange("(n p) d -> n p d", p=128)
        for i in range(HALF // 128):
            p.dinc("s_gp", gp.dma_start(out=mr[i], in_=zeros[:, :D]))
        p.dinc("s_gp", gp.dma_start(out=dram["macc"][HALF:HALF + 1, :],
                                    in_=zeros[0:1, 0:D]))
        for e in range(E):
            xr = dram[f"xtok{e}"].rearrange("(n p) w -> n p w", p=128)
            for i in range(NSL):
                p.dinc("s_gp", gp.dma_start(out=xr[i], in_=zeros[:, :WID]))

        p.barrier(skip_gp=True)

        with contextlib.ExitStack() as st:
            ph1(p, st, ident=ident, onesc=onesc, onesr=onesr, gatet=gatet,
                scores3=scores3, gsumP=gsumP)
            p.barrier(skip_gp=True)

        with contextlib.ExitStack() as st:
            selection(p, st, ident=ident, tri=tri, onesc=onesc, onesr=onesr,
                      maskc=maskc, scores3=scores3, theta=theta, gsumP=gsumP,
                      tokidG=tokidG, idxL=idxL)
            p.barrier()

    with contextlib.ExitStack() as st:
        phMprep(p, st, hvg=hvg)
        ph2(p, st, onesr=onesr, onesc=onesc, lnp=lnp, decay=decay)
        p.barrier()

    with contextlib.ExitStack() as st4a:
        wo = p.sb("p4_wo", [128, 8 * D], stack=st4a)
        wotok = p.dma_open(sy)
        p.dma_put(wotok, sy.dma_start(
            out=wo[:].rearrange("p (k e) -> p k e", k=8),
            in_=dram["wo_t"].rearrange("(k p) e -> p k e", p=128)))
        wo_ld = p.dma_close(wotok)
        with contextlib.ExitStack() as st:
            phM(p, st, ident=ident)
            p.barrier()

        eoFori(p, st4a)
        ph4a(p, st4a, onesr=onesr, onesc=onesc, lnp=lnp, hv=hv,
             wo=wo, w_ld=wo_ld)
        p.barrier()
    with contextlib.ExitStack() as st:
        ph4sh(p, st)
        p.barrier()
    with contextlib.ExitStack() as st:
        ph4b(p, st)
        p.barrier()
    with contextlib.ExitStack() as st:
        ph4c(p, st, ident=ident)
        p.barrier()


def ln_rows(p, stats_ps, a_slice, b_slice, scr):
    nc = p.nc
    ve, ac = nc.vector, nc.scalar
    W = a_slice.shape[1]
    ve.tensor_scalar_mul(a_slice, stats_ps[:, 0:W], 1.0 / D)
    ve.tensor_scalar_mul(b_slice, stats_ps[:, W:2 * W], 1.0 / D)
    ve.tensor_mul(scr, a_slice, a_slice)
    ve.tensor_sub(b_slice, b_slice, scr[:])
    p.inc("s_ve", ve.tensor_scalar_add(b_slice, b_slice, LN_EPS))
    p.wait(ac, "s_ve")
    p.inc("s_ac", ac.activation(b_slice, b_slice, AF.Sqrt))
    p.wait(ve, "s_ac")
    p.inc("s_ve", ve.reciprocal(b_slice, b_slice))
    p.wait(ve, "s_ve")
    ve.tensor_mul(scr, a_slice, b_slice)
    ve.tensor_copy(a_slice, b_slice)
    p.inc("s_ve", ve.tensor_scalar_mul(b_slice, scr, -1.0))


PH1_PARTS = {"blocks", "gsum", "staging", "transpose", "logits"}


def ph1(p, st, *, ident, onesc, onesr, gatet, scores3, gsumP):
    nc = p.nc
    sy, te, ve, ac = nc.sync, nc.tensor, nc.vector, nc.scalar
    dram = p.dram

    xb = [p.sb(f"p1_xb{i}", [128, 8 * BW], stack=st) for i in range(2)]
    xsq = p.sb("p1_xsq", [128, 8 * BW], stack=st)
    arow = p.sb("p1_arow", [1, T], stack=st)
    brow = p.sb("p1_brow", [1, T], stack=st)
    st18 = p.sb("p1_st18", [18, T], stack=st)
    srow = p.sb("p1_srow", [1, 2 * BW], stack=st)
    scr = p.sb("p1_scr", [1, BW], stack=st)
    a3 = p.sb("p1_a3", [128, CC], stack=st)
    b3 = p.sb("p1_b3", [128, CC], stack=st)
    gx3 = p.sb("p1_gx3", [128, E * CC], stack=st)
    grow = p.sb("p1_grow", [1, E], stack=st)
    l3 = p.sb("p1_l3", [128, E * CC], stack=st)
    t3 = p.sb("p1_t3", [128, E * CC], stack=st)
    sume = p.sb("p1_sume", [128, 8 * CC], stack=st)

    gxp = p.ps("p1_gxp", [16, BW], stack=st)
    stp = p.ps("p1_stp", [1, 2 * BW], stack=st)
    trp = [p.ps(f"p1_trp{i}", [128, 32], stack=st) for i in range(2)]
    bcp = p.ps("p1_bcp", [128, E], stack=st)
    gsp = p.ps("p1_gsp", [1, E], stack=st)

    xr = dram["x_fm"].rearrange("(k p) t -> k p t", p=128)

    ld_mark = []
    ac_mark = []
    pe_mark = []
    ve_mark = []

    for blk in range(NB if "blocks" in PH1_PARTS else 0):
        if blk >= 2:
            sy.wait_ge(p.s_pe, pe_mark[blk - 2])
        tok = p.dma_open(sy)
        for k in range(8):
            p.dma_put(tok, sy.dma_start(
                out=xb[blk % 2][:, k * BW:(k + 1) * BW],
                in_=xr[k, :, blk * BW:(blk + 1) * BW]))
        ld_mark.append(p.dma_close(tok))

        p.dma_wait(ac, ld_mark[blk])
        if blk >= 1:
            ac.wait_ge(p.s_pe, pe_mark[blk - 1])
        p.inc("s_ac", ac.activation(xsq[:], xb[blk % 2][:], AF.Square))
        ac_mark.append(p.semv["s_ac"])

        te.wait_ge(p.s_ac, ac_mark[blk])
        if blk >= 1:
            te.wait_ge(p.s_ve, ve_mark[blk - 1])
        for k in range(8):
            kw = slice(k * BW, (k + 1) * BW)
            te.matmul(gxp[:], gatet[:, k * E:(k + 1) * E], xb[blk % 2][:, kw],
                      start=(k == 0), stop=(k == 7))
            te.matmul(stp[:, 0:BW], onesc[:], xb[blk % 2][:, kw],
                      start=(k == 0), stop=(k == 7))
            i = te.matmul(stp[:, BW:2 * BW], onesc[:], xsq[:, kw],
                          start=(k == 0), stop=(k == 7))
        p.inc("s_pe", i)
        pe_mark.append(p.semv["s_pe"])

        ve.wait_ge(p.s_pe, pe_mark[blk])
        ve.tensor_copy(st18[0:16, blk * BW:(blk + 1) * BW], gxp[:])
        ve.tensor_copy(srow[:], stp[:])
        ln_rows(p, srow,
                arow[:, blk * BW:(blk + 1) * BW],
                brow[:, blk * BW:(blk + 1) * BW], scr[:])
        ve_mark.append(p.semv["s_ve"])

    p.tap("arow", arow[:])
    p.tap("brow", brow[:])

    if "gsum" not in PH1_PARTS:
        return
    p.dma_wait_all(te)
    for k in range(8):
        i = te.matmul(gsp[:], onesc[:], gatet[:, k * E:(k + 1) * E],
                      start=(k == 0), stop=(k == 7))
    p.inc("s_pe", i)
    p.wait(ve, "s_pe")
    p.inc("s_ve", ve.tensor_copy(grow[:], gsp[:]))
    p.wait(te, "s_ve")
    i = te.matmul(bcp[:], onesr[:], grow[:], start=True, stop=True)
    p.inc("s_pe", i)
    p.wait(ve, "s_pe")
    p.inc("s_ve", ve.tensor_copy(gsumP[:], bcp[:]))

    if "staging" not in PH1_PARTS:
        return
    p.wait(sy, "s_ve")
    stok = p.dma_open(sy)
    p.dma_put(stok, sy.dma_start(out=st18[16:17, :], in_=arow[:]))
    p.dma_put(stok, sy.dma_start(out=st18[17:18, :], in_=brow[:]))
    p.dma_put(stok, sy.dma_start(out=dram["ab0_sp"][0:1, :], in_=arow[:]))
    p.dma_put(stok, sy.dma_start(out=dram["ab0_sp"][1:2, :], in_=brow[:]))
    stg_ld = p.dma_close(stok)

    if "transpose" not in PH1_PARTS:
        return
    p.dma_wait(te, stg_ld)
    ve_evac_base = p.semv["s_ve"]
    for tb in range(CC):
        if tb >= 2:
            te.wait_ge(p.s_ve, ve_evac_base + tb - 1)
        i = te.transpose(trp[tb % 2][:, 0:18],
                         st18[:, tb * 128:(tb + 1) * 128], ident[0:18, 0:18])
        p.inc("s_pe", i)
        ve.wait_ge(p.s_pe, p.semv["s_pe"])
        gx3v = gx3[:].rearrange("p (e c) -> p e c", c=CC)
        ve.tensor_copy(gx3v[:, :, tb:tb + 1],
                       trp[tb % 2][:, 0:16].rearrange("p (e c) -> p e c", c=1))
        ve.tensor_copy(a3[:, tb:tb + 1], trp[tb % 2][:, 16:17])
        p.inc("s_ve", ve.tensor_copy(b3[:, tb:tb + 1], trp[tb % 2][:, 17:18]))

    if "logits" not in PH1_PARTS:
        return
    a3bc = a3[:].unsqueeze(1).broadcast_to([128, E, CC])
    b3bc = b3[:].unsqueeze(1).broadcast_to([128, E, CC])
    gsbc = gsumP[:].unsqueeze(2).broadcast_to([128, E, CC])
    l3v = l3[:].rearrange("p (e c) -> p e c", c=CC)
    t3v = t3[:].rearrange("p (e c) -> p e c", c=CC)
    p.wait(ve, "s_ve")
    ve.tensor_mul(l3v, gx3[:].rearrange("p (e c) -> p e c", c=CC), a3bc)
    ve.tensor_mul(t3v, b3bc, gsbc)
    p.inc("s_ve", ve.tensor_add(l3[:], l3[:], t3[:]))
    p.tap("logits3", l3[:])
    p.tap("gx3", gx3[:])
    p.tap("a3", a3[:])
    p.tap("b3", b3[:])
    p.tap("gsumP", gsumP[:])
    p.wait(ac, "s_ve")
    p.inc("s_ac", ac.activation(l3[:], l3[:], AF.Exp))
    p.wait(ve, "s_ac")
    l3e = l3[:].rearrange("p (e c) -> p e c", c=CC)
    s8 = sume[:].rearrange("p (e c) -> p e c", c=CC)
    ve.tensor_add(s8[:, 0:8, :], l3e[:, 0:8, :], l3e[:, 8:16, :])
    ve.tensor_add(s8[:, 0:4, :], s8[:, 0:4, :], s8[:, 4:8, :])
    ve.tensor_add(s8[:, 0:2, :], s8[:, 0:2, :], s8[:, 2:4, :])
    p.vf(ve.tensor_add(s8[:, 0:1, :], s8[:, 0:1, :], s8[:, 1:2, :]))
    p.inc("s_ve", ve.reciprocal(sume[:, 0:CC], sume[:, 0:CC]))
    p.wait(ve, "s_ve")
    p.tap("sumr", sume[:, 0:CC])
    rbc = sume[:, 0:CC].unsqueeze(1).broadcast_to([128, E, CC])
    p.inc("s_ve", ve.tensor_mul(
        scores3[:].rearrange("p (e c) -> p e c", c=CC), l3e, rbc))
    p.tap("scores3", scores3[:])


def selection(p, st, *, ident, tri, onesc, onesr, maskc, scores3, theta,
              gsumP, tokidG, idxL):
    dram = p.dram
    """Binary-search theta per expert; build w (e-minor) + scatter slots."""
    nc = p.nc
    sy, te, ve, ac = nc.sync, nc.tensor, nc.vector, nc.scalar

    lohit = p.sb("sl_lohi", [128, 2 * E], stack=st)
    lo = lohit[:, 0:E]
    hi = lohit[:, E:2 * E]
    mid = p.sb("sl_mid", [128, E], stack=st)
    ge = p.sb("sl_ge", [128, E * CC], stack=st)
    red = p.sb("sl_red", [128, E], stack=st)
    crow = p.sb("sl_crow", [1, E], stack=st)
    cntP = p.sb("sl_cntP", [128, E], stack=st)
    cntN = p.sb("sl_cntN", [128, E], stack=st)
    tmp = p.sb("sl_tmp", [128, E], stack=st)
    tmp2 = p.sb("sl_tmp2", [128, E], stack=st)
    w3 = p.sb("sl_w3", [128, E * CC], stack=st)
    cand = p.sb("sl_cand", [128, E * CC], stack=st)
    jj = p.sb("sl_jj", [128, E * CC], stack=st)
    rtot = p.sb("sl_rtot", [128, E], stack=st)
    offp = p.sb("sl_offp", [128, E], stack=st)
    idxf = p.sb("sl_idxf", [128, E * CC], stack=st)
    idx3 = p.sb("sl_idx3", [128, E * CC], I32, stack=st)
    wm2 = p.sb("sl_wm2", [128, CC * (E + 1)], stack=st)

    cp = p.ps("sl_cp", [1, E], stack=st)
    bp = p.ps("sl_bp", [128, E], stack=st)
    op = p.ps("sl_op", [128, E], stack=st)

    s3v = scores3[:].rearrange("p (e c) -> p e c", c=CC)
    gev = ge[:].rearrange("p (e c) -> p e c", c=CC)

    itok = p.dma_open(sy)
    p.dma_put(itok, sy.dma_start(out=lohit[:], in_=dram["selinit"]))
    p.dma_close(itok)
    p.dma_wait(ve, itok)

    for it in range(TH_ITERS):
        p.vf(ve.tensor_add(mid[:], lo, hi))
        p.vf(ve.tensor_scalar_mul(mid[:], mid[:], 0.5))
        midbc = mid[:].unsqueeze(2).broadcast_to([128, E, CC])
        ve.tensor_tensor(gev, s3v, midbc, AL.is_ge)
        p.inc("s_ve", ve.tensor_reduce(
            red[:].unsqueeze(2), gev, AX.X, AL.add))
        if it == 0:
            p.tap("red0", red[:])
            p.tap("mid0", mid[:])
        p.wait(te, "s_ve")
        i = te.matmul(cp[:], onesc[:], red[:], start=True, stop=True)
        p.inc("s_pe", i)
        p.wait(ve, "s_pe")
        p.inc("s_ve", ve.tensor_copy(crow[:], cp[:]))
        p.wait(te, "s_ve")
        i = te.matmul(bp[:], onesr[:], crow[:], start=True, stop=True)
        p.inc("s_pe", i)
        p.wait(ve, "s_pe")
        if it == 0 and "cnt0" in p.debug_taps:
            p.vf(ve.tensor_copy(tmp[:], bp[:]))
            p.tap("cnt0", tmp[:])
        p.vf(ve.tensor_single_scalar(cntP[:], bp[:], 1024.0, AL.is_ge))
        p.vf(ve.tensor_sub(tmp[:], mid[:], lo))
        p.vf(ve.tensor_mul(tmp[:], tmp[:], cntP[:]))
        p.vf(ve.tensor_sub(tmp2[:], mid[:], hi))
        p.vf(ve.tensor_add(lo, lo, tmp[:]))
        p.vf(ve.tensor_scalar(cntN[:], cntP[:], -1.0, 1.0, AL.mult, AL.add))
        p.vf(ve.tensor_mul(tmp2[:], tmp2[:], cntN[:]))
        p.vf(ve.tensor_add(hi, hi, tmp2[:]))

    p.vf(ve.tensor_copy(theta[:], lo))
    p.tap("theta", theta[:])

    thbc = theta[:].unsqueeze(2).broadcast_to([128, E, CC])
    w3v = w3[:].rearrange("p (e c) -> p e c", c=CC)
    ve.tensor_tensor(w3v, s3v, thbc, AL.is_ge)
    ve.tensor_mul(w3[:], w3[:], scores3[:])
    p.vf(ve.tensor_scalar_mul(tmp[:], theta[:], 1.0 - TH_PAD))
    tpbc = tmp[:].unsqueeze(2).broadcast_to([128, E, CC])
    cndv = cand[:].rearrange("p (e c) -> p e c", c=CC)
    ve.tensor_tensor(cndv, s3v, tpbc, AL.is_ge)
    mbc = maskc[:].unsqueeze(1).broadcast_to([128, E, CC])
    p.inc("s_ve", ve.tensor_mul(cndv, cndv, mbc))
    p.tap("cand", cand[:])

    onesbc = onesc[:].broadcast_to([128, CC])
    for e in range(E):
        i = ve.tensor_tensor_scan(jj[:, e * CC:(e + 1) * CC],
                                  onesbc, cand[:, e * CC:(e + 1) * CC],
                                  0.0, AL.mult, AL.add)
        if e == E - 1:
            p.vf(i)
    p.inc("s_ve", ve.tensor_copy(rtot[:], jj[:].rearrange(
        "p (e c) -> p e c", c=CC)[:, :, CC - 1:CC]))
    p.wait(te, "s_ve")
    i = te.matmul(op[:], tri[:], rtot[:], start=True, stop=True)
    p.inc("s_pe", i)
    p.wait(ve, "s_pe")
    p.vf(ve.tensor_copy(offp[:], op[:]))
    offbc = offp[:].unsqueeze(2).broadcast_to([128, E, CC])
    jv = jj[:].rearrange("p (e c) -> p e c", c=CC)
    ve.tensor_add(jv, jv, offbc)
    ve.tensor_single_scalar(ge[:], jj[:], float(CAP), AL.is_le)
    ve.tensor_mul(ge[:], ge[:], cand[:])
    ve.tensor_scalar_add(idxf[:], jj[:], float(-1 - SENT))
    ve.tensor_mul(idxf[:], idxf[:], ge[:])
    ve.tensor_scalar_add(idxf[:], idxf[:], float(SENT))
    p.inc("s_ve", ve.tensor_copy(idx3[:], idxf[:]))
    p.tap("jj", jj[:])

    widv = wm2[:].rearrange("p (c w) -> p c w", w=E + 1)
    w3p = w3[:].rearrange("p (e c) -> p c e", c=CC, e=E)
    ve.tensor_copy(widv[:, :, 0:E], w3p)
    ve.tensor_copy(widv[:, :, E:E + 1], tokidG[:].unsqueeze(2))
    idxev = idxL[:].rearrange("p (c e) -> p c e", e=E)
    idx3p = idx3[:].rearrange("p (e c) -> p c e", c=CC, e=E)
    p.inc("s_ve", ve.tensor_copy(idxev, idx3p))
    p.wait(sy, "s_ve")
    sptok = p.dma_open(sy)
    p.dma_put(sptok, sy.dma_start(out=dram["wid_sp"], in_=wm2[:, 0:CC * (E + 1)]))
    p.dma_put(sptok, sy.dma_start(out=dram["idxe_sp"], in_=idxL[:]))
    p.dma_close(sptok)
    p.tap("widsp", wm2[:, 0:CC * (E + 1)])
    p.tap("idxLf", idxf[:])


def phMprep(p, st, *, hvg):
    nc = p.nc
    gp = nc.gpsimd
    dram = p.dram

    wide = p.sb("mp_wide", [128, WID], stack=st)
    ixt = p.sb("mp_ixt", [128, E], I32, stack=st)

    xtm = dram["x_own_tm"]
    widr = dram["wid_sp"]
    idxr = dram["idxe_sp"]

    gbase = p.semv["s_gp"]
    PER = 3 * 16 + E * 16
    with gp.Fori(0, CCH) as i:
        gp.wait_ge(p.s_gp, i * PER + gbase)
        gp.dma_start(out=wide[:, 0:D],
                     in_=xtm[DynSlice(i * 128, 128), :]).then_inc(p.s_gp, 16)
        gp.dma_start(out=wide[:, D:WID],
                     in_=widr[:, DynSlice((hvg + i) * (E + 1), E + 1)]
                     ).then_inc(p.s_gp, 16)
        gp.dma_start(out=ixt[:],
                     in_=idxr[:, DynSlice((hvg + i) * E, E)]).then_inc(p.s_gp, 16)
        gp.wait_ge(p.s_gp, i * PER + gbase + 48)
        for e in range(E):
            gp.indirect_dma_start(
                dram[f"xtok{e}"],
                IndirectOffsetOnAxis(ixt[:, e:e + 1], 0),
                wide[:], None,
                bounds_check=CAP - 1, oob_is_err=False).then_inc(p.s_gp, 16)
    p.semv["s_gp"] += CCH * PER
    gp.wait_ge(p.s_gp, p.semv["s_gp"])


def ph2(p, st, *, onesr, onesc, lnp, decay):
    nc = p.nc
    sy, te, ve, ac = nc.sync, nc.tensor, nc.vector, nc.scalar
    sc = nc.scalar
    dram = p.dram

    wts = {}
    wtok = p.dma_open(sy)
    for nm in ("wr_t", "ww_t", "wk_t", "wv_t"):
        wts[nm] = p.sb("p2_" + nm, [128, 8 * D], stack=st)
        p.dma_put(wtok, sy.dma_start(
            out=wts[nm][:].rearrange("p (k e) -> p k e", k=8),
            in_=dram[nm].rearrange("(k p) e -> p k e", p=128)))
    w_ld = p.dma_close(wtok)

    xb = [p.sb("p2_xb0", [128, 8 * BW], stack=st)] * 2
    xsq = p.sb("p2_xsq", [128, 4 * BW], stack=st)
    xn = p.sb("p2_xn", [128, 8 * BW], stack=st)
    xn2 = xb[0]
    ab = p.sb("p2_ab", [128, 2 * BW], stack=st)
    srow = p.sb("p2_srow", [1, 2 * BW], stack=st)
    a0r = srow[0:1, 0:BW]
    b0r = srow[0:1, BW:2 * BW]
    a1r = p.sb("p2_a1r", [1, BW], stack=st)
    b1r = p.sb("p2_b1r", [1, BW], stack=st)
    scr = p.sb("p2_scr", [1, BW], stack=st)
    ew = [p.sb(f"p2_ew{i}", [128, BW], stack=st) for i in range(2)]
    eb = [p.sb("p2_eb", [128, BW], stack=st)] * 2
    evb = [p.sb("p2_evb", [128, BW], stack=st)] * 2
    rb = [p.sb(f"p2_rb{i}", [128, BW], stack=st) for i in range(2)]
    nb = [p.sb(f"p2_nb{i}", [128, BW], stack=st) for i in range(2)]
    db = [p.sb(f"p2_db{i}", [128, BW], stack=st) for i in range(2)]
    nst = p.sb("p2_nst", [128, 8], stack=st)
    dst = p.sb("p2_dst", [128, 8], stack=st)

    bc_ps = p.ps("p2_bc", [128, BW], stack=st)
    st_ps = p.ps("p2_st", [1, 2 * BW], stack=st)
    mm_ps = [p.ps(f"p2_mm{i}", [128, BW], stack=st) for i in range(4)]

    xr = dram["x_fm"].rearrange("(k p) t -> k p t", p=128)

    def bcast(row_slice, dst_slice):
        p.wait(te, "s_ve")
        i = te.matmul(bc_ps[:], onesr[:], row_slice, start=True, stop=True)
        p.inc("s_pe", i)
        p.wait(ve, "s_pe")
        p.inc("s_ve", ve.tensor_copy(dst_slice, bc_ps[:]))

    ve.memset(nst[:], 0.0)
    p.vf(ve.memset(dst[:], 0.0))

    pe_mark = [0]
    for blk in range(NB):
        tw = slice(blk * BW, (blk + 1) * BW)
        if blk >= 1:
            sy.wait_ge(p.s_pe, pe_mark[-1])
            sy.wait_ge(p.s_ve, p.semv["s_ve"])
        if blk >= 1:
            p.dma_wait(ve, _SPILLS[(id(p), blk - 1, 7)])
            p.dma_wait(ac, _SPILLS[(id(p), blk - 1, 7)])
        tok = p.dma_open(sy)
        for k in range(8):
            p.dma_put(tok, sy.dma_start(
                out=xb[blk % 2][:, k * BW:(k + 1) * BW], in_=xr[k, :, tw]))
        p.dma_put(tok, sy.dma_start(out=a0r, in_=dram["ab0_sp"][0:1, tw]))
        p.dma_put(tok, sy.dma_start(out=b0r, in_=dram["ab0_sp"][1:2, tw]))
        ldm = p.dma_close(tok)

        p.dma_wait(te, ldm)
        if blk == 0:
            p.dma_wait(te, w_ld)
        bcast(a0r, ab[:, 0:BW])
        bcast(b0r, ab[:, BW:2 * BW])
        a0b = ab[:, 0:BW].unsqueeze(1).broadcast_to([128, 8, BW])
        b0b = ab[:, BW:2 * BW].unsqueeze(1).broadcast_to([128, 8, BW])
        xbv = xb[blk % 2][:].rearrange("p (k w) -> p k w", k=8)
        xnv = xn[:].rearrange("p (k w) -> p k w", k=8)
        p.dma_wait(ve, ldm)
        ve.tensor_mul(xnv, xbv, a0b)
        p.inc("s_ve", ve.tensor_add(xnv, xnv, b0b))

        p.wait(ac, "s_ve")
        p.inc("s_ac", ac.activation(xsq[:], xn[:, 0:4 * BW], AF.Square))
        p.wait(te, "s_ac")
        p.wait(te, "s_ve")
        i = None
        for k in range(8):
            if k == 4:
                p.inc("s_pe", i)
                ac.wait_ge(p.s_pe, p.semv["s_pe"])
                p.inc("s_ac", ac.activation(xsq[:], xn[:, 4 * BW:8 * BW],
                                            AF.Square))
                p.wait(te, "s_ac")
            kw = slice(k * BW, (k + 1) * BW)
            te.matmul(st_ps[:, 0:BW], onesc[:], xn[:, kw],
                      start=(k == 0), stop=(k == 7))
            i = te.matmul(st_ps[:, BW:2 * BW], onesc[:],
                          xsq[:, (k % 4) * BW:(k % 4 + 1) * BW],
                          start=(k == 0), stop=(k == 7))
        p.inc("s_pe", i)
        p.wait(ve, "s_pe")
        ve.tensor_copy(srow[:], st_ps[:])
        ln_rows(p, srow, a1r[:], b1r[:], scr[:])
        bcast(a1r[:], ab[:, 0:BW])
        bcast(b1r[:], ab[:, BW:2 * BW])
        a1b = ab[:, 0:BW].unsqueeze(1).broadcast_to([128, 8, BW])
        b1b = ab[:, BW:2 * BW].unsqueeze(1).broadcast_to([128, 8, BW])
        xn2v = xn2[:].rearrange("p (k w) -> p k w", k=8)
        ve.tensor_mul(xn2v, xnv, a1b)
        ve.tensor_add(xn2v, xn2v, b1b)
        for k in range(8):
            kw = slice(k * BW, (k + 1) * BW)
            ve.tensor_scalar(xn2[:, kw], xn2[:, kw],
                             lnp[:, 0 * 8 + k:0 * 8 + k + 1],
                             lnp[:, 1 * 8 + k:1 * 8 + k + 1], AL.mult, AL.add)
        p.inc("s_ve", ve.engine_nop())
        xn2_ve = p.semv["s_ve"]

        te.wait_ge(p.s_ve, xn2_ve)
        cons = [None] * 4
        for m in range(8):
            bi = m % 2
            pm = st_spill_mark(p, blk, m, 2)
            if pm is not None:
                p.dma_wait(ve, pm)
                p.dma_wait(ac, pm)
            for oi, wname in enumerate(("ww_t", "wk_t", "wv_t", "wr_t")):
                pb = mm_ps[oi]
                if cons[oi] is not None:
                    te.wait_ge(getattr(p, cons[oi][0]), cons[oi][1])
                for k in range(8):
                    i = te.matmul(pb[:],
                                  wts[wname][:, k * D + m * 128:k * D + m * 128 + 128],
                                  xn2[:, k * BW:(k + 1) * BW],
                                  start=(k == 0), stop=(k == 7))
                p.inc("s_pe", i)
                mm_pe = p.semv["s_pe"]
                if wname == "ww_t":
                    ac.wait_ge(p.s_pe, mm_pe)
                    p.inc("s_ac", ac.activation(ew[bi][:], pb[:], AF.Exp))
                    cons[0] = ("s_ac", p.semv["s_ac"])
                elif wname == "wk_t":
                    ve.wait_ge(p.s_pe, mm_pe)
                    ve.wait_ge(p.s_ac, p.semv["s_ac"])
                    p.inc("s_ve", ve.scalar_tensor_tensor(
                        eb[bi][:], pb[:], 1.0, ew[bi][:], AL.mult, AL.subtract))
                    cons[1] = ("s_ve", p.semv["s_ve"])
                    ac.wait_ge(p.s_ve, p.semv["s_ve"])
                    p.inc("s_ac", ac.activation(eb[bi][:], eb[bi][:], AF.Exp))
                elif wname == "wv_t":
                    ve.wait_ge(p.s_pe, mm_pe)
                    ve.wait_ge(p.s_ac, p.semv["s_ac"])
                    p.inc("s_ve", ve.tensor_mul(evb[bi][:], eb[bi][:], pb[:]))
                    cons[2] = ("s_ve", p.semv["s_ve"])
                    dkb = decay[:, m:m + 1].broadcast_to([128, BW])
                    ve.tensor_tensor_scan(nb[bi][:], dkb, evb[bi][:],
                                          nst[:, m:m + 1], AL.mult, AL.add)
                    ve.tensor_tensor_scan(db[bi][:], dkb, eb[bi][:],
                                          dst[:, m:m + 1], AL.mult, AL.add)
                    p.vf(ve.tensor_copy(nst[:, m:m + 1], nb[bi][:, BW - 1:BW]))
                    p.inc("s_ve", ve.tensor_copy(dst[:, m:m + 1],
                                                 db[bi][:, BW - 1:BW]))
                else:
                    ac.wait_ge(p.s_pe, mm_pe)
                    p.inc("s_ac", ac.activation(rb[bi][:], pb[:], AF.Sigmoid))
                    cons[3] = ("s_ac", p.semv["s_ac"])
            sc.wait_ge(p.s_ve, p.semv["s_ve"])
            sc.wait_ge(p.s_ac, p.semv["s_ac"])
            stok = p.dma_open(sc)
            for nm, tile in (("e_sp", nb[bi]), ("ev_sp", db[bi]),
                             ("r_sp", rb[bi])):
                p.dma_put(stok, sc.dma_start(out=dram[nm][m, :, tw],
                                             in_=tile[:]))
            _SPILLS[(id(p), blk, m)] = p.dma_close(stok)
        pe_mark.append(p.semv["s_pe"])


_SPILLS = {}


def st_spill_mark(p, blk, m, dist):
    mm = m - dist
    key = (id(p), blk, mm) if mm >= 0 else (id(p), blk - 1, 8 + mm)
    return _SPILLS.get(key)
def phM(p, st, *, ident):
    nc = p.nc
    sy, te, ve, ac, gp = nc.sync, nc.tensor, nc.vector, nc.scalar, nc.gpsimd
    sc = nc.scalar
    dram = p.dram

    wg = [p.sb(f"pm_wg{i}", [128, 8 * 512], stack=st) for i in range(2)]
    wu = [p.sb(f"pm_wu{i}", [128, 8 * 512], stack=st) for i in range(2)]
    wd = [p.sb(f"pm_wd{i}", [128, 4 * 1024], stack=st) for i in range(2)]
    xt = p.sb("pm_xt", [128, NSL * WID], stack=st)
    eosb = p.sb("pm_eosb", [128, 8 * CAP], stack=st)
    stats = p.sb("pm_stats", [128, 4 * NSL], stack=st)
    xtT = p.sb("pm_xtT", [128, 8 * CAP], stack=st)
    gel = p.sb("pm_gel", [128, 320], stack=st)
    hsb = p.sb("pm_hsb", [128, 4 * 320], stack=st)
    wcol = p.sb("pm_wcol", [128, NSL], stack=st)
    eoT = [p.sb(f"pm_eoT{i}", [128, D], stack=st) for i in range(2)]
    hofftt = p.sb("pm_hofft", [128, 2], stack=st)
    idc = p.sb("pm_idc", [128, NSL], I32, stack=st)
    idcf = p.sb("pm_idcf", [128, NSL], stack=st)
    idcm = p.sb("pm_idcm", [128, NSL], stack=st)

    trp = [p.ps(f"pm_trp{i}", [128, 128], stack=st) for i in range(2)]
    gps = p.ps("pm_gps", [128, BW], stack=st)
    ups = p.ps("pm_ups", [128, BW], stack=st)
    eop = p.ps("pm_eop", [128, BW], stack=st)
    top = p.ps("pm_top", [128, 1024], stack=st)

    xtv = xt[:].rearrange("p (n w) -> p n w", n=NSL)
    eoTr = dram["eoT_sp"].rearrange("(u p) d -> u p d", p=128)
    st_eoT = []

    htok = p.dma_open(sy)
    p.dma_put(htok, sy.dma_start(out=hofftt[:, 0:1], in_=dram["hofft1"]))
    p.dma_put(htok, sy.dma_start(out=hofftt[:, 1:2], in_=dram["hofft2"]))
    p.dma_close(htok)

    wq_tok = {}
    buf_mark = [0, 0]

    def issue_wq(gidx):
        if gidx >= 4 * E:
            return
        eq, q = divmod(gidx, 4)
        b = gidx % 2
        if buf_mark[b] > 0:
            sy.wait_ge(p.s_pe, buf_mark[b])
        fq = slice(q * 512, (q + 1) * 512)
        tok = p.dma_open(sy)
        p.dma_put(tok, sy.dma_start(
            out=wg[b][:].rearrange("p (k f) -> p k f", k=8),
            in_=dram["wg_t"][eq].rearrange("(k p) f -> p k f", p=128)[:, :, fq]))
        p.dma_put(tok, sy.dma_start(
            out=wu[b][:].rearrange("p (k f) -> p k f", k=8),
            in_=dram["wu_t"][eq].rearrange("(k p) f -> p k f", p=128)[:, :, fq]))
        p.dma_put(tok, sy.dma_start(
            out=wd[b][:].rearrange("p (k d) -> p k d", k=4),
            in_=dram["wd_t"][eq].rearrange(
                "(k p) d -> p k d", p=128)[:, q * 4:(q + 1) * 4]))
        wq_tok[gidx] = p.dma_close(tok)

    issue_wq(0)
    issue_wq(1)

    xtok_toks = {}

    def issue_xtok(e2):
        tok = p.dma_open(sy)
        p.dma_put(tok, sy.dma_start(
            out=xtv, in_=dram[f"xtok{e2}"].rearrange("(n p) w -> p n w", p=128)))
        xtok_toks[e2] = p.dma_close(tok)

    issue_xtok(0)
    for e in range(E):
        xtok_t = xtok_toks[e]
        if e == 0:
            p.dma_wait(ve, htok)

        p.dma_wait(ve, xtok_t)
        p.dma_wait(ac, xtok_t)
        ac.wait_ge(p.s_pe, p.semv["s_pe"])
        xsq = eosb[:, 0:D]
        for sl in range(NSL):
            xs = xt[:, sl * WID:sl * WID + D]
            if sl > 0:
                ac.wait_ge(p.s_ve, p.semv["s_ve"])
            p.inc("s_ac", ac.activation(xsq, xs, AF.Square))
            ve.tensor_reduce(stats[:, sl:sl + 1], xs.unsqueeze(1), AX.X, AL.add)
            ve.wait_ge(p.s_ac, p.semv["s_ac"])
            p.inc("s_ve", ve.tensor_reduce(
                stats[:, NSL + sl:NSL + sl + 1], xsq.unsqueeze(1),
                AX.X, AL.add))
        p.wait(ve, "s_ve")
        ve.tensor_scalar_mul(stats[:, 2 * NSL:3 * NSL], stats[:, 0:NSL], 1.0 / D)
        p.vf(ve.tensor_scalar_mul(stats[:, 3 * NSL:4 * NSL],
                                  stats[:, NSL:2 * NSL], 1.0 / D))
        p.vf(ve.tensor_mul(stats[:, 0:NSL], stats[:, 2 * NSL:3 * NSL],
                           stats[:, 2 * NSL:3 * NSL]))
        p.vf(ve.tensor_sub(stats[:, 3 * NSL:4 * NSL], stats[:, 3 * NSL:4 * NSL],
                           stats[:, 0:NSL]))
        p.inc("s_ve", ve.tensor_scalar_add(stats[:, 3 * NSL:4 * NSL],
                                           stats[:, 3 * NSL:4 * NSL], LN_EPS))
        p.wait(ac, "s_ve")
        ac.activation(stats[:, 3 * NSL:4 * NSL],
                      stats[:, 3 * NSL:4 * NSL], AF.Sqrt)
        p.inc("s_ac", ac.activation(eosb[:, 0:512], eosb[:, 0:512],
                                    AF.Square))
        p.wait(ve, "s_ac")
        p.inc("s_ve", ve.reciprocal(stats[:, 3 * NSL:4 * NSL],
                                    stats[:, 3 * NSL:4 * NSL]))
        p.wait(ve, "s_ve")
        p.vf(ve.tensor_mul(stats[:, 2 * NSL:3 * NSL], stats[:, 2 * NSL:3 * NSL],
                           stats[:, 3 * NSL:4 * NSL]))
        p.vf(ve.tensor_scalar_mul(stats[:, 2 * NSL:3 * NSL],
                                  stats[:, 2 * NSL:3 * NSL], -1.0))
        for sl in range(NSL):
            ve.tensor_scalar(xt[:, sl * WID:sl * WID + D],
                             xt[:, sl * WID:sl * WID + D],
                             stats[:, 3 * NSL + sl:3 * NSL + sl + 1],
                             stats[:, 2 * NSL + sl:2 * NSL + sl + 1],
                             AL.mult, AL.add)
        ve.tensor_copy(wcol[:].unsqueeze(2), xtv[:, :, D + e:D + e + 1])
        p.vf(ve.tensor_copy(idcf[:].unsqueeze(2), xtv[:, :, WID - 1:WID]))
        p.vf(ve.tensor_scalar(idcf[:], idcf[:], hofftt[:, 0:1], None,
                              AL.subtract))
        p.vf(ve.tensor_single_scalar(idcm[:], idcf[:], 0.0, AL.is_lt))
        p.vf(ve.scalar_tensor_tensor(idcf[:], idcm[:], hofftt[:, 1:2], idcf[:],
                                     AL.mult, AL.add))
        p.inc("s_ve", ve.tensor_copy(idc[:], idcf[:]))
        rn_ve = p.semv["s_ve"]

        sc.wait_ge(p.s_ve, rn_ve)
        idtok = p.dma_open(sc)
        p.dma_put(idtok, sc.dma_start(
            out=dram["idT_sp"][e * NSL:(e + 1) * NSL, :].rearrange("s p -> p s"),
            in_=idc[:], allow_slow_non_contiguous=True))
        p.dma_close(idtok)

        te.wait_ge(p.s_ve, rn_ve)
        nt = 0
        tr_ve = p.semv["s_ve"]
        for dc in range(8):
            for sl in range(NSL):
                if nt >= 2:
                    te.wait_ge(p.s_ve, tr_ve + nt - 1)
                i = te.transpose(
                    trp[nt % 2][:],
                    xt[:, sl * WID + dc * 128:sl * WID + (dc + 1) * 128],
                    ident[:])
                p.inc("s_pe", i)
                ve.wait_ge(p.s_pe, p.semv["s_pe"])
                p.inc("s_ve", ve.tensor_copy(
                    xtT[:, dc * CAP + sl * 128:dc * CAP + (sl + 1) * 128],
                    trp[nt % 2][:]))
                nt += 1

        if e + 1 < E:
            sy.wait_ge(p.s_pe, p.semv["s_pe"])
            sy.wait_ge(p.s_ve, p.semv["s_ve"])
            issue_xtok(e + 1)

        for q in range(4):
            gidx = e * 4 + q
            b = gidx % 2
            p.dma_wait(te, wq_tok[gidx])
            for nh in range(2):
                for fcl in range(4):
                    te.wait_ge(p.s_ve, p.semv["s_ve"])
                    for k in range(8):
                        te.matmul(gps[:, 0:320],
                                  wg[b][:, k * 512 + fcl * 128:k * 512 + (fcl + 1) * 128],
                                  xtT[:, k * CAP + nh * 320:k * CAP + nh * 320 + 320],
                                  start=(k == 0), stop=(k == 7))
                    for k in range(8):
                        i = te.matmul(ups[:, 0:320],
                                      wu[b][:, k * 512 + fcl * 128:k * 512 + (fcl + 1) * 128],
                                      xtT[:, k * CAP + nh * 320:k * CAP + nh * 320 + 320],
                                      start=(k == 0), stop=(k == 7))
                    p.inc("s_pe", i)
                    ac.wait_ge(p.s_pe, p.semv["s_pe"])
                    p.inc("s_ac", ac.activation(gel[:, 0:320], gps[:, 0:320],
                                                AF.Gelu))
                    ve.wait_ge(p.s_ac, p.semv["s_ac"])
                    p.inc("s_ve", ve.tensor_mul(
                        hsb[:, fcl * 320:fcl * 320 + 320],
                        gel[:, 0:320], ups[:, 0:320]))
                for m in range(8):
                    te.wait_ge(p.s_ve, p.semv["s_ve"])
                    for fcl in range(4):
                        i = te.matmul(eop[:, 0:320],
                                      wd[b][:, fcl * 1024 + m * 128:fcl * 1024 + (m + 1) * 128],
                                      hsb[:, fcl * 320:fcl * 320 + 320],
                                      start=(fcl == 0), stop=(fcl == 3))
                    p.inc("s_pe", i)
                    ve.wait_ge(p.s_pe, p.semv["s_pe"])
                    dst = eosb[:, m * CAP + nh * 320:m * CAP + nh * 320 + 320]
                    if q == 0:
                        p.inc("s_ve", ve.tensor_copy(dst, eop[:, 0:320]))
                    else:
                        p.inc("s_ve", ve.scalar_tensor_tensor(
                            dst, eop[:, 0:320], 1.0, dst, AL.mult, AL.add))
            buf_mark[b] = p.semv["s_pe"]
            issue_wq(gidx + 2)

        for sl in range(NSL):
            te.wait_ge(p.s_ve, p.semv["s_ve"])
            for dc in range(8):
                i = te.transpose(top[:, dc * 128:(dc + 1) * 128],
                                 eosb[:, dc * CAP + sl * 128:dc * CAP + (sl + 1) * 128],
                                 ident[:])
            p.inc("s_pe", i)
            ve.wait_ge(p.s_pe, p.semv["s_pe"])
            if len(st_eoT) >= 2:
                p.dma_wait(ve, st_eoT[-2])
            p.inc("s_ve", ve.tensor_scalar(
                eoT[sl % 2][:], top[:], wcol[:, sl:sl + 1], None, AL.mult))
            sc.wait_ge(p.s_ve, p.semv["s_ve"])
            etok = p.dma_open(sc)
            p.dma_put(etok, sc.dma_start(out=eoTr[e * NSL + sl],
                                         in_=eoT[sl % 2][:]))
            st_eoT.append(p.dma_close(etok))


def eoFori(p, st):
    nc = p.nc
    gp = nc.gpsimd
    dram = p.dram
    eob = p.sb("eof_eob", [128, D], stack=st)
    ixt = p.sb("eof_ixt", [128, 1], I32, stack=st)
    eoTr = dram["eoT_sp"]
    idTr = dram["idT_sp"].rearrange("u p -> p u")
    gbase = p.semv["s_gp"]
    PER = 3 * 16
    NU = E * NSL
    with gp.Fori(0, NU) as u:
        gp.wait_ge(p.s_gp, u * PER + gbase)
        gp.dma_start(out=eob[:],
                     in_=eoTr[DynSlice(u * 128, 128), :]).then_inc(p.s_gp, 16)
        gp.dma_start(out=ixt[:], in_=idTr[:, DynSlice(u, 1)],
                     allow_slow_non_contiguous=True).then_inc(p.s_gp, 16)
        gp.wait_ge(p.s_gp, u * PER + gbase + 32)
        gp.indirect_dma_start(
            dram["macc"], IndirectOffsetOnAxis(ixt[:], 0), eob[:], None,
            bounds_check=HALF, oob_is_err=False,
            compute_op=AL.add).then_inc(p.s_gp, 16)
    p.semv["s_gp"] += NU * PER
    gp.wait_ge(p.s_gp, p.semv["s_gp"])


def ph4a(p, st, *, onesr, onesc, lnp, hv, wo, w_ld):
    hoff_tok = hv * 128
    """xn (from ab0_sp), rw=r*wkv, tm=Wo@rw, y1, LN2->xn3; spills.
    wo/w_ld: prefetched during phM."""
    nc = p.nc
    sy, te, ve, ac, sc = nc.sync, nc.tensor, nc.vector, nc.scalar, nc.scalar
    dram = p.dram

    xb = [p.sb(f"p4_xb{i}", [128, 8 * BW], stack=st) for i in range(2)]
    xsq = [p.sb(f"p4_xsq{i}", [128, 8 * BW], stack=st) for i in range(2)]
    xn = p.sb("p4_xn", [128, 8 * BW], stack=st)
    rbuf = [p.sb(f"p4_rbuf{i}", [128, 8 * BW], stack=st) for i in range(2)]
    wkvb = [p.sb(f"p4_wkvb{i}", [128, 8 * BW], stack=st) for i in range(2)]
    ab = p.sb("p4_ab", [128, 2 * BW], stack=st)
    srow = p.sb("p4_srow", [1, 2 * BW], stack=st)
    a0r = p.sb("p4_a0r", [1, BW], stack=st)
    b0r = p.sb("p4_b0r", [1, BW], stack=st)
    a1r = a0r
    b1r = b0r
    scr = p.sb("p4_scr", [1, BW], stack=st)

    bc_ps = p.ps("p4_bc", [128, BW], stack=st)
    st_ps = p.ps("p4_st", [1, 2 * BW], stack=st)
    mm_ps = [p.ps(f"p4_mm{i}", [128, BW], stack=st) for i in range(2)]

    xr = dram["x_own_fm"].rearrange("(k p) t -> k p t", p=128)

    def bcast(row_slice, dst_slice):
        p.wait(te, "s_ve")
        i = te.matmul(bc_ps[:], onesr[:], row_slice, start=True, stop=True)
        p.inc("s_pe", i)
        p.wait(ve, "s_pe")
        p.inc("s_ve", ve.tensor_copy(dst_slice, bc_ps[:]))

    def ln_full(src_tile, dst_tile, g_off, b_off, sq):
        p.wait(ac, "s_ve")
        p.inc("s_ac", ac.activation(sq[:], src_tile[:], AF.Square))
        p.wait(te, "s_ac")
        for k in range(8):
            kw = slice(k * BW, (k + 1) * BW)
            te.matmul(st_ps[:, 0:BW], onesc[:], src_tile[:, kw],
                      start=(k == 0), stop=(k == 7))
            i = te.matmul(st_ps[:, BW:2 * BW], onesc[:], sq[:, kw],
                          start=(k == 0), stop=(k == 7))
        p.inc("s_pe", i)
        p.wait(ve, "s_pe")
        ve.tensor_copy(srow[:], st_ps[:])
        ln_rows(p, srow, a1r[:], b1r[:], scr[:])
        bcast(a1r[:], ab[:, 0:BW])
        bcast(b1r[:], ab[:, BW:2 * BW])
        a1b = ab[:, 0:BW].unsqueeze(1).broadcast_to([128, 8, BW])
        b1b = ab[:, BW:2 * BW].unsqueeze(1).broadcast_to([128, 8, BW])
        sv = src_tile[:].rearrange("p (k w) -> p k w", k=8)
        dv = dst_tile[:].rearrange("p (k w) -> p k w", k=8)
        ve.tensor_mul(dv, sv, a1b)
        ve.tensor_add(dv, dv, b1b)
        for k in range(8):
            kw = slice(k * BW, (k + 1) * BW)
            ve.tensor_scalar(dst_tile[:, kw], dst_tile[:, kw],
                             lnp[:, g_off + k:g_off + k + 1],
                             lnp[:, b_off + k:b_off + k + 1], AL.mult, AL.add)
        p.inc("s_ve", ve.engine_nop())

    st_mark = {}
    pe_mark = {}
    ve_mark = {}
    bc_mark = {}
    ldms = {}

    def issue_load(blk2):
        if blk2 >= NBO:
            return
        bi = blk2 % 2
        tw2 = slice(blk2 * BW, (blk2 + 1) * BW)
        if blk2 >= 2:
            sy.wait_ge(p.s_pe, pe_mark[blk2 - 2])
            sy.wait_ge(p.s_ve, ve_mark[blk2 - 2])
            p.dma_wait(sy, st_mark[blk2 - 2])
        tok = p.dma_open(sy)
        for k in range(8):
            p.dma_put(tok, sy.dma_start(out=xb[bi][:, k * BW:(k + 1) * BW],
                                        in_=xr[k, :, tw2]))
        p.dma_put(tok, sy.dma_start(
            out=rbuf[bi][:].rearrange("p (k w) -> p k w", k=8),
            in_=dram["r_sp"][:, :, DynSlice(hoff_tok + blk2 * BW, BW)].rearrange(
                "k p w -> p k w")))
        p.dma_put(tok, sy.dma_start(
            out=xsq[bi][:].rearrange("p (k w) -> p k w", k=8),
            in_=dram["e_sp"][:, :, DynSlice(hoff_tok + blk2 * BW, BW)].rearrange(
                "k p w -> p k w")))
        p.dma_put(tok, sy.dma_start(
            out=wkvb[bi][:].rearrange("p (k w) -> p k w", k=8),
            in_=dram["ev_sp"][:, :, DynSlice(hoff_tok + blk2 * BW, BW)].rearrange(
                "k p w -> p k w")))
        ldms[blk2] = p.dma_close(tok)

    def issue_rows(blk2):
        if blk2 >= NBO:
            return
        if blk2 >= 1:
            sy.wait_ge(p.s_pe, pe_mark[blk2 - 1])
        tok = p.dma_open(sy)
        p.dma_put(tok, sy.dma_start(
            out=a0r[:],
            in_=dram["ab0_sp"][0:1, DynSlice(hoff_tok + blk2 * BW, BW)]))
        p.dma_put(tok, sy.dma_start(
            out=b0r[:],
            in_=dram["ab0_sp"][1:2, DynSlice(hoff_tok + blk2 * BW, BW)]))
        rowtoks[blk2] = p.dma_close(tok)

    rowtoks = {}
    issue_load(0)
    issue_load(1)
    issue_rows(0)
    for blk in range(NBO):
        bi = blk % 2
        tw = slice(blk * BW, (blk + 1) * BW)
        ldm = ldms[blk]
        if blk == 0:
            p.dma_wait(te, w_ld)

        p.dma_wait(ve, ldm)
        p.dma_wait(te, ldm)
        p.dma_wait(te, rowtoks[blk])
        bcast(a0r[:], ab[:, 0:BW])
        bcast(b0r[:], ab[:, BW:2 * BW])
        bc_mark[blk] = p.semv["s_pe"]
        a0b = ab[:, 0:BW].unsqueeze(1).broadcast_to([128, 8, BW])
        b0b = ab[:, BW:2 * BW].unsqueeze(1).broadcast_to([128, 8, BW])
        y1 = xb[bi]
        xn3 = rbuf[bi]
        xbv = xb[bi][:].rearrange("p (k w) -> p k w", k=8)
        xnv = xn[:].rearrange("p (k w) -> p k w", k=8)
        if blk >= 1:
            p.dma_wait(ve, st_mark[blk - 1])
        ve.tensor_mul(xnv, xbv, a0b)
        ve.tensor_add(xnv, xnv, b0b)
        ve.tensor_scalar_add(wkvb[bi][:], wkvb[bi][:], WKV_EPS)
        p.inc("s_ve", ve.reciprocal(wkvb[bi][:], wkvb[bi][:]))
        p.wait(ve, "s_ve")
        ve.tensor_mul(wkvb[bi][:], xsq[bi][:], wkvb[bi][:])
        p.inc("s_ve", ve.tensor_mul(rbuf[bi][:], rbuf[bi][:], wkvb[bi][:]))
        rw_ve = p.semv["s_ve"]

        te.wait_ge(p.s_ve, rw_ve)
        for m in range(8):
            mw = slice(m * BW, (m + 1) * BW)
            pb = mm_ps[m % 2]
            if m >= 2:
                te.wait_ge(p.s_ve, p.semv["s_ve"])
            for k in range(8):
                i = te.matmul(pb[:],
                              wo[:, k * D + m * 128:k * D + m * 128 + 128],
                              rbuf[bi][:, k * BW:(k + 1) * BW],
                              start=(k == 0), stop=(k == 7))
            p.inc("s_pe", i)
            ve.wait_ge(p.s_pe, p.semv["s_pe"])
            p.inc("s_ve", ve.scalar_tensor_tensor(
                y1[:, mw], pb[:], 1.0, xn[:, mw], AL.mult, AL.add))

        ln_full(y1, xn3, 16, 24, xsq[bi])

        pe_mark[blk] = p.semv["s_pe"]
        ve_mark[blk] = p.semv["s_ve"]
        sc.wait_ge(p.s_ve, p.semv["s_ve"])
        stok = p.dma_open(sc)
        for nm, tile in (("xn_sp", xn), ("y1_sp", y1), ("xn3_sp", xn3)):
            p.dma_put(stok, sc.dma_start(
                out=dram[nm][:, :, tw].rearrange("k p w -> p k w"),
                in_=tile[:].rearrange("p (k w) -> p k w", k=8)))
        st_mark[blk] = p.dma_close(stok)
        issue_load(blk + 2)
        issue_rows(blk + 1)


def ph4sh(p, st):
    nc = p.nc
    sy, te, ve, ac, sc = nc.sync, nc.tensor, nc.vector, nc.scalar, nc.scalar
    dram = p.dram

    wsg = [p.sb(f"psh_wsg{i}", [128, 8 * 512], stack=st) for i in range(2)]
    wsu = [p.sb(f"psh_wsu{i}", [128, 8 * 512], stack=st) for i in range(2)]
    wsd = [p.sb(f"psh_wsd{i}", [128, 4 * 1024], stack=st) for i in range(2)]
    xnb = [p.sb(f"psh_xn{i}", [128, 8 * BW], stack=st) for i in range(2)]
    gel = p.sb("psh_gel", [128, BW], stack=st)
    hs = p.sb("psh_hs", [128, 4 * BW], stack=st)
    sh = [p.sb(f"psh_sh{i}", [128, 8 * BW], stack=st) for i in range(2)]

    gsp = p.ps("psh_gsp", [128, BW], stack=st)
    usp = p.ps("psh_usp", [128, BW], stack=st)
    dsp = p.ps("psh_dsp", [128, BW], stack=st)

    wq_tok = {}
    buf_mark = [0, 0]

    def issue_wq(q):
        if q >= 4:
            return
        b = q % 2
        if buf_mark[b] > 0:
            sy.wait_ge(p.s_pe, buf_mark[b])
        fq = slice(q * 512, (q + 1) * 512)
        tok = p.dma_open(sy)
        p.dma_put(tok, sy.dma_start(
            out=wsg[b][:].rearrange("p (k f) -> p k f", k=8),
            in_=dram["wsg_t"].rearrange("(k p) f -> p k f", p=128)[:, :, fq]))
        p.dma_put(tok, sy.dma_start(
            out=wsu[b][:].rearrange("p (k f) -> p k f", k=8),
            in_=dram["wsu_t"].rearrange("(k p) f -> p k f", p=128)[:, :, fq]))
        p.dma_put(tok, sy.dma_start(
            out=wsd[b][:].rearrange("p (k e) -> p k e", k=4),
            in_=dram["wsd_t"].rearrange(
                "(k p) e -> p k e", p=128)[:, q * 4:(q + 1) * 4]))
        wq_tok[q] = p.dma_close(tok)

    issue_wq(0)
    issue_wq(1)

    prev_pe = prev_st = None
    for q in range(4):
        b = q % 2
        st_mark = {}
        pe_mark = {}
        ldms = {}

        def issue_load(blk2, q=q, prev_pe=prev_pe, prev_st=prev_st):
            if blk2 >= NBO:
                return
            bi = blk2 % 2
            tw2 = slice(blk2 * BW, (blk2 + 1) * BW)
            if blk2 >= 2:
                sy.wait_ge(p.s_pe, pe_mark[blk2 - 2])
                p.dma_wait(sy, st_mark[blk2 - 2])
            elif prev_pe is not None:
                sy.wait_ge(p.s_pe, prev_pe[NBO - 2 + blk2])
                p.dma_wait(sy, prev_st[NBO - 2 + blk2])
            tok = p.dma_open(sy)
            for k in range(8):
                p.dma_put(tok, sy.dma_start(
                    out=xnb[bi][:, k * BW:(k + 1) * BW],
                    in_=dram["xn_sp"][k, :, tw2]))
            if q >= 1:
                p.dma_put(tok, sy.dma_start(
                    out=sh[bi][:].rearrange("p (k w) -> p k w", k=8),
                    in_=dram["sh_sp"][:, :, tw2].rearrange("k p w -> p k w")))
            ldms[blk2] = p.dma_close(tok)

        issue_load(0)
        issue_load(1)
        for blk in range(NBO):
            bi = blk % 2
            tw = slice(blk * BW, (blk + 1) * BW)
            p.dma_wait(te, ldms[blk])
            p.dma_wait(te, wq_tok[q])
            for fc in range(4):
                te.wait_ge(p.s_ve, p.semv["s_ve"])
                for k in range(8):
                    te.matmul(gsp[:],
                              wsg[b][:, k * 512 + fc * 128:k * 512 + (fc + 1) * 128],
                              xnb[bi][:, k * BW:(k + 1) * BW],
                              start=(k == 0), stop=(k == 7))
                for k in range(8):
                    i = te.matmul(usp[:],
                                  wsu[b][:, k * 512 + fc * 128:k * 512 + (fc + 1) * 128],
                                  xnb[bi][:, k * BW:(k + 1) * BW],
                                  start=(k == 0), stop=(k == 7))
                p.inc("s_pe", i)
                ac.wait_ge(p.s_pe, p.semv["s_pe"])
                p.inc("s_ac", ac.activation(gel[:], gsp[:], AF.Gelu))
                ve.wait_ge(p.s_ac, p.semv["s_ac"])
                p.inc("s_ve", ve.tensor_mul(hs[:, fc * BW:(fc + 1) * BW],
                                            gel[:], usp[:]))
            for m in range(8):
                te.wait_ge(p.s_ve, p.semv["s_ve"])
                for fc in range(4):
                    i = te.matmul(dsp[:],
                                  wsd[b][:, fc * D + m * 128:fc * D + (m + 1) * 128],
                                  hs[:, fc * BW:(fc + 1) * BW],
                                  start=(fc == 0), stop=(fc == 3))
                p.inc("s_pe", i)
                ve.wait_ge(p.s_pe, p.semv["s_pe"])
                if m == 0:
                    if q >= 1:
                        p.dma_wait(ve, ldms[blk])
                    elif blk >= 2:
                        p.dma_wait(ve, st_mark[blk - 2])
                mw = slice(m * BW, (m + 1) * BW)
                if q == 0:
                    p.inc("s_ve", ve.tensor_copy(sh[bi][:, mw], dsp[:]))
                else:
                    p.inc("s_ve", ve.scalar_tensor_tensor(
                        sh[bi][:, mw], dsp[:], 1.0, sh[bi][:, mw],
                        AL.mult, AL.add))
            pe_mark[blk] = p.semv["s_pe"]
            sc.wait_ge(p.s_ve, p.semv["s_ve"])
            stok = p.dma_open(sc)
            p.dma_put(stok, sc.dma_start(
                out=dram["sh_sp"][:, :, tw].rearrange("k p w -> p k w"),
                in_=sh[bi][:].rearrange("p (k w) -> p k w", k=8)))
            st_mark[blk] = p.dma_close(stok)
            issue_load(blk + 2)
        prev_pe, prev_st = pe_mark, st_mark
        buf_mark[b] = p.semv["s_pe"]
        issue_wq(q + 2)


def ph4b(p, st):
    nc = p.nc
    sy, te, ve, ac, sc = nc.sync, nc.tensor, nc.vector, nc.scalar, nc.scalar
    dram = p.dram

    wkey = [p.sb(f"pb_wkey{i}", [128, 8 * 1024], stack=st) for i in range(2)]
    wval = [p.sb(f"pb_wval{i}", [128, 8 * 1024], stack=st) for i in range(2)]
    xnb = [p.sb(f"pb_xn{i}", [128, 8 * BW], stack=st) for i in range(2)]
    relu = p.sb("pb_relu", [128, BW], stack=st)
    kk = p.sb("pb_kk", [128, 8 * BW], stack=st)
    vv = p.sb("pb_vv", [128, 8 * BW], stack=st)

    kkp = p.ps("pb_kkp", [128, BW], stack=st)
    vvp = p.ps("pb_vvp", [128, BW], stack=st)

    wq_tok = {}
    buf_mark = [0, 0]

    def issue_wq(q):
        if q >= 4:
            return
        b = q % 2
        if buf_mark[b] > 0:
            sy.wait_ge(p.s_pe, buf_mark[b])
        tok = p.dma_open(sy)
        p.dma_put(tok, sy.dma_start(
            out=wkey[b][:].rearrange("p (k f) -> p k f", k=8),
            in_=dram["wkey_t"].rearrange("(k p) f -> p k f", p=128)[
                :, :, q * 1024:(q + 1) * 1024]))
        p.dma_put(tok, sy.dma_start(
            out=wval[b][:].rearrange("p (k e) -> p k e", k=8),
            in_=dram["wval_t"].rearrange("(k p) e -> p k e", p=128)[
                :, q * 8:(q + 1) * 8]))
        wq_tok[q] = p.dma_close(tok)

    issue_wq(0)
    issue_wq(1)

    prev_pe = prev_st = None
    for q in range(4):
        b = q % 2
        st_mark = {}
        pe_mark = {}
        ldms = {}
        vvtok = {}

        def issue_load(blk2, q=q, prev_pe=prev_pe, prev_st=prev_st):
            if blk2 >= NBO:
                return
            bi = blk2 % 2
            tw2 = slice(blk2 * BW, (blk2 + 1) * BW)
            if blk2 >= 2:
                sy.wait_ge(p.s_pe, pe_mark[blk2 - 2])
            elif prev_pe is not None:
                sy.wait_ge(p.s_pe, prev_pe[NBO - 2 + blk2])
            tok = p.dma_open(sy)
            for k in range(8):
                p.dma_put(tok, sy.dma_start(
                    out=xnb[bi][:, k * BW:(k + 1) * BW],
                    in_=dram["xn3_sp"][k, :, tw2]))
            ldms[blk2] = p.dma_close(tok)

        def issue_vv(blk2, q=q, prev_st=prev_st):
            if blk2 >= NBO or q == 0:
                return
            tw2 = slice(blk2 * BW, (blk2 + 1) * BW)
            if blk2 >= 1:
                p.dma_wait(sy, st_mark[blk2 - 1])
            elif prev_st is not None:
                p.dma_wait(sy, prev_st[NBO - 1])
            tok = p.dma_open(sy)
            p.dma_put(tok, sy.dma_start(
                out=vv[:].rearrange("p (k w) -> p k w", k=8),
                in_=dram["vvh_sp"][:, :, tw2].rearrange("k p w -> p k w")))
            vvtok[blk2] = p.dma_close(tok)

        issue_load(0)
        issue_load(1)
        issue_vv(0)
        for blk in range(NBO):
            bi = blk % 2
            tw = slice(blk * BW, (blk + 1) * BW)
            p.dma_wait(te, ldms[blk])
            p.dma_wait(te, wq_tok[q])
            for fc in range(8):
                te.wait_ge(p.s_ve, p.semv["s_ve"])
                for k in range(8):
                    i = te.matmul(kkp[:],
                                  wkey[b][:, k * 1024 + fc * 128:k * 1024 + (fc + 1) * 128],
                                  xnb[bi][:, k * BW:(k + 1) * BW],
                                  start=(k == 0), stop=(k == 7))
                p.inc("s_pe", i)
                ac.wait_ge(p.s_pe, p.semv["s_pe"])
                p.inc("s_ac", ac.activation(relu[:], kkp[:], AF.Relu))
                ve.wait_ge(p.s_ac, p.semv["s_ac"])
                p.inc("s_ve", ve.tensor_mul(kk[:, fc * BW:(fc + 1) * BW],
                                            relu[:], relu[:]))
            for m in range(8):
                te.wait_ge(p.s_ve, p.semv["s_ve"])
                for fc in range(8):
                    i = te.matmul(vvp[:],
                                  wval[b][:, fc * D + m * 128:fc * D + (m + 1) * 128],
                                  kk[:, fc * BW:(fc + 1) * BW],
                                  start=(fc == 0), stop=(fc == 7))
                p.inc("s_pe", i)
                ve.wait_ge(p.s_pe, p.semv["s_pe"])
                if m == 0:
                    if q >= 1:
                        p.dma_wait(ve, vvtok[blk])
                    elif blk >= 1:
                        p.dma_wait(ve, st_mark[blk - 1])
                mw = slice(m * BW, (m + 1) * BW)
                if q == 0:
                    p.inc("s_ve", ve.tensor_copy(vv[:, mw], vvp[:]))
                else:
                    p.inc("s_ve", ve.scalar_tensor_tensor(
                        vv[:, mw], vvp[:], 1.0, vv[:, mw],
                        AL.mult, AL.add))
            pe_mark[blk] = p.semv["s_pe"]
            sc.wait_ge(p.s_ve, p.semv["s_ve"])
            stok = p.dma_open(sc)
            p.dma_put(stok, sc.dma_start(
                out=dram["vvh_sp"][:, :, tw].rearrange("k p w -> p k w"),
                in_=vv[:].rearrange("p (k w) -> p k w", k=8)))
            st_mark[blk] = p.dma_close(stok)
            issue_load(blk + 2)
            issue_vv(blk + 1)
        prev_pe, prev_st = pe_mark, st_mark
        buf_mark[b] = p.semv["s_pe"]
        issue_wq(q + 2)


def ph4c(p, st, *, ident):
    nc = p.nc
    sy, te, ve, ac, sc = nc.sync, nc.tensor, nc.vector, nc.scalar, nc.scalar
    dram = p.dram

    wrec = p.sb("pc_wrec", [128, 8 * D], stack=st)
    wtok = p.dma_open(sy)
    p.dma_put(wtok, sy.dma_start(
        out=wrec[:].rearrange("p (k e) -> p k e", k=8),
        in_=dram["wrec_t"].rearrange("(k p) e -> p k e", p=128)))
    w_ld = p.dma_close(wtok)

    xn3b = p.sb("pc_xn3", [128, 8 * BW], stack=st)
    vvb = p.sb("pc_vv", [128, 8 * BW], stack=st)
    y1b = p.sb("pc_y1", [128, 8 * BW], stack=st)
    shb = p.sb("pc_sh", [128, 8 * BW], stack=st)
    xbb = p.sb("pc_xb", [128, 8 * BW], stack=st)
    rr = p.sb("pc_rr", [128, 8 * BW], stack=st)
    mrow = p.sb("pc_mrow", [128, D], stack=st)
    orow = p.sb("pc_orow", [128, D], stack=st)

    mm_ps = [p.ps(f"pc_mm{i}", [128, BW], stack=st) for i in range(2)]
    trp = p.ps("pc_trp", [128, 1024], stack=st)

    xr = dram["x_own_fm"].rearrange("(k p) t -> k p t", p=128)
    outr = dram["out"].rearrange("(n p) d -> n p d", p=128)
    maccr = dram["macc"][0:HALF].rearrange("(n p) d -> n p d", p=128)

    for blk in range(NBO):
        tw = slice(blk * BW, (blk + 1) * BW)
        if blk >= 1:
            sy.wait_ge(p.s_ve, p.semv["s_ve"])
        tok = p.dma_open(sy)
        for k in range(8):
            p.dma_put(tok, sy.dma_start(out=xbb[:, k * BW:(k + 1) * BW],
                                        in_=xr[k, :, tw]))
        for nm, tile in (("xn3_sp", xn3b), ("vvh_sp", vvb),
                         ("y1_sp", y1b), ("sh_sp", shb)):
            p.dma_put(tok, sy.dma_start(
                out=tile[:].rearrange("p (k w) -> p k w", k=8),
                in_=dram[nm][:, :, tw].rearrange("k p w -> p k w")))
        ldm = p.dma_close(tok)
        if blk == 0:
            p.dma_wait(te, w_ld)

        p.dma_wait(te, ldm)
        for m in range(8):
            mw = slice(m * BW, (m + 1) * BW)
            pb = mm_ps[m % 2]
            if m >= 2:
                te.wait_ge(p.s_ac, p.semv["s_ac"])
            for k in range(8):
                i = te.matmul(pb[:],
                              wrec[:, k * D + m * 128:k * D + m * 128 + 128],
                              xn3b[:, k * BW:(k + 1) * BW],
                              start=(k == 0), stop=(k == 7))
            p.inc("s_pe", i)
            ac.wait_ge(p.s_pe, p.semv["s_pe"])
            p.inc("s_ac", ac.activation(rr[:, mw], pb[:], AF.Sigmoid))

        ve.wait_ge(p.s_ac, p.semv["s_ac"])
        p.dma_wait(ve, ldm)
        ve.tensor_mul(rr[:], rr[:], vvb[:])
        ve.tensor_add(rr[:], rr[:], y1b[:])
        ve.tensor_add(rr[:], rr[:], shb[:])
        p.inc("s_ve", ve.tensor_add(rr[:], rr[:], xbb[:]))
        base_ve = p.semv["s_ve"]

        te.wait_ge(p.s_ve, base_ve)
        ot_prev = None
        for t4 in range(4):
            n = blk * 4 + t4
            sy.wait_ge(p.s_ve, p.semv["s_ve"])
            mtok = p.dma_open(sy)
            p.dma_put(mtok, sy.dma_start(out=mrow[:], in_=maccr[n]))
            p.dma_close(mtok)
            if t4 > 0:
                te.wait_ge(p.s_ve, p.semv["s_ve"])
            for k in range(8):
                i = te.transpose(trp[:, k * 128:(k + 1) * 128],
                                 rr[:, k * BW + t4 * 128:k * BW + t4 * 128 + 128],
                                 ident[:])
            p.inc("s_pe", i)
            ve.wait_ge(p.s_pe, p.semv["s_pe"])
            p.dma_wait(ve, mtok)
            if ot_prev is not None:
                p.dma_wait(ve, ot_prev)
            p.inc("s_ve", ve.scalar_tensor_tensor(
                orow[:], trp[:], 1.0, mrow[:], AL.mult, AL.add))
            sc.wait_ge(p.s_ve, p.semv["s_ve"])
            otok = p.dma_open(sc)
            p.dma_put(otok, sc.dma_start(out=outr[n], in_=orow[:]))
            ot_prev = p.dma_close(otok)

    p.dma_wait_all(sc)
    p.dma_wait_all(sy)
    p.wait(nc.gpsimd, "s_gp")


def make_inmaps(inputs):
    x = np.ascontiguousarray(inputs["L0_attn_x"], np.float32)

    def tr(a):
        return np.ascontiguousarray(np.asarray(a, np.float32).T)

    def chunks8(v):
        return np.ascontiguousarray(np.asarray(v, np.float32).reshape(8, 128).T)

    shared = {
        "wr_t": tr(inputs["L0_attn_Wr"]),
        "ww_t": tr(inputs["L0_attn_Ww"]),
        "wk_t": tr(inputs["L0_attn_Wk"]),
        "wv_t": tr(inputs["L0_attn_Wv"]),
        "wo_t": tr(inputs["L0_attn_Wo"]),
        "wrec_t": tr(inputs["L0_attn_Wrec"]),
        "wkey_t": tr(inputs["L0_attn_Wkey"]),
        "wval_t": tr(inputs["L0_attn_Wval"]),
        "wsg_t": tr(inputs["L0_ffn_Ws_g"]),
        "wsu_t": tr(inputs["L0_ffn_Ws_u"]),
        "wsd_t": tr(inputs["L0_ffn_Ws_d"]),
        "gate_t": tr(inputs["L0_ffn_gate_w"]),
        "wg_t": np.ascontiguousarray(
            np.asarray(inputs["L0_ffn_Wg"], np.float32).transpose(0, 2, 1)),
        "wu_t": np.ascontiguousarray(
            np.asarray(inputs["L0_ffn_Wu"], np.float32).transpose(0, 2, 1)),
        "wd_t": np.ascontiguousarray(
            np.asarray(inputs["L0_ffn_Wd"], np.float32).transpose(0, 2, 1)),
        "ln1g": chunks8(inputs["L0_attn_ln1_g"]),
        "ln1b": chunks8(inputs["L0_attn_ln1_b"]),
        "ln2g": chunks8(inputs["L0_attn_ln2_g"]),
        "ln2b": chunks8(inputs["L0_attn_ln2_b"]),
        "td": chunks8(inputs["L0_attn_time_decay"]),
        "ident": np.eye(128, dtype=np.float32),
        "tri": np.ascontiguousarray(
            np.triu(np.ones((128, 128), np.float32), 1)),
        "onesc": np.ones((128, 1), np.float32),
        "onesr": np.ones((1, 128), np.float32),
        "tokidG": np.ascontiguousarray(
            (np.arange(CC)[None, :] * 128 + np.arange(128)[:, None] + 1
             ).astype(np.float32)),
    }
    maps = []
    for c in range(8):
        b, h = c // 2, c % 2
        xb = x[b]
        own = xb[h * HALF:(h + 1) * HALF]
        m = dict(shared)
        m["x_fm"] = np.ascontiguousarray(xb.T)
        m["x_own_tm"] = np.ascontiguousarray(own)
        m["x_own_fm"] = np.ascontiguousarray(own.T)
        mc = np.zeros((128, CC), np.float32)
        mc[:, h * CCH:(h + 1) * CCH] = 1.0
        m["maskc"] = mc
        m["hseg"] = np.array([[h * CCH]], np.int32)
        m["selinit"] = np.concatenate([np.zeros((128, E), np.float32),
                                       np.ones((128, E), np.float32)], 1)
        m["hofft1"] = np.full((128, 1), h * HALF + 1, np.float32)
        m["hofft2"] = np.full((128, 1), HALF + h * HALF + 1, np.float32)
        maps.append(m)
    return maps


def kernel(**inputs):
    prog = build()
    maps = make_inmaps(inputs)
    res = run_bass_kernel_spmd(prog.nc, maps, core_ids=list(range(8)))
    out = np.empty((4, T, D), np.float32)
    for c in range(8):
        b, h = c // 2, c % 2
        out[b, h * HALF:(h + 1) * HALF] = res.results[c]["out"]
    return out
